# version 03
# baseline (speedup 1.0000x reference)
import numpy as np
import ml_dtypes
from contextlib import ExitStack
import concourse.bass as bass
import concourse.mybir as mybir
from concourse.bass_utils import run_bass_kernel_spmd

F32 = mybir.dt.float32
BF16 = mybir.dt.bfloat16
I32 = mybir.dt.int32
AF = mybir.ActivationFunctionType
ALU = mybir.AluOpType
AX = mybir.AxisListType.X
BF = ml_dtypes.bfloat16

B, S, D = 16, 2048, 1024
NCORE = 8
SPC = B // NCORE
T = SPC * S
TG = 512
NG = T // TG
GPS = S // TG
DC = D // 128
IC0 = 2816 // 128
IC1 = 3072 // 128
MAGIC = float(3 * 2.0 ** 22)
LN3 = 1.0986122886681098
QMAX = 127.0

ENGINES = ("sync", "scalar", "vector", "tensor", "gpsimd")


class Ctx:
    def __init__(self, nc, es):
        self.nc = nc
        self.es = es
        self.emit = {e: [] for e in ENGINES}
        self._sems = {}
        self.cnt = {}
        self.waits = []
        self.nbarrier = 0

    def sem(self, name):
        if name not in self._sems:
            self._sems[name] = self.es.enter_context(self.nc.semaphore(name))
            self.cnt[name] = 0
        return self._sems[name]

    def bump(self, name, n):
        self.sem(name)
        self.cnt[name] += n
        return self.cnt[name]

    def sbuf(self, name, shape, dt, scope=None):
        return (scope or self.es).enter_context(
            self.nc.sbuf_tensor(name, list(shape), dt))

    def psum(self, name, shape, dt, scope=None):
        return (scope or self.es).enter_context(
            self.nc.psum_tensor(name, list(shape), dt))

    def dma(self, q, out, in_, sem="LD"):
        s = self.sem(sem)
        v = self.bump(sem, 16)
        self.emit[q].append(
            lambda e, out=out, in_=in_, s=s: e.dma_start(out=out, in_=in_).then_inc(s, 16))
        return v

    def wait(self, eng, sem, val):
        if val <= 0:
            return
        s = self.sem(sem)
        self.waits.append((sem, val))
        self.emit[eng].append(lambda e, s=s, val=val: e.wait_ge(s, val))

    def op(self, eng, fn, sem=None, n=1):
        if sem is not None:
            s = self.sem(sem)
            v = self.bump(sem, n)
            self.emit[eng].append(lambda e, fn=fn, s=s, n=n: fn(e).then_inc(s, n))
            return v
        self.emit[eng].append(fn)
        return None

    def drain(self, eng="vector"):
        self.emit[eng].append(lambda e: e.drain())

    def barrier(self):
        self.nbarrier += 1
        s = self.sem("PH")
        tgt = len(ENGINES) * self.nbarrier
        self.cnt["PH"] = tgt
        for eng in ENGINES:
            self.emit[eng].append(lambda e, s=s: e.sem_inc(s, 1))
            self.emit[eng].append(lambda e, s=s, tgt=tgt: e.wait_ge(s, tgt))

    def drain_dma(self, sems):
        for sname in sems:
            if sname in self.cnt:
                self.wait("sync", sname, self.cnt[sname])

    def build_block(self):
        for sem_name, val in self.waits:
            assert val <= self.cnt[sem_name], (sem_name, val, self.cnt[sem_name])
        block = self.es.enter_context(self.nc.Block())
        for eng in ENGINES:
            if self.emit[eng]:
                def runner(e, eng=eng):
                    for f in self.emit[eng]:
                        f(e)
                getattr(block, eng)(runner)


def alloc_common(ctx):
    t = {}
    t["stage"] = ctx.sbuf("t_stage", [128, DC * TG], F32)
    t["xb"] = ctx.sbuf("t_xb", [128, DC * TG], BF16)
    t["sq"] = ctx.sbuf("t_sq", [128, DC * TG], BF16)
    t["rowf"] = ctx.sbuf("t_rowf", [1, TG], F32)
    t["rowf2"] = ctx.sbuf("t_rowf2", [1, TG], F32)
    t["rowf3"] = ctx.sbuf("t_rowf3", [1, TG], F32)
    t["hrow"] = ctx.sbuf("t_hrow", [1, 4 * TG], BF16)
    t["mbf"] = ctx.sbuf("t_mbf", [128, TG], F32)
    t["rbf"] = ctx.sbuf("t_rbf", [128, TG], F32)
    t["amax"] = ctx.sbuf("t_amax", [128, 1], F32)
    t["ones"] = ctx.sbuf("t_ones", [128, 128], BF16)
    t["idf"] = ctx.sbuf("t_idf", [128, 128], F32)
    t["idb"] = ctx.sbuf("t_idb", [128, 128], BF16)
    t["scl"] = ctx.sbuf("t_scl", [1, 32], F32)
    t["sclb"] = ctx.sbuf("t_sclb", [1, 32], BF16)
    t["col1"] = ctx.sbuf("t_col1", [128, 1], F32)
    t["col2"] = ctx.sbuf("t_col2", [128, 1], F32)
    t["xout"] = [ctx.sbuf(f"t_xo{i}", [128, TG], F32) for i in range(2)]
    t["ep1"] = ctx.sbuf("t_ep1", [128, TG], F32)
    t["ps"] = [ctx.psum(f"t_ps{i}", [128, TG], F32) for i in range(6)]
    t["psb"] = [ctx.psum(f"t_psb{i}", [128, 2 * TG], BF16) for i in range(2)]
    return t


def emit_ln_group(ctx, t, g, want_lnbf=True, want_absmax=False, eps=1e-5, lnbf_dst=None, lnbf_g=0, want_apply=True):
    V, SC, MM = "VE", "SC", "MM"
    ps_s0, ps_s1, ps_b0, ps_b1 = t["ps"][0], t["ps"][1], t["ps"][2], t["ps"][3]
    rowf, rowf2, rowf3, hrow = t["rowf"], t["rowf2"], t["rowf3"], t["hrow"]
    for c in range(DC):
        ctx.op("vector", lambda e, c=c: e.tensor_copy(
            out=t["xb"][:, c * TG:(c + 1) * TG], in_=t["stage"][:, c * TG:(c + 1) * TG]), None)
        ctx.op("vector", lambda e, c=c: e.tensor_tensor(
            out=t["sq"][:, c * TG:(c + 1) * TG], in0=t["xb"][:, c * TG:(c + 1) * TG],
            in1=t["xb"][:, c * TG:(c + 1) * TG], op=ALU.mult),
            V if c == DC - 1 else None)
    v_cast = ctx.cnt[V]
    ctx.wait("tensor", V, v_cast)
    for c in range(DC):
        ctx.op("tensor", lambda e, c=c: e.matmul(
            out=ps_s0[:1, :], lhsT=t["ones"][:, :1], rhs=t["xb"][:, c * TG:(c + 1) * TG],
            start=(c == 0), stop=(c == DC - 1)), None)
    for c in range(DC):
        ctx.op("tensor", lambda e, c=c: e.matmul(
            out=ps_s1[:1, :], lhsT=t["ones"][:, :1], rhs=t["sq"][:, c * TG:(c + 1) * TG],
            start=(c == 0), stop=(c == DC - 1)), MM if c == DC - 1 else None)
    v_stat = ctx.cnt[MM]
    ctx.wait("vector", MM, v_stat)
    ctx.op("vector", lambda e: e.tensor_scalar(
        out=rowf[:], in0=ps_s0[:1, :], scalar1=1.0 / D, scalar2=None, op0=ALU.mult), None)
    ctx.op("vector", lambda e: e.tensor_tensor(
        out=rowf2[:], in0=rowf[:], in1=rowf[:], op=ALU.mult), None)
    ctx.op("vector", lambda e: e.scalar_tensor_tensor(
        out=rowf3[:], in0=ps_s1[:1, :], scalar=1.0 / D, in1=rowf2[:],
        op0=ALU.mult, op1=ALU.subtract), None)
    v_var = ctx.op("vector", lambda e, eps=eps: e.tensor_scalar(
        out=rowf3[:], in0=rowf3[:], scalar1=eps, scalar2=None, op0=ALU.add), V)
    ctx.wait("scalar", V, v_var)
    s_sq = ctx.op("scalar", lambda e: e.activation(
        out=rowf2[:], in_=rowf3[:], func=AF.Sqrt), SC)
    ctx.wait("vector", SC, s_sq)
    ctx.op("vector", lambda e: e.reciprocal(out=rowf3[:], in_=rowf2[:]), None)
    ctx.op("vector", lambda e: e.tensor_copy(out=hrow[:, 0:TG], in_=rowf[:]), None)
    ctx.op("vector", lambda e: e.tensor_tensor(
        out=hrow[:, TG:2 * TG], in0=rowf[:], in1=hrow[:, 0:TG], op=ALU.subtract), None)
    ctx.op("vector", lambda e: e.tensor_copy(out=hrow[:, 2 * TG:3 * TG], in_=rowf3[:]), None)
    v_r = ctx.op("vector", lambda e: e.tensor_tensor(
        out=hrow[:, 3 * TG:4 * TG], in0=rowf3[:], in1=hrow[:, 2 * TG:3 * TG],
        op=ALU.subtract), V)
    ctx.wait("tensor", V, v_r)
    ctx.op("tensor", lambda e: e.matmul(
        out=ps_b0[:], lhsT=t["ones"][:1, :], rhs=hrow[:, 0:TG], start=True, stop=True), None)
    v_m = ctx.op("tensor", lambda e: e.matmul(
        out=ps_b1[:], lhsT=t["ones"][:1, :], rhs=hrow[:, TG:2 * TG], start=True, stop=True), MM)
    ctx.wait("scalar", MM, v_m)
    s_m = ctx.op("scalar", lambda e: e.activation(
        out=t["mbf"][:], in_=ps_b0[:], func=AF.Copy), SC)
    ctx.wait("vector", SC, s_m)
    v_mb = ctx.op("vector", lambda e: e.tensor_tensor(
        out=t["mbf"][:], in0=t["mbf"][:], in1=ps_b1[:], op=ALU.add), V)
    ctx.wait("tensor", V, v_mb)
    ctx.op("tensor", lambda e: e.matmul(
        out=ps_b0[:], lhsT=t["ones"][:1, :], rhs=hrow[:, 2 * TG:3 * TG],
        start=True, stop=True), None)
    v_rr = ctx.op("tensor", lambda e: e.matmul(
        out=ps_b1[:], lhsT=t["ones"][:1, :], rhs=hrow[:, 3 * TG:4 * TG],
        start=True, stop=True), MM)
    ctx.wait("scalar", MM, v_rr)
    s_r = ctx.op("scalar", lambda e: e.activation(
        out=t["rbf"][:], in_=ps_b0[:], func=AF.Copy), SC)
    ctx.wait("vector", SC, s_r)
    ctx.op("vector", lambda e: e.tensor_tensor(
        out=t["rbf"][:], in0=t["rbf"][:], in1=ps_b1[:], op=ALU.add), None)
    if not want_apply:
        return ctx.op("vector", lambda e: e.tensor_copy(
            out=t["rowf"][:, :1], in_=t["rowf"][:, :1]), "VE")
    for c in range(DC):
        ctx.op("vector", lambda e, c=c: e.tensor_tensor(
            out=t["ep1"][:], in0=t["stage"][:, c * TG:(c + 1) * TG],
            in1=t["mbf"][:], op=ALU.subtract), None)
        ctx.op("vector", lambda e, c=c: e.tensor_tensor(
            out=t["lnf"][:, c * TG:(c + 1) * TG], in0=t["ep1"][:], in1=t["rbf"][:],
            op=ALU.mult), None)
        if want_lnbf:
            dst = (t["lnbf"][:, c * TG:(c + 1) * TG] if lnbf_dst is None else
                   lnbf_dst[:, c * T + lnbf_g * TG: c * T + (lnbf_g + 1) * TG])
            ctx.op("vector", lambda e, c=c, dst=dst: e.tensor_copy(
                out=dst, in_=t["lnf"][:, c * TG:(c + 1) * TG]), None)
        if want_absmax:
            ctx.op("vector", lambda e, c=c: e.tensor_reduce(
                out=t["col1"][:], in_=t["lnf"][:, c * TG:(c + 1) * TG], axis=AX,
                op=ALU.max, apply_absolute_value=True), None)
            ctx.drain()
            if g == 0 and c == 0:
                ctx.op("vector", lambda e: e.tensor_copy(
                    out=t["amax"][:], in_=t["col1"][:]), None)
            else:
                ctx.op("vector", lambda e: e.tensor_tensor(
                    out=t["amax"][:], in0=t["amax"][:], in1=t["col1"][:],
                    op=ALU.max), None)
            ctx.drain()
    return ctx.op("vector", lambda e: e.tensor_copy(
        out=t["rowf"][:, :1], in_=t["rowf"][:, :1]), V)


def load_stage(ctx, t, src_ap, g, guard_v=0):
    ctx.wait("sync", "VE", guard_v)
    v = 0
    for c in range(DC):
        v = ctx.dma("sync", t["stage"][:, c * TG:(c + 1) * TG],
                    src_ap[c * 128:(c + 1) * 128, g * TG:(g + 1) * TG], "LD")
    return v


def bcast_scalar_col(ctx, t, src11, dst_col):
    V, SC, MM = "VE", "SC", "MM"
    sb = t["sclb"]
    ctx.op("vector", lambda e: e.tensor_copy(out=sb[:, 0:1], in_=src11), None)
    ctx.drain()
    v1 = ctx.op("vector", lambda e: e.tensor_tensor(
        out=sb[:, 1:2], in0=src11, in1=sb[:, 0:1], op=ALU.subtract), V)
    ctx.drain()
    ctx.wait("tensor", V, v1)
    ctx.op("tensor", lambda e: e.matmul(
        out=t["ps"][2][:, 0:1], lhsT=t["ones"][:1, :], rhs=sb[:, 0:1],
        start=True, stop=True), None)
    v2 = ctx.op("tensor", lambda e: e.matmul(
        out=t["ps"][3][:, 0:1], lhsT=t["ones"][:1, :], rhs=sb[:, 1:2],
        start=True, stop=True), MM)
    ctx.wait("scalar", MM, v2)
    s1 = ctx.op("scalar", lambda e: e.activation(
        out=dst_col, in_=t["ps"][2][:, 0:1], func=AF.Copy), SC)
    ctx.wait("vector", SC, s1)
    r = ctx.op("vector", lambda e: e.tensor_tensor(
        out=dst_col, in0=dst_col, in1=t["ps"][3][:, 0:1], op=ALU.add), V)
    ctx.drain()
    return r



def emit_phase0(ctx, t, d):
    t = dict(t)
    V, SC, MM, LD, ST = "VE", "SC", "MM", "LD", "ST"
    es = ExitStack()
    t["lnf"] = ctx.sbuf("p0_lnf", [128, DC * TG], F32, es)
    t["lnbf"] = ctx.sbuf("p0_lnbf", [128, DC * TG], BF16, es)
    wd_bf = ctx.sbuf("p0_wd", [128, DC * D], BF16, es)
    wb_bf = ctx.sbuf("p0_wb", [128, 128], BF16, es)
    wo_bf = ctx.sbuf("p0_wo", [16, D], BF16, es)
    wstage = ctx.sbuf("p0_wst", [128, D], F32, es)
    Atmp = ctx.sbuf("p0_At", [16, 1], F32, es)
    bdelta = ctx.sbuf("p0_bd", [128, 8], F32, es)
    bB = ctx.sbuf("p0_bB", [16, 1], F32, es)
    Acol = ctx.sbuf("p0_Ac", [16, 1], F32, es)
    Dcol = ctx.sbuf("p0_Dc", [128, 8], F32, es)
    bout = ctx.sbuf("p0_bo", [128, 8], F32, es)
    sg = [ctx.sbuf(f"p0_sg{i}", [128, TG], BF16, es) for i in range(2)]
    delta_bf = ctx.sbuf("p0_deltab", [1, TG], BF16, es)
    mh_bf = ctx.sbuf("p0_mh", [1, TG], BF16, es)
    dm_bf = ctx.sbuf("p0_dm", [1, TG], BF16, es)
    Bm = ctx.sbuf("p0_Bm", [16, TG], BF16, es)
    ab_t = ctx.sbuf("p0_ab", [16, TG], BF16, es)
    bu_t = ctx.sbuf("p0_bu", [16, TG], BF16, es)
    st_bf = ctx.sbuf("p0_st", [16, TG], BF16, es)
    carry = ctx.sbuf("p0_carry", [16, 1], BF16, es)
    ctx.dma("sync", t["ones"][:], d["ones_bf"].ap(), LD)
    ctx.dma("sync", bdelta[:], d["bdelta8"].ap(), LD)
    ctx.dma("sync", bB[:], d["bB"].ap(), LD)
    ctx.dma("sync", Acol[:], d["Alog"].ap(), LD)
    ctx.dma("sync", Dcol[:], d["Dcol8"].ap(), LD)
    ctx.dma("sync", bout[:], d["bout8"].ap(), LD)
    ctx.dma("sync", wstage[:, :128], d["WbT128"].ap(), LD)
    ctx.dma("sync", t["idf"][:], d["ident_f"].ap(), LD)
    ctx.dma("sync", t["idb"][:], d["ident_b"].ap(), LD)
    v = ctx.dma("sync", t["lnf"][:16, :D], d["WoT"].ap(), LD)
    ctx.wait("scalar", LD, v)
    ctx.op("scalar", lambda e: e.activation(out=Atmp[:], in_=Acol[:], func=AF.Exp), None)
    ctx.drain("scalar")
    ctx.op("scalar", lambda e: e.activation(
        out=Acol[:], in_=Atmp[:], func=AF.Copy, scale=-1.0), SC)
    ctx.drain("scalar")
    ctx.wait("vector", LD, v)
    ctx.op("vector", lambda e: e.tensor_copy(out=wb_bf[:], in_=wstage[:, :128]), None)
    ctx.op("vector", lambda e: e.tensor_copy(out=wo_bf[:], in_=t["lnf"][:16, :D]), V)
    for c in range(DC):
        ctx.wait("sync", V, ctx.cnt[V])
        v = ctx.dma("sync", wstage[:, :D], d["WdT"].ap()[c * 128:(c + 1) * 128, :], "WL")
        ctx.wait("vector", "WL", v)
        ctx.op("vector", lambda e, c=c: e.tensor_copy(
            out=wd_bf[:, c * D:(c + 1) * D], in_=wstage[:, :D]), V)

    last_ep_v = ctx.cnt[V]
    ctx.sem(ST)
    st_base = ctx.cnt[ST]
    for g in range(NG):
        lv = load_stage(ctx, t, d["x0T"].ap(), g, guard_v=last_ep_v)
        ctx.wait("vector", LD, lv)
        v_ln = emit_ln_group(ctx, t, g, want_lnbf=True, want_absmax=False)
        ctx.wait("tensor", V, v_ln)
        for ec in range(8):
            pm = t["ps"][4 + ec % 2]
            for k in range(DC):
                ctx.op("tensor", lambda e, ec=ec, k=k, pm=pm: e.matmul(
                    out=pm[:], lhsT=wd_bf[:, k * D + ec * 128: k * D + (ec + 1) * 128],
                    rhs=t["lnbf"][:, k * TG:(k + 1) * TG],
                    start=(k == 0), stop=(k == DC - 1)), MM if k == DC - 1 else None)
            mmv = ctx.cnt[MM]
            ctx.wait("scalar", MM, mmv)
            sv = ctx.op("scalar", lambda e, ec=ec, pm=pm: e.activation(
                out=sg[ec % 2][:], in_=pm[:], func=AF.Sigmoid,
                bias=bdelta[:, ec:ec + 1]), SC)
            ctx.wait("tensor", SC, sv)
            ctx.op("tensor", lambda e, ec=ec: e.matmul(
                out=t["ps"][0][:1, :], lhsT=t["ones"][:, :1], rhs=sg[ec % 2][:],
                start=(ec == 0), stop=(ec == 7)), MM if ec == 7 else None)
        v_dsum = ctx.cnt[MM]
        ctx.wait("vector", MM, v_dsum)
        v_d = ctx.op("vector", lambda e: e.tensor_scalar(
            out=delta_bf[:], in0=t["ps"][0][:1, :], scalar1=0.099 / D, scalar2=0.001,
            op0=ALU.mult, op1=ALU.add), V)
        ctx.wait("tensor", V, v_d)
        for k in range(DC):
            ctx.op("tensor", lambda e, k=k: e.matmul(
                out=t["ps"][1][:16, :], lhsT=wb_bf[:, k * 16:(k + 1) * 16],
                rhs=t["lnbf"][:, k * TG:(k + 1) * TG],
                start=(k == 0), stop=(k == DC - 1)), None)
        for k in range(DC):
            ctx.op("tensor", lambda e, k=k: e.matmul(
                out=t["ps"][0][:1, :], lhsT=t["ones"][:, :1],
                rhs=t["lnbf"][:, k * TG:(k + 1) * TG],
                start=(k == 0), stop=(k == DC - 1)), MM if k == DC - 1 else None)
        v_bmmh = ctx.cnt[MM]
        ctx.wait("scalar", MM, v_bmmh)
        s_bm = ctx.op("scalar", lambda e: e.activation(
            out=Bm[:], in_=t["ps"][1][:16, :], func=AF.Identity, bias=bB[:]), SC)
        ctx.wait("vector", MM, v_bmmh)
        ctx.op("vector", lambda e: e.tensor_scalar(
            out=mh_bf[:], in0=t["ps"][0][:1, :], scalar1=1.0 / D, scalar2=None,
            op0=ALU.mult), None)
        v_dm = ctx.op("vector", lambda e: e.tensor_tensor(
            out=dm_bf[:], in0=delta_bf[:], in1=mh_bf[:], op=ALU.mult), V)
        ctx.wait("tensor", V, v_dm)
        ctx.op("tensor", lambda e: e.matmul(
            out=t["ps"][2][:16, :], lhsT=t["ones"][:1, :16], rhs=delta_bf[:],
            start=True, stop=True), None)
        v_bc2 = ctx.op("tensor", lambda e: e.matmul(
            out=t["ps"][3][:16, :], lhsT=t["ones"][:1, :16], rhs=dm_bf[:],
            start=True, stop=True), MM)
        ctx.wait("scalar", MM, v_bc2)
        s_ab = ctx.op("scalar", lambda e: e.activation(
            out=ab_t[:], in_=t["ps"][2][:16, :], func=AF.Exp, scale=Acol[:]), SC)
        ctx.wait("vector", SC, s_ab)
        ctx.op("vector", lambda e: e.tensor_tensor(
            out=bu_t[:], in0=t["ps"][3][:16, :], in1=Bm[:], op=ALU.mult), None)
        if g % GPS == 0:
            ctx.op("vector", lambda e: e.tensor_tensor_scan(
                out=st_bf[:], data0=ab_t[:], data1=bu_t[:], initial=0.0,
                op0=ALU.mult, op1=ALU.add), None)
        else:
            ctx.op("vector", lambda e: e.tensor_tensor_scan(
                out=st_bf[:], data0=ab_t[:], data1=bu_t[:], initial=carry[:],
                op0=ALU.mult, op1=ALU.add), None)
        ctx.drain()
        v_scan = ctx.op("vector", lambda e: e.tensor_copy(
            out=carry[:], in_=st_bf[:, TG - 1:TG]), V)
        ctx.drain()
        ctx.wait("tensor", V, v_scan)
        prev_vv = None
        for c in range(DC):
            if prev_vv is not None:
                ctx.wait("tensor", V, prev_vv)
            mmv = ctx.op("tensor", lambda e, c=c: e.matmul(
                out=t["ps"][4][:], lhsT=wo_bf[:, c * 128:(c + 1) * 128],
                rhs=st_bf[:], start=True, stop=True), MM)
            j = g * DC + c
            if j >= 2:
                ctx.wait("vector", ST, st_base + 16 * (j - 1))
            ctx.wait("vector", MM, mmv)
            ctx.op("vector", lambda e, c=c: e.scalar_tensor_tensor(
                out=t["ep1"][:], in0=t["lnf"][:, c * TG:(c + 1) * TG],
                scalar=Dcol[:, c:c + 1], in1=t["stage"][:, c * TG:(c + 1) * TG],
                op0=ALU.mult, op1=ALU.add), None)
            vv = ctx.op("vector", lambda e, c=c, j=j: e.scalar_tensor_tensor(
                out=t["xout"][j % 2][:], in0=t["ps"][4][:], scalar=bout[:, c:c + 1],
                in1=t["ep1"][:], op0=ALU.add, op1=ALU.add), V)
            prev_vv = vv
            ctx.wait("gpsimd", V, vv)
            ctx.dma("gpsimd", d["x1T"].ap()[c * 128:(c + 1) * 128,
                                            g * TG:(g + 1) * TG],
                    t["xout"][j % 2][:], ST)
        last_ep_v = prev_vv
    ctx.drain_dma([ST, LD, "WL"])
    ctx.barrier()
    es.close()



def emit_phase1(ctx, t, d, load_consts=False, upto="G"):
    t = dict(t)
    V, SC, MM, LD, ST = "VE", "SC", "MM", "LD", "ST"
    cols = ctx.sbuf("p1_cols", [128, 8], F32)
    es = ExitStack()
    t["lnf"] = ctx.sbuf("p1_lnf", [128, DC * TG], F32, es)
    xint = ctx.sbuf("p1_xint", [128, DC * T], BF16, es)
    h2sl = [ctx.sbuf(f"p1_h2{i}", [128, T], F32, es) for i in range(2)]
    wnat = ctx.sbuf("p1_wnat", [128, 2 * D], F32, es)
    wTg = ctx.sbuf("p1_wTg", [128, DC * 128], F32, es)
    wTu = ctx.sbuf("p1_wTu", [128, DC * 128], F32, es)
    wqg = ctx.sbuf("p1_wqg", [128, DC * 128], BF16, es)
    wqu = ctx.sbuf("p1_wqu", [128, DC * 128], BF16, es)
    itmp = ctx.sbuf("p1_itmp", [128, 1024], I32, es)
    rinvg = ctx.sbuf("p1_rig", [128, 128], F32, es)
    rinvu = ctx.sbuf("p1_riu", [128, 128], F32, es)
    rcol2 = ctx.sbuf("p1_rc2", [128, 2], BF16, es)
    rrow2 = ctx.sbuf("p1_rr2", [1, 256], BF16, es)
    am2 = ctx.sbuf("p1_am2", [128, 1], F32, es)
    clipx = ctx.sbuf("p1_cx", [1, 2], F32, es)
    silu_t = [ctx.sbuf(f"p1_si{i}", [128, TG], F32, es) for i in range(2)]
    if load_consts:
        ctx.dma("sync", t["ones"][:], d["ones_bf"].ap(), LD)
        ctx.dma("sync", t["idf"][:], d["ident_f"].ap(), LD)
        ctx.dma("sync", t["idb"][:], d["ident_b"].ap(), LD)
    for nm in (ST, "H2S", "CC", "CD", "H2LA", "H2LB", "X2S", "WL", LD):
        ctx.sem(nm)
    if upto == "NOP":
        ctx.drain_dma([ST, LD, "WL"])
        ctx.barrier()
        es.close()
        return
    if upto == "LOAD1":
        lv = load_stage(ctx, t, d["x1T"].ap(), 0, guard_v=0)
        ctx.wait("vector", LD, lv)
        vv = ctx.op("vector", lambda e: e.tensor_copy(out=t["xb"][:, 0:TG], in_=t["stage"][:, 0:TG]), "VE")
        ctx.wait("gpsimd", "VE", vv)
        ctx.dma("gpsimd", d["x2T"].ap()[0:128, 0:TG], t["xb"][:, 0:TG], "X2S")
        ctx.drain_dma([ST, LD, "WL", "X2S"])
        ctx.barrier()
        es.close()
        return
    last_v = 0
    for g in range(NG):
        lv = load_stage(ctx, t, d["x1T"].ap(), g, guard_v=last_v)
        ctx.wait("vector", LD, lv)
        ctx.wait("vector", ST, ctx.cnt[ST])
        last_v = emit_ln_group(ctx, t, g, want_lnbf=False, want_absmax=True)
        ctx.wait("gpsimd", V, last_v)
        for c in range(DC):
            ctx.dma("gpsimd", d["lnfs"].ap()[c * 128:(c + 1) * 128, g * TG:(g + 1) * TG],
                    t["lnf"][:, c * TG:(c + 1) * TG], ST)
    if upto == "A":
        vv = ctx.op("vector", lambda e: e.tensor_copy(out=t["xout"][0][:], in_=t["stage"][:, 0:TG]), "VE")
        ctx.wait("gpsimd", "VE", vv)
        ctx.dma("gpsimd", d["x2T"].ap()[0:128, 0:TG], t["xout"][0][:], "X2S")
        ctx.drain_dma(["X2S", "ST", "LD", "WL", "H2S", "CC"])
        ctx.barrier()
        es.close()
        return

    ctx.dma("sync", clipx[:, 0:2], d["clip2"].ap(), LD)
    lv_c = ctx.cnt[LD]
    ctx.wait("tensor", V, last_v)
    mt = ctx.op("tensor", lambda e: e.transpose(
        out=t["ps"][4][:1, :128], in_=t["amax"][:], identity=t["idf"][:]), MM)
    ctx.wait("vector", MM, mt)
    vv = ctx.op("vector", lambda e: e.tensor_reduce(
        out=t["scl"][:, 0:1], in_=t["ps"][4][:1, :128], axis=AX, op=ALU.max), V)
    ctx.wait("gpsimd", V, vv)
    ctx.dma("gpsimd", d["cci"].ap(), t["scl"][:1, 0:1], "CC")
    ctx.wait("gpsimd", "CC", ctx.cnt["CC"])
    cv = ctx.op("gpsimd", lambda e: e.collective_compute(
        "AllReduce", ALU.max, replica_groups=[list(range(NCORE))],
        ins=[d["cci"].ap()], outs=[d["cco"].ap()]), "CD")
    ctx.wait("gpsimd", "CD", cv)
    ctx.dma("gpsimd", t["scl"][:1, 2:3], d["cco"].ap(), "CC")
    ctx.wait("scalar", "CC", ctx.cnt["CC"])
    ctx.wait("scalar", LD, lv_c)
    s_sig = ctx.op("scalar", lambda e: e.activation(
        out=t["scl"][:, 4:5], in_=clipx[:, 0:1], func=AF.Sigmoid), SC)
    ctx.wait("vector", SC, s_sig)
    ctx.wait("vector", "CC", ctx.cnt["CC"])
    ctx.op("vector", lambda e: e.tensor_tensor(
        out=t["scl"][:, 5:6], in0=t["scl"][:, 2:3], in1=t["scl"][:, 4:5], op=ALU.mult), None)
    ctx.drain()
    ctx.op("vector", lambda e: e.reciprocal(out=t["scl"][:, 6:7], in_=t["scl"][:, 5:6]), None)
    ctx.drain()
    ctx.op("vector", lambda e: e.tensor_scalar(
        out=t["scl"][:, 7:8], in0=t["scl"][:, 6:7], scalar1=QMAX, scalar2=None,
        op0=ALU.mult), None)
    ctx.drain()
    ctx.op("vector", lambda e: e.tensor_scalar(
        out=t["scl"][:, 8:9], in0=t["scl"][:, 5:6], scalar1=1.0 / QMAX, scalar2=None,
        op0=ALU.mult), V)
    ctx.drain()
    bcast_scalar_col(ctx, t, t["scl"][:1, 7:8], cols[:, 4:5])
    v_c = bcast_scalar_col(ctx, t, t["scl"][:1, 8:9], cols[:, 5:6])
    if upto == "Cdump":
        ctx.op("vector", lambda e: e.tensor_copy(out=t["xout"][0][:1, 0:32], in_=t["scl"][:]), None)
        ctx.op("vector", lambda e: e.tensor_copy(out=t["xout"][0][:, 32:40], in_=cols[:]), None)
        vv = ctx.op("vector", lambda e: e.tensor_copy(out=t["xout"][0][:, 40:41], in_=t["amax"][:]), "VE")
        ctx.wait("gpsimd", "VE", vv)
        ctx.dma("gpsimd", d["dbg"].ap()[0:128, 0:TG], t["xout"][0][:], "X2S")
        ctx.drain_dma(["X2S", "ST", "LD", "WL", "H2S", "CC"])
        ctx.barrier()
        es.close()
        return
    if upto == "C":
        vv = ctx.op("vector", lambda e: e.tensor_copy(out=t["xout"][0][:], in_=t["stage"][:, 0:TG]), "VE")
        ctx.wait("gpsimd", "VE", vv)
        ctx.dma("gpsimd", d["x2T"].ap()[0:128, 0:TG], t["xout"][0][:], "X2S")
        ctx.drain_dma(["X2S", "ST", "LD", "WL", "H2S", "CC"])
        ctx.barrier()
        es.close()
        return

    ctx.wait("sync", ST, ctx.cnt[ST])
    vq_last = ctx.cnt[V]
    for g in range(NG):
        ctx.wait("sync", V, vq_last)
        lv = 0
        for c in range(DC):
            lv = ctx.dma("sync", t["lnf"][:, c * TG:(c + 1) * TG],
                         d["lnfs"].ap()[c * 128:(c + 1) * 128, g * TG:(g + 1) * TG], LD)
        ctx.wait("vector", LD, lv)
        for c in range(DC):
            ctx.op("vector", lambda e, c=c: e.tensor_scalar(
                out=t["lnf"][:, c * TG:(c + 1) * TG], in0=t["lnf"][:, c * TG:(c + 1) * TG],
                scalar1=cols[:, 4:5], scalar2=MAGIC, op0=ALU.mult, op1=ALU.add), None)
            ctx.op("vector", lambda e, c=c: e.tensor_scalar(
                out=t["lnf"][:, c * TG:(c + 1) * TG], in0=t["lnf"][:, c * TG:(c + 1) * TG],
                scalar1=MAGIC, scalar2=-128.0, op0=ALU.subtract, op1=ALU.max), None)
            vq_last = ctx.op("vector", lambda e, c=c, g=g: e.tensor_scalar(
                out=xint[:, c * T + g * TG: c * T + (g + 1) * TG],
                in0=t["lnf"][:, c * TG:(c + 1) * TG],
                scalar1=127.0, scalar2=None, op0=ALU.min), V)

    if upto == "Ddump":
        for c in range(DC):
            vv = ctx.op("vector", lambda e, c=c: e.tensor_copy(out=t["xout"][0][:], in_=xint[:, c * T: c * T + TG]), "VE")
            ctx.wait("gpsimd", "VE", vv)
            ctx.dma("gpsimd", d["dbg"].ap()[c * 128:(c + 1) * 128, 0:TG], t["xout"][0][:], "X2S")
            ctx.wait("vector", "X2S", ctx.cnt["X2S"])
        ctx.drain_dma(["X2S", "ST", "LD", "WL", "H2S", "CC"])
        ctx.barrier()
        es.close()
        return
    if upto == "D":
        vv = ctx.op("vector", lambda e: e.tensor_copy(out=t["xout"][0][:], in_=t["stage"][:, 0:TG]), "VE")
        ctx.wait("gpsimd", "VE", vv)
        ctx.dma("gpsimd", d["x2T"].ap()[0:128, 0:TG], t["xout"][0][:], "X2S")
        ctx.drain_dma(["X2S", "ST", "LD", "WL", "H2S", "CC"])
        ctx.barrier()
        es.close()
        return
    h2_store_v = [0, 0]
    prev_quant_v = 0
    h2v = [0, 0]
    for i in range(IC0):
        ctx.wait("sync", V, prev_quant_v)
        ctx.wait("vector", "H2S", h2_store_v[i % 2])
        ctx.dma("sync", wnat[:, 0:D], d["wg_nat"].ap()[i * 128:(i + 1) * 128, :], "WL")
        ctx.dma("sync", wnat[:, D:2 * D], d["wu_nat"].ap()[i * 128:(i + 1) * 128, :], "WL")
        for c in range(DC):
            ctx.dma("sync", wTg[:, c * 128:(c + 1) * 128],
                    d["wgT"].ap()[c * 128:(c + 1) * 128, i * 128:(i + 1) * 128], "WL")
        for c in range(DC):
            ctx.dma("sync", wTu[:, c * 128:(c + 1) * 128],
                    d["wuT"].ap()[c * 128:(c + 1) * 128, i * 128:(i + 1) * 128], "WL")
        lv = ctx.cnt["WL"]
        ctx.wait("vector", "WL", lv)
        for w, natsl, wT, wq, rinvb, smax_i, esc_i in (
                (0, wnat[:, 0:D], wTg, wqg, rinvg, 0, 2),
                (1, wnat[:, D:2 * D], wTu, wqu, rinvu, 3, 6)):
            ctx.op("vector", lambda e, natsl=natsl, smax_i=smax_i: e.tensor_reduce(
                out=cols[:, smax_i:smax_i + 1], in_=natsl, axis=AX, op=ALU.max,
                apply_absolute_value=True), None)
            ctx.drain()
            ctx.op("vector", lambda e, smax_i=smax_i: e.reciprocal(
                out=t["col1"][:], in_=cols[:, smax_i:smax_i + 1]), None)
            ctx.drain()
            ctx.op("vector", lambda e: e.tensor_scalar(
                out=t["col2"][:], in0=t["col1"][:], scalar1=QMAX, scalar2=None,
                op0=ALU.mult), None)
            ctx.drain()
            ctx.op("vector", lambda e: e.tensor_copy(out=rcol2[:, 0:1], in_=t["col2"][:]), None)
            ctx.drain()
            v1 = ctx.op("vector", lambda e: e.tensor_tensor(
                out=rcol2[:, 1:2], in0=t["col2"][:], in1=rcol2[:, 0:1],
                op=ALU.subtract), V)
            ctx.drain()
            ctx.wait("tensor", V, v1)
            ctx.op("tensor", lambda e: e.transpose(
                out=t["psb"][0][:1, 0:128], in_=rcol2[:, 0:1], identity=t["idb"][:]), None)
            mt = ctx.op("tensor", lambda e: e.transpose(
                out=t["psb"][0][:1, 128:256], in_=rcol2[:, 1:2], identity=t["idb"][:]), MM)
            ctx.wait("vector", MM, mt)
            v2 = ctx.op("vector", lambda e: e.tensor_copy(
                out=rrow2[:], in_=t["psb"][0][:1, 0:256]), V)
            ctx.wait("tensor", V, v2)
            ctx.op("tensor", lambda e: e.matmul(
                out=t["ps"][2][:, :128], lhsT=t["ones"][:1, :], rhs=rrow2[:, 0:128],
                start=True, stop=True), None)
            v3 = ctx.op("tensor", lambda e: e.matmul(
                out=t["ps"][3][:, :128], lhsT=t["ones"][:1, :], rhs=rrow2[:, 128:256],
                start=True, stop=True), MM)
            ctx.wait("scalar", MM, v3)
            s1 = ctx.op("scalar", lambda e, rinvb=rinvb: e.activation(
                out=rinvb[:], in_=t["ps"][2][:, :128], func=AF.Copy), SC)
            ctx.wait("vector", SC, s1)
            ctx.op("vector", lambda e, rinvb=rinvb: e.tensor_tensor(
                out=rinvb[:], in0=rinvb[:], in1=t["ps"][3][:, :128], op=ALU.add), None)
            for c in range(DC):
                ctx.op("vector", lambda e, c=c, wT=wT, rinvb=rinvb: e.tensor_tensor(
                    out=itmp[:, c * 128:(c + 1) * 128], in0=wT[:, c * 128:(c + 1) * 128],
                    in1=rinvb[:], op=ALU.mult), None)
            for c in range(DC):
                ctx.op("vector", lambda e, c=c, wq=wq: e.tensor_copy(
                    out=wq[:, c * 128:(c + 1) * 128], in_=itmp[:, c * 128:(c + 1) * 128]),
                    None)
            ctx.op("vector", lambda e, smax_i=smax_i: e.tensor_tensor(
                out=t["col1"][:], in0=cols[:, smax_i:smax_i + 1], in1=cols[:, 5:6],
                op=ALU.mult), None)
            ctx.drain()
            prev_quant_v = ctx.op("vector", lambda e, esc_i=esc_i: e.tensor_scalar(
                out=cols[:, esc_i:esc_i + 1], in0=t["col1"][:], scalar1=1.0 / QMAX,
                scalar2=None, op0=ALU.mult), V)
            ctx.drain()
        ctx.wait("tensor", V, prev_quant_v)
        if i == 0:
            ctx.wait("tensor", V, vq_last)
        for tg in range(NG):
            pA = t["ps"][4] if tg % 2 == 0 else t["ps"][2]
            pB = t["ps"][5] if tg % 2 == 0 else t["ps"][3]
            if tg >= 2:
                ctx.wait("tensor", V, h2v[tg % 2])
            for k in range(DC):
                ctx.op("tensor", lambda e, k=k, tg=tg, pA=pA: e.matmul(
                    out=pA[:], lhsT=wqg[:, k * 128:(k + 1) * 128],
                    rhs=xint[:, k * T + tg * TG: k * T + (tg + 1) * TG],
                    start=(k == 0), stop=(k == DC - 1)), None)
            for k in range(DC):
                ctx.op("tensor", lambda e, k=k, tg=tg, pB=pB: e.matmul(
                    out=pB[:], lhsT=wqu[:, k * 128:(k + 1) * 128],
                    rhs=xint[:, k * T + tg * TG: k * T + (tg + 1) * TG],
                    start=(k == 0), stop=(k == DC - 1)), MM if k == DC - 1 else None)
            mmv = ctx.cnt[MM]
            ctx.wait("scalar", MM, mmv)
            sv = ctx.op("scalar", lambda e, tg=tg, pA=pA: e.activation(
                out=silu_t[tg % 2][:], in_=pA[:], func=AF.Silu,
                scale=cols[:, 2:3]), SC)
            ctx.wait("vector", SC, sv)
            h2v[tg % 2] = ctx.op("vector", lambda e, tg=tg, i=i, pB=pB: e.scalar_tensor_tensor(
                out=h2sl[i % 2][:, tg * TG:(tg + 1) * TG], in0=pB[:],
                scalar=cols[:, 6:7], in1=silu_t[tg % 2][:],
                op0=ALU.mult, op1=ALU.mult), V)
        ctx.op("vector", lambda e, i=i: e.tensor_reduce(
            out=t["col1"][:], in_=h2sl[i % 2][:], axis=AX, op=ALU.max,
            apply_absolute_value=True), None)
        ctx.drain()
        if i == 0:
            vv = ctx.op("vector", lambda e: e.tensor_copy(out=am2[:], in_=t["col1"][:]), V)
        else:
            vv = ctx.op("vector", lambda e: e.tensor_tensor(
                out=am2[:], in0=am2[:], in1=t["col1"][:], op=ALU.max), V)
        ctx.drain()
        ctx.wait("gpsimd", V, vv)
        h2_store_v[i % 2] = ctx.dma(
            "gpsimd", d["h2s"].ap()[i * 128:(i + 1) * 128, :], h2sl[i % 2][:], "H2S")
    if upto == "E":
        vv = ctx.op("vector", lambda e: e.tensor_copy(out=t["xout"][0][:], in_=t["stage"][:, 0:TG]), "VE")
        ctx.wait("gpsimd", "VE", vv)
        ctx.dma("gpsimd", d["x2T"].ap()[0:128, 0:TG], t["xout"][0][:], "X2S")
        ctx.drain_dma(["X2S", "ST", "LD", "WL", "H2S", "CC"])
        ctx.barrier()
        es.close()
        return

    if upto == "Edump":
        ctx.wait("gpsimd", "H2S", ctx.cnt["H2S"])
        for i in range(DC):
            ctx.dma("gpsimd", d["dbg"].ap()[i * 128:(i + 1) * 128, :],
                    d["h2s"].ap()[i * 128:(i + 1) * 128, :], "X2S")
        ctx.drain_dma(["X2S", "ST", "LD", "WL", "H2S", "CC"])
        ctx.barrier()
        es.close()
        return
    ctx.wait("tensor", V, ctx.cnt[V])
    mt = ctx.op("tensor", lambda e: e.transpose(
        out=t["ps"][4][:1, :128], in_=am2[:], identity=t["idf"][:]), MM)
    ctx.wait("vector", MM, mt)
    vv = ctx.op("vector", lambda e: e.tensor_reduce(
        out=t["scl"][:, 10:11], in_=t["ps"][4][:1, :128], axis=AX, op=ALU.max), V)
    ctx.wait("gpsimd", V, vv)
    ctx.dma("gpsimd", d["cci"].ap(), t["scl"][:1, 10:11], "CC")
    ctx.wait("gpsimd", "CC", ctx.cnt["CC"])
    cv = ctx.op("gpsimd", lambda e: e.collective_compute(
        "AllReduce", ALU.max, replica_groups=[list(range(NCORE))],
        ins=[d["cci"].ap()], outs=[d["cco"].ap()]), "CD")
    ctx.wait("gpsimd", "CD", cv)
    ctx.dma("gpsimd", t["scl"][:1, 12:13], d["cco"].ap(), "CC")
    ctx.wait("scalar", "CC", ctx.cnt["CC"])
    s_sig = ctx.op("scalar", lambda e: e.activation(
        out=t["scl"][:, 14:15], in_=clipx[:, 1:2], func=AF.Sigmoid), SC)
    ctx.wait("vector", SC, s_sig)
    ctx.wait("vector", "CC", ctx.cnt["CC"])
    ctx.op("vector", lambda e: e.tensor_tensor(
        out=t["scl"][:, 15:16], in0=t["scl"][:, 12:13], in1=t["scl"][:, 14:15], op=ALU.mult), None)
    ctx.drain()
    ctx.op("vector", lambda e: e.reciprocal(out=t["scl"][:, 16:17], in_=t["scl"][:, 15:16]), None)
    ctx.drain()
    ctx.op("vector", lambda e: e.tensor_scalar(
        out=t["scl"][:, 17:18], in0=t["scl"][:, 16:17], scalar1=QMAX, scalar2=None,
        op0=ALU.mult), None)
    ctx.drain()
    ctx.op("vector", lambda e: e.tensor_scalar(
        out=t["scl"][:, 18:19], in0=t["scl"][:, 15:16], scalar1=1.0 / QMAX, scalar2=None,
        op0=ALU.mult), V)
    ctx.drain()
    bcast_scalar_col(ctx, t, t["scl"][:1, 17:18], cols[:, 4:5])
    v_sx2 = bcast_scalar_col(ctx, t, t["scl"][:1, 18:19], cols[:, 5:6])
    if upto == "F":
        vv = ctx.op("vector", lambda e: e.tensor_copy(out=t["xout"][0][:], in_=t["stage"][:, 0:TG]), "VE")
        ctx.wait("gpsimd", "VE", vv)
        ctx.dma("gpsimd", d["x2T"].ap()[0:128, 0:TG], t["xout"][0][:], "X2S")
        ctx.drain_dma(["X2S", "ST", "LD", "WL", "H2S", "CC"])
        ctx.barrier()
        es.close()
        return

    es.close()

    es2 = ExitStack()
    wdqT = ctx.sbuf("p1_wdqT", [128, IC0 * 1024], BF16, es2)
    h2ld = [ctx.sbuf(f"p1_h2ld{i}", [128, TG], F32, es2) for i in range(2)]
    h2q = ctx.sbuf("p1_h2q", [128, IC0 * TG], BF16, es2)
    wTsl = ctx.sbuf("p1_wTsl", [128, 1024], F32, es2)
    itmp2 = ctx.sbuf("p1_itmp2", [128, 1024], I32, es2)
    rinvd = ctx.sbuf("p1_rinvd", [128, 1024], F32, es2)
    rrowd = ctx.sbuf("p1_rrowd", [1, 2048], BF16, es2)
    rcol2b = ctx.sbuf("p1_rc2b", [128, 2], BF16, es2)
    dcols = ctx.sbuf("p1_dcols", [128, 16], F32, es2)
    wdnat = ctx.sbuf("p1_wdnat", [128, 2816], F32, es2)
    if upto == "F1":
        vv = ctx.op("vector", lambda e: e.tensor_copy(out=t["xout"][0][:], in_=t["stage"][:, 0:TG]), "VE")
        ctx.wait("gpsimd", "VE", vv)
        ctx.dma("gpsimd", d["x2T"].ap()[0:128, 0:TG], t["xout"][0][:], "X2S")
        ctx.drain_dma(["X2S", "ST", "LD", "WL", "H2S", "CC"])
        ctx.barrier()
        es2.close()
        return

    prev_v = v_sx2
    for h in range(DC):
        ctx.wait("sync", V, prev_v)
        lv = ctx.dma("sync", wdnat[:], d["wd_nat"].ap()[h * 128:(h + 1) * 128, :], "WL")
        ctx.wait("vector", "WL", lv)
        ctx.op("vector", lambda e, h=h: e.tensor_reduce(
            out=dcols[:, h:h + 1], in_=wdnat[:], axis=AX, op=ALU.max,
            apply_absolute_value=True), None)
        ctx.drain()
        ctx.op("vector", lambda e, h=h: e.reciprocal(out=t["col1"][:], in_=dcols[:, h:h + 1]), None)
        ctx.drain()
        ctx.op("vector", lambda e: e.tensor_scalar(
            out=t["col2"][:], in0=t["col1"][:], scalar1=QMAX, scalar2=None, op0=ALU.mult), None)
        ctx.drain()
        ctx.op("vector", lambda e: e.tensor_copy(out=rcol2b[:, 0:1], in_=t["col2"][:]), None)
        ctx.drain()
        v1 = ctx.op("vector", lambda e: e.tensor_tensor(
            out=rcol2b[:, 1:2], in0=t["col2"][:], in1=rcol2b[:, 0:1], op=ALU.subtract), V)
        ctx.drain()
        ctx.wait("tensor", V, v1)
        ctx.op("tensor", lambda e: e.transpose(
            out=t["psb"][0][:1, 0:128], in_=rcol2b[:, 0:1], identity=t["idb"][:]), None)
        mt = ctx.op("tensor", lambda e: e.transpose(
            out=t["psb"][0][:1, 128:256], in_=rcol2b[:, 1:2], identity=t["idb"][:]), MM)
        ctx.wait("vector", MM, mt)
        ctx.op("vector", lambda e, h=h: e.tensor_copy(
            out=rrowd[:, h * 128:(h + 1) * 128], in_=t["psb"][0][:1, 0:128]), None)
        ctx.op("vector", lambda e, h=h: e.tensor_copy(
            out=rrowd[:, 1024 + h * 128: 1024 + (h + 1) * 128],
            in_=t["psb"][0][:1, 128:256]), None)
        ctx.op("vector", lambda e, h=h: e.tensor_tensor(
            out=t["col1"][:], in0=dcols[:, h:h + 1], in1=cols[:, 5:6], op=ALU.mult), None)
        ctx.drain()
        prev_v = ctx.op("vector", lambda e, h=h: e.tensor_scalar(
            out=dcols[:, 8 + h:9 + h], in0=t["col1"][:], scalar1=1.0 / QMAX,
            scalar2=None, op0=ALU.mult), V)
        ctx.drain()
    if upto == "F1b":
        vv = ctx.op("vector", lambda e: e.tensor_copy(out=t["xout"][0][:], in_=t["stage"][:, 0:TG]), "VE")
        ctx.wait("gpsimd", "VE", vv)
        ctx.dma("gpsimd", d["x2T"].ap()[0:128, 0:TG], t["xout"][0][:], "X2S")
        ctx.drain_dma(["X2S", "ST", "LD", "WL", "H2S", "CC"])
        ctx.barrier()
        es2.close()
        return

    ctx.wait("tensor", V, prev_v)
    if upto in ("F2h", "F2i", "F2j", "F2m", "F2n", "F2o"):
        rhs_src = t["hrow"][:, 0:128] if upto == "F2h" else (
            rrowd[:, 0:128] if upto == "F2i" else rrowd[:, 0:512])
        outp = t["ps"][2][:, :128] if upto in ("F2h", "F2i") else t["ps"][2][:]
        ctx.op("tensor", lambda e: e.matmul(
            out=outp, lhsT=t["ones"][:1, :], rhs=rhs_src,
            start=True, stop=True), "MM")
        if upto in ("F2m", "F2n", "F2o"):
            rhs2 = rrowd[:, 0:512] if upto == "F2n" else rrowd[:, 1024:1536]
            ctx.op("tensor", lambda e, rhs2=rhs2: e.matmul(
                out=t["ps"][3][:], lhsT=t["ones"][:1, :], rhs=rhs2,
                start=True, stop=True), "MM")
        mmv = ctx.cnt["MM"]
        ctx.wait("vector", "MM", mmv)
        dsto = rinvd[:, 0:512] if upto == "F2o" else t["rbf"][:, :128]
        srco = t["ps"][2][:] if upto == "F2o" else t["ps"][2][:, :128]
        ctx.op("vector", lambda e, dsto=dsto, srco=srco: e.tensor_copy(out=dsto, in_=srco), "VE")
    if upto == "F2pad":
        for _ in range(8):
            ctx.op("vector", lambda e: e.tensor_copy(out=t["col2"][:], in_=t["col1"][:]), None)
    for half in range(2):
        if upto in ("F2a", "F2pad", "F2h", "F2i", "F2j", "F2m", "F2n", "F2o"):
            continue
        ctx.op("tensor", lambda e, half=half: e.matmul(
            out=t["ps"][2][:], lhsT=t["ones"][:1, :], rhs=rrowd[:, half * TG:(half + 1) * TG],
            start=True, stop=True), None)
        v3 = ctx.op("tensor", lambda e, half=half: e.matmul(
            out=t["ps"][3][:], lhsT=t["ones"][:1, :],
            rhs=rrowd[:, 1024 + half * TG: 1024 + (half + 1) * TG],
            start=True, stop=True), MM)
        if upto == "F2b":
            continue
        ctx.wait("scalar", MM, v3)
        if upto == "F2e":
            continue
        if upto == "F2f":
            s1 = ctx.op("vector", lambda e, half=half: e.tensor_copy(
                out=rinvd[:, half * TG:(half + 1) * TG], in_=t["ps"][2][:]), "VE")
            continue
        if upto == "F2d":
            s1 = ctx.op("scalar", lambda e: e.activation(
                out=t["rbf"][:], in_=t["ps"][2][:], func=AF.Copy), SC)
            continue
        s1 = ctx.op("scalar", lambda e, half=half: e.activation(
            out=rinvd[:, half * TG:(half + 1) * TG], in_=t["ps"][2][:], func=AF.Copy), SC)
        if upto == "F2c":
            continue
        ctx.wait("vector", SC, s1)
        v_add = ctx.op("vector", lambda e, half=half: e.tensor_tensor(
            out=rinvd[:, half * TG:(half + 1) * TG], in0=rinvd[:, half * TG:(half + 1) * TG],
            in1=t["ps"][3][:], op=ALU.add), V)
        ctx.wait("tensor", V, v_add)
    if upto.startswith("F2"):
        vv = ctx.op("vector", lambda e: e.tensor_copy(out=t["xout"][0][:], in_=t["stage"][:, 0:TG]), "VE")
        ctx.wait("gpsimd", "VE", vv)
        ctx.dma("gpsimd", d["x2T"].ap()[0:128, 0:TG], t["xout"][0][:], "X2S")
        ctx.drain_dma(["X2S", "ST", "LD", "WL", "H2S", "CC"])
        ctx.barrier()
        es2.close()
        return

    prev_v = ctx.op("vector", lambda e: e.tensor_copy(out=t["col1"][:], in_=t["col1"][:]), V)
    for i in range(IC0):
        ctx.wait("sync", V, prev_v)
        lv = ctx.dma("sync", wTsl[:], d["wdT"].ap()[i * 128:(i + 1) * 128, :], "WL")
        ctx.wait("vector", "WL", lv)
        ctx.op("vector", lambda e: e.tensor_tensor(
            out=itmp2[:], in0=wTsl[:], in1=rinvd[:], op=ALU.mult), None)
        prev_v = ctx.op("vector", lambda e, i=i: e.tensor_copy(
            out=wdqT[:, i * 1024:(i + 1) * 1024], in_=itmp2[:]), V)
    if upto == "F3":
        vv = ctx.op("vector", lambda e: e.tensor_copy(out=t["xout"][0][:], in_=t["stage"][:, 0:TG]), "VE")
        ctx.wait("gpsimd", "VE", vv)
        ctx.dma("gpsimd", d["x2T"].ap()[0:128, 0:TG], t["xout"][0][:], "X2S")
        ctx.drain_dma(["X2S", "ST", "LD", "WL", "H2S", "CC"])
        ctx.barrier()
        es2.close()
        return

    ctx.wait("sync", "H2S", ctx.cnt["H2S"])
    v_q = prev_v
    dmm = 0
    last_vv = prev_v
    jj = 0
    hq_read_v = [0, 0]
    for tg in range(NG):
        if tg > 0:
            ctx.wait("vector", MM, dmm)
        for i in range(IC0):
            hsem = "H2LA" if i % 2 == 0 else "H2LB"
            ctx.wait("sync", V, hq_read_v[i % 2])
            lv = ctx.dma("sync", h2ld[i % 2][:],
                         d["h2s"].ap()[i * 128:(i + 1) * 128, tg * TG:(tg + 1) * TG], hsem)
            ctx.wait("vector", hsem, lv)
            ctx.op("vector", lambda e, i=i: e.tensor_scalar(
                out=h2ld[i % 2][:], in0=h2ld[i % 2][:],
                scalar1=cols[:, 4:5], scalar2=MAGIC, op0=ALU.mult, op1=ALU.add), None)
            ctx.op("vector", lambda e, i=i: e.tensor_scalar(
                out=h2ld[i % 2][:], in0=h2ld[i % 2][:],
                scalar1=MAGIC, scalar2=-128.0, op0=ALU.subtract, op1=ALU.max), None)
            hq_read_v[i % 2] = ctx.op("vector", lambda e, i=i: e.tensor_scalar(
                out=h2q[:, i * TG:(i + 1) * TG], in0=h2ld[i % 2][:],
                scalar1=127.0, scalar2=None, op0=ALU.min), V)
        v_q = hq_read_v[(IC0 - 1) % 2]
        ctx.wait("tensor", V, v_q)
        for hh in range(2):
            if hh == 1:
                ctx.wait("tensor", V, last_vv)
            for i in range(IC0):
                for h4 in range(4):
                    h = hh * 4 + h4
                    ctx.op("tensor", lambda e, i=i, h=h, h4=h4: e.matmul(
                        out=t["ps"][h4][:], lhsT=wdqT[:, i * 1024 + h * 128:
                                                      i * 1024 + (h + 1) * 128],
                        rhs=h2q[:, i * TG:(i + 1) * TG],
                        start=(i == 0), stop=(i == IC0 - 1)),
                        MM if (i == IC0 - 1 and h4 == 3) else None)
            dmm = ctx.cnt[MM]
            ctx.wait("sync", V, last_vv)
            lv = 0
            for h4 in range(4):
                h = hh * 4 + h4
                lv = ctx.dma("sync", t["stage"][:, h * TG:(h + 1) * TG],
                             d["x1T"].ap()[h * 128:(h + 1) * 128, tg * TG:(tg + 1) * TG], LD)
            ctx.wait("vector", LD, lv)
            ctx.wait("vector", MM, dmm)
            for h4 in range(4):
                h = hh * 4 + h4
                if jj >= 2:
                    ctx.wait("vector", "X2S", 16 * (jj - 1))
                vv = ctx.op("vector", lambda e, h=h, h4=h4, jj=jj: e.scalar_tensor_tensor(
                    out=t["xout"][jj % 2][:], in0=t["ps"][h4][:],
                    scalar=dcols[:, 8 + h:9 + h], in1=t["stage"][:, h * TG:(h + 1) * TG],
                    op0=ALU.mult, op1=ALU.add), V)
                last_vv = vv
                ctx.wait("gpsimd", V, vv)
                ctx.dma("gpsimd", d["x2T"].ap()[h * 128:(h + 1) * 128, tg * TG:(tg + 1) * TG],
                        t["xout"][jj % 2][:], "X2S")
                jj += 1
        if upto == "G1":
            ctx.drain_dma(["X2S", "ST", "LD", "WL", "H2S", "CC", "H2LA", "H2LB"])
            ctx.barrier()
            es2.close()
            return

    ctx.drain_dma(["X2S", LD, "WL", "H2LA", "H2LB", ST, "H2S", "CC"])
    ctx.barrier()
    es2.close()



def emit_phase2(ctx, t, d, load_consts=False):
    t = dict(t)
    V, SC, MM, LD, ST = "VE", "SC", "MM", "LD", "ST"
    for nm in ("WL", LD, "QS", "QL", "X3S", "TB", "S3", "S3L"):
        ctx.sem(nm)
    es = ExitStack()
    wprojb = ctx.sbuf("p2_wp", [128, DC * 1024], BF16, es)
    wsl = [ctx.sbuf(f"p2_wsl{i}", [128, 1024], F32, es) for i in range(2)]
    wqb = [ctx.sbuf(f"p2_wqb{i}", [128, 1024], BF16, es) for i in range(2)]
    wv0 = ctx.sbuf("p2_wv0", [128, DC * 512], BF16, es)
    bqkv = ctx.sbuf("p2_bqkv", [128, 24], F32, es)
    bvb = ctx.sbuf("p2_bvb", [128, 1024], F32, es)
    bproj = ctx.sbuf("p2_bproj", [128, 8], F32, es)
    g8 = ctx.sbuf("p2_g8", [128, 8], F32, es)
    arb = ctx.sbuf("p2_arb", [128, TG], F32, es)
    kf = ctx.sbuf("p2_kf", [128, DC * S], BF16, es)
    vt = ctx.sbuf("p2_vt", [128, 16 * 1024], BF16, es)
    qtile = ctx.sbuf("p2_qt", [128, DC * TG], BF16, es)
    qtmp = [ctx.sbuf(f"p2_qtmp{i}", [128, TG], BF16, es) for i in range(2)]
    tabs = ctx.sbuf("p2_tabs", [128, 3 * TG], BF16, es)
    scol = ctx.sbuf("p2_scol", [128, 8], F32, es)


    if load_consts:
        ctx.dma("sync", t["ones"][:], d["ones_bf"].ap(), LD)
        ctx.dma("sync", t["idf"][:], d["ident_f"].ap(), LD)
        ctx.dma("sync", t["idb"][:], d["ident_b"].ap(), LD)
    ctx.dma("sync", bqkv[:], d["bqkv24"].ap(), LD)
    ctx.dma("sync", t["stage"][:1, 0:1024], d["bvrow"].ap(), LD)
    ctx.dma("sync", bproj[:], d["bproj8"].ap(), LD)
    lv = ctx.dma("sync", g8[:], d["g8"].ap(), LD)
    prev_v = 0
    for c in range(DC):
        ctx.wait("sync", V, prev_v)
        wv = ctx.dma("sync", wsl[0][:], d["wproj"].ap()[c * 128:(c + 1) * 128, :], "WL")
        ctx.wait("vector", "WL", wv)
        prev_v = ctx.op("vector", lambda e, c=c: e.tensor_copy(
            out=wprojb[:, c * 1024:(c + 1) * 1024], in_=wsl[0][:]), V)
    ctx.wait("vector", LD, lv)
    ctx.op("vector", lambda e: e.tensor_copy(out=t["hrow"][:, 0:1024], in_=t["stage"][:1, 0:1024]), None)
    v1 = ctx.op("vector", lambda e: e.tensor_tensor(
        out=t["hrow"][:, 1024:2048], in0=t["stage"][:1, 0:1024], in1=t["hrow"][:, 0:1024],
        op=ALU.subtract), V)
    ctx.wait("tensor", V, v1)
    for half in range(2):
        ctx.op("tensor", lambda e, half=half: e.matmul(
            out=t["ps"][2][:], lhsT=t["ones"][:1, :], rhs=t["hrow"][:, half * TG:(half + 1) * TG],
            start=True, stop=True), None)
        v3 = ctx.op("tensor", lambda e, half=half: e.matmul(
            out=t["ps"][3][:], lhsT=t["ones"][:1, :],
            rhs=t["hrow"][:, 1024 + half * TG: 1024 + (half + 1) * TG],
            start=True, stop=True), MM)
        ctx.wait("scalar", MM, v3)
        s1 = ctx.op("scalar", lambda e, half=half: e.activation(
            out=bvb[:, half * TG:(half + 1) * TG], in_=t["ps"][2][:], func=AF.Copy), SC)
        ctx.wait("vector", SC, s1)
        va = ctx.op("vector", lambda e, half=half: e.tensor_tensor(
            out=bvb[:, half * TG:(half + 1) * TG], in0=bvb[:, half * TG:(half + 1) * TG],
            in1=t["ps"][3][:], op=ALU.add), V)
        ctx.wait("tensor", V, va)

    last_v = ctx.cnt[V]
    jj = 0
    for s_ in range(SPC):
        es_hq = ExitStack()
        hq = ctx.sbuf(f"p2_hq{s_}", [128, DC * S], BF16, es_hq)
        for gl in range(GPS):
            g = s_ * GPS + gl
            lv = load_stage(ctx, t, d["x2T"].ap(), g, guard_v=last_v)
            ctx.wait("vector", LD, lv)
            emit_ln_group(ctx, t, g, want_lnbf=False, want_absmax=False, want_apply=False)
            ctx.op("vector", lambda e: e.tensor_tensor(
                out=t["rowf2"][:], in0=t["rowf3"][:], in1=t["rowf3"][:], op=ALU.mult), None)
            vv = ctx.op("vector", lambda e: e.tensor_scalar(
                out=t["rowf2"][:], in0=t["rowf2"][:], scalar1=-1e-5, scalar2=1.0 + 1e-6,
                op0=ALU.mult, op1=ALU.add), V)
            ctx.wait("scalar", V, vv)
            sv = ctx.op("scalar", lambda e: e.activation(
                out=t["rowf2"][:], in_=t["rowf2"][:], func=AF.Sqrt), SC)
            ctx.wait("vector", SC, sv)
            ctx.op("vector", lambda e: e.reciprocal(out=t["rowf2"][:], in_=t["rowf2"][:]), None)
            ctx.op("vector", lambda e: e.tensor_tensor(
                out=t["rowf2"][:], in0=t["rowf2"][:], in1=t["rowf3"][:], op=ALU.mult), None)
            vs3 = ctx.op("vector", lambda e: e.tensor_copy(
                out=t["rowf2"][:] if False else t["rowf"][:, :1], in_=t["rowf"][:, :1]), V)
            ctx.wait("gpsimd", V, vs3)
            ctx.dma("gpsimd", d["stats3"].ap()[0:1, gl * TG:(gl + 1) * TG], t["rowf"][:], "S3")
            ctx.dma("gpsimd", d["stats3"].ap()[1:2, gl * TG:(gl + 1) * TG], t["rowf3"][:], "S3")
            ctx.op("vector", lambda e: e.tensor_copy(out=t["hrow"][:, 0:TG], in_=t["rowf2"][:]), None)
            v1 = ctx.op("vector", lambda e: e.tensor_tensor(
                out=t["hrow"][:, TG:2 * TG], in0=t["rowf2"][:], in1=t["hrow"][:, 0:TG],
                op=ALU.subtract), V)
            ctx.wait("tensor", V, v1)
            ctx.op("tensor", lambda e: e.matmul(
                out=t["ps"][2][:], lhsT=t["ones"][:1, :], rhs=t["hrow"][:, 0:TG],
                start=True, stop=True), None)
            v3 = ctx.op("tensor", lambda e: e.matmul(
                out=t["ps"][3][:], lhsT=t["ones"][:1, :], rhs=t["hrow"][:, TG:2 * TG],
                start=True, stop=True), MM)
            ctx.wait("scalar", MM, v3)
            s1 = ctx.op("scalar", lambda e: e.activation(
                out=arb[:], in_=t["ps"][2][:], func=AF.Copy), SC)
            ctx.wait("vector", SC, s1)
            va = ctx.op("vector", lambda e: e.tensor_tensor(
                out=arb[:], in0=arb[:], in1=t["ps"][3][:], op=ALU.add), V)
            ctx.wait("tensor", V, va)
            for c in range(DC):
                ctx.op("vector", lambda e, c=c: e.tensor_tensor(
                    out=t["ep1"][:], in0=t["stage"][:, c * TG:(c + 1) * TG],
                    in1=t["mbf"][:], op=ALU.subtract), None)
                vv = ctx.op("vector", lambda e, c=c, gl=gl, hq=hq: e.scalar_tensor_tensor(
                    out=hq[:, c * S + gl * TG: c * S + (gl + 1) * TG], in0=arb[:],
                    scalar=g8[:, c:c + 1], in1=t["ep1"][:], op0=ALU.mult, op1=ALU.mult),
                    V if c == DC - 1 else None)
            last_v = ctx.cnt[V]
        for qk in range(2):
            cast_v = [0, 0]
            mm_last = [0, 0]
            sc_v = [0, 0]
            for tgl in range(GPS):
                for fc in range(DC):
                    b = fc % 2
                    ctx.wait("sync", V, cast_v[b])
                    wv = 0
                    for c in range(DC):
                        wv = ctx.dma("sync", wsl[b][:, c * 128:(c + 1) * 128],
                                     d["wqkv"].ap()[c * 128:(c + 1) * 128,
                                                    qk * 1024 + fc * 128:
                                                    qk * 1024 + (fc + 1) * 128], "WL")
                    ctx.wait("vector", "WL", wv)
                    ctx.wait("vector", MM, mm_last[b])
                    cast_v[b] = ctx.op("vector", lambda e, b=b: e.tensor_copy(
                        out=wqb[b][:], in_=wsl[b][:]), V)
                    ctx.wait("tensor", V, cast_v[b])
                    ctx.wait("tensor", SC, sc_v[b])
                    pm = t["ps"][4 + b]
                    for k in range(DC):
                        ctx.op("tensor", lambda e, k=k, tgl=tgl, pm=pm, b=b, hq=hq: e.matmul(
                            out=pm[:], lhsT=wqb[b][:, k * 128:(k + 1) * 128],
                            rhs=hq[:, k * S + tgl * TG: k * S + (tgl + 1) * TG],
                            start=(k == 0), stop=(k == DC - 1)), MM if k == DC - 1 else None)
                    mm_last[b] = ctx.cnt[MM]
                    ctx.wait("scalar", MM, mm_last[b])
                    dst = (qtile[:, fc * TG:(fc + 1) * TG] if qk == 0 else
                           kf[:, fc * S + tgl * TG: fc * S + (tgl + 1) * TG])
                    sc_v[b] = ctx.op("scalar", lambda e, pm=pm, dst=dst, qk=qk, fc=fc:
                        e.activation(out=dst, in_=pm[:], func=AF.Identity,
                                     bias=bqkv[:, qk * 8 + fc: qk * 8 + fc + 1]), SC)
                ctx.wait("vector", SC, max(sc_v))
                for hc in range(4):
                    ctx.wait("gpsimd", V, ctx.cnt[V])
                    tv = 0
                    for ti, nm in enumerate(("ropeC", "ropeSp", "ropeSn")):
                        tv = ctx.dma("gpsimd", tabs[:, ti * TG:(ti + 1) * TG],
                                     d[nm].ap()[hc * 128:(hc + 1) * 128,
                                                tgl * TG:(tgl + 1) * TG], "TB")
                    ctx.wait("vector", "TB", tv)
                    ca, cb = hc, hc + 4
                    if qk == 0:
                        src_a = qtile[:, ca * TG:(ca + 1) * TG]
                        src_b = qtile[:, cb * TG:(cb + 1) * TG]
                    else:
                        src_a = kf[:, ca * S + tgl * TG: ca * S + (tgl + 1) * TG]
                        src_b = kf[:, cb * S + tgl * TG: cb * S + (tgl + 1) * TG]
                    ctx.op("vector", lambda e, src_a=src_a: e.tensor_tensor(
                        out=qtmp[0][:], in0=src_a, in1=tabs[:, 0:TG], op=ALU.mult), None)
                    ctx.op("vector", lambda e, src_b=src_b: e.tensor_tensor(
                        out=qtmp[1][:], in0=src_b, in1=tabs[:, 2 * TG:3 * TG], op=ALU.mult), None)
                    ctx.op("vector", lambda e, src_b=src_b: e.tensor_tensor(
                        out=t["ep1"][:], in0=src_b, in1=tabs[:, 0:TG], op=ALU.mult), None)
                    ctx.op("vector", lambda e, src_a=src_a: e.tensor_tensor(
                        out=t["xout"][0][:], in0=src_a, in1=tabs[:, TG:2 * TG], op=ALU.mult), None)
                    ctx.op("vector", lambda e, src_a=src_a: e.tensor_tensor(
                        out=src_a, in0=qtmp[0][:], in1=qtmp[1][:], op=ALU.add), None)
                    rv = ctx.op("vector", lambda e, src_b=src_b: e.tensor_tensor(
                        out=src_b, in0=t["ep1"][:], in1=t["xout"][0][:], op=ALU.add), V)
                if qk == 0:
                    ctx.wait("gpsimd", V, rv)
                    for c in range(DC):
                        ctx.dma("gpsimd", d["qs"].ap()[c * 128:(c + 1) * 128,
                                                       tgl * TG:(tgl + 1) * TG],
                                qtile[:, c * TG:(c + 1) * TG], "QS")
                    ctx.wait("scalar", "QS", ctx.cnt["QS"])
        v_ev = [0, 0]
        for fg in range(2):
            for c in range(DC):
                ctx.wait("sync", V, ctx.cnt[V] if (fg == 0 and c == 0) else cast_done)
                wv = ctx.dma("sync", wsl[0][:, 0:512],
                             d["wqkv"].ap()[c * 128:(c + 1) * 128,
                                            2048 + fg * 512: 2048 + (fg + 1) * 512], "WL")
                ctx.wait("vector", "WL", wv)
                ctx.wait("vector", MM, ctx.cnt[MM])
                cast_done = ctx.op("vector", lambda e, c=c: e.tensor_copy(
                    out=wv0[:, c * 512:(c + 1) * 512], in_=wsl[0][:, 0:512]), V)
            ctx.wait("tensor", V, cast_done)
            for tc in range(16):
                tgl = tc // 4
                toff = (tc % 4) * 128
                pm = t["ps"][4 + tc % 2]
                ctx.wait("tensor", V, v_ev[tc % 2])
                for k in range(DC):
                    ctx.op("tensor", lambda e, k=k, tgl=tgl, toff=toff, pm=pm, hq=hq: e.matmul(
                        out=pm[:], lhsT=hq[:, k * S + tgl * TG + toff:
                                           k * S + tgl * TG + toff + 128],
                        rhs=wv0[:, k * 512:(k + 1) * 512],
                        start=(k == 0), stop=(k == DC - 1)), MM if k == DC - 1 else None)
                mmv = ctx.cnt[MM]
                ctx.wait("vector", MM, mmv)
                v_ev[tc % 2] = ctx.op("vector", lambda e, tc=tc, fg=fg, pm=pm: e.tensor_tensor(
                    out=vt[:, tc * 1024 + fg * 512: tc * 1024 + (fg + 1) * 512],
                    in0=pm[:], in1=bvb[:, fg * 512:(fg + 1) * 512], op=ALU.add), V)
        v_done = max(v_ev)
        es_hq.close()
        es_at = ExitStack()
        Pt = ctx.sbuf(f"p2_P{s_}", [128, S], BF16, es_at)
        PT_sb = ctx.sbuf(f"p2_PT{s_}", [128, 16 * TG], BF16, es_at)
        osl = ctx.sbuf(f"p2_o{s_}", [128, DC * TG], BF16, es_at)
        ctx.wait("sync", "QS", ctx.cnt["QS"])
        pt_cp = 0
        pv_mm = 0
        for gl in range(GPS):
            ctx.wait("sync", V, ctx.cnt[V])
            qv = 0
            for c in range(DC):
                qv = ctx.dma("sync", qtile[:, c * TG:(c + 1) * TG],
                             d["qs"].ap()[c * 128:(c + 1) * 128, gl * TG:(gl + 1) * TG], "QL")
            ctx.wait("tensor", "QL", qv)
            ctx.wait("tensor", V, v_done)
            exp_sv = 0
            for qt in range(4):
                ctx.wait("tensor", SC, exp_sv)
                for kg in range(4):
                    for c in range(DC):
                        ctx.op("tensor", lambda e, c=c, kg=kg, qt=qt: e.matmul(
                            out=t["ps"][kg][:, 0:TG],
                            lhsT=qtile[:, c * TG + qt * 128: c * TG + qt * 128 + 128],
                            rhs=kf[:, c * S + kg * TG: c * S + (kg + 1) * TG],
                            start=(c == 0), stop=(c == DC - 1)),
                            MM if (c == DC - 1 and kg == 3) else None)
                mmv = ctx.cnt[MM]
                ctx.wait("vector", MM, mmv)
                for kg in range(4):
                    ctx.op("vector", lambda e, kg=kg: e.tensor_reduce(
                        out=scol[:, kg:kg + 1], in_=t["ps"][kg][:, 0:TG], axis=AX,
                        op=ALU.max), None)
                ctx.drain()
                ctx.op("vector", lambda e: e.tensor_tensor(
                    out=scol[:, 4:5], in0=scol[:, 0:1], in1=scol[:, 1:2], op=ALU.max), None)
                ctx.op("vector", lambda e: e.tensor_tensor(
                    out=scol[:, 5:6], in0=scol[:, 2:3], in1=scol[:, 3:4], op=ALU.max), None)
                ctx.drain()
                ctx.op("vector", lambda e: e.tensor_tensor(
                    out=scol[:, 4:5], in0=scol[:, 4:5], in1=scol[:, 5:6], op=ALU.max), None)
                ctx.drain()
                vb = ctx.op("vector", lambda e: e.tensor_scalar(
                    out=scol[:, 6:7], in0=scol[:, 4:5], scalar1=-LN3 / 32.0, scalar2=None,
                    op0=ALU.mult), V)
                ctx.drain()
                ctx.wait("scalar", V, vb)
                for kg in range(4):
                    ctx.op("scalar", lambda e, kg=kg, Pt=Pt: e.activation(
                        out=Pt[:, kg * TG:(kg + 1) * TG], in_=t["ps"][kg][:, 0:TG],
                        func=AF.Exp, scale=LN3 / 32.0, bias=scol[:, 6:7],
                        accum_out=scol[:, kg:kg + 1]), SC if kg == 3 else None)
                exp_sv = ctx.cnt[SC]
                ctx.wait("vector", SC, exp_sv)
                ctx.op("vector", lambda e: e.tensor_tensor(
                    out=scol[:, 4:5], in0=scol[:, 0:1], in1=scol[:, 1:2], op=ALU.add), None)
                ctx.op("vector", lambda e: e.tensor_tensor(
                    out=scol[:, 5:6], in0=scol[:, 2:3], in1=scol[:, 3:4], op=ALU.add), None)
                ctx.drain()
                ctx.op("vector", lambda e: e.tensor_tensor(
                    out=scol[:, 4:5], in0=scol[:, 4:5], in1=scol[:, 5:6], op=ALU.add), None)
                ctx.drain()
                ctx.op("vector", lambda e: e.reciprocal(out=scol[:, 7:8], in_=scol[:, 4:5]), None)
                ctx.drain()
                vn = ctx.op("vector", lambda e, Pt=Pt: e.tensor_scalar(
                    out=Pt[:], in0=Pt[:], scalar1=scol[:, 7:8], scalar2=None,
                    op0=ALU.mult), V)
                ctx.wait("tensor", V, vn)
                ctx.wait("tensor", V, pt_cp)
                for kc in range(16):
                    bb = kc // 8
                    ctx.op("tensor", lambda e, kc=kc, bb=bb, Pt=Pt: e.transpose(
                        out=t["psb"][bb][:, (kc % 8) * 128:(kc % 8 + 1) * 128],
                        in_=Pt[:, kc * 128:(kc + 1) * 128], identity=t["idb"][:]),
                        MM if kc == 15 else None)
                mtv = ctx.cnt[MM]
                ctx.wait("vector", MM, mtv)
                ctx.wait("vector", MM, pv_mm)
                for bb in range(2):
                    pt_cp = ctx.op("vector", lambda e, bb=bb, qt=qt, PT_sb=PT_sb: e.tensor_copy(
                        out=PT_sb[:].rearrange("p (kc q) -> p kc q", kc=16)[
                            :, bb * 8:(bb + 1) * 8, qt * 128:(qt + 1) * 128],
                        in_=t["psb"][bb][:, 0:1024].rearrange(
                            "p (kc q) -> p kc q", kc=8)), V)
            o_sc = [0, 0]
            for dc in range(DC):
                pm = t["ps"][4 + dc % 2]
                ctx.wait("tensor", V, pt_cp)
                ctx.wait("tensor", SC, o_sc[dc % 2])
                for kc in range(16):
                    ctx.op("tensor", lambda e, kc=kc, dc=dc, pm=pm, PT_sb=PT_sb: e.matmul(
                        out=pm[:], lhsT=vt[:, kc * 1024 + dc * 128: kc * 1024 + (dc + 1) * 128],
                        rhs=PT_sb[:, kc * TG:(kc + 1) * TG],
                        start=(kc == 0), stop=(kc == 15)), MM if kc == 15 else None)
                pv_mm = ctx.cnt[MM]
                ctx.wait("scalar", MM, pv_mm)
                o_sc[dc % 2] = ctx.op("scalar", lambda e, dc=dc, pm=pm, osl=osl: e.activation(
                    out=osl[:, dc * TG:(dc + 1) * TG], in_=pm[:], func=AF.Copy), SC)
            pj_ev = [0, 0]
            ctx.wait("tensor", SC, max(o_sc))
            for fc in range(DC):
                pm = t["ps"][fc % 2]
                ctx.wait("tensor", V, pj_ev[fc % 2])
                for dc in range(DC):
                    ctx.op("tensor", lambda e, dc=dc, fc=fc, pm=pm, osl=osl: e.matmul(
                        out=pm[:], lhsT=wprojb[:, dc * 1024 + fc * 128:
                                               dc * 1024 + (fc + 1) * 128],
                        rhs=osl[:, dc * TG:(dc + 1) * TG],
                        start=(dc == 0), stop=(dc == DC - 1)), MM if dc == DC - 1 else None)
                pjm = ctx.cnt[MM]
                if fc == 0:
                    ctx.wait("sync", "S3", ctx.cnt["S3"])
                    ctx.wait("sync", V, ctx.cnt[V])
                    s3v = 0
                    s3v = ctx.dma("sync", t["rowf"][:], d["stats3"].ap()[0:1, gl * TG:(gl + 1) * TG], "S3L")
                    s3v = ctx.dma("sync", t["rowf3"][:], d["stats3"].ap()[1:2, gl * TG:(gl + 1) * TG], "S3L")
                    ctx.wait("vector", "S3L", s3v)
                    ctx.wait("vector", SC, max(o_sc))
                    ctx.op("vector", lambda e: e.tensor_copy(
                        out=t["hrow"][:, 0:TG], in_=t["rowf"][:]), None)
                    ctx.op("vector", lambda e: e.tensor_tensor(
                        out=t["hrow"][:, TG:2 * TG], in0=t["rowf"][:],
                        in1=t["hrow"][:, 0:TG], op=ALU.subtract), None)
                    ctx.op("vector", lambda e: e.tensor_copy(
                        out=t["hrow"][:, 2 * TG:3 * TG], in_=t["rowf3"][:]), None)
                    v1 = ctx.op("vector", lambda e: e.tensor_tensor(
                        out=t["hrow"][:, 3 * TG:4 * TG], in0=t["rowf3"][:],
                        in1=t["hrow"][:, 2 * TG:3 * TG], op=ALU.subtract), V)
                    ctx.wait("tensor", V, v1)
                    ctx.op("tensor", lambda e: e.matmul(
                        out=t["ps"][2][:], lhsT=t["ones"][:1, :], rhs=t["hrow"][:, 0:TG],
                        start=True, stop=True), None)
                    v3 = ctx.op("tensor", lambda e: e.matmul(
                        out=t["ps"][3][:], lhsT=t["ones"][:1, :], rhs=t["hrow"][:, TG:2 * TG],
                        start=True, stop=True), MM)
                    ctx.wait("scalar", MM, v3)
                    s1 = ctx.op("scalar", lambda e: e.activation(
                        out=t["mbf"][:], in_=t["ps"][2][:], func=AF.Copy), SC)
                    ctx.wait("vector", SC, s1)
                    vmb = ctx.op("vector", lambda e: e.tensor_tensor(
                        out=t["mbf"][:], in0=t["mbf"][:], in1=t["ps"][3][:], op=ALU.add), V)
                    ctx.wait("tensor", V, vmb)
                    ctx.op("tensor", lambda e: e.matmul(
                        out=t["ps"][2][:], lhsT=t["ones"][:1, :], rhs=t["hrow"][:, 2 * TG:3 * TG],
                        start=True, stop=True), None)
                    v3 = ctx.op("tensor", lambda e: e.matmul(
                        out=t["ps"][3][:], lhsT=t["ones"][:1, :], rhs=t["hrow"][:, 3 * TG:4 * TG],
                        start=True, stop=True), MM)
                    ctx.wait("scalar", MM, v3)
                    s1 = ctx.op("scalar", lambda e: e.activation(
                        out=t["rbf"][:], in_=t["ps"][2][:], func=AF.Copy), SC)
                    ctx.wait("vector", SC, s1)
                    vrb = ctx.op("vector", lambda e: e.tensor_tensor(
                        out=t["rbf"][:], in0=t["rbf"][:], in1=t["ps"][3][:], op=ALU.add), V)
                    ctx.wait("sync", V, vrb)
                    lv = 0
                    for c in range(DC):
                        lv = ctx.dma("sync", t["stage"][:, c * TG:(c + 1) * TG],
                                     d["x2T"].ap()[c * 128:(c + 1) * 128,
                                                   (s_ * GPS + gl) * TG:(s_ * GPS + gl + 1) * TG], LD)
                    ctx.wait("vector", LD, lv)
                ctx.wait("vector", MM, pjm)
                if jj >= 2:
                    ctx.wait("vector", "X3S", 16 * (jj - 1))
                ctx.op("vector", lambda e, fc=fc: e.tensor_tensor(
                    out=t["ep1"][:], in0=t["stage"][:, fc * TG:(fc + 1) * TG],
                    in1=t["mbf"][:], op=ALU.subtract), None)
                ctx.op("vector", lambda e: e.tensor_tensor(
                    out=t["ep1"][:], in0=t["ep1"][:], in1=t["rbf"][:], op=ALU.mult), None)
                ctx.op("vector", lambda e, fc=fc, pm=pm: e.scalar_tensor_tensor(
                    out=t["ep1"][:], in0=pm[:], scalar=bproj[:, fc:fc + 1],
                    in1=t["ep1"][:], op0=ALU.add, op1=ALU.add), None)
                vv = ctx.op("vector", lambda e, fc=fc, jj=jj: e.tensor_tensor(
                    out=t["xout"][jj % 2][:], in0=t["ep1"][:],
                    in1=t["stage"][:, fc * TG:(fc + 1) * TG], op=ALU.add), V)
                pj_ev[fc % 2] = vv
                ctx.wait("gpsimd", V, vv)
                ctx.dma("gpsimd", d["x3T"].ap()[fc * 128:(fc + 1) * 128,
                                                (s_ * GPS + gl) * TG:(s_ * GPS + gl + 1) * TG],
                        t["xout"][jj % 2][:], "X3S")
                jj += 1
            last_v = ctx.cnt[V]
        ctx.drain_dma(["X3S", LD, "WL", "QS", "QL", "TB", "S3", "S3L"])
        ctx.barrier()
        es_at.close()
    es.close()


def declare_p2(nc, d, x2_is_input):
    for name, shape, dt in [
            ("wqkv", [D, 3072], F32), ("wproj", [D, D], F32),
            ("bqkv24", [128, 24], F32), ("bvrow", [1, 1024], F32),
            ("bproj8", [128, 8], F32), ("g8", [128, 8], F32),
            ("ropeC", [512, S], BF16), ("ropeSp", [512, S], BF16),
            ("ropeSn", [512, S], BF16)]:
        d[name] = nc.dram_tensor(name, shape, dt, kind="ExternalInput")
    if x2_is_input:
        d["x2T"] = nc.dram_tensor("x2T", [D, T], F32, kind="ExternalInput")
        for name, shape, dt in [
                ("ones_bf", [128, 128], BF16), ("ident_f", [128, 128], F32),
                ("ident_b", [128, 128], BF16)]:
            d[name] = nc.dram_tensor(name, shape, dt, kind="ExternalInput")
    d["qs"] = nc.dram_tensor("qs", [D, S], BF16)
    d["stats3"] = nc.dram_tensor("stats3", [2, S], F32)
    d["x3T"] = nc.dram_tensor("x3T", [D, T], F32, kind="ExternalOutput")


def build_p2():
    nc = bass.Bass("TRN2", target_bir_lowering=False)
    d = {}
    declare_p2(nc, d, x2_is_input=True)
    es = ExitStack()
    ctx = Ctx(nc, es)
    with es:
        t = alloc_common(ctx)
        emit_phase2(ctx, t, d, load_consts=True)
        ctx.build_block()
    return nc



def emit_phase3(ctx, t, d, load_consts=False, upto="G"):
    t = dict(t)
    V, SC, MM, LD, ST = "VE", "SC", "MM", "LD", "ST"
    for nm in (ST, "WL", LD, "H3S", "H3L", "X4S"):
        ctx.sem(nm)
    es = ExitStack()
    t["lnf"] = ctx.sbuf("p3_lnf", [128, DC * TG], F32, es)
    xq4 = ctx.sbuf("p3_xq4", [128, DC * T], BF16, es)
    h3sl = [ctx.sbuf(f"p3_h3{i}", [128, T], BF16, es) for i in range(2)]
    wsl = ctx.sbuf("p3_wsl", [128, 2 * 1024], I32, es)
    wqg = ctx.sbuf("p3_wqg", [128, DC * 128], BF16, es)
    wqu = ctx.sbuf("p3_wqu", [128, DC * 128], BF16, es)
    silu_t = [ctx.sbuf(f"p3_si{i}", [128, TG], F32, es) for i in range(2)]
    if load_consts:
        ctx.dma("sync", t["ones"][:], d["ones_bf"].ap(), LD)
        ctx.dma("sync", t["idf"][:], d["ident_f"].ap(), LD)
        ctx.dma("sync", t["idb"][:], d["ident_b"].ap(), LD)
    last_v = 0
    for g in range(NG):
        lv = load_stage(ctx, t, d["x3T"].ap(), g, guard_v=last_v)
        ctx.wait("vector", LD, lv)
        last_v = emit_ln_group(ctx, t, g, want_lnbf=True, want_absmax=False,
                               lnbf_dst=xq4, lnbf_g=g)
    if upto == "LN":
        for c in range(DC):
            vv = ctx.op("vector", lambda e, c=c: e.tensor_copy(out=t["xout"][0][:], in_=xq4[:, c * T: c * T + TG]), "VE")
            ctx.wait("gpsimd", "VE", vv)
            ctx.dma("gpsimd", d["dbg3"].ap()[c * 128:(c + 1) * 128, 0:TG], t["xout"][0][:], "X4S")
            ctx.wait("vector", "X4S", ctx.cnt["X4S"])
        for c in range(DC):
            vv = ctx.op("vector", lambda e, c=c: e.tensor_copy(out=t["xout"][0][:], in_=xq4[:, c * T + 3 * TG: c * T + 4 * TG]), "VE")
            ctx.wait("gpsimd", "VE", vv)
            ctx.dma("gpsimd", d["dbg3"].ap()[c * 128:(c + 1) * 128, TG:2 * TG], t["xout"][0][:], "X4S")
            ctx.wait("vector", "X4S", ctx.cnt["X4S"])
        ctx.drain_dma(["X4S", "ST", LD, "WL"])
        ctx.barrier()
        es.close()
        return
    h3_store_v = [0, 0]
    prev_cast_v = last_v
    h2v = [0, 0]
    for i in range(IC1):
        ctx.wait("sync", V, prev_cast_v)
        ctx.wait("vector", "H3S", h3_store_v[i % 2])
        for c in range(DC):
            ctx.dma("sync", wsl[:, c * 128:(c + 1) * 128],
                    d["w1gT"].ap()[c * 128:(c + 1) * 128, i * 128:(i + 1) * 128], "WL")
        for c in range(DC):
            ctx.dma("sync", wsl[:, 1024 + c * 128: 1024 + (c + 1) * 128],
                    d["w1uT"].ap()[c * 128:(c + 1) * 128, i * 128:(i + 1) * 128], "WL")
        lv = ctx.cnt["WL"]
        ctx.wait("vector", "WL", lv)
        ctx.op("vector", lambda e: e.tensor_copy(out=wqg[:], in_=wsl[:, 0:1024]), None)
        prev_cast_v = ctx.op("vector", lambda e: e.tensor_copy(
            out=wqu[:], in_=wsl[:, 1024:2048]), V)
        ctx.wait("tensor", V, prev_cast_v)
        for tg in range(NG):
            pA = t["ps"][4] if tg % 2 == 0 else t["ps"][2]
            pB = t["ps"][5] if tg % 2 == 0 else t["ps"][3]
            if tg >= 2:
                ctx.wait("tensor", V, h2v[tg % 2])
            for k in range(DC):
                ctx.op("tensor", lambda e, k=k, tg=tg, pA=pA: e.matmul(
                    out=pA[:], lhsT=wqg[:, k * 128:(k + 1) * 128],
                    rhs=xq4[:, k * T + tg * TG: k * T + (tg + 1) * TG],
                    start=(k == 0), stop=(k == DC - 1)), None)
            for k in range(DC):
                ctx.op("tensor", lambda e, k=k, tg=tg, pB=pB: e.matmul(
                    out=pB[:], lhsT=wqu[:, k * 128:(k + 1) * 128],
                    rhs=xq4[:, k * T + tg * TG: k * T + (tg + 1) * TG],
                    start=(k == 0), stop=(k == DC - 1)), MM if k == DC - 1 else None)
            mmv = ctx.cnt[MM]
            ctx.wait("scalar", MM, mmv)
            sv = ctx.op("scalar", lambda e, tg=tg, pA=pA: e.activation(
                out=silu_t[tg % 2][:], in_=pA[:], func=AF.Silu,
                scale=1.0 / QMAX), SC)
            ctx.wait("vector", SC, sv)
            h2v[tg % 2] = ctx.op("vector", lambda e, tg=tg, i=i, pB=pB: e.tensor_tensor(
                out=h3sl[i % 2][:, tg * TG:(tg + 1) * TG], in0=pB[:],
                in1=silu_t[tg % 2][:], op=ALU.mult), V)
        ctx.wait("gpsimd", V, h2v[(NG - 1) % 2])
        h3_store_v[i % 2] = ctx.dma(
            "gpsimd", d["h3s"].ap()[i * 128:(i + 1) * 128, :], h3sl[i % 2][:], "H3S")
    if upto.startswith("E"):
        ioff = {"E": 0, "E2": 8, "E3": 16}[upto]
        ctx.wait("gpsimd", "H3S", ctx.cnt["H3S"])
        for i in range(DC):
            ctx.dma("gpsimd", d["dbg3"].ap()[i * 128:(i + 1) * 128, :],
                    d["h3s"].ap()[(ioff + i) * 128:(ioff + i + 1) * 128, :], "X4S")
        ctx.drain_dma(["X4S", "ST", LD, "WL", "H3S"])
        ctx.barrier()
        es.close()
        return
    es.close()

    es2 = ExitStack()
    wdqT = ctx.sbuf("p3_wdqT", [128, IC1 * 1024], BF16, es2)
    h3q = ctx.sbuf("p3_h3q", [128, IC1 * TG], BF16, es2)
    wTsl = ctx.sbuf("p3_wTsl", [128, 1024], I32, es2)
    ctx.wait("sync", "H3S", ctx.cnt["H3S"])
    prev_v = ctx.cnt[V]
    for i in range(IC1):
        ctx.wait("sync", V, prev_v)
        lv = ctx.dma("sync", wTsl[:], d["w1dT"].ap()[i * 128:(i + 1) * 128, :], "WL")
        ctx.wait("vector", "WL", lv)
        prev_v = ctx.op("vector", lambda e, i=i: e.tensor_copy(
            out=wdqT[:, i * 1024:(i + 1) * 1024], in_=wTsl[:]), V)
    ctx.wait("sync", "H3S", ctx.cnt["H3S"])
    dmm = 0
    last_vv = prev_v
    jj = 0
    for tg in range(NG):
        if tg > 0:
            ctx.wait("sync", MM, dmm)
        lv = 0
        for i in range(IC1):
            lv = ctx.dma("sync", h3q[:, i * TG:(i + 1) * TG],
                         d["h3s"].ap()[i * 128:(i + 1) * 128, tg * TG:(tg + 1) * TG], "H3L")
        if tg == 0:
            ctx.wait("tensor", V, prev_v)
        ctx.wait("tensor", "H3L", lv)
        for hh in range(2):
            if not (tg == 0 and hh == 0):
                ctx.wait("tensor", V, last_vv)
            for i in range(IC1):
                for h4 in range(4):
                    h = hh * 4 + h4
                    ctx.op("tensor", lambda e, i=i, h=h, h4=h4: e.matmul(
                        out=t["ps"][h4][:], lhsT=wdqT[:, i * 1024 + h * 128:
                                                      i * 1024 + (h + 1) * 128],
                        rhs=h3q[:, i * TG:(i + 1) * TG],
                        start=(i == 0), stop=(i == IC1 - 1)),
                        MM if (i == IC1 - 1 and h4 == 3) else None)
            dmm = ctx.cnt[MM]
            ctx.wait("sync", V, last_vv)
            lv = 0
            for h4 in range(4):
                h = hh * 4 + h4
                lv = ctx.dma("sync", t["stage"][:, h * TG:(h + 1) * TG],
                             d["x3T"].ap()[h * 128:(h + 1) * 128, tg * TG:(tg + 1) * TG], LD)
            ctx.wait("vector", LD, lv)
            ctx.wait("vector", MM, dmm)
            for h4 in range(4):
                h = hh * 4 + h4
                if jj >= 2:
                    ctx.wait("vector", "X4S", 16 * (jj - 1))
                if upto == "G2":
                    vv = ctx.op("vector", lambda e, h4=h4, jj=jj: e.tensor_scalar(
                        out=t["xout"][jj % 2][:], in0=t["ps"][h4][:],
                        scalar1=1.0 / (QMAX * QMAX), scalar2=None, op0=ALU.mult), V)
                else:
                    vv = ctx.op("vector", lambda e, h=h, h4=h4, jj=jj: e.scalar_tensor_tensor(
                        out=t["xout"][jj % 2][:], in0=t["ps"][h4][:],
                        scalar=1.0 / (QMAX * QMAX), in1=t["stage"][:, h * TG:(h + 1) * TG],
                        op0=ALU.mult, op1=ALU.add), V)
                last_vv = vv
                ctx.wait("gpsimd", V, vv)
                ctx.dma("gpsimd", d["x4T"].ap()[h * 128:(h + 1) * 128, tg * TG:(tg + 1) * TG],
                        t["xout"][jj % 2][:], "X4S")
                jj += 1
    ctx.drain_dma(["X4S", LD, "WL", "H3L", "H3S"])
    ctx.barrier()
    es2.close()


def declare_p3(nc, d, x3_is_input):
    for name, shape, dt in [
            ("w1gT", [D, 3072], I32), ("w1uT", [D, 3072], I32),
            ("w1dT", [3072, D], I32)]:
        d[name] = nc.dram_tensor(name, shape, dt, kind="ExternalInput")
    if x3_is_input:
        d["x3T"] = nc.dram_tensor("x3T", [D, T], F32, kind="ExternalInput")
        for name, shape, dt in [
                ("ones_bf", [128, 128], BF16), ("ident_f", [128, 128], F32),
                ("ident_b", [128, 128], BF16)]:
            d[name] = nc.dram_tensor(name, shape, dt, kind="ExternalInput")
    d["h3s"] = nc.dram_tensor("h3s", [3072, T], BF16)
    d["x4T"] = nc.dram_tensor("x4T", [D, T], F32, kind="ExternalOutput")
    d["dbg3"] = nc.dram_tensor("dbg3", [D, T], F32, kind="ExternalOutput")


def build_p3(upto="G"):
    nc = bass.Bass("TRN2", target_bir_lowering=False)
    d = {}
    declare_p3(nc, d, x3_is_input=True)
    es = ExitStack()
    ctx = Ctx(nc, es)
    with es:
        t = alloc_common(ctx)
        emit_phase3(ctx, t, d, load_consts=True, upto=upto)
        ctx.build_block()
    return nc



def host_prep(inputs):
    c = {}
    c["WdT"] = np.ascontiguousarray(inputs["L0_attn_W_delta"].T)
    c["bdelta8"] = np.ascontiguousarray(inputs["L0_attn_b_delta"].reshape(8, 128).T)
    c["WbT128"] = np.ascontiguousarray(
        inputs["L0_attn_W_B"].T.reshape(8, 128, 16).transpose(1, 0, 2).reshape(128, 128))
    c["bB"] = np.ascontiguousarray(inputs["L0_attn_b_B"].reshape(16, 1))
    c["Alog"] = np.ascontiguousarray(inputs["L0_attn_A_log"].reshape(16, 1))
    c["Dcol8"] = np.ascontiguousarray(inputs["L0_attn_D_param"].reshape(8, 128).T)
    c["WoT"] = np.ascontiguousarray(inputs["L0_attn_W_out"].T)
    c["bout8"] = np.ascontiguousarray(inputs["L0_attn_b_out"].reshape(8, 128).T)
    c["ones_bf"] = np.ones((128, 128), dtype=BF)
    c["ident_f"] = np.eye(128, dtype=np.float32)
    c["ident_b"] = np.eye(128, dtype=BF)
    c["clip2"] = np.concatenate([inputs["L0_ffn_clip_x"],
                                 inputs["L0_ffn_clip_down"]]).reshape(1, 2).astype(np.float32)
    c["wg_nat"] = np.ascontiguousarray(inputs["L0_ffn_w_gate"])
    c["wu_nat"] = np.ascontiguousarray(inputs["L0_ffn_w_up"])
    c["wgT"] = np.ascontiguousarray(inputs["L0_ffn_w_gate"].T)
    c["wuT"] = np.ascontiguousarray(inputs["L0_ffn_w_up"].T)
    c["wd_nat"] = np.ascontiguousarray(inputs["L0_ffn_w_down"])
    c["wdT"] = np.ascontiguousarray(inputs["L0_ffn_w_down"].T)
    c["wqkv"] = np.ascontiguousarray(inputs["L1_attn_w_qkv"])
    c["wproj"] = np.ascontiguousarray(inputs["L1_attn_w_proj"])
    c["bqkv24"] = np.ascontiguousarray(inputs["L1_attn_b_qkv"].reshape(24, 128).T)
    c["bvrow"] = np.ascontiguousarray(inputs["L1_attn_b_qkv"][2048:].reshape(1, 1024))
    c["bproj8"] = np.ascontiguousarray(inputs["L1_attn_b_proj"].reshape(8, 128).T)
    c["g8"] = np.ascontiguousarray(inputs["L1_attn_g_norm"].reshape(8, 128).T)
    half = D // 2
    invf = 1.0 / 10000.0 ** (np.arange(half) / half)
    ang = np.arange(S)[:, None] * invf[None, :]
    cosr = np.round(np.cos(ang)).astype(BF)
    sinr = np.round(np.sin(ang)).astype(BF)
    c["ropeC"] = np.ascontiguousarray(cosr.T)
    c["ropeSp"] = np.ascontiguousarray(sinr.T)
    c["ropeSn"] = np.ascontiguousarray((-sinr).T)
    c["w1gT"] = np.ascontiguousarray(inputs["L1_ffn_w_gate"].T)
    c["w1uT"] = np.ascontiguousarray(inputs["L1_ffn_w_up"].T)
    c["w1dT"] = np.ascontiguousarray(inputs["L1_ffn_w_down"].T)
    return c


_P0_KEYS = ["ones_bf", "ident_f", "ident_b", "x0T", "WdT", "bdelta8", "WbT128",
            "bB", "Alog", "Dcol8", "WoT", "bout8"]
_P1_KEYS = ["ones_bf", "ident_f", "ident_b", "clip2", "wg_nat", "wu_nat",
            "wgT", "wuT", "wd_nat", "wdT"]
_P3_KEYS = ["ones_bf", "ident_f", "ident_b", "w1gT", "w1uT", "w1dT"]
_P2_KEYS = ["ones_bf", "ident_f", "ident_b", "wqkv", "wproj", "bqkv24", "bvrow",
            "bproj8", "g8", "ropeC", "ropeSp", "ropeSn"]


def declare_p0(nc, d):
    for name, shape, dt in [
            ("ones_bf", [128, 128], BF16), ("ident_f", [128, 128], F32),
            ("ident_b", [128, 128], BF16), ("x0T", [D, T], F32),
            ("WdT", [D, D], F32), ("bdelta8", [128, 8], F32),
            ("WbT128", [128, 128], F32), ("bB", [16, 1], F32),
            ("Alog", [16, 1], F32), ("Dcol8", [128, 8], F32),
            ("WoT", [16, D], F32), ("bout8", [128, 8], F32)]:
        d[name] = nc.dram_tensor(name, shape, dt, kind="ExternalInput")


def declare_p1(nc, d, x1_is_input):
    for name, shape, dt in [
            ("clip2", [1, 2], F32), ("wg_nat", [2816, D], F32),
            ("wu_nat", [2816, D], F32), ("wgT", [D, 2816], F32),
            ("wuT", [D, 2816], F32), ("wd_nat", [D, 2816], F32),
            ("wdT", [2816, D], F32)]:
        d[name] = nc.dram_tensor(name, shape, dt, kind="ExternalInput")
    if x1_is_input:
        d["x1T"] = nc.dram_tensor("x1T", [D, T], F32, kind="ExternalInput")
        for name, shape, dt in [
                ("ones_bf", [128, 128], BF16), ("ident_f", [128, 128], F32),
                ("ident_b", [128, 128], BF16)]:
            d[name] = nc.dram_tensor(name, shape, dt, kind="ExternalInput")
    d["lnfs"] = nc.dram_tensor("lnfs", [D, T], F32)
    d["h2s"] = nc.dram_tensor("h2s", [2816, T], F32)
    d["cci"] = nc.dram_tensor("cci", [1, 1], F32)
    d["cco"] = nc.dram_tensor("cco", [1, 1], F32)
    d["x2T"] = nc.dram_tensor("x2T", [D, T], F32, kind="ExternalOutput")
    d["dbg"] = nc.dram_tensor("dbg", [D, T], F32, kind="ExternalOutput")


def build_p0():
    nc = bass.Bass("TRN2", target_bir_lowering=False)
    d = {}
    declare_p0(nc, d)
    d["x1T"] = nc.dram_tensor("x1T", [D, T], F32, kind="ExternalOutput")
    es = ExitStack()
    ctx = Ctx(nc, es)
    with es:
        t = alloc_common(ctx)
        emit_phase0(ctx, t, d)
        ctx.build_block()
    return nc


def build_p1(upto="G"):
    nc = bass.Bass("TRN2", target_bir_lowering=False)
    d = {}
    declare_p1(nc, d, x1_is_input=True)
    es = ExitStack()
    ctx = Ctx(nc, es)
    with es:
        t = alloc_common(ctx)
        emit_phase1(ctx, t, d, load_consts=True, upto=upto)
        ctx.build_block()
    return nc


def kernel_unfused(**inputs):
    x = np.asarray(inputs["L0_attn_x"])
    com = host_prep(inputs)
    xouts = {}
    nc0 = build_p0()
    maps = []
    for core in range(NCORE):
        m = {k: com[k] for k in _P0_KEYS if k != "x0T"}
        xc = x[core * SPC:(core + 1) * SPC].reshape(T, D)
        m["x0T"] = np.ascontiguousarray(xc.T)
        maps.append(m)
    r0 = run_bass_kernel_spmd(nc0, maps, core_ids=list(range(NCORE)))
    nc1 = build_p1()
    maps = []
    for core in range(NCORE):
        m = {k: com[k] for k in _P1_KEYS}
        m["x1T"] = np.asarray(r0.results[core]["x1T"])
        maps.append(m)
    r1 = run_bass_kernel_spmd(nc1, maps, core_ids=list(range(NCORE)))
    nc2 = build_p2()
    maps = []
    for core in range(NCORE):
        m = {k: com[k] for k in _P2_KEYS}
        m["x2T"] = np.asarray(r1.results[core]["x2T"])
        maps.append(m)
    r2 = run_bass_kernel_spmd(nc2, maps, core_ids=list(range(NCORE)))
    nc3 = build_p3()
    maps = []
    for core in range(NCORE):
        m = {k: com[k] for k in _P3_KEYS}
        m["x3T"] = np.asarray(r2.results[core]["x3T"])
        maps.append(m)
    r3 = run_bass_kernel_spmd(nc3, maps, core_ids=list(range(NCORE)))
    out = np.empty((B, S, D), dtype=np.float32)
    for core in range(NCORE):
        out[core * SPC:(core + 1) * SPC] = (
            np.asarray(r3.results[core]["x4T"]).T.reshape(SPC, S, D))
    return out


def build_fused(dbg=False, nphases=4, p1upto="G"):
    nc = bass.Bass("TRN2", target_bir_lowering=False)
    d = {}
    declare_p0(nc, d)
    declare_p1(nc, d, x1_is_input=False)
    declare_p2(nc, d, x2_is_input=False)
    declare_p3(nc, d, x3_is_input=False)
    kind = "ExternalOutput" if dbg else "Internal"
    d["x1T"] = nc.dram_tensor("x1T", [D, T], F32, kind=kind)
    d["x2T"] = nc.dram_tensor("x2T_i", [D, T], F32, kind=kind)
    d["x3T"] = nc.dram_tensor("x3T_i", [D, T], F32, kind=kind)
    es = ExitStack()
    ctx = Ctx(nc, es)
    with es:
        t = alloc_common(ctx)
        emit_phase0(ctx, t, d)
        if nphases == -1:
            ctx.barrier()
        if nphases == -2:
            for nm in ("H2S", "CC", "CD", "H2LA", "H2LB", "X2S"):
                ctx.sem(nm)
        if nphases == -3:
            cols = ctx.sbuf("p1_cols", [128, 8], F32)
            es3 = ExitStack()
            a1 = ctx.sbuf("p1_lnf", [128, DC * TG], F32, es3)
            a2 = ctx.sbuf("p1_xint", [128, DC * T], BF16, es3)
            a3 = [ctx.sbuf(f"p1_h2{i}", [128, T], F32, es3) for i in range(2)]
            ctx.barrier()
            es3.close()
        if nphases == -4:
            ctx.drain_dma(["ST", "LD", "WL"])
            ctx.barrier()
        if nphases == -5:
            cols = ctx.sbuf("p1_cols", [128, 8], F32)
            es3 = ExitStack()
            names = [("p1_lnf", [128, DC * TG], F32), ("p1_xint", [128, DC * T], BF16),
                     ("p1_h20", [128, T], F32), ("p1_h21", [128, T], F32),
                     ("p1_wnat", [128, 2 * D], F32), ("p1_wTg", [128, DC * 128], F32),
                     ("p1_wTu", [128, DC * 128], F32), ("p1_wqg", [128, DC * 128], BF16),
                     ("p1_wqu", [128, DC * 128], BF16), ("p1_itmp", [128, 1024], I32),
                     ("p1_rig", [128, 128], F32), ("p1_riu", [128, 128], F32),
                     ("p1_rc2", [128, 2], BF16), ("p1_rr2", [1, 256], BF16),
                     ("p1_am2", [128, 1], F32), ("p1_cx", [1, 2], F32),
                     ("p1_si0", [128, TG], F32), ("p1_si1", [128, TG], F32)]
            for nm_, sh_, dt_ in names:
                ctx.sbuf(nm_, sh_, dt_, es3)
            ctx.barrier()
            es3.close()
        if nphases > 1:
            emit_phase1(ctx, t, d, upto=p1upto)
        if nphases > 2:
            emit_phase2(ctx, t, d)
        if nphases > 3:
            emit_phase3(ctx, t, d)
        ctx.build_block()
    return nc


_ALL_KEYS = sorted(set(_P0_KEYS + _P1_KEYS + _P2_KEYS + _P3_KEYS))


def kernel(**inputs):
    x = np.asarray(inputs["L0_attn_x"])
    com = host_prep(inputs)
    nc = build_fused()
    maps = []
    for core in range(NCORE):
        m = {k: com[k] for k in _ALL_KEYS if k != "x0T"}
        xc = x[core * SPC:(core + 1) * SPC].reshape(T, D)
        m["x0T"] = np.ascontiguousarray(xc.T)
        maps.append(m)
    r = run_bass_kernel_spmd(nc, maps, core_ids=list(range(NCORE)))
    out = np.empty((B, S, D), dtype=np.float32)
    for core in range(NCORE):
        out[core * SPC:(core + 1) * SPC] = (
            np.asarray(r.results[core]["x4T"]).T.reshape(SPC, S, D))
    return out


# version 04
# speedup vs baseline: 1.0115x; 1.0115x over previous
import numpy as np
import ml_dtypes
from contextlib import ExitStack
import concourse.bass as bass
import concourse.mybir as mybir
from concourse.bass_utils import run_bass_kernel_spmd

F32 = mybir.dt.float32
BF16 = mybir.dt.bfloat16
I32 = mybir.dt.int32
AF = mybir.ActivationFunctionType
ALU = mybir.AluOpType
AX = mybir.AxisListType.X
BF = ml_dtypes.bfloat16

B, S, D = 16, 2048, 1024
NCORE = 8
SPC = B // NCORE
T = SPC * S
TG = 512
NG = T // TG
GPS = S // TG
DC = D // 128
IC0 = 2816 // 128
IC1 = 3072 // 128
MAGIC = float(3 * 2.0 ** 22)
LN3 = 1.0986122886681098
QMAX = 127.0

ENGINES = ("sync", "scalar", "vector", "tensor", "gpsimd")


class Ctx:
    def __init__(self, nc, es):
        self.nc = nc
        self.es = es
        self.emit = {e: [] for e in ENGINES}
        self._sems = {}
        self.cnt = {}
        self.waits = []
        self.nbarrier = 0

    def sem(self, name):
        if name not in self._sems:
            self._sems[name] = self.es.enter_context(self.nc.semaphore(name))
            self.cnt[name] = 0
        return self._sems[name]

    def bump(self, name, n):
        self.sem(name)
        self.cnt[name] += n
        return self.cnt[name]

    def sbuf(self, name, shape, dt, scope=None):
        return (scope or self.es).enter_context(
            self.nc.sbuf_tensor(name, list(shape), dt))

    def psum(self, name, shape, dt, scope=None):
        return (scope or self.es).enter_context(
            self.nc.psum_tensor(name, list(shape), dt))

    def dma(self, q, out, in_, sem="LD"):
        s = self.sem(sem)
        v = self.bump(sem, 16)
        self.emit[q].append(
            lambda e, out=out, in_=in_, s=s: e.dma_start(out=out, in_=in_).then_inc(s, 16))
        return v

    def wait(self, eng, sem, val):
        if val <= 0:
            return
        s = self.sem(sem)
        self.waits.append((sem, val))
        self.emit[eng].append(lambda e, s=s, val=val: e.wait_ge(s, val))

    def op(self, eng, fn, sem=None, n=1):
        if sem is not None:
            s = self.sem(sem)
            v = self.bump(sem, n)
            self.emit[eng].append(lambda e, fn=fn, s=s, n=n: fn(e).then_inc(s, n))
            return v
        self.emit[eng].append(fn)
        return None

    def drain(self, eng="vector"):
        self.emit[eng].append(lambda e: e.drain())

    def barrier(self):
        self.nbarrier += 1
        s = self.sem("PH")
        tgt = len(ENGINES) * self.nbarrier
        self.cnt["PH"] = tgt
        for eng in ENGINES:
            self.emit[eng].append(lambda e, s=s: e.sem_inc(s, 1))
            self.emit[eng].append(lambda e, s=s, tgt=tgt: e.wait_ge(s, tgt))

    def drain_dma(self, sems):
        for sname in sems:
            if sname in self.cnt:
                self.wait("sync", sname, self.cnt[sname])

    def build_block(self):
        for sem_name, val in self.waits:
            assert val <= self.cnt[sem_name], (sem_name, val, self.cnt[sem_name])
        block = self.es.enter_context(self.nc.Block())
        for eng in ENGINES:
            if self.emit[eng]:
                def runner(e, eng=eng):
                    for f in self.emit[eng]:
                        f(e)
                getattr(block, eng)(runner)


def alloc_common(ctx):
    t = {}
    t["stage"] = ctx.sbuf("t_stage", [128, DC * TG], F32)
    t["xb"] = ctx.sbuf("t_xb", [128, DC * TG], BF16)
    t["sq"] = ctx.sbuf("t_sq", [128, DC * TG], BF16)
    t["rowf"] = ctx.sbuf("t_rowf", [1, TG], F32)
    t["rowf2"] = ctx.sbuf("t_rowf2", [1, TG], F32)
    t["rowf3"] = ctx.sbuf("t_rowf3", [1, TG], F32)
    t["hrow"] = ctx.sbuf("t_hrow", [1, 4 * TG], BF16)
    t["mbf"] = ctx.sbuf("t_mbf", [128, TG], F32)
    t["rbf"] = ctx.sbuf("t_rbf", [128, TG], F32)
    t["amax"] = ctx.sbuf("t_amax", [128, 1], F32)
    t["ones"] = ctx.sbuf("t_ones", [128, 128], BF16)
    t["idf"] = ctx.sbuf("t_idf", [128, 128], F32)
    t["idb"] = ctx.sbuf("t_idb", [128, 128], BF16)
    t["scl"] = ctx.sbuf("t_scl", [1, 32], F32)
    t["sclb"] = ctx.sbuf("t_sclb", [1, 32], BF16)
    t["col1"] = ctx.sbuf("t_col1", [128, 1], F32)
    t["col2"] = ctx.sbuf("t_col2", [128, 1], F32)
    t["xout"] = [ctx.sbuf(f"t_xo{i}", [128, TG], F32) for i in range(2)]
    t["ep1"] = ctx.sbuf("t_ep1", [128, TG], F32)
    t["ps"] = [ctx.psum(f"t_ps{i}", [128, TG], F32) for i in range(6)]
    t["psb"] = [ctx.psum(f"t_psb{i}", [128, 2 * TG], BF16) for i in range(2)]
    return t


def emit_ln_group(ctx, t, g, want_lnbf=True, want_absmax=False, eps=1e-5, lnbf_dst=None, lnbf_g=0, want_apply=True):
    V, SC, MM = "VE", "SC", "MM"
    ps_s0, ps_s1, ps_b0, ps_b1 = t["ps"][0], t["ps"][1], t["ps"][2], t["ps"][3]
    rowf, rowf2, rowf3, hrow = t["rowf"], t["rowf2"], t["rowf3"], t["hrow"]
    for c in range(DC):
        ctx.op("vector", lambda e, c=c: e.tensor_copy(
            out=t["xb"][:, c * TG:(c + 1) * TG], in_=t["stage"][:, c * TG:(c + 1) * TG]), None)
        ctx.op("vector", lambda e, c=c: e.tensor_tensor(
            out=t["sq"][:, c * TG:(c + 1) * TG], in0=t["xb"][:, c * TG:(c + 1) * TG],
            in1=t["xb"][:, c * TG:(c + 1) * TG], op=ALU.mult),
            V if c == DC - 1 else None)
    v_cast = ctx.cnt[V]
    ctx.wait("tensor", V, v_cast)
    for c in range(DC):
        ctx.op("tensor", lambda e, c=c: e.matmul(
            out=ps_s0[:1, :], lhsT=t["ones"][:, :1], rhs=t["xb"][:, c * TG:(c + 1) * TG],
            start=(c == 0), stop=(c == DC - 1)), None)
    for c in range(DC):
        ctx.op("tensor", lambda e, c=c: e.matmul(
            out=ps_s1[:1, :], lhsT=t["ones"][:, :1], rhs=t["sq"][:, c * TG:(c + 1) * TG],
            start=(c == 0), stop=(c == DC - 1)), MM if c == DC - 1 else None)
    v_stat = ctx.cnt[MM]
    ctx.wait("vector", MM, v_stat)
    ctx.op("vector", lambda e: e.tensor_scalar(
        out=rowf[:], in0=ps_s0[:1, :], scalar1=1.0 / D, scalar2=None, op0=ALU.mult), None)
    ctx.op("vector", lambda e: e.tensor_tensor(
        out=rowf2[:], in0=rowf[:], in1=rowf[:], op=ALU.mult), None)
    ctx.op("vector", lambda e: e.scalar_tensor_tensor(
        out=rowf3[:], in0=ps_s1[:1, :], scalar=1.0 / D, in1=rowf2[:],
        op0=ALU.mult, op1=ALU.subtract), None)
    v_var = ctx.op("vector", lambda e, eps=eps: e.tensor_scalar(
        out=rowf3[:], in0=rowf3[:], scalar1=eps, scalar2=None, op0=ALU.add), V)
    ctx.wait("scalar", V, v_var)
    s_sq = ctx.op("scalar", lambda e: e.activation(
        out=rowf2[:], in_=rowf3[:], func=AF.Sqrt), SC)
    ctx.wait("vector", SC, s_sq)
    ctx.op("vector", lambda e: e.reciprocal(out=rowf3[:], in_=rowf2[:]), None)
    ctx.op("vector", lambda e: e.tensor_copy(out=hrow[:, 0:TG], in_=rowf[:]), None)
    ctx.op("vector", lambda e: e.tensor_tensor(
        out=hrow[:, TG:2 * TG], in0=rowf[:], in1=hrow[:, 0:TG], op=ALU.subtract), None)
    ctx.op("vector", lambda e: e.tensor_copy(out=hrow[:, 2 * TG:3 * TG], in_=rowf3[:]), None)
    v_r = ctx.op("vector", lambda e: e.tensor_tensor(
        out=hrow[:, 3 * TG:4 * TG], in0=rowf3[:], in1=hrow[:, 2 * TG:3 * TG],
        op=ALU.subtract), V)
    ctx.wait("tensor", V, v_r)
    ctx.op("tensor", lambda e: e.matmul(
        out=ps_b0[:], lhsT=t["ones"][:1, :], rhs=hrow[:, 0:TG], start=True, stop=True), None)
    v_m = ctx.op("tensor", lambda e: e.matmul(
        out=ps_b1[:], lhsT=t["ones"][:1, :], rhs=hrow[:, TG:2 * TG], start=True, stop=True), MM)
    ctx.wait("scalar", MM, v_m)
    s_m = ctx.op("scalar", lambda e: e.activation(
        out=t["mbf"][:], in_=ps_b0[:], func=AF.Copy), SC)
    ctx.wait("vector", SC, s_m)
    v_mb = ctx.op("vector", lambda e: e.tensor_tensor(
        out=t["mbf"][:], in0=t["mbf"][:], in1=ps_b1[:], op=ALU.add), V)
    ctx.wait("tensor", V, v_mb)
    ctx.op("tensor", lambda e: e.matmul(
        out=ps_b0[:], lhsT=t["ones"][:1, :], rhs=hrow[:, 2 * TG:3 * TG],
        start=True, stop=True), None)
    v_rr = ctx.op("tensor", lambda e: e.matmul(
        out=ps_b1[:], lhsT=t["ones"][:1, :], rhs=hrow[:, 3 * TG:4 * TG],
        start=True, stop=True), MM)
    ctx.wait("scalar", MM, v_rr)
    s_r = ctx.op("scalar", lambda e: e.activation(
        out=t["rbf"][:], in_=ps_b0[:], func=AF.Copy), SC)
    ctx.wait("vector", SC, s_r)
    ctx.op("vector", lambda e: e.tensor_tensor(
        out=t["rbf"][:], in0=t["rbf"][:], in1=ps_b1[:], op=ALU.add), None)
    if not want_apply:
        return ctx.op("vector", lambda e: e.tensor_copy(
            out=t["rowf"][:, :1], in_=t["rowf"][:, :1]), "VE")
    for c in range(DC):
        ctx.op("vector", lambda e, c=c: e.tensor_tensor(
            out=t["ep1"][:], in0=t["stage"][:, c * TG:(c + 1) * TG],
            in1=t["mbf"][:], op=ALU.subtract), None)
        ctx.op("vector", lambda e, c=c: e.tensor_tensor(
            out=t["lnf"][:, c * TG:(c + 1) * TG], in0=t["ep1"][:], in1=t["rbf"][:],
            op=ALU.mult), None)
        if want_lnbf:
            dst = (t["lnbf"][:, c * TG:(c + 1) * TG] if lnbf_dst is None else
                   lnbf_dst[:, c * T + lnbf_g * TG: c * T + (lnbf_g + 1) * TG])
            ctx.op("vector", lambda e, c=c, dst=dst: e.tensor_copy(
                out=dst, in_=t["lnf"][:, c * TG:(c + 1) * TG]), None)
        if want_absmax:
            ctx.op("vector", lambda e, c=c: e.tensor_reduce(
                out=t["col1"][:], in_=t["lnf"][:, c * TG:(c + 1) * TG], axis=AX,
                op=ALU.max, apply_absolute_value=True), None)
            ctx.drain()
            if g == 0 and c == 0:
                ctx.op("vector", lambda e: e.tensor_copy(
                    out=t["amax"][:], in_=t["col1"][:]), None)
            else:
                ctx.op("vector", lambda e: e.tensor_tensor(
                    out=t["amax"][:], in0=t["amax"][:], in1=t["col1"][:],
                    op=ALU.max), None)
            ctx.drain()
    return ctx.op("vector", lambda e: e.tensor_copy(
        out=t["rowf"][:, :1], in_=t["rowf"][:, :1]), V)


def load_stage(ctx, t, src_ap, g, guard_v=0):
    ctx.wait("sync", "VE", guard_v)
    v = 0
    for c in range(DC):
        v = ctx.dma("sync", t["stage"][:, c * TG:(c + 1) * TG],
                    src_ap[c * 128:(c + 1) * 128, g * TG:(g + 1) * TG], "LD")
    return v


def bcast_scalar_col(ctx, t, src11, dst_col):
    V, SC, MM = "VE", "SC", "MM"
    sb = t["sclb"]
    ctx.op("vector", lambda e: e.tensor_copy(out=sb[:, 0:1], in_=src11), None)
    ctx.drain()
    v1 = ctx.op("vector", lambda e: e.tensor_tensor(
        out=sb[:, 1:2], in0=src11, in1=sb[:, 0:1], op=ALU.subtract), V)
    ctx.drain()
    ctx.wait("tensor", V, v1)
    ctx.op("tensor", lambda e: e.matmul(
        out=t["ps"][2][:, 0:1], lhsT=t["ones"][:1, :], rhs=sb[:, 0:1],
        start=True, stop=True), None)
    v2 = ctx.op("tensor", lambda e: e.matmul(
        out=t["ps"][3][:, 0:1], lhsT=t["ones"][:1, :], rhs=sb[:, 1:2],
        start=True, stop=True), MM)
    ctx.wait("scalar", MM, v2)
    s1 = ctx.op("scalar", lambda e: e.activation(
        out=dst_col, in_=t["ps"][2][:, 0:1], func=AF.Copy), SC)
    ctx.wait("vector", SC, s1)
    r = ctx.op("vector", lambda e: e.tensor_tensor(
        out=dst_col, in0=dst_col, in1=t["ps"][3][:, 0:1], op=ALU.add), V)
    ctx.drain()
    return r



def emit_phase0(ctx, t, d):
    t = dict(t)
    V, SC, MM, LD, ST = "VE", "SC", "MM", "LD", "ST"
    es = ExitStack()
    t["lnf"] = ctx.sbuf("p0_lnf", [128, DC * TG], F32, es)
    t["lnbf"] = ctx.sbuf("p0_lnbf", [128, DC * TG], BF16, es)
    wd_bf = ctx.sbuf("p0_wd", [128, DC * D], BF16, es)
    wb_bf = ctx.sbuf("p0_wb", [128, 128], BF16, es)
    wo_bf = ctx.sbuf("p0_wo", [16, D], BF16, es)
    wstage = ctx.sbuf("p0_wst", [128, D], F32, es)
    Atmp = ctx.sbuf("p0_At", [16, 1], F32, es)
    bdelta = ctx.sbuf("p0_bd", [128, 8], F32, es)
    bB = ctx.sbuf("p0_bB", [16, 1], F32, es)
    Acol = ctx.sbuf("p0_Ac", [16, 1], F32, es)
    Dcol = ctx.sbuf("p0_Dc", [128, 8], F32, es)
    bout = ctx.sbuf("p0_bo", [128, 8], F32, es)
    sg = [ctx.sbuf(f"p0_sg{i}", [128, TG], BF16, es) for i in range(2)]
    delta_bf = ctx.sbuf("p0_deltab", [1, TG], BF16, es)
    mh_bf = ctx.sbuf("p0_mh", [1, TG], BF16, es)
    dm_bf = ctx.sbuf("p0_dm", [1, TG], BF16, es)
    Bm = ctx.sbuf("p0_Bm", [16, TG], BF16, es)
    ab_t = ctx.sbuf("p0_ab", [16, TG], BF16, es)
    bu_t = ctx.sbuf("p0_bu", [16, TG], BF16, es)
    st_bf = ctx.sbuf("p0_st", [16, TG], BF16, es)
    carry = ctx.sbuf("p0_carry", [16, 1], BF16, es)
    ctx.dma("sync", t["ones"][:], d["ones_bf"].ap(), LD)
    ctx.dma("sync", bdelta[:], d["bdelta8"].ap(), LD)
    ctx.dma("sync", bB[:], d["bB"].ap(), LD)
    ctx.dma("sync", Acol[:], d["Alog"].ap(), LD)
    ctx.dma("sync", Dcol[:], d["Dcol8"].ap(), LD)
    ctx.dma("sync", bout[:], d["bout8"].ap(), LD)
    ctx.dma("sync", wstage[:, :128], d["WbT128"].ap(), LD)
    ctx.dma("sync", t["idf"][:], d["ident_f"].ap(), LD)
    ctx.dma("sync", t["idb"][:], d["ident_b"].ap(), LD)
    v = ctx.dma("sync", t["lnf"][:16, :D], d["WoT"].ap(), LD)
    ctx.wait("scalar", LD, v)
    ctx.op("scalar", lambda e: e.activation(out=Atmp[:], in_=Acol[:], func=AF.Exp), None)
    ctx.drain("scalar")
    ctx.op("scalar", lambda e: e.activation(
        out=Acol[:], in_=Atmp[:], func=AF.Copy, scale=-1.0), SC)
    ctx.drain("scalar")
    ctx.wait("vector", LD, v)
    ctx.op("vector", lambda e: e.tensor_copy(out=wb_bf[:], in_=wstage[:, :128]), None)
    ctx.op("vector", lambda e: e.tensor_copy(out=wo_bf[:], in_=t["lnf"][:16, :D]), V)
    for c in range(DC):
        ctx.wait("sync", V, ctx.cnt[V])
        v = ctx.dma("sync", wstage[:, :D], d["WdT"].ap()[c * 128:(c + 1) * 128, :], "WL")
        ctx.wait("vector", "WL", v)
        ctx.op("vector", lambda e, c=c: e.tensor_copy(
            out=wd_bf[:, c * D:(c + 1) * D], in_=wstage[:, :D]), V)

    last_ep_v = ctx.cnt[V]
    ctx.sem(ST)
    st_base = ctx.cnt[ST]
    for g in range(NG):
        lv = load_stage(ctx, t, d["x0T"].ap(), g, guard_v=last_ep_v)
        ctx.wait("vector", LD, lv)
        v_ln = emit_ln_group(ctx, t, g, want_lnbf=True, want_absmax=False)
        ctx.wait("tensor", V, v_ln)
        for ec in range(8):
            pm = t["ps"][4 + ec % 2]
            for k in range(DC):
                ctx.op("tensor", lambda e, ec=ec, k=k, pm=pm: e.matmul(
                    out=pm[:], lhsT=wd_bf[:, k * D + ec * 128: k * D + (ec + 1) * 128],
                    rhs=t["lnbf"][:, k * TG:(k + 1) * TG],
                    start=(k == 0), stop=(k == DC - 1)), MM if k == DC - 1 else None)
            mmv = ctx.cnt[MM]
            ctx.wait("scalar", MM, mmv)
            sv = ctx.op("scalar", lambda e, ec=ec, pm=pm: e.activation(
                out=sg[ec % 2][:], in_=pm[:], func=AF.Sigmoid,
                bias=bdelta[:, ec:ec + 1]), SC)
            ctx.wait("tensor", SC, sv)
            ctx.op("tensor", lambda e, ec=ec: e.matmul(
                out=t["ps"][0][:1, :], lhsT=t["ones"][:, :1], rhs=sg[ec % 2][:],
                start=(ec == 0), stop=(ec == 7)), MM if ec == 7 else None)
        v_dsum = ctx.cnt[MM]
        ctx.wait("vector", MM, v_dsum)
        v_d = ctx.op("vector", lambda e: e.tensor_scalar(
            out=delta_bf[:], in0=t["ps"][0][:1, :], scalar1=0.099 / D, scalar2=0.001,
            op0=ALU.mult, op1=ALU.add), V)
        ctx.wait("tensor", V, v_d)
        for k in range(DC):
            ctx.op("tensor", lambda e, k=k: e.matmul(
                out=t["ps"][1][:16, :], lhsT=wb_bf[:, k * 16:(k + 1) * 16],
                rhs=t["lnbf"][:, k * TG:(k + 1) * TG],
                start=(k == 0), stop=(k == DC - 1)), None)
        for k in range(DC):
            ctx.op("tensor", lambda e, k=k: e.matmul(
                out=t["ps"][0][:1, :], lhsT=t["ones"][:, :1],
                rhs=t["lnbf"][:, k * TG:(k + 1) * TG],
                start=(k == 0), stop=(k == DC - 1)), MM if k == DC - 1 else None)
        v_bmmh = ctx.cnt[MM]
        ctx.wait("scalar", MM, v_bmmh)
        s_bm = ctx.op("scalar", lambda e: e.activation(
            out=Bm[:], in_=t["ps"][1][:16, :], func=AF.Identity, bias=bB[:]), SC)
        ctx.wait("vector", MM, v_bmmh)
        ctx.op("vector", lambda e: e.tensor_scalar(
            out=mh_bf[:], in0=t["ps"][0][:1, :], scalar1=1.0 / D, scalar2=None,
            op0=ALU.mult), None)
        v_dm = ctx.op("vector", lambda e: e.tensor_tensor(
            out=dm_bf[:], in0=delta_bf[:], in1=mh_bf[:], op=ALU.mult), V)
        ctx.wait("tensor", V, v_dm)
        ctx.op("tensor", lambda e: e.matmul(
            out=t["ps"][2][:16, :], lhsT=t["ones"][:1, :16], rhs=delta_bf[:],
            start=True, stop=True), None)
        v_bc2 = ctx.op("tensor", lambda e: e.matmul(
            out=t["ps"][3][:16, :], lhsT=t["ones"][:1, :16], rhs=dm_bf[:],
            start=True, stop=True), MM)
        ctx.wait("scalar", MM, v_bc2)
        s_ab = ctx.op("scalar", lambda e: e.activation(
            out=ab_t[:], in_=t["ps"][2][:16, :], func=AF.Exp, scale=Acol[:]), SC)
        ctx.wait("vector", SC, s_ab)
        ctx.op("vector", lambda e: e.tensor_tensor(
            out=bu_t[:], in0=t["ps"][3][:16, :], in1=Bm[:], op=ALU.mult), None)
        if g % GPS == 0:
            ctx.op("vector", lambda e: e.tensor_tensor_scan(
                out=st_bf[:], data0=ab_t[:], data1=bu_t[:], initial=0.0,
                op0=ALU.mult, op1=ALU.add), None)
        else:
            ctx.op("vector", lambda e: e.tensor_tensor_scan(
                out=st_bf[:], data0=ab_t[:], data1=bu_t[:], initial=carry[:],
                op0=ALU.mult, op1=ALU.add), None)
        ctx.drain()
        v_scan = ctx.op("vector", lambda e: e.tensor_copy(
            out=carry[:], in_=st_bf[:, TG - 1:TG]), V)
        ctx.drain()
        ctx.wait("tensor", V, v_scan)
        prev_vv = None
        for c in range(DC):
            if prev_vv is not None:
                ctx.wait("tensor", V, prev_vv)
            mmv = ctx.op("tensor", lambda e, c=c: e.matmul(
                out=t["ps"][4][:], lhsT=wo_bf[:, c * 128:(c + 1) * 128],
                rhs=st_bf[:], start=True, stop=True), MM)
            j = g * DC + c
            if j >= 2:
                ctx.wait("vector", ST, st_base + 16 * (j - 1))
            ctx.wait("vector", MM, mmv)
            ctx.op("vector", lambda e, c=c: e.scalar_tensor_tensor(
                out=t["ep1"][:], in0=t["lnf"][:, c * TG:(c + 1) * TG],
                scalar=Dcol[:, c:c + 1], in1=t["stage"][:, c * TG:(c + 1) * TG],
                op0=ALU.mult, op1=ALU.add), None)
            vv = ctx.op("vector", lambda e, c=c, j=j: e.scalar_tensor_tensor(
                out=t["xout"][j % 2][:], in0=t["ps"][4][:], scalar=bout[:, c:c + 1],
                in1=t["ep1"][:], op0=ALU.add, op1=ALU.add), V)
            prev_vv = vv
            ctx.wait("gpsimd", V, vv)
            ctx.dma("gpsimd", d["x1T"].ap()[c * 128:(c + 1) * 128,
                                            g * TG:(g + 1) * TG],
                    t["xout"][j % 2][:], ST)
        last_ep_v = prev_vv
    ctx.drain_dma([ST, LD, "WL"])
    ctx.barrier()
    es.close()



def emit_phase1(ctx, t, d, load_consts=False, upto="G"):
    t = dict(t)
    V, SC, MM, LD, ST = "VE", "SC", "MM", "LD", "ST"
    cols = ctx.sbuf("p1_cols", [128, 8], F32)
    es = ExitStack()
    t["lnf"] = ctx.sbuf("p1_lnf", [128, DC * TG], F32, es)
    xint = ctx.sbuf("p1_xint", [128, DC * T], BF16, es)
    h2sl = [ctx.sbuf(f"p1_h2{i}", [128, T // 2], F32, es) for i in range(2)]
    wnat = [ctx.sbuf(f"p1_wnat{i}", [128, 2 * D], F32, es) for i in range(2)]
    wTg = [ctx.sbuf(f"p1_wTg{i}", [128, DC * 128], F32, es) for i in range(2)]
    wTu = [ctx.sbuf(f"p1_wTu{i}", [128, DC * 128], F32, es) for i in range(2)]
    wqg = [ctx.sbuf(f"p1_wqg{i}", [128, DC * 128], BF16, es) for i in range(2)]
    wqu = [ctx.sbuf(f"p1_wqu{i}", [128, DC * 128], BF16, es) for i in range(2)]
    itmp = ctx.sbuf("p1_itmp", [128, 1024], I32, es)
    rinvg = ctx.sbuf("p1_rig", [128, 128], F32, es)
    rinvu = ctx.sbuf("p1_riu", [128, 128], F32, es)
    rcol2 = ctx.sbuf("p1_rc2", [128, 2], BF16, es)
    rrow2 = ctx.sbuf("p1_rr2", [1, 256], BF16, es)
    am2 = ctx.sbuf("p1_am2", [128, 1], F32, es)
    clipx = ctx.sbuf("p1_cx", [1, 2], F32, es)
    silu_t = [ctx.sbuf(f"p1_si{i}", [128, TG], F32, es) for i in range(2)]
    if load_consts:
        ctx.dma("sync", t["ones"][:], d["ones_bf"].ap(), LD)
        ctx.dma("sync", t["idf"][:], d["ident_f"].ap(), LD)
        ctx.dma("sync", t["idb"][:], d["ident_b"].ap(), LD)
    for nm in (ST, "H2S", "CC", "CD", "H2LA", "H2LB", "X2S", "WL", LD):
        ctx.sem(nm)
    if upto == "NOP":
        ctx.drain_dma([ST, LD, "WL"])
        ctx.barrier()
        es.close()
        return
    if upto == "LOAD1":
        lv = load_stage(ctx, t, d["x1T"].ap(), 0, guard_v=0)
        ctx.wait("vector", LD, lv)
        vv = ctx.op("vector", lambda e: e.tensor_copy(out=t["xb"][:, 0:TG], in_=t["stage"][:, 0:TG]), "VE")
        ctx.wait("gpsimd", "VE", vv)
        ctx.dma("gpsimd", d["x2T"].ap()[0:128, 0:TG], t["xb"][:, 0:TG], "X2S")
        ctx.drain_dma([ST, LD, "WL", "X2S"])
        ctx.barrier()
        es.close()
        return
    last_v = 0
    for g in range(NG):
        lv = load_stage(ctx, t, d["x1T"].ap(), g, guard_v=last_v)
        ctx.wait("vector", LD, lv)
        ctx.wait("vector", ST, ctx.cnt[ST])
        last_v = emit_ln_group(ctx, t, g, want_lnbf=False, want_absmax=True)
        ctx.wait("gpsimd", V, last_v)
        for c in range(DC):
            ctx.dma("gpsimd", d["lnfs"].ap()[c * 128:(c + 1) * 128, g * TG:(g + 1) * TG],
                    t["lnf"][:, c * TG:(c + 1) * TG], ST)
    if upto == "A":
        vv = ctx.op("vector", lambda e: e.tensor_copy(out=t["xout"][0][:], in_=t["stage"][:, 0:TG]), "VE")
        ctx.wait("gpsimd", "VE", vv)
        ctx.dma("gpsimd", d["x2T"].ap()[0:128, 0:TG], t["xout"][0][:], "X2S")
        ctx.drain_dma(["X2S", "ST", "LD", "WL", "H2S", "CC"])
        ctx.barrier()
        es.close()
        return

    ctx.dma("sync", clipx[:, 0:2], d["clip2"].ap(), LD)
    lv_c = ctx.cnt[LD]
    ctx.wait("tensor", V, last_v)
    mt = ctx.op("tensor", lambda e: e.transpose(
        out=t["ps"][4][:1, :128], in_=t["amax"][:], identity=t["idf"][:]), MM)
    ctx.wait("vector", MM, mt)
    vv = ctx.op("vector", lambda e: e.tensor_reduce(
        out=t["scl"][:, 0:1], in_=t["ps"][4][:1, :128], axis=AX, op=ALU.max), V)
    ctx.wait("gpsimd", V, vv)
    ctx.dma("gpsimd", d["cci"].ap(), t["scl"][:1, 0:1], "CC")
    ctx.wait("gpsimd", "CC", ctx.cnt["CC"])
    cv = ctx.op("gpsimd", lambda e: e.collective_compute(
        "AllReduce", ALU.max, replica_groups=[list(range(NCORE))],
        ins=[d["cci"].ap()], outs=[d["cco"].ap()]), "CD")
    ctx.wait("gpsimd", "CD", cv)
    ctx.dma("gpsimd", t["scl"][:1, 2:3], d["cco"].ap(), "CC")
    ctx.wait("scalar", "CC", ctx.cnt["CC"])
    ctx.wait("scalar", LD, lv_c)
    s_sig = ctx.op("scalar", lambda e: e.activation(
        out=t["scl"][:, 4:5], in_=clipx[:, 0:1], func=AF.Sigmoid), SC)
    ctx.wait("vector", SC, s_sig)
    ctx.wait("vector", "CC", ctx.cnt["CC"])
    ctx.op("vector", lambda e: e.tensor_tensor(
        out=t["scl"][:, 5:6], in0=t["scl"][:, 2:3], in1=t["scl"][:, 4:5], op=ALU.mult), None)
    ctx.drain()
    ctx.op("vector", lambda e: e.reciprocal(out=t["scl"][:, 6:7], in_=t["scl"][:, 5:6]), None)
    ctx.drain()
    ctx.op("vector", lambda e: e.tensor_scalar(
        out=t["scl"][:, 7:8], in0=t["scl"][:, 6:7], scalar1=QMAX, scalar2=None,
        op0=ALU.mult), None)
    ctx.drain()
    ctx.op("vector", lambda e: e.tensor_scalar(
        out=t["scl"][:, 8:9], in0=t["scl"][:, 5:6], scalar1=1.0 / QMAX, scalar2=None,
        op0=ALU.mult), V)
    ctx.drain()
    bcast_scalar_col(ctx, t, t["scl"][:1, 7:8], cols[:, 4:5])
    v_c = bcast_scalar_col(ctx, t, t["scl"][:1, 8:9], cols[:, 5:6])
    if upto == "Cdump":
        ctx.op("vector", lambda e: e.tensor_copy(out=t["xout"][0][:1, 0:32], in_=t["scl"][:]), None)
        ctx.op("vector", lambda e: e.tensor_copy(out=t["xout"][0][:, 32:40], in_=cols[:]), None)
        vv = ctx.op("vector", lambda e: e.tensor_copy(out=t["xout"][0][:, 40:41], in_=t["amax"][:]), "VE")
        ctx.wait("gpsimd", "VE", vv)
        ctx.dma("gpsimd", d["dbg"].ap()[0:128, 0:TG], t["xout"][0][:], "X2S")
        ctx.drain_dma(["X2S", "ST", "LD", "WL", "H2S", "CC"])
        ctx.barrier()
        es.close()
        return
    if upto == "C":
        vv = ctx.op("vector", lambda e: e.tensor_copy(out=t["xout"][0][:], in_=t["stage"][:, 0:TG]), "VE")
        ctx.wait("gpsimd", "VE", vv)
        ctx.dma("gpsimd", d["x2T"].ap()[0:128, 0:TG], t["xout"][0][:], "X2S")
        ctx.drain_dma(["X2S", "ST", "LD", "WL", "H2S", "CC"])
        ctx.barrier()
        es.close()
        return

    ctx.wait("sync", ST, ctx.cnt[ST])
    vq_last = ctx.cnt[V]
    for g in range(NG):
        ctx.wait("sync", V, vq_last)
        lv = 0
        for c in range(DC):
            lv = ctx.dma("sync", t["lnf"][:, c * TG:(c + 1) * TG],
                         d["lnfs"].ap()[c * 128:(c + 1) * 128, g * TG:(g + 1) * TG], LD)
        ctx.wait("vector", LD, lv)
        for c in range(DC):
            ctx.op("vector", lambda e, c=c: e.tensor_scalar(
                out=t["lnf"][:, c * TG:(c + 1) * TG], in0=t["lnf"][:, c * TG:(c + 1) * TG],
                scalar1=cols[:, 4:5], scalar2=MAGIC, op0=ALU.mult, op1=ALU.add), None)
            ctx.op("vector", lambda e, c=c: e.tensor_scalar(
                out=t["lnf"][:, c * TG:(c + 1) * TG], in0=t["lnf"][:, c * TG:(c + 1) * TG],
                scalar1=MAGIC, scalar2=-128.0, op0=ALU.subtract, op1=ALU.max), None)
            vq_last = ctx.op("vector", lambda e, c=c, g=g: e.tensor_scalar(
                out=xint[:, c * T + g * TG: c * T + (g + 1) * TG],
                in0=t["lnf"][:, c * TG:(c + 1) * TG],
                scalar1=127.0, scalar2=None, op0=ALU.min), V)

    if upto == "Ddump":
        for c in range(DC):
            vv = ctx.op("vector", lambda e, c=c: e.tensor_copy(out=t["xout"][0][:], in_=xint[:, c * T: c * T + TG]), "VE")
            ctx.wait("gpsimd", "VE", vv)
            ctx.dma("gpsimd", d["dbg"].ap()[c * 128:(c + 1) * 128, 0:TG], t["xout"][0][:], "X2S")
            ctx.wait("vector", "X2S", ctx.cnt["X2S"])
        ctx.drain_dma(["X2S", "ST", "LD", "WL", "H2S", "CC"])
        ctx.barrier()
        es.close()
        return
    if upto == "D":
        vv = ctx.op("vector", lambda e: e.tensor_copy(out=t["xout"][0][:], in_=t["stage"][:, 0:TG]), "VE")
        ctx.wait("gpsimd", "VE", vv)
        ctx.dma("gpsimd", d["x2T"].ap()[0:128, 0:TG], t["xout"][0][:], "X2S")
        ctx.drain_dma(["X2S", "ST", "LD", "WL", "H2S", "CC"])
        ctx.barrier()
        es.close()
        return
    h2_store_v = [0, 0]
    quant_v = [0, 0]
    mm_done = [0, 0]
    wl_v = [0, 0]
    h2v = [0, 0]
    def p1_load(i):
        ib = i % 2
        ctx.wait("sync", V, quant_v[ib])
        ctx.dma("sync", wnat[ib][:, 0:D], d["wg_nat"].ap()[i * 128:(i + 1) * 128, :], "WL")
        ctx.dma("sync", wnat[ib][:, D:2 * D], d["wu_nat"].ap()[i * 128:(i + 1) * 128, :], "WL")
        for c in range(DC):
            ctx.dma("sync", wTg[ib][:, c * 128:(c + 1) * 128],
                    d["wgT"].ap()[c * 128:(c + 1) * 128, i * 128:(i + 1) * 128], "WL")
        for c in range(DC):
            ctx.dma("sync", wTu[ib][:, c * 128:(c + 1) * 128],
                    d["wuT"].ap()[c * 128:(c + 1) * 128, i * 128:(i + 1) * 128], "WL")
        wl_v[ib] = ctx.cnt["WL"]
    p1_load(0)
    for i in range(IC0):
        ib = i % 2
        if i + 1 < IC0:
            p1_load(i + 1)
        lv = wl_v[ib]
        ctx.wait("vector", "WL", lv)
        ctx.wait("vector", MM, mm_done[ib])
        for w, natsl, wT, wq, rinvb, smax_i, esc_i in (
                (0, wnat[ib][:, 0:D], wTg[ib], wqg[ib], rinvg, 0, 2 if ib == 0 else 1),
                (1, wnat[ib][:, D:2 * D], wTu[ib], wqu[ib], rinvu, 3, 6 if ib == 0 else 7)):
            ctx.op("vector", lambda e, natsl=natsl, smax_i=smax_i: e.tensor_reduce(
                out=cols[:, smax_i:smax_i + 1], in_=natsl, axis=AX, op=ALU.max,
                apply_absolute_value=True), None)
            ctx.drain()
            ctx.op("vector", lambda e, smax_i=smax_i: e.reciprocal(
                out=t["col1"][:], in_=cols[:, smax_i:smax_i + 1]), None)
            ctx.drain()
            ctx.op("vector", lambda e: e.tensor_scalar(
                out=t["col2"][:], in0=t["col1"][:], scalar1=QMAX, scalar2=None,
                op0=ALU.mult), None)
            ctx.drain()
            ctx.op("vector", lambda e: e.tensor_copy(out=rcol2[:, 0:1], in_=t["col2"][:]), None)
            ctx.drain()
            v1 = ctx.op("vector", lambda e: e.tensor_tensor(
                out=rcol2[:, 1:2], in0=t["col2"][:], in1=rcol2[:, 0:1],
                op=ALU.subtract), V)
            ctx.drain()
            ctx.wait("tensor", V, v1)
            ctx.op("tensor", lambda e: e.transpose(
                out=t["psb"][0][:1, 0:128], in_=rcol2[:, 0:1], identity=t["idb"][:]), None)
            mt = ctx.op("tensor", lambda e: e.transpose(
                out=t["psb"][0][:1, 128:256], in_=rcol2[:, 1:2], identity=t["idb"][:]), MM)
            ctx.wait("vector", MM, mt)
            v2 = ctx.op("vector", lambda e: e.tensor_copy(
                out=rrow2[:], in_=t["psb"][0][:1, 0:256]), V)
            ctx.wait("tensor", V, v2)
            ctx.op("tensor", lambda e: e.matmul(
                out=t["ps"][2][:, :128], lhsT=t["ones"][:1, :], rhs=rrow2[:, 0:128],
                start=True, stop=True), None)
            v3 = ctx.op("tensor", lambda e: e.matmul(
                out=t["ps"][3][:, :128], lhsT=t["ones"][:1, :], rhs=rrow2[:, 128:256],
                start=True, stop=True), MM)
            ctx.wait("scalar", MM, v3)
            s1 = ctx.op("scalar", lambda e, rinvb=rinvb: e.activation(
                out=rinvb[:], in_=t["ps"][2][:, :128], func=AF.Copy), SC)
            ctx.wait("vector", SC, s1)
            ctx.op("vector", lambda e, rinvb=rinvb: e.tensor_tensor(
                out=rinvb[:], in0=rinvb[:], in1=t["ps"][3][:, :128], op=ALU.add), None)
            for c in range(DC):
                ctx.op("vector", lambda e, c=c, wT=wT, rinvb=rinvb: e.tensor_tensor(
                    out=itmp[:, c * 128:(c + 1) * 128], in0=wT[:, c * 128:(c + 1) * 128],
                    in1=rinvb[:], op=ALU.mult), None)
            for c in range(DC):
                ctx.op("vector", lambda e, c=c, wq=wq: e.tensor_copy(
                    out=wq[:, c * 128:(c + 1) * 128], in_=itmp[:, c * 128:(c + 1) * 128]),
                    None)
            ctx.op("vector", lambda e, smax_i=smax_i: e.tensor_tensor(
                out=t["col1"][:], in0=cols[:, smax_i:smax_i + 1], in1=cols[:, 5:6],
                op=ALU.mult), None)
            ctx.drain()
            quant_v[ib] = ctx.op("vector", lambda e, esc_i=esc_i: e.tensor_scalar(
                out=cols[:, esc_i:esc_i + 1], in0=t["col1"][:], scalar1=1.0 / QMAX,
                scalar2=None, op0=ALU.mult), V)
            ctx.drain()
        ctx.wait("tensor", V, quant_v[ib])
        if i == 0:
            ctx.wait("tensor", V, vq_last)
        for tg in range(NG):
            pA = t["ps"][4] if tg % 2 == 0 else t["ps"][2]
            pB = t["ps"][5] if tg % 2 == 0 else t["ps"][3]
            if tg >= 2:
                ctx.wait("tensor", V, h2v[tg % 2])
            for k in range(DC):
                ctx.op("tensor", lambda e, k=k, tg=tg, pA=pA, ib=ib: e.matmul(
                    out=pA[:], lhsT=wqg[ib][:, k * 128:(k + 1) * 128],
                    rhs=xint[:, k * T + tg * TG: k * T + (tg + 1) * TG],
                    start=(k == 0), stop=(k == DC - 1)), None)
            for k in range(DC):
                ctx.op("tensor", lambda e, k=k, tg=tg, pB=pB, ib=ib: e.matmul(
                    out=pB[:], lhsT=wqu[ib][:, k * 128:(k + 1) * 128],
                    rhs=xint[:, k * T + tg * TG: k * T + (tg + 1) * TG],
                    start=(k == 0), stop=(k == DC - 1)), MM if k == DC - 1 else None)
            mmv = ctx.cnt[MM]
            ctx.wait("scalar", MM, mmv)
            sv = ctx.op("scalar", lambda e, tg=tg, pA=pA, gi=(2 if ib == 0 else 1): e.activation(
                out=silu_t[tg % 2][:], in_=pA[:], func=AF.Silu,
                scale=cols[:, gi:gi + 1]), SC)
            ctx.wait("vector", SC, sv)
            half = tg // 4
            if tg % 4 == 0:
                ctx.wait("vector", "H2S", h2_store_v[half])
            h2v[tg % 2] = ctx.op("vector", lambda e, tg=tg, pB=pB, half=half, ui=(6 if ib == 0 else 7): e.scalar_tensor_tensor(
                out=h2sl[half][:, (tg % 4) * TG:(tg % 4 + 1) * TG], in0=pB[:],
                scalar=cols[:, ui:ui + 1], in1=silu_t[tg % 2][:],
                op0=ALU.mult, op1=ALU.mult), V)
            if tg % 4 == 3:
                ctx.op("vector", lambda e, half=half: e.tensor_reduce(
                    out=t["col1"][:], in_=h2sl[half][:], axis=AX, op=ALU.max,
                    apply_absolute_value=True), None)
                ctx.drain()
                if i == 0 and half == 0:
                    hv = ctx.op("vector", lambda e: e.tensor_copy(out=am2[:], in_=t["col1"][:]), V)
                else:
                    hv = ctx.op("vector", lambda e: e.tensor_tensor(
                        out=am2[:], in0=am2[:], in1=t["col1"][:], op=ALU.max), V)
                ctx.drain()
                ctx.wait("gpsimd", V, hv)
                h2_store_v[half] = ctx.dma(
                    "gpsimd", d["h2s"].ap()[i * 128:(i + 1) * 128,
                                            half * (T // 2):(half + 1) * (T // 2)],
                    h2sl[half][:], "H2S")
        mm_done[ib] = ctx.cnt[MM]
    if upto == "E":
        vv = ctx.op("vector", lambda e: e.tensor_copy(out=t["xout"][0][:], in_=t["stage"][:, 0:TG]), "VE")
        ctx.wait("gpsimd", "VE", vv)
        ctx.dma("gpsimd", d["x2T"].ap()[0:128, 0:TG], t["xout"][0][:], "X2S")
        ctx.drain_dma(["X2S", "ST", "LD", "WL", "H2S", "CC"])
        ctx.barrier()
        es.close()
        return

    if upto == "Edump":
        ctx.wait("gpsimd", "H2S", ctx.cnt["H2S"])
        for i in range(DC):
            ctx.dma("gpsimd", d["dbg"].ap()[i * 128:(i + 1) * 128, :],
                    d["h2s"].ap()[i * 128:(i + 1) * 128, :], "X2S")
        ctx.drain_dma(["X2S", "ST", "LD", "WL", "H2S", "CC"])
        ctx.barrier()
        es.close()
        return
    ctx.wait("tensor", V, ctx.cnt[V])
    mt = ctx.op("tensor", lambda e: e.transpose(
        out=t["ps"][4][:1, :128], in_=am2[:], identity=t["idf"][:]), MM)
    ctx.wait("vector", MM, mt)
    vv = ctx.op("vector", lambda e: e.tensor_reduce(
        out=t["scl"][:, 10:11], in_=t["ps"][4][:1, :128], axis=AX, op=ALU.max), V)
    ctx.wait("gpsimd", V, vv)
    ctx.dma("gpsimd", d["cci"].ap(), t["scl"][:1, 10:11], "CC")
    ctx.wait("gpsimd", "CC", ctx.cnt["CC"])
    cv = ctx.op("gpsimd", lambda e: e.collective_compute(
        "AllReduce", ALU.max, replica_groups=[list(range(NCORE))],
        ins=[d["cci"].ap()], outs=[d["cco"].ap()]), "CD")
    ctx.wait("gpsimd", "CD", cv)
    ctx.dma("gpsimd", t["scl"][:1, 12:13], d["cco"].ap(), "CC")
    ctx.wait("scalar", "CC", ctx.cnt["CC"])
    s_sig = ctx.op("scalar", lambda e: e.activation(
        out=t["scl"][:, 14:15], in_=clipx[:, 1:2], func=AF.Sigmoid), SC)
    ctx.wait("vector", SC, s_sig)
    ctx.wait("vector", "CC", ctx.cnt["CC"])
    ctx.op("vector", lambda e: e.tensor_tensor(
        out=t["scl"][:, 15:16], in0=t["scl"][:, 12:13], in1=t["scl"][:, 14:15], op=ALU.mult), None)
    ctx.drain()
    ctx.op("vector", lambda e: e.reciprocal(out=t["scl"][:, 16:17], in_=t["scl"][:, 15:16]), None)
    ctx.drain()
    ctx.op("vector", lambda e: e.tensor_scalar(
        out=t["scl"][:, 17:18], in0=t["scl"][:, 16:17], scalar1=QMAX, scalar2=None,
        op0=ALU.mult), None)
    ctx.drain()
    ctx.op("vector", lambda e: e.tensor_scalar(
        out=t["scl"][:, 18:19], in0=t["scl"][:, 15:16], scalar1=1.0 / QMAX, scalar2=None,
        op0=ALU.mult), V)
    ctx.drain()
    bcast_scalar_col(ctx, t, t["scl"][:1, 17:18], cols[:, 4:5])
    v_sx2 = bcast_scalar_col(ctx, t, t["scl"][:1, 18:19], cols[:, 5:6])
    if upto == "F":
        vv = ctx.op("vector", lambda e: e.tensor_copy(out=t["xout"][0][:], in_=t["stage"][:, 0:TG]), "VE")
        ctx.wait("gpsimd", "VE", vv)
        ctx.dma("gpsimd", d["x2T"].ap()[0:128, 0:TG], t["xout"][0][:], "X2S")
        ctx.drain_dma(["X2S", "ST", "LD", "WL", "H2S", "CC"])
        ctx.barrier()
        es.close()
        return

    es.close()

    es2 = ExitStack()
    wdqT = ctx.sbuf("p1_wdqT", [128, IC0 * 1024], BF16, es2)
    h2ld = [ctx.sbuf(f"p1_h2ld{i}", [128, TG], F32, es2) for i in range(2)]
    h2q = ctx.sbuf("p1_h2q", [128, IC0 * TG], BF16, es2)
    wTsl = ctx.sbuf("p1_wTsl", [128, 1024], F32, es2)
    itmp2 = ctx.sbuf("p1_itmp2", [128, 1024], I32, es2)
    rinvd = ctx.sbuf("p1_rinvd", [128, 1024], F32, es2)
    rrowd = ctx.sbuf("p1_rrowd", [1, 2048], BF16, es2)
    rcol2b = ctx.sbuf("p1_rc2b", [128, 2], BF16, es2)
    dcols = ctx.sbuf("p1_dcols", [128, 16], F32, es2)
    wdnat = ctx.sbuf("p1_wdnat", [128, 2816], F32, es2)
    if upto == "F1":
        vv = ctx.op("vector", lambda e: e.tensor_copy(out=t["xout"][0][:], in_=t["stage"][:, 0:TG]), "VE")
        ctx.wait("gpsimd", "VE", vv)
        ctx.dma("gpsimd", d["x2T"].ap()[0:128, 0:TG], t["xout"][0][:], "X2S")
        ctx.drain_dma(["X2S", "ST", "LD", "WL", "H2S", "CC"])
        ctx.barrier()
        es2.close()
        return

    prev_v = v_sx2
    for h in range(DC):
        ctx.wait("sync", V, prev_v)
        lv = ctx.dma("sync", wdnat[:], d["wd_nat"].ap()[h * 128:(h + 1) * 128, :], "WL")
        ctx.wait("vector", "WL", lv)
        ctx.op("vector", lambda e, h=h: e.tensor_reduce(
            out=dcols[:, h:h + 1], in_=wdnat[:], axis=AX, op=ALU.max,
            apply_absolute_value=True), None)
        ctx.drain()
        ctx.op("vector", lambda e, h=h: e.reciprocal(out=t["col1"][:], in_=dcols[:, h:h + 1]), None)
        ctx.drain()
        ctx.op("vector", lambda e: e.tensor_scalar(
            out=t["col2"][:], in0=t["col1"][:], scalar1=QMAX, scalar2=None, op0=ALU.mult), None)
        ctx.drain()
        ctx.op("vector", lambda e: e.tensor_copy(out=rcol2b[:, 0:1], in_=t["col2"][:]), None)
        ctx.drain()
        v1 = ctx.op("vector", lambda e: e.tensor_tensor(
            out=rcol2b[:, 1:2], in0=t["col2"][:], in1=rcol2b[:, 0:1], op=ALU.subtract), V)
        ctx.drain()
        ctx.wait("tensor", V, v1)
        ctx.op("tensor", lambda e: e.transpose(
            out=t["psb"][0][:1, 0:128], in_=rcol2b[:, 0:1], identity=t["idb"][:]), None)
        mt = ctx.op("tensor", lambda e: e.transpose(
            out=t["psb"][0][:1, 128:256], in_=rcol2b[:, 1:2], identity=t["idb"][:]), MM)
        ctx.wait("vector", MM, mt)
        ctx.op("vector", lambda e, h=h: e.tensor_copy(
            out=rrowd[:, h * 128:(h + 1) * 128], in_=t["psb"][0][:1, 0:128]), None)
        ctx.op("vector", lambda e, h=h: e.tensor_copy(
            out=rrowd[:, 1024 + h * 128: 1024 + (h + 1) * 128],
            in_=t["psb"][0][:1, 128:256]), None)
        ctx.op("vector", lambda e, h=h: e.tensor_tensor(
            out=t["col1"][:], in0=dcols[:, h:h + 1], in1=cols[:, 5:6], op=ALU.mult), None)
        ctx.drain()
        prev_v = ctx.op("vector", lambda e, h=h: e.tensor_scalar(
            out=dcols[:, 8 + h:9 + h], in0=t["col1"][:], scalar1=1.0 / QMAX,
            scalar2=None, op0=ALU.mult), V)
        ctx.drain()
    if upto == "F1b":
        vv = ctx.op("vector", lambda e: e.tensor_copy(out=t["xout"][0][:], in_=t["stage"][:, 0:TG]), "VE")
        ctx.wait("gpsimd", "VE", vv)
        ctx.dma("gpsimd", d["x2T"].ap()[0:128, 0:TG], t["xout"][0][:], "X2S")
        ctx.drain_dma(["X2S", "ST", "LD", "WL", "H2S", "CC"])
        ctx.barrier()
        es2.close()
        return

    ctx.wait("tensor", V, prev_v)
    if upto in ("F2h", "F2i", "F2j", "F2m", "F2n", "F2o"):
        rhs_src = t["hrow"][:, 0:128] if upto == "F2h" else (
            rrowd[:, 0:128] if upto == "F2i" else rrowd[:, 0:512])
        outp = t["ps"][2][:, :128] if upto in ("F2h", "F2i") else t["ps"][2][:]
        ctx.op("tensor", lambda e: e.matmul(
            out=outp, lhsT=t["ones"][:1, :], rhs=rhs_src,
            start=True, stop=True), "MM")
        if upto in ("F2m", "F2n", "F2o"):
            rhs2 = rrowd[:, 0:512] if upto == "F2n" else rrowd[:, 1024:1536]
            ctx.op("tensor", lambda e, rhs2=rhs2: e.matmul(
                out=t["ps"][3][:], lhsT=t["ones"][:1, :], rhs=rhs2,
                start=True, stop=True), "MM")
        mmv = ctx.cnt["MM"]
        ctx.wait("vector", "MM", mmv)
        dsto = rinvd[:, 0:512] if upto == "F2o" else t["rbf"][:, :128]
        srco = t["ps"][2][:] if upto == "F2o" else t["ps"][2][:, :128]
        ctx.op("vector", lambda e, dsto=dsto, srco=srco: e.tensor_copy(out=dsto, in_=srco), "VE")
    if upto == "F2pad":
        for _ in range(8):
            ctx.op("vector", lambda e: e.tensor_copy(out=t["col2"][:], in_=t["col1"][:]), None)
    for half in range(2):
        if upto in ("F2a", "F2pad", "F2h", "F2i", "F2j", "F2m", "F2n", "F2o"):
            continue
        ctx.op("tensor", lambda e, half=half: e.matmul(
            out=t["ps"][2][:], lhsT=t["ones"][:1, :], rhs=rrowd[:, half * TG:(half + 1) * TG],
            start=True, stop=True), None)
        v3 = ctx.op("tensor", lambda e, half=half: e.matmul(
            out=t["ps"][3][:], lhsT=t["ones"][:1, :],
            rhs=rrowd[:, 1024 + half * TG: 1024 + (half + 1) * TG],
            start=True, stop=True), MM)
        if upto == "F2b":
            continue
        ctx.wait("scalar", MM, v3)
        if upto == "F2e":
            continue
        if upto == "F2f":
            s1 = ctx.op("vector", lambda e, half=half: e.tensor_copy(
                out=rinvd[:, half * TG:(half + 1) * TG], in_=t["ps"][2][:]), "VE")
            continue
        if upto == "F2d":
            s1 = ctx.op("scalar", lambda e: e.activation(
                out=t["rbf"][:], in_=t["ps"][2][:], func=AF.Copy), SC)
            continue
        s1 = ctx.op("scalar", lambda e, half=half: e.activation(
            out=rinvd[:, half * TG:(half + 1) * TG], in_=t["ps"][2][:], func=AF.Copy), SC)
        if upto == "F2c":
            continue
        ctx.wait("vector", SC, s1)
        v_add = ctx.op("vector", lambda e, half=half: e.tensor_tensor(
            out=rinvd[:, half * TG:(half + 1) * TG], in0=rinvd[:, half * TG:(half + 1) * TG],
            in1=t["ps"][3][:], op=ALU.add), V)
        ctx.wait("tensor", V, v_add)
    if upto.startswith("F2"):
        vv = ctx.op("vector", lambda e: e.tensor_copy(out=t["xout"][0][:], in_=t["stage"][:, 0:TG]), "VE")
        ctx.wait("gpsimd", "VE", vv)
        ctx.dma("gpsimd", d["x2T"].ap()[0:128, 0:TG], t["xout"][0][:], "X2S")
        ctx.drain_dma(["X2S", "ST", "LD", "WL", "H2S", "CC"])
        ctx.barrier()
        es2.close()
        return

    prev_v = ctx.op("vector", lambda e: e.tensor_copy(out=t["col1"][:], in_=t["col1"][:]), V)
    for i in range(IC0):
        ctx.wait("sync", V, prev_v)
        lv = ctx.dma("sync", wTsl[:], d["wdT"].ap()[i * 128:(i + 1) * 128, :], "WL")
        ctx.wait("vector", "WL", lv)
        ctx.op("vector", lambda e: e.tensor_tensor(
            out=itmp2[:], in0=wTsl[:], in1=rinvd[:], op=ALU.mult), None)
        prev_v = ctx.op("vector", lambda e, i=i: e.tensor_copy(
            out=wdqT[:, i * 1024:(i + 1) * 1024], in_=itmp2[:]), V)
    if upto == "F3":
        vv = ctx.op("vector", lambda e: e.tensor_copy(out=t["xout"][0][:], in_=t["stage"][:, 0:TG]), "VE")
        ctx.wait("gpsimd", "VE", vv)
        ctx.dma("gpsimd", d["x2T"].ap()[0:128, 0:TG], t["xout"][0][:], "X2S")
        ctx.drain_dma(["X2S", "ST", "LD", "WL", "H2S", "CC"])
        ctx.barrier()
        es2.close()
        return

    ctx.wait("sync", "H2S", ctx.cnt["H2S"])
    v_q = prev_v
    dmm = 0
    last_vv = prev_v
    jj = 0
    hq_read_v = [0, 0]
    for tg in range(NG):
        if tg > 0:
            ctx.wait("vector", MM, dmm)
        for i in range(IC0):
            hsem = "H2LA" if i % 2 == 0 else "H2LB"
            ctx.wait("sync", V, hq_read_v[i % 2])
            lv = ctx.dma("sync", h2ld[i % 2][:],
                         d["h2s"].ap()[i * 128:(i + 1) * 128, tg * TG:(tg + 1) * TG], hsem)
            ctx.wait("vector", hsem, lv)
            ctx.op("vector", lambda e, i=i: e.tensor_scalar(
                out=h2ld[i % 2][:], in0=h2ld[i % 2][:],
                scalar1=cols[:, 4:5], scalar2=MAGIC, op0=ALU.mult, op1=ALU.add), None)
            ctx.op("vector", lambda e, i=i: e.tensor_scalar(
                out=h2ld[i % 2][:], in0=h2ld[i % 2][:],
                scalar1=MAGIC, scalar2=-128.0, op0=ALU.subtract, op1=ALU.max), None)
            hq_read_v[i % 2] = ctx.op("vector", lambda e, i=i: e.tensor_scalar(
                out=h2q[:, i * TG:(i + 1) * TG], in0=h2ld[i % 2][:],
                scalar1=127.0, scalar2=None, op0=ALU.min), V)
        v_q = hq_read_v[(IC0 - 1) % 2]
        ctx.wait("tensor", V, v_q)
        for hh in range(2):
            if hh == 1:
                ctx.wait("tensor", V, last_vv)
            for i in range(IC0):
                for h4 in range(4):
                    h = hh * 4 + h4
                    ctx.op("tensor", lambda e, i=i, h=h, h4=h4: e.matmul(
                        out=t["ps"][h4][:], lhsT=wdqT[:, i * 1024 + h * 128:
                                                      i * 1024 + (h + 1) * 128],
                        rhs=h2q[:, i * TG:(i + 1) * TG],
                        start=(i == 0), stop=(i == IC0 - 1)),
                        MM if (i == IC0 - 1 and h4 == 3) else None)
            dmm = ctx.cnt[MM]
            ctx.wait("sync", V, last_vv)
            lv = 0
            for h4 in range(4):
                h = hh * 4 + h4
                lv = ctx.dma("sync", t["stage"][:, h * TG:(h + 1) * TG],
                             d["x1T"].ap()[h * 128:(h + 1) * 128, tg * TG:(tg + 1) * TG], LD)
            ctx.wait("vector", LD, lv)
            ctx.wait("vector", MM, dmm)
            for h4 in range(4):
                h = hh * 4 + h4
                if jj >= 2:
                    ctx.wait("vector", "X2S", 16 * (jj - 1))
                vv = ctx.op("vector", lambda e, h=h, h4=h4, jj=jj: e.scalar_tensor_tensor(
                    out=t["xout"][jj % 2][:], in0=t["ps"][h4][:],
                    scalar=dcols[:, 8 + h:9 + h], in1=t["stage"][:, h * TG:(h + 1) * TG],
                    op0=ALU.mult, op1=ALU.add), V)
                last_vv = vv
                ctx.wait("gpsimd", V, vv)
                ctx.dma("gpsimd", d["x2T"].ap()[h * 128:(h + 1) * 128, tg * TG:(tg + 1) * TG],
                        t["xout"][jj % 2][:], "X2S")
                jj += 1
        if upto == "G1":
            ctx.drain_dma(["X2S", "ST", "LD", "WL", "H2S", "CC", "H2LA", "H2LB"])
            ctx.barrier()
            es2.close()
            return

    ctx.drain_dma(["X2S", LD, "WL", "H2LA", "H2LB", ST, "H2S", "CC"])
    ctx.barrier()
    es2.close()



def emit_phase2(ctx, t, d, load_consts=False):
    t = dict(t)
    V, SC, MM, LD, ST = "VE", "SC", "MM", "LD", "ST"
    for nm in ("WL", LD, "QS", "QL", "X3S", "TB", "S3", "S3L"):
        ctx.sem(nm)
    es = ExitStack()
    wprojb = ctx.sbuf("p2_wp", [128, DC * 1024], BF16, es)
    wsl = [ctx.sbuf(f"p2_wsl{i}", [128, 1024], F32, es) for i in range(2)]
    wqb = [ctx.sbuf(f"p2_wqb{i}", [128, 1024], BF16, es) for i in range(2)]
    wv0 = ctx.sbuf("p2_wv0", [128, DC * 512], BF16, es)
    bqkv = ctx.sbuf("p2_bqkv", [128, 24], F32, es)
    bvb = ctx.sbuf("p2_bvb", [128, 1024], F32, es)
    bproj = ctx.sbuf("p2_bproj", [128, 8], F32, es)
    g8 = ctx.sbuf("p2_g8", [128, 8], F32, es)
    arb = ctx.sbuf("p2_arb", [128, TG], F32, es)
    kf = ctx.sbuf("p2_kf", [128, DC * S], BF16, es)
    vt = ctx.sbuf("p2_vt", [128, 16 * 1024], BF16, es)
    qtile = ctx.sbuf("p2_qt", [128, DC * TG], BF16, es)
    qtmp = [ctx.sbuf(f"p2_qtmp{i}", [128, TG], BF16, es) for i in range(2)]
    tabs = ctx.sbuf("p2_tabs", [128, 3 * TG], BF16, es)
    scol = ctx.sbuf("p2_scol", [128, 8], F32, es)


    if load_consts:
        ctx.dma("sync", t["ones"][:], d["ones_bf"].ap(), LD)
        ctx.dma("sync", t["idf"][:], d["ident_f"].ap(), LD)
        ctx.dma("sync", t["idb"][:], d["ident_b"].ap(), LD)
    ctx.dma("sync", bqkv[:], d["bqkv24"].ap(), LD)
    ctx.dma("sync", t["stage"][:1, 0:1024], d["bvrow"].ap(), LD)
    ctx.dma("sync", bproj[:], d["bproj8"].ap(), LD)
    lv = ctx.dma("sync", g8[:], d["g8"].ap(), LD)
    prev_v = 0
    for c in range(DC):
        ctx.wait("sync", V, prev_v)
        wv = ctx.dma("sync", wsl[0][:], d["wproj"].ap()[c * 128:(c + 1) * 128, :], "WL")
        ctx.wait("vector", "WL", wv)
        prev_v = ctx.op("vector", lambda e, c=c: e.tensor_copy(
            out=wprojb[:, c * 1024:(c + 1) * 1024], in_=wsl[0][:]), V)
    ctx.wait("vector", LD, lv)
    ctx.op("vector", lambda e: e.tensor_copy(out=t["hrow"][:, 0:1024], in_=t["stage"][:1, 0:1024]), None)
    v1 = ctx.op("vector", lambda e: e.tensor_tensor(
        out=t["hrow"][:, 1024:2048], in0=t["stage"][:1, 0:1024], in1=t["hrow"][:, 0:1024],
        op=ALU.subtract), V)
    ctx.wait("tensor", V, v1)
    for half in range(2):
        ctx.op("tensor", lambda e, half=half: e.matmul(
            out=t["ps"][2][:], lhsT=t["ones"][:1, :], rhs=t["hrow"][:, half * TG:(half + 1) * TG],
            start=True, stop=True), None)
        v3 = ctx.op("tensor", lambda e, half=half: e.matmul(
            out=t["ps"][3][:], lhsT=t["ones"][:1, :],
            rhs=t["hrow"][:, 1024 + half * TG: 1024 + (half + 1) * TG],
            start=True, stop=True), MM)
        ctx.wait("scalar", MM, v3)
        s1 = ctx.op("scalar", lambda e, half=half: e.activation(
            out=bvb[:, half * TG:(half + 1) * TG], in_=t["ps"][2][:], func=AF.Copy), SC)
        ctx.wait("vector", SC, s1)
        va = ctx.op("vector", lambda e, half=half: e.tensor_tensor(
            out=bvb[:, half * TG:(half + 1) * TG], in0=bvb[:, half * TG:(half + 1) * TG],
            in1=t["ps"][3][:], op=ALU.add), V)
        ctx.wait("tensor", V, va)

    last_v = ctx.cnt[V]
    jj = 0
    for s_ in range(SPC):
        es_hq = ExitStack()
        hq = ctx.sbuf(f"p2_hq{s_}", [128, DC * S], BF16, es_hq)
        for gl in range(GPS):
            g = s_ * GPS + gl
            lv = load_stage(ctx, t, d["x2T"].ap(), g, guard_v=last_v)
            ctx.wait("vector", LD, lv)
            emit_ln_group(ctx, t, g, want_lnbf=False, want_absmax=False, want_apply=False)
            ctx.op("vector", lambda e: e.tensor_tensor(
                out=t["rowf2"][:], in0=t["rowf3"][:], in1=t["rowf3"][:], op=ALU.mult), None)
            vv = ctx.op("vector", lambda e: e.tensor_scalar(
                out=t["rowf2"][:], in0=t["rowf2"][:], scalar1=-1e-5, scalar2=1.0 + 1e-6,
                op0=ALU.mult, op1=ALU.add), V)
            ctx.wait("scalar", V, vv)
            sv = ctx.op("scalar", lambda e: e.activation(
                out=t["rowf2"][:], in_=t["rowf2"][:], func=AF.Sqrt), SC)
            ctx.wait("vector", SC, sv)
            ctx.op("vector", lambda e: e.reciprocal(out=t["rowf2"][:], in_=t["rowf2"][:]), None)
            ctx.op("vector", lambda e: e.tensor_tensor(
                out=t["rowf2"][:], in0=t["rowf2"][:], in1=t["rowf3"][:], op=ALU.mult), None)
            vs3 = ctx.op("vector", lambda e: e.tensor_copy(
                out=t["rowf2"][:] if False else t["rowf"][:, :1], in_=t["rowf"][:, :1]), V)
            ctx.wait("gpsimd", V, vs3)
            ctx.dma("gpsimd", d["stats3"].ap()[0:1, gl * TG:(gl + 1) * TG], t["rowf"][:], "S3")
            ctx.dma("gpsimd", d["stats3"].ap()[1:2, gl * TG:(gl + 1) * TG], t["rowf3"][:], "S3")
            ctx.op("vector", lambda e: e.tensor_copy(out=t["hrow"][:, 0:TG], in_=t["rowf2"][:]), None)
            v1 = ctx.op("vector", lambda e: e.tensor_tensor(
                out=t["hrow"][:, TG:2 * TG], in0=t["rowf2"][:], in1=t["hrow"][:, 0:TG],
                op=ALU.subtract), V)
            ctx.wait("tensor", V, v1)
            ctx.op("tensor", lambda e: e.matmul(
                out=t["ps"][2][:], lhsT=t["ones"][:1, :], rhs=t["hrow"][:, 0:TG],
                start=True, stop=True), None)
            v3 = ctx.op("tensor", lambda e: e.matmul(
                out=t["ps"][3][:], lhsT=t["ones"][:1, :], rhs=t["hrow"][:, TG:2 * TG],
                start=True, stop=True), MM)
            ctx.wait("scalar", MM, v3)
            s1 = ctx.op("scalar", lambda e: e.activation(
                out=arb[:], in_=t["ps"][2][:], func=AF.Copy), SC)
            ctx.wait("vector", SC, s1)
            va = ctx.op("vector", lambda e: e.tensor_tensor(
                out=arb[:], in0=arb[:], in1=t["ps"][3][:], op=ALU.add), V)
            ctx.wait("tensor", V, va)
            for c in range(DC):
                ctx.op("vector", lambda e, c=c: e.tensor_tensor(
                    out=t["ep1"][:], in0=t["stage"][:, c * TG:(c + 1) * TG],
                    in1=t["mbf"][:], op=ALU.subtract), None)
                vv = ctx.op("vector", lambda e, c=c, gl=gl, hq=hq: e.scalar_tensor_tensor(
                    out=hq[:, c * S + gl * TG: c * S + (gl + 1) * TG], in0=arb[:],
                    scalar=g8[:, c:c + 1], in1=t["ep1"][:], op0=ALU.mult, op1=ALU.mult),
                    V if c == DC - 1 else None)
            last_v = ctx.cnt[V]
        for qk in range(2):
            cast_v = [0, 0]
            mm_last = [0, 0]
            sc_v = [0, 0]
            for tgl in range(GPS):
                for fc in range(DC):
                    b = fc % 2
                    ctx.wait("sync", V, cast_v[b])
                    wv = 0
                    for c in range(DC):
                        wv = ctx.dma("sync", wsl[b][:, c * 128:(c + 1) * 128],
                                     d["wqkv"].ap()[c * 128:(c + 1) * 128,
                                                    qk * 1024 + fc * 128:
                                                    qk * 1024 + (fc + 1) * 128], "WL")
                    ctx.wait("vector", "WL", wv)
                    ctx.wait("vector", MM, mm_last[b])
                    cast_v[b] = ctx.op("vector", lambda e, b=b: e.tensor_copy(
                        out=wqb[b][:], in_=wsl[b][:]), V)
                    ctx.wait("tensor", V, cast_v[b])
                    ctx.wait("tensor", SC, sc_v[b])
                    pm = t["ps"][4 + b]
                    for k in range(DC):
                        ctx.op("tensor", lambda e, k=k, tgl=tgl, pm=pm, b=b, hq=hq: e.matmul(
                            out=pm[:], lhsT=wqb[b][:, k * 128:(k + 1) * 128],
                            rhs=hq[:, k * S + tgl * TG: k * S + (tgl + 1) * TG],
                            start=(k == 0), stop=(k == DC - 1)), MM if k == DC - 1 else None)
                    mm_last[b] = ctx.cnt[MM]
                    ctx.wait("scalar", MM, mm_last[b])
                    dst = (qtile[:, fc * TG:(fc + 1) * TG] if qk == 0 else
                           kf[:, fc * S + tgl * TG: fc * S + (tgl + 1) * TG])
                    sc_v[b] = ctx.op("scalar", lambda e, pm=pm, dst=dst, qk=qk, fc=fc:
                        e.activation(out=dst, in_=pm[:], func=AF.Identity,
                                     bias=bqkv[:, qk * 8 + fc: qk * 8 + fc + 1]), SC)
                ctx.wait("vector", SC, max(sc_v))
                for hc in range(4):
                    ctx.wait("gpsimd", V, ctx.cnt[V])
                    tv = 0
                    for ti, nm in enumerate(("ropeC", "ropeSp", "ropeSn")):
                        tv = ctx.dma("gpsimd", tabs[:, ti * TG:(ti + 1) * TG],
                                     d[nm].ap()[hc * 128:(hc + 1) * 128,
                                                tgl * TG:(tgl + 1) * TG], "TB")
                    ctx.wait("vector", "TB", tv)
                    ca, cb = hc, hc + 4
                    if qk == 0:
                        src_a = qtile[:, ca * TG:(ca + 1) * TG]
                        src_b = qtile[:, cb * TG:(cb + 1) * TG]
                    else:
                        src_a = kf[:, ca * S + tgl * TG: ca * S + (tgl + 1) * TG]
                        src_b = kf[:, cb * S + tgl * TG: cb * S + (tgl + 1) * TG]
                    ctx.op("vector", lambda e, src_a=src_a: e.tensor_tensor(
                        out=qtmp[0][:], in0=src_a, in1=tabs[:, 0:TG], op=ALU.mult), None)
                    ctx.op("vector", lambda e, src_b=src_b: e.tensor_tensor(
                        out=qtmp[1][:], in0=src_b, in1=tabs[:, 2 * TG:3 * TG], op=ALU.mult), None)
                    ctx.op("vector", lambda e, src_b=src_b: e.tensor_tensor(
                        out=t["ep1"][:], in0=src_b, in1=tabs[:, 0:TG], op=ALU.mult), None)
                    ctx.op("vector", lambda e, src_a=src_a: e.tensor_tensor(
                        out=t["xout"][0][:], in0=src_a, in1=tabs[:, TG:2 * TG], op=ALU.mult), None)
                    ctx.op("vector", lambda e, src_a=src_a: e.tensor_tensor(
                        out=src_a, in0=qtmp[0][:], in1=qtmp[1][:], op=ALU.add), None)
                    rv = ctx.op("vector", lambda e, src_b=src_b: e.tensor_tensor(
                        out=src_b, in0=t["ep1"][:], in1=t["xout"][0][:], op=ALU.add), V)
                if qk == 0:
                    ctx.wait("gpsimd", V, rv)
                    for c in range(DC):
                        ctx.dma("gpsimd", d["qs"].ap()[c * 128:(c + 1) * 128,
                                                       tgl * TG:(tgl + 1) * TG],
                                qtile[:, c * TG:(c + 1) * TG], "QS")
                    ctx.wait("scalar", "QS", ctx.cnt["QS"])
        v_ev = [0, 0]
        for fg in range(2):
            for c in range(DC):
                ctx.wait("sync", V, ctx.cnt[V] if (fg == 0 and c == 0) else cast_done)
                wv = ctx.dma("sync", wsl[0][:, 0:512],
                             d["wqkv"].ap()[c * 128:(c + 1) * 128,
                                            2048 + fg * 512: 2048 + (fg + 1) * 512], "WL")
                ctx.wait("vector", "WL", wv)
                ctx.wait("vector", MM, ctx.cnt[MM])
                cast_done = ctx.op("vector", lambda e, c=c: e.tensor_copy(
                    out=wv0[:, c * 512:(c + 1) * 512], in_=wsl[0][:, 0:512]), V)
            ctx.wait("tensor", V, cast_done)
            for tc in range(16):
                tgl = tc // 4
                toff = (tc % 4) * 128
                pm = t["ps"][4 + tc % 2]
                ctx.wait("tensor", V, v_ev[tc % 2])
                for k in range(DC):
                    ctx.op("tensor", lambda e, k=k, tgl=tgl, toff=toff, pm=pm, hq=hq: e.matmul(
                        out=pm[:], lhsT=hq[:, k * S + tgl * TG + toff:
                                           k * S + tgl * TG + toff + 128],
                        rhs=wv0[:, k * 512:(k + 1) * 512],
                        start=(k == 0), stop=(k == DC - 1)), MM if k == DC - 1 else None)
                mmv = ctx.cnt[MM]
                ctx.wait("vector", MM, mmv)
                v_ev[tc % 2] = ctx.op("vector", lambda e, tc=tc, fg=fg, pm=pm: e.tensor_tensor(
                    out=vt[:, tc * 1024 + fg * 512: tc * 1024 + (fg + 1) * 512],
                    in0=pm[:], in1=bvb[:, fg * 512:(fg + 1) * 512], op=ALU.add), V)
        v_done = max(v_ev)
        es_hq.close()
        es_at = ExitStack()
        Pt = ctx.sbuf(f"p2_P{s_}", [128, S], BF16, es_at)
        PT_sb = ctx.sbuf(f"p2_PT{s_}", [128, 16 * TG], BF16, es_at)
        osl = ctx.sbuf(f"p2_o{s_}", [128, DC * TG], BF16, es_at)
        ctx.wait("sync", "QS", ctx.cnt["QS"])
        pt_cp = 0
        pv_mm = 0
        for gl in range(GPS):
            ctx.wait("sync", V, ctx.cnt[V])
            qv = 0
            for c in range(DC):
                qv = ctx.dma("sync", qtile[:, c * TG:(c + 1) * TG],
                             d["qs"].ap()[c * 128:(c + 1) * 128, gl * TG:(gl + 1) * TG], "QL")
            ctx.wait("tensor", "QL", qv)
            ctx.wait("tensor", V, v_done)
            exp_sv = 0
            for qt in range(4):
                ctx.wait("tensor", SC, exp_sv)
                for kg in range(4):
                    for c in range(DC):
                        ctx.op("tensor", lambda e, c=c, kg=kg, qt=qt: e.matmul(
                            out=t["ps"][kg][:, 0:TG],
                            lhsT=qtile[:, c * TG + qt * 128: c * TG + qt * 128 + 128],
                            rhs=kf[:, c * S + kg * TG: c * S + (kg + 1) * TG],
                            start=(c == 0), stop=(c == DC - 1)),
                            MM if (c == DC - 1 and kg == 3) else None)
                mmv = ctx.cnt[MM]
                ctx.wait("vector", MM, mmv)
                for kg in range(4):
                    ctx.op("vector", lambda e, kg=kg: e.tensor_reduce(
                        out=scol[:, kg:kg + 1], in_=t["ps"][kg][:, 0:TG], axis=AX,
                        op=ALU.max), None)
                ctx.drain()
                ctx.op("vector", lambda e: e.tensor_tensor(
                    out=scol[:, 4:5], in0=scol[:, 0:1], in1=scol[:, 1:2], op=ALU.max), None)
                ctx.op("vector", lambda e: e.tensor_tensor(
                    out=scol[:, 5:6], in0=scol[:, 2:3], in1=scol[:, 3:4], op=ALU.max), None)
                ctx.drain()
                ctx.op("vector", lambda e: e.tensor_tensor(
                    out=scol[:, 4:5], in0=scol[:, 4:5], in1=scol[:, 5:6], op=ALU.max), None)
                ctx.drain()
                vb = ctx.op("vector", lambda e: e.tensor_scalar(
                    out=scol[:, 6:7], in0=scol[:, 4:5], scalar1=-LN3 / 32.0, scalar2=None,
                    op0=ALU.mult), V)
                ctx.drain()
                ctx.wait("scalar", V, vb)
                for kg in range(4):
                    ctx.op("scalar", lambda e, kg=kg, Pt=Pt: e.activation(
                        out=Pt[:, kg * TG:(kg + 1) * TG], in_=t["ps"][kg][:, 0:TG],
                        func=AF.Exp, scale=LN3 / 32.0, bias=scol[:, 6:7],
                        accum_out=scol[:, kg:kg + 1]), SC if kg == 3 else None)
                exp_sv = ctx.cnt[SC]
                ctx.wait("vector", SC, exp_sv)
                ctx.op("vector", lambda e: e.tensor_tensor(
                    out=scol[:, 4:5], in0=scol[:, 0:1], in1=scol[:, 1:2], op=ALU.add), None)
                ctx.op("vector", lambda e: e.tensor_tensor(
                    out=scol[:, 5:6], in0=scol[:, 2:3], in1=scol[:, 3:4], op=ALU.add), None)
                ctx.drain()
                ctx.op("vector", lambda e: e.tensor_tensor(
                    out=scol[:, 4:5], in0=scol[:, 4:5], in1=scol[:, 5:6], op=ALU.add), None)
                ctx.drain()
                ctx.op("vector", lambda e: e.reciprocal(out=scol[:, 7:8], in_=scol[:, 4:5]), None)
                ctx.drain()
                vn = ctx.op("vector", lambda e, Pt=Pt: e.tensor_scalar(
                    out=Pt[:], in0=Pt[:], scalar1=scol[:, 7:8], scalar2=None,
                    op0=ALU.mult), V)
                ctx.wait("tensor", V, vn)
                ctx.wait("tensor", V, pt_cp)
                for kc in range(16):
                    bb = kc // 8
                    ctx.op("tensor", lambda e, kc=kc, bb=bb, Pt=Pt: e.transpose(
                        out=t["psb"][bb][:, (kc % 8) * 128:(kc % 8 + 1) * 128],
                        in_=Pt[:, kc * 128:(kc + 1) * 128], identity=t["idb"][:]),
                        MM if kc == 15 else None)
                mtv = ctx.cnt[MM]
                ctx.wait("vector", MM, mtv)
                ctx.wait("vector", MM, pv_mm)
                for bb in range(2):
                    pt_cp = ctx.op("vector", lambda e, bb=bb, qt=qt, PT_sb=PT_sb: e.tensor_copy(
                        out=PT_sb[:].rearrange("p (kc q) -> p kc q", kc=16)[
                            :, bb * 8:(bb + 1) * 8, qt * 128:(qt + 1) * 128],
                        in_=t["psb"][bb][:, 0:1024].rearrange(
                            "p (kc q) -> p kc q", kc=8)), V)
            o_sc = [0, 0]
            for dc in range(DC):
                pm = t["ps"][4 + dc % 2]
                ctx.wait("tensor", V, pt_cp)
                ctx.wait("tensor", SC, o_sc[dc % 2])
                for kc in range(16):
                    ctx.op("tensor", lambda e, kc=kc, dc=dc, pm=pm, PT_sb=PT_sb: e.matmul(
                        out=pm[:], lhsT=vt[:, kc * 1024 + dc * 128: kc * 1024 + (dc + 1) * 128],
                        rhs=PT_sb[:, kc * TG:(kc + 1) * TG],
                        start=(kc == 0), stop=(kc == 15)), MM if kc == 15 else None)
                pv_mm = ctx.cnt[MM]
                ctx.wait("scalar", MM, pv_mm)
                o_sc[dc % 2] = ctx.op("scalar", lambda e, dc=dc, pm=pm, osl=osl: e.activation(
                    out=osl[:, dc * TG:(dc + 1) * TG], in_=pm[:], func=AF.Copy), SC)
            pj_ev = [0, 0]
            ctx.wait("tensor", SC, max(o_sc))
            for fc in range(DC):
                pm = t["ps"][fc % 2]
                ctx.wait("tensor", V, pj_ev[fc % 2])
                for dc in range(DC):
                    ctx.op("tensor", lambda e, dc=dc, fc=fc, pm=pm, osl=osl: e.matmul(
                        out=pm[:], lhsT=wprojb[:, dc * 1024 + fc * 128:
                                               dc * 1024 + (fc + 1) * 128],
                        rhs=osl[:, dc * TG:(dc + 1) * TG],
                        start=(dc == 0), stop=(dc == DC - 1)), MM if dc == DC - 1 else None)
                pjm = ctx.cnt[MM]
                if fc == 0:
                    ctx.wait("sync", "S3", ctx.cnt["S3"])
                    ctx.wait("sync", V, ctx.cnt[V])
                    s3v = 0
                    s3v = ctx.dma("sync", t["rowf"][:], d["stats3"].ap()[0:1, gl * TG:(gl + 1) * TG], "S3L")
                    s3v = ctx.dma("sync", t["rowf3"][:], d["stats3"].ap()[1:2, gl * TG:(gl + 1) * TG], "S3L")
                    ctx.wait("vector", "S3L", s3v)
                    ctx.wait("vector", SC, max(o_sc))
                    ctx.op("vector", lambda e: e.tensor_copy(
                        out=t["hrow"][:, 0:TG], in_=t["rowf"][:]), None)
                    ctx.op("vector", lambda e: e.tensor_tensor(
                        out=t["hrow"][:, TG:2 * TG], in0=t["rowf"][:],
                        in1=t["hrow"][:, 0:TG], op=ALU.subtract), None)
                    ctx.op("vector", lambda e: e.tensor_copy(
                        out=t["hrow"][:, 2 * TG:3 * TG], in_=t["rowf3"][:]), None)
                    v1 = ctx.op("vector", lambda e: e.tensor_tensor(
                        out=t["hrow"][:, 3 * TG:4 * TG], in0=t["rowf3"][:],
                        in1=t["hrow"][:, 2 * TG:3 * TG], op=ALU.subtract), V)
                    ctx.wait("tensor", V, v1)
                    ctx.op("tensor", lambda e: e.matmul(
                        out=t["ps"][2][:], lhsT=t["ones"][:1, :], rhs=t["hrow"][:, 0:TG],
                        start=True, stop=True), None)
                    v3 = ctx.op("tensor", lambda e: e.matmul(
                        out=t["ps"][3][:], lhsT=t["ones"][:1, :], rhs=t["hrow"][:, TG:2 * TG],
                        start=True, stop=True), MM)
                    ctx.wait("scalar", MM, v3)
                    s1 = ctx.op("scalar", lambda e: e.activation(
                        out=t["mbf"][:], in_=t["ps"][2][:], func=AF.Copy), SC)
                    ctx.wait("vector", SC, s1)
                    vmb = ctx.op("vector", lambda e: e.tensor_tensor(
                        out=t["mbf"][:], in0=t["mbf"][:], in1=t["ps"][3][:], op=ALU.add), V)
                    ctx.wait("tensor", V, vmb)
                    ctx.op("tensor", lambda e: e.matmul(
                        out=t["ps"][2][:], lhsT=t["ones"][:1, :], rhs=t["hrow"][:, 2 * TG:3 * TG],
                        start=True, stop=True), None)
                    v3 = ctx.op("tensor", lambda e: e.matmul(
                        out=t["ps"][3][:], lhsT=t["ones"][:1, :], rhs=t["hrow"][:, 3 * TG:4 * TG],
                        start=True, stop=True), MM)
                    ctx.wait("scalar", MM, v3)
                    s1 = ctx.op("scalar", lambda e: e.activation(
                        out=t["rbf"][:], in_=t["ps"][2][:], func=AF.Copy), SC)
                    ctx.wait("vector", SC, s1)
                    vrb = ctx.op("vector", lambda e: e.tensor_tensor(
                        out=t["rbf"][:], in0=t["rbf"][:], in1=t["ps"][3][:], op=ALU.add), V)
                    ctx.wait("sync", V, vrb)
                    lv = 0
                    for c in range(DC):
                        lv = ctx.dma("sync", t["stage"][:, c * TG:(c + 1) * TG],
                                     d["x2T"].ap()[c * 128:(c + 1) * 128,
                                                   (s_ * GPS + gl) * TG:(s_ * GPS + gl + 1) * TG], LD)
                    ctx.wait("vector", LD, lv)
                ctx.wait("vector", MM, pjm)
                if jj >= 2:
                    ctx.wait("vector", "X3S", 16 * (jj - 1))
                ctx.op("vector", lambda e, fc=fc: e.tensor_tensor(
                    out=t["ep1"][:], in0=t["stage"][:, fc * TG:(fc + 1) * TG],
                    in1=t["mbf"][:], op=ALU.subtract), None)
                ctx.op("vector", lambda e: e.tensor_tensor(
                    out=t["ep1"][:], in0=t["ep1"][:], in1=t["rbf"][:], op=ALU.mult), None)
                ctx.op("vector", lambda e, fc=fc, pm=pm: e.scalar_tensor_tensor(
                    out=t["ep1"][:], in0=pm[:], scalar=bproj[:, fc:fc + 1],
                    in1=t["ep1"][:], op0=ALU.add, op1=ALU.add), None)
                vv = ctx.op("vector", lambda e, fc=fc, jj=jj: e.tensor_tensor(
                    out=t["xout"][jj % 2][:], in0=t["ep1"][:],
                    in1=t["stage"][:, fc * TG:(fc + 1) * TG], op=ALU.add), V)
                pj_ev[fc % 2] = vv
                ctx.wait("gpsimd", V, vv)
                ctx.dma("gpsimd", d["x3T"].ap()[fc * 128:(fc + 1) * 128,
                                                (s_ * GPS + gl) * TG:(s_ * GPS + gl + 1) * TG],
                        t["xout"][jj % 2][:], "X3S")
                jj += 1
            last_v = ctx.cnt[V]
        ctx.drain_dma(["X3S", LD, "WL", "QS", "QL", "TB", "S3", "S3L"])
        ctx.barrier()
        es_at.close()
    es.close()


def declare_p2(nc, d, x2_is_input):
    for name, shape, dt in [
            ("wqkv", [D, 3072], F32), ("wproj", [D, D], F32),
            ("bqkv24", [128, 24], F32), ("bvrow", [1, 1024], F32),
            ("bproj8", [128, 8], F32), ("g8", [128, 8], F32),
            ("ropeC", [512, S], BF16), ("ropeSp", [512, S], BF16),
            ("ropeSn", [512, S], BF16)]:
        d[name] = nc.dram_tensor(name, shape, dt, kind="ExternalInput")
    if x2_is_input:
        d["x2T"] = nc.dram_tensor("x2T", [D, T], F32, kind="ExternalInput")
        for name, shape, dt in [
                ("ones_bf", [128, 128], BF16), ("ident_f", [128, 128], F32),
                ("ident_b", [128, 128], BF16)]:
            d[name] = nc.dram_tensor(name, shape, dt, kind="ExternalInput")
    d["qs"] = nc.dram_tensor("qs", [D, S], BF16)
    d["stats3"] = nc.dram_tensor("stats3", [2, S], F32)
    d["x3T"] = nc.dram_tensor("x3T", [D, T], F32, kind="ExternalOutput")


def build_p2():
    nc = bass.Bass("TRN2", target_bir_lowering=False)
    d = {}
    declare_p2(nc, d, x2_is_input=True)
    es = ExitStack()
    ctx = Ctx(nc, es)
    with es:
        t = alloc_common(ctx)
        emit_phase2(ctx, t, d, load_consts=True)
        ctx.build_block()
    return nc



def emit_phase3(ctx, t, d, load_consts=False, upto="G"):
    t = dict(t)
    V, SC, MM, LD, ST = "VE", "SC", "MM", "LD", "ST"
    for nm in (ST, "WL", LD, "H3S", "H3L", "X4S"):
        ctx.sem(nm)
    es = ExitStack()
    t["lnf"] = ctx.sbuf("p3_lnf", [128, DC * TG], F32, es)
    xq4 = ctx.sbuf("p3_xq4", [128, DC * T], BF16, es)
    h3sl = [ctx.sbuf(f"p3_h3{i}", [128, T], BF16, es) for i in range(2)]
    wsl = [ctx.sbuf(f"p3_wsl{i}", [128, 2 * 1024], I32, es) for i in range(2)]
    wqg = [ctx.sbuf(f"p3_wqg{i}", [128, DC * 128], BF16, es) for i in range(2)]
    wqu = [ctx.sbuf(f"p3_wqu{i}", [128, DC * 128], BF16, es) for i in range(2)]
    silu_t = [ctx.sbuf(f"p3_si{i}", [128, TG], F32, es) for i in range(2)]
    if load_consts:
        ctx.dma("sync", t["ones"][:], d["ones_bf"].ap(), LD)
        ctx.dma("sync", t["idf"][:], d["ident_f"].ap(), LD)
        ctx.dma("sync", t["idb"][:], d["ident_b"].ap(), LD)
    last_v = 0
    for g in range(NG):
        lv = load_stage(ctx, t, d["x3T"].ap(), g, guard_v=last_v)
        ctx.wait("vector", LD, lv)
        last_v = emit_ln_group(ctx, t, g, want_lnbf=True, want_absmax=False,
                               lnbf_dst=xq4, lnbf_g=g)
    if upto == "LN":
        for c in range(DC):
            vv = ctx.op("vector", lambda e, c=c: e.tensor_copy(out=t["xout"][0][:], in_=xq4[:, c * T: c * T + TG]), "VE")
            ctx.wait("gpsimd", "VE", vv)
            ctx.dma("gpsimd", d["dbg3"].ap()[c * 128:(c + 1) * 128, 0:TG], t["xout"][0][:], "X4S")
            ctx.wait("vector", "X4S", ctx.cnt["X4S"])
        for c in range(DC):
            vv = ctx.op("vector", lambda e, c=c: e.tensor_copy(out=t["xout"][0][:], in_=xq4[:, c * T + 3 * TG: c * T + 4 * TG]), "VE")
            ctx.wait("gpsimd", "VE", vv)
            ctx.dma("gpsimd", d["dbg3"].ap()[c * 128:(c + 1) * 128, TG:2 * TG], t["xout"][0][:], "X4S")
            ctx.wait("vector", "X4S", ctx.cnt["X4S"])
        ctx.drain_dma(["X4S", "ST", LD, "WL"])
        ctx.barrier()
        es.close()
        return
    h3_store_v = [0, 0]
    h2v = [0, 0]
    cast_v = [0, 0]
    mm_done = [0, 0]
    def p3_load(i):
        ib = i % 2
        ctx.wait("sync", V, cast_v[ib])
        for c in range(DC):
            ctx.dma("sync", wsl[ib][:, c * 128:(c + 1) * 128],
                    d["w1gT"].ap()[c * 128:(c + 1) * 128, i * 128:(i + 1) * 128], "WL")
        for c in range(DC):
            ctx.dma("sync", wsl[ib][:, 1024 + c * 128: 1024 + (c + 1) * 128],
                    d["w1uT"].ap()[c * 128:(c + 1) * 128, i * 128:(i + 1) * 128], "WL")
        return ctx.cnt["WL"]
    def p3_cast(i, lv):
        ib = i % 2
        ctx.wait("vector", "WL", lv)
        ctx.wait("vector", MM, mm_done[ib])
        ctx.op("vector", lambda e, ib=ib: e.tensor_copy(
            out=wqg[ib][:], in_=wsl[ib][:, 0:1024]), None)
        cast_v[ib] = ctx.op("vector", lambda e, ib=ib: e.tensor_copy(
            out=wqu[ib][:], in_=wsl[ib][:, 1024:2048]), V)
    lv0 = p3_load(0)
    p3_cast(0, lv0)
    for i in range(IC1):
        ib = i % 2
        if i + 1 < IC1:
            lvn = p3_load(i + 1)
            p3_cast(i + 1, lvn)
        ctx.wait("vector", "H3S", h3_store_v[i % 2])
        ctx.wait("tensor", V, cast_v[ib])
        for tg in range(NG):
            pA = t["ps"][4] if tg % 2 == 0 else t["ps"][2]
            pB = t["ps"][5] if tg % 2 == 0 else t["ps"][3]
            if tg >= 2:
                ctx.wait("tensor", V, h2v[tg % 2])
            for k in range(DC):
                ctx.op("tensor", lambda e, k=k, tg=tg, pA=pA, ib=ib: e.matmul(
                    out=pA[:], lhsT=wqg[ib][:, k * 128:(k + 1) * 128],
                    rhs=xq4[:, k * T + tg * TG: k * T + (tg + 1) * TG],
                    start=(k == 0), stop=(k == DC - 1)), None)
            for k in range(DC):
                ctx.op("tensor", lambda e, k=k, tg=tg, pB=pB, ib=ib: e.matmul(
                    out=pB[:], lhsT=wqu[ib][:, k * 128:(k + 1) * 128],
                    rhs=xq4[:, k * T + tg * TG: k * T + (tg + 1) * TG],
                    start=(k == 0), stop=(k == DC - 1)), MM if k == DC - 1 else None)
            mmv = ctx.cnt[MM]
            ctx.wait("scalar", MM, mmv)
            sv = ctx.op("scalar", lambda e, tg=tg, pA=pA: e.activation(
                out=silu_t[tg % 2][:], in_=pA[:], func=AF.Silu,
                scale=1.0 / QMAX), SC)
            ctx.wait("vector", SC, sv)
            h2v[tg % 2] = ctx.op("vector", lambda e, tg=tg, i=i, pB=pB: e.tensor_tensor(
                out=h3sl[i % 2][:, tg * TG:(tg + 1) * TG], in0=pB[:],
                in1=silu_t[tg % 2][:], op=ALU.mult), V)
        ctx.wait("gpsimd", V, h2v[(NG - 1) % 2])
        h3_store_v[i % 2] = ctx.dma(
            "gpsimd", d["h3s"].ap()[i * 128:(i + 1) * 128, :], h3sl[i % 2][:], "H3S")
    if upto.startswith("E"):
        ioff = {"E": 0, "E2": 8, "E3": 16}[upto]
        ctx.wait("gpsimd", "H3S", ctx.cnt["H3S"])
        for i in range(DC):
            ctx.dma("gpsimd", d["dbg3"].ap()[i * 128:(i + 1) * 128, :],
                    d["h3s"].ap()[(ioff + i) * 128:(ioff + i + 1) * 128, :], "X4S")
        ctx.drain_dma(["X4S", "ST", LD, "WL", "H3S"])
        ctx.barrier()
        es.close()
        return
    es.close()

    es2 = ExitStack()
    wdqT = ctx.sbuf("p3_wdqT", [128, IC1 * 1024], BF16, es2)
    h3q = ctx.sbuf("p3_h3q", [128, IC1 * TG], BF16, es2)
    wTsl = ctx.sbuf("p3_wTsl", [128, 1024], I32, es2)
    ctx.wait("sync", "H3S", ctx.cnt["H3S"])
    prev_v = ctx.cnt[V]
    for i in range(IC1):
        ctx.wait("sync", V, prev_v)
        lv = ctx.dma("sync", wTsl[:], d["w1dT"].ap()[i * 128:(i + 1) * 128, :], "WL")
        ctx.wait("vector", "WL", lv)
        prev_v = ctx.op("vector", lambda e, i=i: e.tensor_copy(
            out=wdqT[:, i * 1024:(i + 1) * 1024], in_=wTsl[:]), V)
    ctx.wait("sync", "H3S", ctx.cnt["H3S"])
    dmm = 0
    last_vv = prev_v
    jj = 0
    for tg in range(NG):
        if tg > 0:
            ctx.wait("sync", MM, dmm)
        lv = 0
        for i in range(IC1):
            lv = ctx.dma("sync", h3q[:, i * TG:(i + 1) * TG],
                         d["h3s"].ap()[i * 128:(i + 1) * 128, tg * TG:(tg + 1) * TG], "H3L")
        if tg == 0:
            ctx.wait("tensor", V, prev_v)
        ctx.wait("tensor", "H3L", lv)
        for hh in range(2):
            if not (tg == 0 and hh == 0):
                ctx.wait("tensor", V, last_vv)
            for i in range(IC1):
                for h4 in range(4):
                    h = hh * 4 + h4
                    ctx.op("tensor", lambda e, i=i, h=h, h4=h4: e.matmul(
                        out=t["ps"][h4][:], lhsT=wdqT[:, i * 1024 + h * 128:
                                                      i * 1024 + (h + 1) * 128],
                        rhs=h3q[:, i * TG:(i + 1) * TG],
                        start=(i == 0), stop=(i == IC1 - 1)),
                        MM if (i == IC1 - 1 and h4 == 3) else None)
            dmm = ctx.cnt[MM]
            ctx.wait("sync", V, last_vv)
            lv = 0
            for h4 in range(4):
                h = hh * 4 + h4
                lv = ctx.dma("sync", t["stage"][:, h * TG:(h + 1) * TG],
                             d["x3T"].ap()[h * 128:(h + 1) * 128, tg * TG:(tg + 1) * TG], LD)
            ctx.wait("vector", LD, lv)
            ctx.wait("vector", MM, dmm)
            for h4 in range(4):
                h = hh * 4 + h4
                if jj >= 2:
                    ctx.wait("vector", "X4S", 16 * (jj - 1))
                if upto == "G2":
                    vv = ctx.op("vector", lambda e, h4=h4, jj=jj: e.tensor_scalar(
                        out=t["xout"][jj % 2][:], in0=t["ps"][h4][:],
                        scalar1=1.0 / (QMAX * QMAX), scalar2=None, op0=ALU.mult), V)
                else:
                    vv = ctx.op("vector", lambda e, h=h, h4=h4, jj=jj: e.scalar_tensor_tensor(
                        out=t["xout"][jj % 2][:], in0=t["ps"][h4][:],
                        scalar=1.0 / (QMAX * QMAX), in1=t["stage"][:, h * TG:(h + 1) * TG],
                        op0=ALU.mult, op1=ALU.add), V)
                last_vv = vv
                ctx.wait("gpsimd", V, vv)
                ctx.dma("gpsimd", d["x4T"].ap()[h * 128:(h + 1) * 128, tg * TG:(tg + 1) * TG],
                        t["xout"][jj % 2][:], "X4S")
                jj += 1
    ctx.drain_dma(["X4S", LD, "WL", "H3L", "H3S"])
    ctx.barrier()
    es2.close()


def declare_p3(nc, d, x3_is_input):
    for name, shape, dt in [
            ("w1gT", [D, 3072], I32), ("w1uT", [D, 3072], I32),
            ("w1dT", [3072, D], I32)]:
        d[name] = nc.dram_tensor(name, shape, dt, kind="ExternalInput")
    if x3_is_input:
        d["x3T"] = nc.dram_tensor("x3T", [D, T], F32, kind="ExternalInput")
        for name, shape, dt in [
                ("ones_bf", [128, 128], BF16), ("ident_f", [128, 128], F32),
                ("ident_b", [128, 128], BF16)]:
            d[name] = nc.dram_tensor(name, shape, dt, kind="ExternalInput")
    d["h3s"] = nc.dram_tensor("h3s", [3072, T], BF16)
    d["x4T"] = nc.dram_tensor("x4T", [D, T], F32, kind="ExternalOutput")
    d["dbg3"] = nc.dram_tensor("dbg3", [D, T], F32, kind="ExternalOutput")


def build_p3(upto="G"):
    nc = bass.Bass("TRN2", target_bir_lowering=False)
    d = {}
    declare_p3(nc, d, x3_is_input=True)
    es = ExitStack()
    ctx = Ctx(nc, es)
    with es:
        t = alloc_common(ctx)
        emit_phase3(ctx, t, d, load_consts=True, upto=upto)
        ctx.build_block()
    return nc



def host_prep(inputs):
    c = {}
    c["WdT"] = np.ascontiguousarray(inputs["L0_attn_W_delta"].T)
    c["bdelta8"] = np.ascontiguousarray(inputs["L0_attn_b_delta"].reshape(8, 128).T)
    c["WbT128"] = np.ascontiguousarray(
        inputs["L0_attn_W_B"].T.reshape(8, 128, 16).transpose(1, 0, 2).reshape(128, 128))
    c["bB"] = np.ascontiguousarray(inputs["L0_attn_b_B"].reshape(16, 1))
    c["Alog"] = np.ascontiguousarray(inputs["L0_attn_A_log"].reshape(16, 1))
    c["Dcol8"] = np.ascontiguousarray(inputs["L0_attn_D_param"].reshape(8, 128).T)
    c["WoT"] = np.ascontiguousarray(inputs["L0_attn_W_out"].T)
    c["bout8"] = np.ascontiguousarray(inputs["L0_attn_b_out"].reshape(8, 128).T)
    c["ones_bf"] = np.ones((128, 128), dtype=BF)
    c["ident_f"] = np.eye(128, dtype=np.float32)
    c["ident_b"] = np.eye(128, dtype=BF)
    c["clip2"] = np.concatenate([inputs["L0_ffn_clip_x"],
                                 inputs["L0_ffn_clip_down"]]).reshape(1, 2).astype(np.float32)
    c["wg_nat"] = np.ascontiguousarray(inputs["L0_ffn_w_gate"])
    c["wu_nat"] = np.ascontiguousarray(inputs["L0_ffn_w_up"])
    c["wgT"] = np.ascontiguousarray(inputs["L0_ffn_w_gate"].T)
    c["wuT"] = np.ascontiguousarray(inputs["L0_ffn_w_up"].T)
    c["wd_nat"] = np.ascontiguousarray(inputs["L0_ffn_w_down"])
    c["wdT"] = np.ascontiguousarray(inputs["L0_ffn_w_down"].T)
    c["wqkv"] = np.ascontiguousarray(inputs["L1_attn_w_qkv"])
    c["wproj"] = np.ascontiguousarray(inputs["L1_attn_w_proj"])
    c["bqkv24"] = np.ascontiguousarray(inputs["L1_attn_b_qkv"].reshape(24, 128).T)
    c["bvrow"] = np.ascontiguousarray(inputs["L1_attn_b_qkv"][2048:].reshape(1, 1024))
    c["bproj8"] = np.ascontiguousarray(inputs["L1_attn_b_proj"].reshape(8, 128).T)
    c["g8"] = np.ascontiguousarray(inputs["L1_attn_g_norm"].reshape(8, 128).T)
    half = D // 2
    invf = 1.0 / 10000.0 ** (np.arange(half) / half)
    ang = np.arange(S)[:, None] * invf[None, :]
    cosr = np.round(np.cos(ang)).astype(BF)
    sinr = np.round(np.sin(ang)).astype(BF)
    c["ropeC"] = np.ascontiguousarray(cosr.T)
    c["ropeSp"] = np.ascontiguousarray(sinr.T)
    c["ropeSn"] = np.ascontiguousarray((-sinr).T)
    c["w1gT"] = np.ascontiguousarray(inputs["L1_ffn_w_gate"].T)
    c["w1uT"] = np.ascontiguousarray(inputs["L1_ffn_w_up"].T)
    c["w1dT"] = np.ascontiguousarray(inputs["L1_ffn_w_down"].T)
    return c


_P0_KEYS = ["ones_bf", "ident_f", "ident_b", "x0T", "WdT", "bdelta8", "WbT128",
            "bB", "Alog", "Dcol8", "WoT", "bout8"]
_P1_KEYS = ["ones_bf", "ident_f", "ident_b", "clip2", "wg_nat", "wu_nat",
            "wgT", "wuT", "wd_nat", "wdT"]
_P3_KEYS = ["ones_bf", "ident_f", "ident_b", "w1gT", "w1uT", "w1dT"]
_P2_KEYS = ["ones_bf", "ident_f", "ident_b", "wqkv", "wproj", "bqkv24", "bvrow",
            "bproj8", "g8", "ropeC", "ropeSp", "ropeSn"]


def declare_p0(nc, d):
    for name, shape, dt in [
            ("ones_bf", [128, 128], BF16), ("ident_f", [128, 128], F32),
            ("ident_b", [128, 128], BF16), ("x0T", [D, T], F32),
            ("WdT", [D, D], F32), ("bdelta8", [128, 8], F32),
            ("WbT128", [128, 128], F32), ("bB", [16, 1], F32),
            ("Alog", [16, 1], F32), ("Dcol8", [128, 8], F32),
            ("WoT", [16, D], F32), ("bout8", [128, 8], F32)]:
        d[name] = nc.dram_tensor(name, shape, dt, kind="ExternalInput")


def declare_p1(nc, d, x1_is_input):
    for name, shape, dt in [
            ("clip2", [1, 2], F32), ("wg_nat", [2816, D], F32),
            ("wu_nat", [2816, D], F32), ("wgT", [D, 2816], F32),
            ("wuT", [D, 2816], F32), ("wd_nat", [D, 2816], F32),
            ("wdT", [2816, D], F32)]:
        d[name] = nc.dram_tensor(name, shape, dt, kind="ExternalInput")
    if x1_is_input:
        d["x1T"] = nc.dram_tensor("x1T", [D, T], F32, kind="ExternalInput")
        for name, shape, dt in [
                ("ones_bf", [128, 128], BF16), ("ident_f", [128, 128], F32),
                ("ident_b", [128, 128], BF16)]:
            d[name] = nc.dram_tensor(name, shape, dt, kind="ExternalInput")
    d["lnfs"] = nc.dram_tensor("lnfs", [D, T], F32)
    d["h2s"] = nc.dram_tensor("h2s", [2816, T], F32)
    d["cci"] = nc.dram_tensor("cci", [1, 1], F32)
    d["cco"] = nc.dram_tensor("cco", [1, 1], F32)
    d["x2T"] = nc.dram_tensor("x2T", [D, T], F32, kind="ExternalOutput")
    d["dbg"] = nc.dram_tensor("dbg", [D, T], F32, kind="ExternalOutput")


def build_p0():
    nc = bass.Bass("TRN2", target_bir_lowering=False)
    d = {}
    declare_p0(nc, d)
    d["x1T"] = nc.dram_tensor("x1T", [D, T], F32, kind="ExternalOutput")
    es = ExitStack()
    ctx = Ctx(nc, es)
    with es:
        t = alloc_common(ctx)
        emit_phase0(ctx, t, d)
        ctx.build_block()
    return nc


def build_p1(upto="G"):
    nc = bass.Bass("TRN2", target_bir_lowering=False)
    d = {}
    declare_p1(nc, d, x1_is_input=True)
    es = ExitStack()
    ctx = Ctx(nc, es)
    with es:
        t = alloc_common(ctx)
        emit_phase1(ctx, t, d, load_consts=True, upto=upto)
        ctx.build_block()
    return nc


def kernel_unfused(**inputs):
    x = np.asarray(inputs["L0_attn_x"])
    com = host_prep(inputs)
    xouts = {}
    nc0 = build_p0()
    maps = []
    for core in range(NCORE):
        m = {k: com[k] for k in _P0_KEYS if k != "x0T"}
        xc = x[core * SPC:(core + 1) * SPC].reshape(T, D)
        m["x0T"] = np.ascontiguousarray(xc.T)
        maps.append(m)
    r0 = run_bass_kernel_spmd(nc0, maps, core_ids=list(range(NCORE)))
    nc1 = build_p1()
    maps = []
    for core in range(NCORE):
        m = {k: com[k] for k in _P1_KEYS}
        m["x1T"] = np.asarray(r0.results[core]["x1T"])
        maps.append(m)
    r1 = run_bass_kernel_spmd(nc1, maps, core_ids=list(range(NCORE)))
    nc2 = build_p2()
    maps = []
    for core in range(NCORE):
        m = {k: com[k] for k in _P2_KEYS}
        m["x2T"] = np.asarray(r1.results[core]["x2T"])
        maps.append(m)
    r2 = run_bass_kernel_spmd(nc2, maps, core_ids=list(range(NCORE)))
    nc3 = build_p3()
    maps = []
    for core in range(NCORE):
        m = {k: com[k] for k in _P3_KEYS}
        m["x3T"] = np.asarray(r2.results[core]["x3T"])
        maps.append(m)
    r3 = run_bass_kernel_spmd(nc3, maps, core_ids=list(range(NCORE)))
    out = np.empty((B, S, D), dtype=np.float32)
    for core in range(NCORE):
        out[core * SPC:(core + 1) * SPC] = (
            np.asarray(r3.results[core]["x4T"]).T.reshape(SPC, S, D))
    return out


def build_fused(dbg=False, nphases=4, p1upto="G"):
    nc = bass.Bass("TRN2", target_bir_lowering=False)
    d = {}
    declare_p0(nc, d)
    declare_p1(nc, d, x1_is_input=False)
    declare_p2(nc, d, x2_is_input=False)
    declare_p3(nc, d, x3_is_input=False)
    kind = "ExternalOutput" if dbg else "Internal"
    d["x1T"] = nc.dram_tensor("x1T", [D, T], F32, kind=kind)
    d["x2T"] = nc.dram_tensor("x2T_i", [D, T], F32, kind=kind)
    d["x3T"] = nc.dram_tensor("x3T_i", [D, T], F32, kind=kind)
    es = ExitStack()
    ctx = Ctx(nc, es)
    with es:
        t = alloc_common(ctx)
        emit_phase0(ctx, t, d)
        if nphases == -1:
            ctx.barrier()
        if nphases == -2:
            for nm in ("H2S", "CC", "CD", "H2LA", "H2LB", "X2S"):
                ctx.sem(nm)
        if nphases == -3:
            cols = ctx.sbuf("p1_cols", [128, 8], F32)
            es3 = ExitStack()
            a1 = ctx.sbuf("p1_lnf", [128, DC * TG], F32, es3)
            a2 = ctx.sbuf("p1_xint", [128, DC * T], BF16, es3)
            a3 = [ctx.sbuf(f"p1_h2{i}", [128, T], F32, es3) for i in range(2)]
            ctx.barrier()
            es3.close()
        if nphases == -4:
            ctx.drain_dma(["ST", "LD", "WL"])
            ctx.barrier()
        if nphases == -5:
            cols = ctx.sbuf("p1_cols", [128, 8], F32)
            es3 = ExitStack()
            names = [("p1_lnf", [128, DC * TG], F32), ("p1_xint", [128, DC * T], BF16),
                     ("p1_h20", [128, T], F32), ("p1_h21", [128, T], F32),
                     ("p1_wnat", [128, 2 * D], F32), ("p1_wTg", [128, DC * 128], F32),
                     ("p1_wTu", [128, DC * 128], F32), ("p1_wqg", [128, DC * 128], BF16),
                     ("p1_wqu", [128, DC * 128], BF16), ("p1_itmp", [128, 1024], I32),
                     ("p1_rig", [128, 128], F32), ("p1_riu", [128, 128], F32),
                     ("p1_rc2", [128, 2], BF16), ("p1_rr2", [1, 256], BF16),
                     ("p1_am2", [128, 1], F32), ("p1_cx", [1, 2], F32),
                     ("p1_si0", [128, TG], F32), ("p1_si1", [128, TG], F32)]
            for nm_, sh_, dt_ in names:
                ctx.sbuf(nm_, sh_, dt_, es3)
            ctx.barrier()
            es3.close()
        if nphases > 1:
            emit_phase1(ctx, t, d, upto=p1upto)
        if nphases > 2:
            emit_phase2(ctx, t, d)
        if nphases > 3:
            emit_phase3(ctx, t, d)
        ctx.build_block()
    return nc


_ALL_KEYS = sorted(set(_P0_KEYS + _P1_KEYS + _P2_KEYS + _P3_KEYS))


def kernel(**inputs):
    x = np.asarray(inputs["L0_attn_x"])
    com = host_prep(inputs)
    nc = build_fused()
    maps = []
    for core in range(NCORE):
        m = {k: com[k] for k in _ALL_KEYS if k != "x0T"}
        xc = x[core * SPC:(core + 1) * SPC].reshape(T, D)
        m["x0T"] = np.ascontiguousarray(xc.T)
        maps.append(m)
    r = run_bass_kernel_spmd(nc, maps, core_ids=list(range(NCORE)))
    out = np.empty((B, S, D), dtype=np.float32)
    for core in range(NCORE):
        out[core * SPC:(core + 1) * SPC] = (
            np.asarray(r.results[core]["x4T"]).T.reshape(SPC, S, D))
    return out


# version 05
# speedup vs baseline: 1.0271x; 1.0154x over previous
import numpy as np
import ml_dtypes
from contextlib import ExitStack
import concourse.bass as bass
import concourse.mybir as mybir
from concourse.bass_utils import run_bass_kernel_spmd

F32 = mybir.dt.float32
BF16 = mybir.dt.bfloat16
I32 = mybir.dt.int32
AF = mybir.ActivationFunctionType
ALU = mybir.AluOpType
AX = mybir.AxisListType.X
BF = ml_dtypes.bfloat16

B, S, D = 16, 2048, 1024
NCORE = 8
SPC = B // NCORE
T = SPC * S
TG = 512
NG = T // TG
GPS = S // TG
DC = D // 128
IC0 = 2816 // 128
IC1 = 3072 // 128
MAGIC = float(3 * 2.0 ** 22)
LN3 = 1.0986122886681098
QMAX = 127.0

ENGINES = ("sync", "scalar", "vector", "tensor", "gpsimd")


class Ctx:
    def __init__(self, nc, es):
        self.nc = nc
        self.es = es
        self.emit = {e: [] for e in ENGINES}
        self._sems = {}
        self.cnt = {}
        self.waits = []
        self.nbarrier = 0

    def sem(self, name):
        if name not in self._sems:
            self._sems[name] = self.es.enter_context(self.nc.semaphore(name))
            self.cnt[name] = 0
        return self._sems[name]

    def bump(self, name, n):
        self.sem(name)
        self.cnt[name] += n
        return self.cnt[name]

    def sbuf(self, name, shape, dt, scope=None):
        return (scope or self.es).enter_context(
            self.nc.sbuf_tensor(name, list(shape), dt))

    def psum(self, name, shape, dt, scope=None):
        return (scope or self.es).enter_context(
            self.nc.psum_tensor(name, list(shape), dt))

    def dma(self, q, out, in_, sem="LD"):
        s = self.sem(sem)
        v = self.bump(sem, 16)
        self.emit[q].append(
            lambda e, out=out, in_=in_, s=s: e.dma_start(out=out, in_=in_).then_inc(s, 16))
        return v

    def wait(self, eng, sem, val):
        if val <= 0:
            return
        s = self.sem(sem)
        self.waits.append((sem, val))
        self.emit[eng].append(lambda e, s=s, val=val: e.wait_ge(s, val))

    def op(self, eng, fn, sem=None, n=1):
        if sem is not None:
            s = self.sem(sem)
            v = self.bump(sem, n)
            self.emit[eng].append(lambda e, fn=fn, s=s, n=n: fn(e).then_inc(s, n))
            return v
        self.emit[eng].append(fn)
        return None

    def drain(self, eng="vector"):
        self.emit[eng].append(lambda e: e.drain())

    def barrier(self):
        self.nbarrier += 1
        s = self.sem("PH")
        tgt = len(ENGINES) * self.nbarrier
        self.cnt["PH"] = tgt
        for eng in ENGINES:
            self.emit[eng].append(lambda e, s=s: e.sem_inc(s, 1))
            self.emit[eng].append(lambda e, s=s, tgt=tgt: e.wait_ge(s, tgt))

    def drain_dma(self, sems):
        for sname in sems:
            if sname in self.cnt:
                self.wait("sync", sname, self.cnt[sname])

    def build_block(self):
        for sem_name, val in self.waits:
            assert val <= self.cnt[sem_name], (sem_name, val, self.cnt[sem_name])
        block = self.es.enter_context(self.nc.Block())
        for eng in ENGINES:
            if self.emit[eng]:
                def runner(e, eng=eng):
                    for f in self.emit[eng]:
                        f(e)
                getattr(block, eng)(runner)


def alloc_common(ctx):
    t = {}
    t["stage"] = ctx.sbuf("t_stage", [128, DC * TG], F32)
    t["xb"] = ctx.sbuf("t_xb", [128, DC * TG], BF16)
    t["sq"] = ctx.sbuf("t_sq", [128, DC * TG], BF16)
    t["rowf"] = ctx.sbuf("t_rowf", [1, TG], F32)
    t["rowf2"] = ctx.sbuf("t_rowf2", [1, TG], F32)
    t["rowf3"] = ctx.sbuf("t_rowf3", [1, TG], F32)
    t["hrow"] = ctx.sbuf("t_hrow", [1, 4 * TG], BF16)
    t["mbf"] = ctx.sbuf("t_mbf", [128, TG], F32)
    t["rbf"] = ctx.sbuf("t_rbf", [128, TG], F32)
    t["amax"] = ctx.sbuf("t_amax", [128, 1], F32)
    t["ones"] = ctx.sbuf("t_ones", [128, 128], BF16)
    t["idf"] = ctx.sbuf("t_idf", [128, 128], F32)
    t["idb"] = ctx.sbuf("t_idb", [128, 128], BF16)
    t["scl"] = ctx.sbuf("t_scl", [1, 32], F32)
    t["sclb"] = ctx.sbuf("t_sclb", [1, 32], BF16)
    t["col1"] = ctx.sbuf("t_col1", [128, 1], F32)
    t["col2"] = ctx.sbuf("t_col2", [128, 1], F32)
    t["xout"] = [ctx.sbuf(f"t_xo{i}", [128, TG], F32) for i in range(2)]
    t["ep1"] = ctx.sbuf("t_ep1", [128, TG], F32)
    t["ps"] = [ctx.psum(f"t_ps{i}", [128, TG], F32) for i in range(6)]
    t["psb"] = [ctx.psum(f"t_psb{i}", [128, 2 * TG], BF16) for i in range(2)]
    return t


def emit_ln_group(ctx, t, g, want_lnbf=True, want_absmax=False, eps=1e-5, lnbf_dst=None, lnbf_g=0, want_apply=True):
    V, SC, MM = "VE", "SC", "MM"
    ps_s0, ps_s1, ps_b0, ps_b1 = t["ps"][0], t["ps"][1], t["ps"][2], t["ps"][3]
    rowf, rowf2, rowf3, hrow = t["rowf"], t["rowf2"], t["rowf3"], t["hrow"]
    for c in range(DC):
        ctx.op("vector", lambda e, c=c: e.tensor_copy(
            out=t["xb"][:, c * TG:(c + 1) * TG], in_=t["stage"][:, c * TG:(c + 1) * TG]), None)
        ctx.op("vector", lambda e, c=c: e.tensor_tensor(
            out=t["sq"][:, c * TG:(c + 1) * TG], in0=t["xb"][:, c * TG:(c + 1) * TG],
            in1=t["xb"][:, c * TG:(c + 1) * TG], op=ALU.mult),
            V if c == DC - 1 else None)
    v_cast = ctx.cnt[V]
    ctx.wait("tensor", V, v_cast)
    for c in range(DC):
        ctx.op("tensor", lambda e, c=c: e.matmul(
            out=ps_s0[:1, :], lhsT=t["ones"][:, :1], rhs=t["xb"][:, c * TG:(c + 1) * TG],
            start=(c == 0), stop=(c == DC - 1)), None)
    for c in range(DC):
        ctx.op("tensor", lambda e, c=c: e.matmul(
            out=ps_s1[:1, :], lhsT=t["ones"][:, :1], rhs=t["sq"][:, c * TG:(c + 1) * TG],
            start=(c == 0), stop=(c == DC - 1)), MM if c == DC - 1 else None)
    v_stat = ctx.cnt[MM]
    ctx.wait("vector", MM, v_stat)
    ctx.op("vector", lambda e: e.tensor_scalar(
        out=rowf[:], in0=ps_s0[:1, :], scalar1=1.0 / D, scalar2=None, op0=ALU.mult), None)
    ctx.op("vector", lambda e: e.tensor_tensor(
        out=rowf2[:], in0=rowf[:], in1=rowf[:], op=ALU.mult), None)
    ctx.op("vector", lambda e: e.scalar_tensor_tensor(
        out=rowf3[:], in0=ps_s1[:1, :], scalar=1.0 / D, in1=rowf2[:],
        op0=ALU.mult, op1=ALU.subtract), None)
    v_var = ctx.op("vector", lambda e, eps=eps: e.tensor_scalar(
        out=rowf3[:], in0=rowf3[:], scalar1=eps, scalar2=None, op0=ALU.add), V)
    ctx.wait("scalar", V, v_var)
    s_sq = ctx.op("scalar", lambda e: e.activation(
        out=rowf2[:], in_=rowf3[:], func=AF.Sqrt), SC)
    ctx.wait("vector", SC, s_sq)
    ctx.op("vector", lambda e: e.reciprocal(out=rowf3[:], in_=rowf2[:]), None)
    ctx.op("vector", lambda e: e.tensor_copy(out=hrow[:, 0:TG], in_=rowf[:]), None)
    ctx.op("vector", lambda e: e.tensor_tensor(
        out=hrow[:, TG:2 * TG], in0=rowf[:], in1=hrow[:, 0:TG], op=ALU.subtract), None)
    ctx.op("vector", lambda e: e.tensor_copy(out=hrow[:, 2 * TG:3 * TG], in_=rowf3[:]), None)
    v_r = ctx.op("vector", lambda e: e.tensor_tensor(
        out=hrow[:, 3 * TG:4 * TG], in0=rowf3[:], in1=hrow[:, 2 * TG:3 * TG],
        op=ALU.subtract), V)
    ctx.wait("tensor", V, v_r)
    ctx.op("tensor", lambda e: e.matmul(
        out=ps_b0[:], lhsT=t["ones"][:1, :], rhs=hrow[:, 0:TG], start=True, stop=True), None)
    v_m = ctx.op("tensor", lambda e: e.matmul(
        out=ps_b1[:], lhsT=t["ones"][:1, :], rhs=hrow[:, TG:2 * TG], start=True, stop=True), MM)
    ctx.wait("scalar", MM, v_m)
    s_m = ctx.op("scalar", lambda e: e.activation(
        out=t["mbf"][:], in_=ps_b0[:], func=AF.Copy), SC)
    ctx.wait("vector", SC, s_m)
    v_mb = ctx.op("vector", lambda e: e.tensor_tensor(
        out=t["mbf"][:], in0=t["mbf"][:], in1=ps_b1[:], op=ALU.add), V)
    ctx.wait("tensor", V, v_mb)
    ctx.op("tensor", lambda e: e.matmul(
        out=ps_b0[:], lhsT=t["ones"][:1, :], rhs=hrow[:, 2 * TG:3 * TG],
        start=True, stop=True), None)
    v_rr = ctx.op("tensor", lambda e: e.matmul(
        out=ps_b1[:], lhsT=t["ones"][:1, :], rhs=hrow[:, 3 * TG:4 * TG],
        start=True, stop=True), MM)
    ctx.wait("scalar", MM, v_rr)
    s_r = ctx.op("scalar", lambda e: e.activation(
        out=t["rbf"][:], in_=ps_b0[:], func=AF.Copy), SC)
    ctx.wait("vector", SC, s_r)
    ctx.op("vector", lambda e: e.tensor_tensor(
        out=t["rbf"][:], in0=t["rbf"][:], in1=ps_b1[:], op=ALU.add), None)
    if not want_apply:
        return ctx.op("vector", lambda e: e.tensor_copy(
            out=t["rowf"][:, :1], in_=t["rowf"][:, :1]), "VE")
    for c in range(DC):
        ctx.op("vector", lambda e, c=c: e.tensor_tensor(
            out=t["ep1"][:], in0=t["stage"][:, c * TG:(c + 1) * TG],
            in1=t["mbf"][:], op=ALU.subtract), None)
        ctx.op("vector", lambda e, c=c: e.tensor_tensor(
            out=t["lnf"][:, c * TG:(c + 1) * TG], in0=t["ep1"][:], in1=t["rbf"][:],
            op=ALU.mult), None)
        if want_lnbf:
            dst = (t["lnbf"][:, c * TG:(c + 1) * TG] if lnbf_dst is None else
                   lnbf_dst[:, c * T + lnbf_g * TG: c * T + (lnbf_g + 1) * TG])
            ctx.op("vector", lambda e, c=c, dst=dst: e.tensor_copy(
                out=dst, in_=t["lnf"][:, c * TG:(c + 1) * TG]), None)
        if want_absmax:
            ctx.op("vector", lambda e, c=c: e.tensor_reduce(
                out=t["col1"][:], in_=t["lnf"][:, c * TG:(c + 1) * TG], axis=AX,
                op=ALU.max, apply_absolute_value=True), None)
            ctx.drain()
            if g == 0 and c == 0:
                ctx.op("vector", lambda e: e.tensor_copy(
                    out=t["amax"][:], in_=t["col1"][:]), None)
            else:
                ctx.op("vector", lambda e: e.tensor_tensor(
                    out=t["amax"][:], in0=t["amax"][:], in1=t["col1"][:],
                    op=ALU.max), None)
            ctx.drain()
    return ctx.op("vector", lambda e: e.tensor_copy(
        out=t["rowf"][:, :1], in_=t["rowf"][:, :1]), V)


def load_stage(ctx, t, src_ap, g, guard_v=0):
    ctx.wait("sync", "VE", guard_v)
    v = 0
    for c in range(DC):
        v = ctx.dma("sync", t["stage"][:, c * TG:(c + 1) * TG],
                    src_ap[c * 128:(c + 1) * 128, g * TG:(g + 1) * TG], "LD")
    return v


def bcast_scalar_col(ctx, t, src11, dst_col):
    V, SC, MM = "VE", "SC", "MM"
    sb = t["sclb"]
    ctx.op("vector", lambda e: e.tensor_copy(out=sb[:, 0:1], in_=src11), None)
    ctx.drain()
    v1 = ctx.op("vector", lambda e: e.tensor_tensor(
        out=sb[:, 1:2], in0=src11, in1=sb[:, 0:1], op=ALU.subtract), V)
    ctx.drain()
    ctx.wait("tensor", V, v1)
    ctx.op("tensor", lambda e: e.matmul(
        out=t["ps"][2][:, 0:1], lhsT=t["ones"][:1, :], rhs=sb[:, 0:1],
        start=True, stop=True), None)
    v2 = ctx.op("tensor", lambda e: e.matmul(
        out=t["ps"][3][:, 0:1], lhsT=t["ones"][:1, :], rhs=sb[:, 1:2],
        start=True, stop=True), MM)
    ctx.wait("scalar", MM, v2)
    s1 = ctx.op("scalar", lambda e: e.activation(
        out=dst_col, in_=t["ps"][2][:, 0:1], func=AF.Copy), SC)
    ctx.wait("vector", SC, s1)
    r = ctx.op("vector", lambda e: e.tensor_tensor(
        out=dst_col, in0=dst_col, in1=t["ps"][3][:, 0:1], op=ALU.add), V)
    ctx.drain()
    return r



def emit_phase0(ctx, t, d):
    t = dict(t)
    V, SC, MM, LD, ST = "VE", "SC", "MM", "LD", "ST"
    es = ExitStack()
    t["lnf"] = ctx.sbuf("p0_lnf", [128, DC * TG], F32, es)
    t["lnbf"] = ctx.sbuf("p0_lnbf", [128, DC * TG], BF16, es)
    wd_bf = ctx.sbuf("p0_wd", [128, DC * D], BF16, es)
    wb_bf = ctx.sbuf("p0_wb", [128, 128], BF16, es)
    wo_bf = ctx.sbuf("p0_wo", [16, D], BF16, es)
    wstage = ctx.sbuf("p0_wst", [128, D], F32, es)
    Atmp = ctx.sbuf("p0_At", [16, 1], F32, es)
    bdelta = ctx.sbuf("p0_bd", [128, 8], F32, es)
    bB = ctx.sbuf("p0_bB", [16, 1], F32, es)
    Acol = ctx.sbuf("p0_Ac", [16, 1], F32, es)
    Dcol = ctx.sbuf("p0_Dc", [128, 8], F32, es)
    bout = ctx.sbuf("p0_bo", [128, 8], F32, es)
    sg = [ctx.sbuf(f"p0_sg{i}", [128, TG], BF16, es) for i in range(2)]
    delta_bf = ctx.sbuf("p0_deltab", [1, TG], BF16, es)
    mh_bf = ctx.sbuf("p0_mh", [1, TG], BF16, es)
    dm_bf = ctx.sbuf("p0_dm", [1, TG], BF16, es)
    Bm = ctx.sbuf("p0_Bm", [16, TG], BF16, es)
    ab_t = ctx.sbuf("p0_ab", [16, TG], BF16, es)
    bu_t = ctx.sbuf("p0_bu", [16, TG], BF16, es)
    st_bf = ctx.sbuf("p0_st", [16, TG], BF16, es)
    carry = ctx.sbuf("p0_carry", [16, 1], BF16, es)
    ctx.dma("sync", t["ones"][:], d["ones_bf"].ap(), LD)
    ctx.dma("sync", bdelta[:], d["bdelta8"].ap(), LD)
    ctx.dma("sync", bB[:], d["bB"].ap(), LD)
    ctx.dma("sync", Acol[:], d["Alog"].ap(), LD)
    ctx.dma("sync", Dcol[:], d["Dcol8"].ap(), LD)
    ctx.dma("sync", bout[:], d["bout8"].ap(), LD)
    ctx.dma("sync", wstage[:, :128], d["WbT128"].ap(), LD)
    ctx.dma("sync", t["idf"][:], d["ident_f"].ap(), LD)
    ctx.dma("sync", t["idb"][:], d["ident_b"].ap(), LD)
    v = ctx.dma("sync", t["lnf"][:16, :D], d["WoT"].ap(), LD)
    ctx.wait("scalar", LD, v)
    ctx.op("scalar", lambda e: e.activation(out=Atmp[:], in_=Acol[:], func=AF.Exp), None)
    ctx.drain("scalar")
    ctx.op("scalar", lambda e: e.activation(
        out=Acol[:], in_=Atmp[:], func=AF.Copy, scale=-1.0), SC)
    ctx.drain("scalar")
    ctx.wait("vector", LD, v)
    ctx.op("vector", lambda e: e.tensor_copy(out=wb_bf[:], in_=wstage[:, :128]), None)
    ctx.op("vector", lambda e: e.tensor_copy(out=wo_bf[:], in_=t["lnf"][:16, :D]), V)
    for c in range(DC):
        ctx.wait("sync", V, ctx.cnt[V])
        v = ctx.dma("sync", wstage[:, :D], d["WdT"].ap()[c * 128:(c + 1) * 128, :], "WL")
        ctx.wait("vector", "WL", v)
        ctx.op("vector", lambda e, c=c: e.tensor_copy(
            out=wd_bf[:, c * D:(c + 1) * D], in_=wstage[:, :D]), V)

    last_ep_v = ctx.cnt[V]
    ctx.sem(ST)
    st_base = ctx.cnt[ST]
    for g in range(NG):
        lv = load_stage(ctx, t, d["x0T"].ap(), g, guard_v=last_ep_v)
        ctx.wait("vector", LD, lv)
        v_ln = emit_ln_group(ctx, t, g, want_lnbf=True, want_absmax=False)
        ctx.wait("tensor", V, v_ln)
        for ec in range(8):
            pm = t["ps"][4 + ec % 2]
            for k in range(DC):
                ctx.op("tensor", lambda e, ec=ec, k=k, pm=pm: e.matmul(
                    out=pm[:], lhsT=wd_bf[:, k * D + ec * 128: k * D + (ec + 1) * 128],
                    rhs=t["lnbf"][:, k * TG:(k + 1) * TG],
                    start=(k == 0), stop=(k == DC - 1)), MM if k == DC - 1 else None)
            mmv = ctx.cnt[MM]
            ctx.wait("scalar", MM, mmv)
            sv = ctx.op("scalar", lambda e, ec=ec, pm=pm: e.activation(
                out=sg[ec % 2][:], in_=pm[:], func=AF.Sigmoid,
                bias=bdelta[:, ec:ec + 1]), SC)
            ctx.wait("tensor", SC, sv)
            ctx.op("tensor", lambda e, ec=ec: e.matmul(
                out=t["ps"][0][:1, :], lhsT=t["ones"][:, :1], rhs=sg[ec % 2][:],
                start=(ec == 0), stop=(ec == 7)), MM if ec == 7 else None)
        v_dsum = ctx.cnt[MM]
        ctx.wait("vector", MM, v_dsum)
        v_d = ctx.op("vector", lambda e: e.tensor_scalar(
            out=delta_bf[:], in0=t["ps"][0][:1, :], scalar1=0.099 / D, scalar2=0.001,
            op0=ALU.mult, op1=ALU.add), V)
        ctx.wait("tensor", V, v_d)
        for k in range(DC):
            ctx.op("tensor", lambda e, k=k: e.matmul(
                out=t["ps"][1][:16, :], lhsT=wb_bf[:, k * 16:(k + 1) * 16],
                rhs=t["lnbf"][:, k * TG:(k + 1) * TG],
                start=(k == 0), stop=(k == DC - 1)), None)
        for k in range(DC):
            ctx.op("tensor", lambda e, k=k: e.matmul(
                out=t["ps"][0][:1, :], lhsT=t["ones"][:, :1],
                rhs=t["lnbf"][:, k * TG:(k + 1) * TG],
                start=(k == 0), stop=(k == DC - 1)), MM if k == DC - 1 else None)
        v_bmmh = ctx.cnt[MM]
        ctx.wait("scalar", MM, v_bmmh)
        s_bm = ctx.op("scalar", lambda e: e.activation(
            out=Bm[:], in_=t["ps"][1][:16, :], func=AF.Identity, bias=bB[:]), SC)
        ctx.wait("vector", MM, v_bmmh)
        ctx.op("vector", lambda e: e.tensor_scalar(
            out=mh_bf[:], in0=t["ps"][0][:1, :], scalar1=1.0 / D, scalar2=None,
            op0=ALU.mult), None)
        v_dm = ctx.op("vector", lambda e: e.tensor_tensor(
            out=dm_bf[:], in0=delta_bf[:], in1=mh_bf[:], op=ALU.mult), V)
        ctx.wait("tensor", V, v_dm)
        ctx.op("tensor", lambda e: e.matmul(
            out=t["ps"][2][:16, :], lhsT=t["ones"][:1, :16], rhs=delta_bf[:],
            start=True, stop=True), None)
        v_bc2 = ctx.op("tensor", lambda e: e.matmul(
            out=t["ps"][3][:16, :], lhsT=t["ones"][:1, :16], rhs=dm_bf[:],
            start=True, stop=True), MM)
        ctx.wait("scalar", MM, v_bc2)
        s_ab = ctx.op("scalar", lambda e: e.activation(
            out=ab_t[:], in_=t["ps"][2][:16, :], func=AF.Exp, scale=Acol[:]), SC)
        ctx.wait("vector", SC, s_ab)
        ctx.op("vector", lambda e: e.tensor_tensor(
            out=bu_t[:], in0=t["ps"][3][:16, :], in1=Bm[:], op=ALU.mult), None)
        if g % GPS == 0:
            ctx.op("vector", lambda e: e.tensor_tensor_scan(
                out=st_bf[:], data0=ab_t[:], data1=bu_t[:], initial=0.0,
                op0=ALU.mult, op1=ALU.add), None)
        else:
            ctx.op("vector", lambda e: e.tensor_tensor_scan(
                out=st_bf[:], data0=ab_t[:], data1=bu_t[:], initial=carry[:],
                op0=ALU.mult, op1=ALU.add), None)
        ctx.drain()
        v_scan = ctx.op("vector", lambda e: e.tensor_copy(
            out=carry[:], in_=st_bf[:, TG - 1:TG]), V)
        ctx.drain()
        ctx.wait("tensor", V, v_scan)
        prev_vv = None
        for c in range(DC):
            if prev_vv is not None:
                ctx.wait("tensor", V, prev_vv)
            mmv = ctx.op("tensor", lambda e, c=c: e.matmul(
                out=t["ps"][4][:], lhsT=wo_bf[:, c * 128:(c + 1) * 128],
                rhs=st_bf[:], start=True, stop=True), MM)
            j = g * DC + c
            if j >= 2:
                ctx.wait("vector", ST, st_base + 16 * (j - 1))
            ctx.wait("vector", MM, mmv)
            ctx.op("vector", lambda e, c=c: e.scalar_tensor_tensor(
                out=t["ep1"][:], in0=t["lnf"][:, c * TG:(c + 1) * TG],
                scalar=Dcol[:, c:c + 1], in1=t["stage"][:, c * TG:(c + 1) * TG],
                op0=ALU.mult, op1=ALU.add), None)
            vv = ctx.op("vector", lambda e, c=c, j=j: e.scalar_tensor_tensor(
                out=t["xout"][j % 2][:], in0=t["ps"][4][:], scalar=bout[:, c:c + 1],
                in1=t["ep1"][:], op0=ALU.add, op1=ALU.add), V)
            prev_vv = vv
            ctx.wait("gpsimd", V, vv)
            ctx.dma("gpsimd", d["x1T"].ap()[c * 128:(c + 1) * 128,
                                            g * TG:(g + 1) * TG],
                    t["xout"][j % 2][:], ST)
        last_ep_v = prev_vv
    ctx.drain_dma([ST, LD, "WL"])
    ctx.barrier()
    es.close()



def emit_phase1(ctx, t, d, load_consts=False, upto="G"):
    t = dict(t)
    V, SC, MM, LD, ST = "VE", "SC", "MM", "LD", "ST"
    cols = ctx.sbuf("p1_cols", [128, 8], F32)
    es = ExitStack()
    t["lnf"] = ctx.sbuf("p1_lnf", [128, DC * TG], F32, es)
    xint = ctx.sbuf("p1_xint", [128, DC * T], BF16, es)
    h2sl = [ctx.sbuf(f"p1_h2{i}", [128, T // 2], F32, es) for i in range(2)]
    wnat = [ctx.sbuf(f"p1_wnat{i}", [128, 2 * D], F32, es) for i in range(2)]
    wTg = [ctx.sbuf(f"p1_wTg{i}", [128, DC * 128], F32, es) for i in range(2)]
    wTu = [ctx.sbuf(f"p1_wTu{i}", [128, DC * 128], F32, es) for i in range(2)]
    wqg = [ctx.sbuf(f"p1_wqg{i}", [128, DC * 128], BF16, es) for i in range(2)]
    wqu = [ctx.sbuf(f"p1_wqu{i}", [128, DC * 128], BF16, es) for i in range(2)]
    itmp = ctx.sbuf("p1_itmp", [128, 1024], I32, es)
    rinvg = ctx.sbuf("p1_rig", [128, 128], F32, es)
    rinvu = ctx.sbuf("p1_riu", [128, 128], F32, es)
    rcol2 = ctx.sbuf("p1_rc2", [128, 2], BF16, es)
    rrow2 = ctx.sbuf("p1_rr2", [1, 256], BF16, es)
    am2 = ctx.sbuf("p1_am2", [128, 1], F32, es)
    clipx = ctx.sbuf("p1_cx", [1, 2], F32, es)
    silu_t = [ctx.sbuf(f"p1_si{i}", [128, TG], F32, es) for i in range(2)]
    if load_consts:
        ctx.dma("sync", t["ones"][:], d["ones_bf"].ap(), LD)
        ctx.dma("sync", t["idf"][:], d["ident_f"].ap(), LD)
        ctx.dma("sync", t["idb"][:], d["ident_b"].ap(), LD)
    for nm in (ST, "H2S", "CC", "CD", "H2LA", "H2LB", "X2S", "WL", LD):
        ctx.sem(nm)
    if upto == "NOP":
        ctx.drain_dma([ST, LD, "WL"])
        ctx.barrier()
        es.close()
        return
    if upto == "LOAD1":
        lv = load_stage(ctx, t, d["x1T"].ap(), 0, guard_v=0)
        ctx.wait("vector", LD, lv)
        vv = ctx.op("vector", lambda e: e.tensor_copy(out=t["xb"][:, 0:TG], in_=t["stage"][:, 0:TG]), "VE")
        ctx.wait("gpsimd", "VE", vv)
        ctx.dma("gpsimd", d["x2T"].ap()[0:128, 0:TG], t["xb"][:, 0:TG], "X2S")
        ctx.drain_dma([ST, LD, "WL", "X2S"])
        ctx.barrier()
        es.close()
        return
    last_v = 0
    for g in range(NG):
        lv = load_stage(ctx, t, d["x1T"].ap(), g, guard_v=last_v)
        ctx.wait("vector", LD, lv)
        ctx.wait("vector", ST, ctx.cnt[ST])
        last_v = emit_ln_group(ctx, t, g, want_lnbf=False, want_absmax=True)
        ctx.wait("gpsimd", V, last_v)
        for c in range(DC):
            ctx.dma("gpsimd", d["lnfs"].ap()[c * 128:(c + 1) * 128, g * TG:(g + 1) * TG],
                    t["lnf"][:, c * TG:(c + 1) * TG], ST)
    if upto == "A":
        vv = ctx.op("vector", lambda e: e.tensor_copy(out=t["xout"][0][:], in_=t["stage"][:, 0:TG]), "VE")
        ctx.wait("gpsimd", "VE", vv)
        ctx.dma("gpsimd", d["x2T"].ap()[0:128, 0:TG], t["xout"][0][:], "X2S")
        ctx.drain_dma(["X2S", "ST", "LD", "WL", "H2S", "CC"])
        ctx.barrier()
        es.close()
        return

    ctx.dma("sync", clipx[:, 0:2], d["clip2"].ap(), LD)
    lv_c = ctx.cnt[LD]
    ctx.wait("tensor", V, last_v)
    mt = ctx.op("tensor", lambda e: e.transpose(
        out=t["ps"][4][:1, :128], in_=t["amax"][:], identity=t["idf"][:]), MM)
    ctx.wait("vector", MM, mt)
    vv = ctx.op("vector", lambda e: e.tensor_reduce(
        out=t["scl"][:, 0:1], in_=t["ps"][4][:1, :128], axis=AX, op=ALU.max), V)
    ctx.wait("gpsimd", V, vv)
    ctx.dma("gpsimd", d["cci"].ap(), t["scl"][:1, 0:1], "CC")
    ctx.wait("gpsimd", "CC", ctx.cnt["CC"])
    cv = ctx.op("gpsimd", lambda e: e.collective_compute(
        "AllReduce", ALU.max, replica_groups=[list(range(NCORE))],
        ins=[d["cci"].ap()], outs=[d["cco"].ap()]), "CD")
    ctx.wait("gpsimd", "CD", cv)
    ctx.dma("gpsimd", t["scl"][:1, 2:3], d["cco"].ap(), "CC")
    ctx.wait("scalar", "CC", ctx.cnt["CC"])
    ctx.wait("scalar", LD, lv_c)
    s_sig = ctx.op("scalar", lambda e: e.activation(
        out=t["scl"][:, 4:5], in_=clipx[:, 0:1], func=AF.Sigmoid), SC)
    ctx.wait("vector", SC, s_sig)
    ctx.wait("vector", "CC", ctx.cnt["CC"])
    ctx.op("vector", lambda e: e.tensor_tensor(
        out=t["scl"][:, 5:6], in0=t["scl"][:, 2:3], in1=t["scl"][:, 4:5], op=ALU.mult), None)
    ctx.drain()
    ctx.op("vector", lambda e: e.reciprocal(out=t["scl"][:, 6:7], in_=t["scl"][:, 5:6]), None)
    ctx.drain()
    ctx.op("vector", lambda e: e.tensor_scalar(
        out=t["scl"][:, 7:8], in0=t["scl"][:, 6:7], scalar1=QMAX, scalar2=None,
        op0=ALU.mult), None)
    ctx.drain()
    ctx.op("vector", lambda e: e.tensor_scalar(
        out=t["scl"][:, 8:9], in0=t["scl"][:, 5:6], scalar1=1.0 / QMAX, scalar2=None,
        op0=ALU.mult), V)
    ctx.drain()
    bcast_scalar_col(ctx, t, t["scl"][:1, 7:8], cols[:, 4:5])
    v_c = bcast_scalar_col(ctx, t, t["scl"][:1, 8:9], cols[:, 5:6])
    if upto == "Cdump":
        ctx.op("vector", lambda e: e.tensor_copy(out=t["xout"][0][:1, 0:32], in_=t["scl"][:]), None)
        ctx.op("vector", lambda e: e.tensor_copy(out=t["xout"][0][:, 32:40], in_=cols[:]), None)
        vv = ctx.op("vector", lambda e: e.tensor_copy(out=t["xout"][0][:, 40:41], in_=t["amax"][:]), "VE")
        ctx.wait("gpsimd", "VE", vv)
        ctx.dma("gpsimd", d["dbg"].ap()[0:128, 0:TG], t["xout"][0][:], "X2S")
        ctx.drain_dma(["X2S", "ST", "LD", "WL", "H2S", "CC"])
        ctx.barrier()
        es.close()
        return
    if upto == "C":
        vv = ctx.op("vector", lambda e: e.tensor_copy(out=t["xout"][0][:], in_=t["stage"][:, 0:TG]), "VE")
        ctx.wait("gpsimd", "VE", vv)
        ctx.dma("gpsimd", d["x2T"].ap()[0:128, 0:TG], t["xout"][0][:], "X2S")
        ctx.drain_dma(["X2S", "ST", "LD", "WL", "H2S", "CC"])
        ctx.barrier()
        es.close()
        return

    ctx.wait("sync", ST, ctx.cnt[ST])
    vq_last = ctx.cnt[V]
    for g in range(NG):
        ctx.wait("sync", V, vq_last)
        lv = 0
        for c in range(DC):
            lv = ctx.dma("sync", t["lnf"][:, c * TG:(c + 1) * TG],
                         d["lnfs"].ap()[c * 128:(c + 1) * 128, g * TG:(g + 1) * TG], LD)
        ctx.wait("vector", LD, lv)
        for c in range(DC):
            ctx.op("vector", lambda e, c=c: e.tensor_scalar(
                out=t["lnf"][:, c * TG:(c + 1) * TG], in0=t["lnf"][:, c * TG:(c + 1) * TG],
                scalar1=cols[:, 4:5], scalar2=MAGIC, op0=ALU.mult, op1=ALU.add), None)
            ctx.op("vector", lambda e, c=c: e.tensor_scalar(
                out=t["lnf"][:, c * TG:(c + 1) * TG], in0=t["lnf"][:, c * TG:(c + 1) * TG],
                scalar1=MAGIC, scalar2=-128.0, op0=ALU.subtract, op1=ALU.max), None)
            vq_last = ctx.op("vector", lambda e, c=c, g=g: e.tensor_scalar(
                out=xint[:, c * T + g * TG: c * T + (g + 1) * TG],
                in0=t["lnf"][:, c * TG:(c + 1) * TG],
                scalar1=127.0, scalar2=None, op0=ALU.min), V)

    if upto == "Ddump":
        for c in range(DC):
            vv = ctx.op("vector", lambda e, c=c: e.tensor_copy(out=t["xout"][0][:], in_=xint[:, c * T: c * T + TG]), "VE")
            ctx.wait("gpsimd", "VE", vv)
            ctx.dma("gpsimd", d["dbg"].ap()[c * 128:(c + 1) * 128, 0:TG], t["xout"][0][:], "X2S")
            ctx.wait("vector", "X2S", ctx.cnt["X2S"])
        ctx.drain_dma(["X2S", "ST", "LD", "WL", "H2S", "CC"])
        ctx.barrier()
        es.close()
        return
    if upto == "D":
        vv = ctx.op("vector", lambda e: e.tensor_copy(out=t["xout"][0][:], in_=t["stage"][:, 0:TG]), "VE")
        ctx.wait("gpsimd", "VE", vv)
        ctx.dma("gpsimd", d["x2T"].ap()[0:128, 0:TG], t["xout"][0][:], "X2S")
        ctx.drain_dma(["X2S", "ST", "LD", "WL", "H2S", "CC"])
        ctx.barrier()
        es.close()
        return
    h2_store_v = [0, 0]
    quant_v = [0, 0]
    mm_done = [0, 0]
    wl_v = [0, 0]
    h2v = [0, 0]
    def p1_load(i):
        ib = i % 2
        ctx.wait("sync", V, quant_v[ib])
        ctx.dma("sync", wnat[ib][:, 0:D], d["wg_nat"].ap()[i * 128:(i + 1) * 128, :], "WL")
        ctx.dma("sync", wnat[ib][:, D:2 * D], d["wu_nat"].ap()[i * 128:(i + 1) * 128, :], "WL")
        for c in range(DC):
            ctx.dma("sync", wTg[ib][:, c * 128:(c + 1) * 128],
                    d["wgT"].ap()[c * 128:(c + 1) * 128, i * 128:(i + 1) * 128], "WL")
        for c in range(DC):
            ctx.dma("sync", wTu[ib][:, c * 128:(c + 1) * 128],
                    d["wuT"].ap()[c * 128:(c + 1) * 128, i * 128:(i + 1) * 128], "WL")
        wl_v[ib] = ctx.cnt["WL"]
    p1_load(0)
    for i in range(IC0):
        ib = i % 2
        if i + 1 < IC0:
            p1_load(i + 1)
        lv = wl_v[ib]
        ctx.wait("vector", "WL", lv)
        ctx.wait("vector", MM, mm_done[ib])
        for w, natsl, wT, wq, rinvb, smax_i, esc_i in (
                (0, wnat[ib][:, 0:D], wTg[ib], wqg[ib], rinvg, 0, 2 if ib == 0 else 1),
                (1, wnat[ib][:, D:2 * D], wTu[ib], wqu[ib], rinvu, 3, 6 if ib == 0 else 7)):
            ctx.op("vector", lambda e, natsl=natsl, smax_i=smax_i: e.tensor_reduce(
                out=cols[:, smax_i:smax_i + 1], in_=natsl, axis=AX, op=ALU.max,
                apply_absolute_value=True), None)
            ctx.drain()
            ctx.op("vector", lambda e, smax_i=smax_i: e.reciprocal(
                out=t["col1"][:], in_=cols[:, smax_i:smax_i + 1]), None)
            ctx.drain()
            ctx.op("vector", lambda e: e.tensor_scalar(
                out=t["col2"][:], in0=t["col1"][:], scalar1=QMAX, scalar2=None,
                op0=ALU.mult), None)
            ctx.drain()
            ctx.op("vector", lambda e: e.tensor_copy(out=rcol2[:, 0:1], in_=t["col2"][:]), None)
            ctx.drain()
            v1 = ctx.op("vector", lambda e: e.tensor_tensor(
                out=rcol2[:, 1:2], in0=t["col2"][:], in1=rcol2[:, 0:1],
                op=ALU.subtract), V)
            ctx.drain()
            ctx.wait("tensor", V, v1)
            ctx.op("tensor", lambda e: e.transpose(
                out=t["psb"][0][:1, 0:128], in_=rcol2[:, 0:1], identity=t["idb"][:]), None)
            mt = ctx.op("tensor", lambda e: e.transpose(
                out=t["psb"][0][:1, 128:256], in_=rcol2[:, 1:2], identity=t["idb"][:]), MM)
            ctx.wait("vector", MM, mt)
            v2 = ctx.op("vector", lambda e: e.tensor_copy(
                out=rrow2[:], in_=t["psb"][0][:1, 0:256]), V)
            ctx.wait("tensor", V, v2)
            ctx.op("tensor", lambda e: e.matmul(
                out=t["ps"][2][:, :128], lhsT=t["ones"][:1, :], rhs=rrow2[:, 0:128],
                start=True, stop=True), None)
            v3 = ctx.op("tensor", lambda e: e.matmul(
                out=t["ps"][3][:, :128], lhsT=t["ones"][:1, :], rhs=rrow2[:, 128:256],
                start=True, stop=True), MM)
            ctx.wait("scalar", MM, v3)
            s1 = ctx.op("scalar", lambda e, rinvb=rinvb: e.activation(
                out=rinvb[:], in_=t["ps"][2][:, :128], func=AF.Copy), SC)
            ctx.wait("vector", SC, s1)
            ctx.op("vector", lambda e, rinvb=rinvb: e.tensor_tensor(
                out=rinvb[:], in0=rinvb[:], in1=t["ps"][3][:, :128], op=ALU.add), None)
            for c in range(DC):
                ctx.op("vector", lambda e, c=c, wT=wT, rinvb=rinvb: e.tensor_tensor(
                    out=itmp[:, c * 128:(c + 1) * 128], in0=wT[:, c * 128:(c + 1) * 128],
                    in1=rinvb[:], op=ALU.mult), None)
            for c in range(DC):
                ctx.op("vector", lambda e, c=c, wq=wq: e.tensor_copy(
                    out=wq[:, c * 128:(c + 1) * 128], in_=itmp[:, c * 128:(c + 1) * 128]),
                    None)
            ctx.op("vector", lambda e, smax_i=smax_i: e.tensor_tensor(
                out=t["col1"][:], in0=cols[:, smax_i:smax_i + 1], in1=cols[:, 5:6],
                op=ALU.mult), None)
            ctx.drain()
            quant_v[ib] = ctx.op("vector", lambda e, esc_i=esc_i: e.tensor_scalar(
                out=cols[:, esc_i:esc_i + 1], in0=t["col1"][:], scalar1=1.0 / QMAX,
                scalar2=None, op0=ALU.mult), V)
            ctx.drain()
        ctx.wait("tensor", V, quant_v[ib])
        if i == 0:
            ctx.wait("tensor", V, vq_last)
        for tg in range(NG):
            pA = t["ps"][4] if tg % 2 == 0 else t["ps"][2]
            pB = t["ps"][5] if tg % 2 == 0 else t["ps"][3]
            if tg >= 2:
                ctx.wait("tensor", V, h2v[tg % 2])
            for k in range(DC):
                ctx.op("tensor", lambda e, k=k, tg=tg, pA=pA, ib=ib: e.matmul(
                    out=pA[:], lhsT=wqg[ib][:, k * 128:(k + 1) * 128],
                    rhs=xint[:, k * T + tg * TG: k * T + (tg + 1) * TG],
                    start=(k == 0), stop=(k == DC - 1)), None)
            for k in range(DC):
                ctx.op("tensor", lambda e, k=k, tg=tg, pB=pB, ib=ib: e.matmul(
                    out=pB[:], lhsT=wqu[ib][:, k * 128:(k + 1) * 128],
                    rhs=xint[:, k * T + tg * TG: k * T + (tg + 1) * TG],
                    start=(k == 0), stop=(k == DC - 1)), MM if k == DC - 1 else None)
            mmv = ctx.cnt[MM]
            ctx.wait("scalar", MM, mmv)
            sv = ctx.op("scalar", lambda e, tg=tg, pA=pA, gi=(2 if ib == 0 else 1): e.activation(
                out=silu_t[tg % 2][:], in_=pA[:], func=AF.Silu,
                scale=cols[:, gi:gi + 1]), SC)
            ctx.wait("vector", SC, sv)
            half = tg // 4
            if tg % 4 == 0:
                ctx.wait("vector", "H2S", h2_store_v[half])
            h2v[tg % 2] = ctx.op("vector", lambda e, tg=tg, pB=pB, half=half, ui=(6 if ib == 0 else 7): e.scalar_tensor_tensor(
                out=h2sl[half][:, (tg % 4) * TG:(tg % 4 + 1) * TG], in0=pB[:],
                scalar=cols[:, ui:ui + 1], in1=silu_t[tg % 2][:],
                op0=ALU.mult, op1=ALU.mult), V)
            if tg % 4 == 3:
                ctx.op("vector", lambda e, half=half: e.tensor_reduce(
                    out=t["col1"][:], in_=h2sl[half][:], axis=AX, op=ALU.max,
                    apply_absolute_value=True), None)
                ctx.drain()
                if i == 0 and half == 0:
                    hv = ctx.op("vector", lambda e: e.tensor_copy(out=am2[:], in_=t["col1"][:]), V)
                else:
                    hv = ctx.op("vector", lambda e: e.tensor_tensor(
                        out=am2[:], in0=am2[:], in1=t["col1"][:], op=ALU.max), V)
                ctx.drain()
                ctx.wait("gpsimd", V, hv)
                h2_store_v[half] = ctx.dma(
                    "gpsimd", d["h2s"].ap()[i * 128:(i + 1) * 128,
                                            half * (T // 2):(half + 1) * (T // 2)],
                    h2sl[half][:], "H2S")
        mm_done[ib] = ctx.cnt[MM]
    if upto == "E":
        vv = ctx.op("vector", lambda e: e.tensor_copy(out=t["xout"][0][:], in_=t["stage"][:, 0:TG]), "VE")
        ctx.wait("gpsimd", "VE", vv)
        ctx.dma("gpsimd", d["x2T"].ap()[0:128, 0:TG], t["xout"][0][:], "X2S")
        ctx.drain_dma(["X2S", "ST", "LD", "WL", "H2S", "CC"])
        ctx.barrier()
        es.close()
        return

    if upto == "Edump":
        ctx.wait("gpsimd", "H2S", ctx.cnt["H2S"])
        for i in range(DC):
            ctx.dma("gpsimd", d["dbg"].ap()[i * 128:(i + 1) * 128, :],
                    d["h2s"].ap()[i * 128:(i + 1) * 128, :], "X2S")
        ctx.drain_dma(["X2S", "ST", "LD", "WL", "H2S", "CC"])
        ctx.barrier()
        es.close()
        return
    ctx.wait("tensor", V, ctx.cnt[V])
    mt = ctx.op("tensor", lambda e: e.transpose(
        out=t["ps"][4][:1, :128], in_=am2[:], identity=t["idf"][:]), MM)
    ctx.wait("vector", MM, mt)
    vv = ctx.op("vector", lambda e: e.tensor_reduce(
        out=t["scl"][:, 10:11], in_=t["ps"][4][:1, :128], axis=AX, op=ALU.max), V)
    ctx.wait("gpsimd", V, vv)
    ctx.dma("gpsimd", d["cci"].ap(), t["scl"][:1, 10:11], "CC")
    ctx.wait("gpsimd", "CC", ctx.cnt["CC"])
    cv = ctx.op("gpsimd", lambda e: e.collective_compute(
        "AllReduce", ALU.max, replica_groups=[list(range(NCORE))],
        ins=[d["cci"].ap()], outs=[d["cco"].ap()]), "CD")
    ctx.wait("gpsimd", "CD", cv)
    ctx.dma("gpsimd", t["scl"][:1, 12:13], d["cco"].ap(), "CC")
    ctx.wait("scalar", "CC", ctx.cnt["CC"])
    s_sig = ctx.op("scalar", lambda e: e.activation(
        out=t["scl"][:, 14:15], in_=clipx[:, 1:2], func=AF.Sigmoid), SC)
    ctx.wait("vector", SC, s_sig)
    ctx.wait("vector", "CC", ctx.cnt["CC"])
    ctx.op("vector", lambda e: e.tensor_tensor(
        out=t["scl"][:, 15:16], in0=t["scl"][:, 12:13], in1=t["scl"][:, 14:15], op=ALU.mult), None)
    ctx.drain()
    ctx.op("vector", lambda e: e.reciprocal(out=t["scl"][:, 16:17], in_=t["scl"][:, 15:16]), None)
    ctx.drain()
    ctx.op("vector", lambda e: e.tensor_scalar(
        out=t["scl"][:, 17:18], in0=t["scl"][:, 16:17], scalar1=QMAX, scalar2=None,
        op0=ALU.mult), None)
    ctx.drain()
    ctx.op("vector", lambda e: e.tensor_scalar(
        out=t["scl"][:, 18:19], in0=t["scl"][:, 15:16], scalar1=1.0 / QMAX, scalar2=None,
        op0=ALU.mult), V)
    ctx.drain()
    bcast_scalar_col(ctx, t, t["scl"][:1, 17:18], cols[:, 4:5])
    v_sx2 = bcast_scalar_col(ctx, t, t["scl"][:1, 18:19], cols[:, 5:6])
    if upto == "F":
        vv = ctx.op("vector", lambda e: e.tensor_copy(out=t["xout"][0][:], in_=t["stage"][:, 0:TG]), "VE")
        ctx.wait("gpsimd", "VE", vv)
        ctx.dma("gpsimd", d["x2T"].ap()[0:128, 0:TG], t["xout"][0][:], "X2S")
        ctx.drain_dma(["X2S", "ST", "LD", "WL", "H2S", "CC"])
        ctx.barrier()
        es.close()
        return

    es.close()

    es2 = ExitStack()
    wdqT = ctx.sbuf("p1_wdqT", [128, IC0 * 1024], BF16, es2)
    h2ld = [ctx.sbuf(f"p1_h2ld{i}", [128, TG], F32, es2) for i in range(2)]
    h2q = ctx.sbuf("p1_h2q", [128, IC0 * TG], BF16, es2)
    wTsl = [ctx.sbuf(f"p1_wTsl{i}", [128, 1024], F32, es2) for i in range(2)]
    itmp2 = ctx.sbuf("p1_itmp2", [128, 1024], I32, es2)
    rinvd = ctx.sbuf("p1_rinvd", [128, 1024], F32, es2)
    rrowd = ctx.sbuf("p1_rrowd", [1, 2048], BF16, es2)
    rcol2b = ctx.sbuf("p1_rc2b", [128, 2], BF16, es2)
    dcols = ctx.sbuf("p1_dcols", [128, 16], F32, es2)
    wdnat = ctx.sbuf("p1_wdnat", [128, 2816], F32, es2)
    if upto == "F1":
        vv = ctx.op("vector", lambda e: e.tensor_copy(out=t["xout"][0][:], in_=t["stage"][:, 0:TG]), "VE")
        ctx.wait("gpsimd", "VE", vv)
        ctx.dma("gpsimd", d["x2T"].ap()[0:128, 0:TG], t["xout"][0][:], "X2S")
        ctx.drain_dma(["X2S", "ST", "LD", "WL", "H2S", "CC"])
        ctx.barrier()
        es2.close()
        return

    prev_v = v_sx2
    for h in range(DC):
        ctx.wait("sync", V, prev_v)
        lv = ctx.dma("sync", wdnat[:], d["wd_nat"].ap()[h * 128:(h + 1) * 128, :], "WL")
        ctx.wait("vector", "WL", lv)
        ctx.op("vector", lambda e, h=h: e.tensor_reduce(
            out=dcols[:, h:h + 1], in_=wdnat[:], axis=AX, op=ALU.max,
            apply_absolute_value=True), None)
        ctx.drain()
        ctx.op("vector", lambda e, h=h: e.reciprocal(out=t["col1"][:], in_=dcols[:, h:h + 1]), None)
        ctx.drain()
        ctx.op("vector", lambda e: e.tensor_scalar(
            out=t["col2"][:], in0=t["col1"][:], scalar1=QMAX, scalar2=None, op0=ALU.mult), None)
        ctx.drain()
        ctx.op("vector", lambda e: e.tensor_copy(out=rcol2b[:, 0:1], in_=t["col2"][:]), None)
        ctx.drain()
        v1 = ctx.op("vector", lambda e: e.tensor_tensor(
            out=rcol2b[:, 1:2], in0=t["col2"][:], in1=rcol2b[:, 0:1], op=ALU.subtract), V)
        ctx.drain()
        ctx.wait("tensor", V, v1)
        ctx.op("tensor", lambda e: e.transpose(
            out=t["psb"][0][:1, 0:128], in_=rcol2b[:, 0:1], identity=t["idb"][:]), None)
        mt = ctx.op("tensor", lambda e: e.transpose(
            out=t["psb"][0][:1, 128:256], in_=rcol2b[:, 1:2], identity=t["idb"][:]), MM)
        ctx.wait("vector", MM, mt)
        ctx.op("vector", lambda e, h=h: e.tensor_copy(
            out=rrowd[:, h * 128:(h + 1) * 128], in_=t["psb"][0][:1, 0:128]), None)
        ctx.op("vector", lambda e, h=h: e.tensor_copy(
            out=rrowd[:, 1024 + h * 128: 1024 + (h + 1) * 128],
            in_=t["psb"][0][:1, 128:256]), None)
        ctx.op("vector", lambda e, h=h: e.tensor_tensor(
            out=t["col1"][:], in0=dcols[:, h:h + 1], in1=cols[:, 5:6], op=ALU.mult), None)
        ctx.drain()
        prev_v = ctx.op("vector", lambda e, h=h: e.tensor_scalar(
            out=dcols[:, 8 + h:9 + h], in0=t["col1"][:], scalar1=1.0 / QMAX,
            scalar2=None, op0=ALU.mult), V)
        ctx.drain()
    if upto == "F1b":
        vv = ctx.op("vector", lambda e: e.tensor_copy(out=t["xout"][0][:], in_=t["stage"][:, 0:TG]), "VE")
        ctx.wait("gpsimd", "VE", vv)
        ctx.dma("gpsimd", d["x2T"].ap()[0:128, 0:TG], t["xout"][0][:], "X2S")
        ctx.drain_dma(["X2S", "ST", "LD", "WL", "H2S", "CC"])
        ctx.barrier()
        es2.close()
        return

    ctx.wait("tensor", V, prev_v)
    if upto in ("F2h", "F2i", "F2j", "F2m", "F2n", "F2o"):
        rhs_src = t["hrow"][:, 0:128] if upto == "F2h" else (
            rrowd[:, 0:128] if upto == "F2i" else rrowd[:, 0:512])
        outp = t["ps"][2][:, :128] if upto in ("F2h", "F2i") else t["ps"][2][:]
        ctx.op("tensor", lambda e: e.matmul(
            out=outp, lhsT=t["ones"][:1, :], rhs=rhs_src,
            start=True, stop=True), "MM")
        if upto in ("F2m", "F2n", "F2o"):
            rhs2 = rrowd[:, 0:512] if upto == "F2n" else rrowd[:, 1024:1536]
            ctx.op("tensor", lambda e, rhs2=rhs2: e.matmul(
                out=t["ps"][3][:], lhsT=t["ones"][:1, :], rhs=rhs2,
                start=True, stop=True), "MM")
        mmv = ctx.cnt["MM"]
        ctx.wait("vector", "MM", mmv)
        dsto = rinvd[:, 0:512] if upto == "F2o" else t["rbf"][:, :128]
        srco = t["ps"][2][:] if upto == "F2o" else t["ps"][2][:, :128]
        ctx.op("vector", lambda e, dsto=dsto, srco=srco: e.tensor_copy(out=dsto, in_=srco), "VE")
    if upto == "F2pad":
        for _ in range(8):
            ctx.op("vector", lambda e: e.tensor_copy(out=t["col2"][:], in_=t["col1"][:]), None)
    for half in range(2):
        if upto in ("F2a", "F2pad", "F2h", "F2i", "F2j", "F2m", "F2n", "F2o"):
            continue
        ctx.op("tensor", lambda e, half=half: e.matmul(
            out=t["ps"][2][:], lhsT=t["ones"][:1, :], rhs=rrowd[:, half * TG:(half + 1) * TG],
            start=True, stop=True), None)
        v3 = ctx.op("tensor", lambda e, half=half: e.matmul(
            out=t["ps"][3][:], lhsT=t["ones"][:1, :],
            rhs=rrowd[:, 1024 + half * TG: 1024 + (half + 1) * TG],
            start=True, stop=True), MM)
        if upto == "F2b":
            continue
        ctx.wait("scalar", MM, v3)
        if upto == "F2e":
            continue
        if upto == "F2f":
            s1 = ctx.op("vector", lambda e, half=half: e.tensor_copy(
                out=rinvd[:, half * TG:(half + 1) * TG], in_=t["ps"][2][:]), "VE")
            continue
        if upto == "F2d":
            s1 = ctx.op("scalar", lambda e: e.activation(
                out=t["rbf"][:], in_=t["ps"][2][:], func=AF.Copy), SC)
            continue
        s1 = ctx.op("scalar", lambda e, half=half: e.activation(
            out=rinvd[:, half * TG:(half + 1) * TG], in_=t["ps"][2][:], func=AF.Copy), SC)
        if upto == "F2c":
            continue
        ctx.wait("vector", SC, s1)
        v_add = ctx.op("vector", lambda e, half=half: e.tensor_tensor(
            out=rinvd[:, half * TG:(half + 1) * TG], in0=rinvd[:, half * TG:(half + 1) * TG],
            in1=t["ps"][3][:], op=ALU.add), V)
        ctx.wait("tensor", V, v_add)
    if upto.startswith("F2"):
        vv = ctx.op("vector", lambda e: e.tensor_copy(out=t["xout"][0][:], in_=t["stage"][:, 0:TG]), "VE")
        ctx.wait("gpsimd", "VE", vv)
        ctx.dma("gpsimd", d["x2T"].ap()[0:128, 0:TG], t["xout"][0][:], "X2S")
        ctx.drain_dma(["X2S", "ST", "LD", "WL", "H2S", "CC"])
        ctx.barrier()
        es2.close()
        return

    prev_v = ctx.op("vector", lambda e: e.tensor_copy(out=t["col1"][:], in_=t["col1"][:]), V)
    cast_done = [prev_v, prev_v]
    wlv = [0, 0]
    def p1d_load(i):
        ib = i % 2
        ctx.wait("sync", V, cast_done[ib])
        wlv[ib] = ctx.dma("sync", wTsl[ib][:], d["wdT"].ap()[i * 128:(i + 1) * 128, :], "WL")
    p1d_load(0)
    for i in range(IC0):
        ib = i % 2
        if i + 1 < IC0:
            p1d_load(i + 1)
        ctx.wait("vector", "WL", wlv[ib])
        ctx.op("vector", lambda e, ib=ib: e.tensor_tensor(
            out=itmp2[:], in0=wTsl[ib][:], in1=rinvd[:], op=ALU.mult), None)
        prev_v = ctx.op("vector", lambda e, i=i: e.tensor_copy(
            out=wdqT[:, i * 1024:(i + 1) * 1024], in_=itmp2[:]), V)
        cast_done[ib] = prev_v
    if upto == "F3":
        vv = ctx.op("vector", lambda e: e.tensor_copy(out=t["xout"][0][:], in_=t["stage"][:, 0:TG]), "VE")
        ctx.wait("gpsimd", "VE", vv)
        ctx.dma("gpsimd", d["x2T"].ap()[0:128, 0:TG], t["xout"][0][:], "X2S")
        ctx.drain_dma(["X2S", "ST", "LD", "WL", "H2S", "CC"])
        ctx.barrier()
        es2.close()
        return

    ctx.wait("sync", "H2S", ctx.cnt["H2S"])
    v_q = prev_v
    dmm = 0
    last_vv = prev_v
    jj = 0
    hq_read_v = [0, 0]
    for tg in range(NG):
        if tg > 0:
            ctx.wait("vector", MM, dmm)
        for i in range(IC0):
            hsem = "H2LA" if i % 2 == 0 else "H2LB"
            ctx.wait("sync", V, hq_read_v[i % 2])
            lv = ctx.dma("sync", h2ld[i % 2][:],
                         d["h2s"].ap()[i * 128:(i + 1) * 128, tg * TG:(tg + 1) * TG], hsem)
            ctx.wait("vector", hsem, lv)
            ctx.op("vector", lambda e, i=i: e.tensor_scalar(
                out=h2ld[i % 2][:], in0=h2ld[i % 2][:],
                scalar1=cols[:, 4:5], scalar2=MAGIC, op0=ALU.mult, op1=ALU.add), None)
            ctx.op("vector", lambda e, i=i: e.tensor_scalar(
                out=h2ld[i % 2][:], in0=h2ld[i % 2][:],
                scalar1=MAGIC, scalar2=-128.0, op0=ALU.subtract, op1=ALU.max), None)
            hq_read_v[i % 2] = ctx.op("vector", lambda e, i=i: e.tensor_scalar(
                out=h2q[:, i * TG:(i + 1) * TG], in0=h2ld[i % 2][:],
                scalar1=127.0, scalar2=None, op0=ALU.min), V)
        v_q = hq_read_v[(IC0 - 1) % 2]
        ctx.wait("tensor", V, v_q)
        for hh in range(2):
            if hh == 1:
                ctx.wait("tensor", V, last_vv)
            for i in range(IC0):
                for h4 in range(4):
                    h = hh * 4 + h4
                    ctx.op("tensor", lambda e, i=i, h=h, h4=h4: e.matmul(
                        out=t["ps"][h4][:], lhsT=wdqT[:, i * 1024 + h * 128:
                                                      i * 1024 + (h + 1) * 128],
                        rhs=h2q[:, i * TG:(i + 1) * TG],
                        start=(i == 0), stop=(i == IC0 - 1)),
                        MM if (i == IC0 - 1 and h4 == 3) else None)
            dmm = ctx.cnt[MM]
            ctx.wait("sync", V, last_vv)
            lv = 0
            for h4 in range(4):
                h = hh * 4 + h4
                lv = ctx.dma("sync", t["stage"][:, h * TG:(h + 1) * TG],
                             d["x1T"].ap()[h * 128:(h + 1) * 128, tg * TG:(tg + 1) * TG], LD)
            ctx.wait("vector", LD, lv)
            ctx.wait("vector", MM, dmm)
            for h4 in range(4):
                h = hh * 4 + h4
                if jj >= 2:
                    ctx.wait("vector", "X2S", 16 * (jj - 1))
                vv = ctx.op("vector", lambda e, h=h, h4=h4, jj=jj: e.scalar_tensor_tensor(
                    out=t["xout"][jj % 2][:], in0=t["ps"][h4][:],
                    scalar=dcols[:, 8 + h:9 + h], in1=t["stage"][:, h * TG:(h + 1) * TG],
                    op0=ALU.mult, op1=ALU.add), V)
                last_vv = vv
                ctx.wait("gpsimd", V, vv)
                ctx.dma("gpsimd", d["x2T"].ap()[h * 128:(h + 1) * 128, tg * TG:(tg + 1) * TG],
                        t["xout"][jj % 2][:], "X2S")
                jj += 1
        if upto == "G1":
            ctx.drain_dma(["X2S", "ST", "LD", "WL", "H2S", "CC", "H2LA", "H2LB"])
            ctx.barrier()
            es2.close()
            return

    ctx.drain_dma(["X2S", LD, "WL", "H2LA", "H2LB", ST, "H2S", "CC"])
    ctx.barrier()
    es2.close()



def emit_phase2(ctx, t, d, load_consts=False):
    t = dict(t)
    V, SC, MM, LD, ST = "VE", "SC", "MM", "LD", "ST"
    for nm in ("WL", LD, "QS", "QL", "X3S", "TB", "S3", "S3L"):
        ctx.sem(nm)
    es = ExitStack()
    wprojb = ctx.sbuf("p2_wp", [128, DC * 1024], BF16, es)
    wsl = [ctx.sbuf(f"p2_wsl{i}", [128, 1024], F32, es) for i in range(2)]
    wqb = [ctx.sbuf(f"p2_wqb{i}", [128, 1024], BF16, es) for i in range(2)]
    wv0 = ctx.sbuf("p2_wv0", [128, DC * 512], BF16, es)
    bqkv = ctx.sbuf("p2_bqkv", [128, 24], F32, es)
    bvb = ctx.sbuf("p2_bvb", [128, 1024], F32, es)
    bproj = ctx.sbuf("p2_bproj", [128, 8], F32, es)
    g8 = ctx.sbuf("p2_g8", [128, 8], F32, es)
    arb = ctx.sbuf("p2_arb", [128, TG], F32, es)
    kf = ctx.sbuf("p2_kf", [128, DC * S], BF16, es)
    vt = ctx.sbuf("p2_vt", [128, 16 * 1024], BF16, es)
    qtile = ctx.sbuf("p2_qt", [128, DC * TG], BF16, es)
    qtmp = [ctx.sbuf(f"p2_qtmp{i}", [128, TG], BF16, es) for i in range(2)]
    tabs = ctx.sbuf("p2_tabs", [128, 3 * TG], BF16, es)
    scol = ctx.sbuf("p2_scol", [128, 8], F32, es)


    if load_consts:
        ctx.dma("sync", t["ones"][:], d["ones_bf"].ap(), LD)
        ctx.dma("sync", t["idf"][:], d["ident_f"].ap(), LD)
        ctx.dma("sync", t["idb"][:], d["ident_b"].ap(), LD)
    ctx.dma("sync", bqkv[:], d["bqkv24"].ap(), LD)
    ctx.dma("sync", t["stage"][:1, 0:1024], d["bvrow"].ap(), LD)
    ctx.dma("sync", bproj[:], d["bproj8"].ap(), LD)
    lv = ctx.dma("sync", g8[:], d["g8"].ap(), LD)
    prev_v = 0
    for c in range(DC):
        ctx.wait("sync", V, prev_v)
        wv = ctx.dma("sync", wsl[0][:], d["wproj"].ap()[c * 128:(c + 1) * 128, :], "WL")
        ctx.wait("vector", "WL", wv)
        prev_v = ctx.op("vector", lambda e, c=c: e.tensor_copy(
            out=wprojb[:, c * 1024:(c + 1) * 1024], in_=wsl[0][:]), V)
    ctx.wait("vector", LD, lv)
    ctx.op("vector", lambda e: e.tensor_copy(out=t["hrow"][:, 0:1024], in_=t["stage"][:1, 0:1024]), None)
    v1 = ctx.op("vector", lambda e: e.tensor_tensor(
        out=t["hrow"][:, 1024:2048], in0=t["stage"][:1, 0:1024], in1=t["hrow"][:, 0:1024],
        op=ALU.subtract), V)
    ctx.wait("tensor", V, v1)
    for half in range(2):
        ctx.op("tensor", lambda e, half=half: e.matmul(
            out=t["ps"][2][:], lhsT=t["ones"][:1, :], rhs=t["hrow"][:, half * TG:(half + 1) * TG],
            start=True, stop=True), None)
        v3 = ctx.op("tensor", lambda e, half=half: e.matmul(
            out=t["ps"][3][:], lhsT=t["ones"][:1, :],
            rhs=t["hrow"][:, 1024 + half * TG: 1024 + (half + 1) * TG],
            start=True, stop=True), MM)
        ctx.wait("scalar", MM, v3)
        s1 = ctx.op("scalar", lambda e, half=half: e.activation(
            out=bvb[:, half * TG:(half + 1) * TG], in_=t["ps"][2][:], func=AF.Copy), SC)
        ctx.wait("vector", SC, s1)
        va = ctx.op("vector", lambda e, half=half: e.tensor_tensor(
            out=bvb[:, half * TG:(half + 1) * TG], in0=bvb[:, half * TG:(half + 1) * TG],
            in1=t["ps"][3][:], op=ALU.add), V)
        ctx.wait("tensor", V, va)

    last_v = ctx.cnt[V]
    jj = 0
    for s_ in range(SPC):
        es_hq = ExitStack()
        hq = ctx.sbuf(f"p2_hq{s_}", [128, DC * S], BF16, es_hq)
        for gl in range(GPS):
            g = s_ * GPS + gl
            lv = load_stage(ctx, t, d["x2T"].ap(), g, guard_v=last_v)
            ctx.wait("vector", LD, lv)
            emit_ln_group(ctx, t, g, want_lnbf=False, want_absmax=False, want_apply=False)
            ctx.op("vector", lambda e: e.tensor_tensor(
                out=t["rowf2"][:], in0=t["rowf3"][:], in1=t["rowf3"][:], op=ALU.mult), None)
            vv = ctx.op("vector", lambda e: e.tensor_scalar(
                out=t["rowf2"][:], in0=t["rowf2"][:], scalar1=-1e-5, scalar2=1.0 + 1e-6,
                op0=ALU.mult, op1=ALU.add), V)
            ctx.wait("scalar", V, vv)
            sv = ctx.op("scalar", lambda e: e.activation(
                out=t["rowf2"][:], in_=t["rowf2"][:], func=AF.Sqrt), SC)
            ctx.wait("vector", SC, sv)
            ctx.op("vector", lambda e: e.reciprocal(out=t["rowf2"][:], in_=t["rowf2"][:]), None)
            ctx.op("vector", lambda e: e.tensor_tensor(
                out=t["rowf2"][:], in0=t["rowf2"][:], in1=t["rowf3"][:], op=ALU.mult), None)
            vs3 = ctx.op("vector", lambda e: e.tensor_copy(
                out=t["rowf2"][:] if False else t["rowf"][:, :1], in_=t["rowf"][:, :1]), V)
            ctx.wait("gpsimd", V, vs3)
            ctx.dma("gpsimd", d["stats3"].ap()[0:1, gl * TG:(gl + 1) * TG], t["rowf"][:], "S3")
            ctx.dma("gpsimd", d["stats3"].ap()[1:2, gl * TG:(gl + 1) * TG], t["rowf3"][:], "S3")
            ctx.op("vector", lambda e: e.tensor_copy(out=t["hrow"][:, 0:TG], in_=t["rowf2"][:]), None)
            v1 = ctx.op("vector", lambda e: e.tensor_tensor(
                out=t["hrow"][:, TG:2 * TG], in0=t["rowf2"][:], in1=t["hrow"][:, 0:TG],
                op=ALU.subtract), V)
            ctx.wait("tensor", V, v1)
            ctx.op("tensor", lambda e: e.matmul(
                out=t["ps"][2][:], lhsT=t["ones"][:1, :], rhs=t["hrow"][:, 0:TG],
                start=True, stop=True), None)
            v3 = ctx.op("tensor", lambda e: e.matmul(
                out=t["ps"][3][:], lhsT=t["ones"][:1, :], rhs=t["hrow"][:, TG:2 * TG],
                start=True, stop=True), MM)
            ctx.wait("scalar", MM, v3)
            s1 = ctx.op("scalar", lambda e: e.activation(
                out=arb[:], in_=t["ps"][2][:], func=AF.Copy), SC)
            ctx.wait("vector", SC, s1)
            va = ctx.op("vector", lambda e: e.tensor_tensor(
                out=arb[:], in0=arb[:], in1=t["ps"][3][:], op=ALU.add), V)
            ctx.wait("tensor", V, va)
            for c in range(DC):
                ctx.op("vector", lambda e, c=c: e.tensor_tensor(
                    out=t["ep1"][:], in0=t["stage"][:, c * TG:(c + 1) * TG],
                    in1=t["mbf"][:], op=ALU.subtract), None)
                vv = ctx.op("vector", lambda e, c=c, gl=gl, hq=hq: e.scalar_tensor_tensor(
                    out=hq[:, c * S + gl * TG: c * S + (gl + 1) * TG], in0=arb[:],
                    scalar=g8[:, c:c + 1], in1=t["ep1"][:], op0=ALU.mult, op1=ALU.mult),
                    V if c == DC - 1 else None)
            last_v = ctx.cnt[V]
        for qk in range(2):
            cast_v = [0, 0]
            mm_last = [0, 0]
            sc_v = [0, 0]
            for tgl in range(GPS):
                for fc in range(DC):
                    b = fc % 2
                    ctx.wait("sync", V, cast_v[b])
                    wv = 0
                    for c in range(DC):
                        wv = ctx.dma("sync", wsl[b][:, c * 128:(c + 1) * 128],
                                     d["wqkv"].ap()[c * 128:(c + 1) * 128,
                                                    qk * 1024 + fc * 128:
                                                    qk * 1024 + (fc + 1) * 128], "WL")
                    ctx.wait("vector", "WL", wv)
                    ctx.wait("vector", MM, mm_last[b])
                    cast_v[b] = ctx.op("vector", lambda e, b=b: e.tensor_copy(
                        out=wqb[b][:], in_=wsl[b][:]), V)
                    ctx.wait("tensor", V, cast_v[b])
                    ctx.wait("tensor", SC, sc_v[b])
                    pm = t["ps"][4 + b]
                    for k in range(DC):
                        ctx.op("tensor", lambda e, k=k, tgl=tgl, pm=pm, b=b, hq=hq: e.matmul(
                            out=pm[:], lhsT=wqb[b][:, k * 128:(k + 1) * 128],
                            rhs=hq[:, k * S + tgl * TG: k * S + (tgl + 1) * TG],
                            start=(k == 0), stop=(k == DC - 1)), MM if k == DC - 1 else None)
                    mm_last[b] = ctx.cnt[MM]
                    ctx.wait("scalar", MM, mm_last[b])
                    dst = (qtile[:, fc * TG:(fc + 1) * TG] if qk == 0 else
                           kf[:, fc * S + tgl * TG: fc * S + (tgl + 1) * TG])
                    sc_v[b] = ctx.op("scalar", lambda e, pm=pm, dst=dst, qk=qk, fc=fc:
                        e.activation(out=dst, in_=pm[:], func=AF.Identity,
                                     bias=bqkv[:, qk * 8 + fc: qk * 8 + fc + 1]), SC)
                ctx.wait("vector", SC, max(sc_v))
                for hc in range(4):
                    ctx.wait("gpsimd", V, ctx.cnt[V])
                    tv = 0
                    for ti, nm in enumerate(("ropeC", "ropeSp", "ropeSn")):
                        tv = ctx.dma("gpsimd", tabs[:, ti * TG:(ti + 1) * TG],
                                     d[nm].ap()[hc * 128:(hc + 1) * 128,
                                                tgl * TG:(tgl + 1) * TG], "TB")
                    ctx.wait("vector", "TB", tv)
                    ca, cb = hc, hc + 4
                    if qk == 0:
                        src_a = qtile[:, ca * TG:(ca + 1) * TG]
                        src_b = qtile[:, cb * TG:(cb + 1) * TG]
                    else:
                        src_a = kf[:, ca * S + tgl * TG: ca * S + (tgl + 1) * TG]
                        src_b = kf[:, cb * S + tgl * TG: cb * S + (tgl + 1) * TG]
                    ctx.op("vector", lambda e, src_a=src_a: e.tensor_tensor(
                        out=qtmp[0][:], in0=src_a, in1=tabs[:, 0:TG], op=ALU.mult), None)
                    ctx.op("vector", lambda e, src_b=src_b: e.tensor_tensor(
                        out=qtmp[1][:], in0=src_b, in1=tabs[:, 2 * TG:3 * TG], op=ALU.mult), None)
                    ctx.op("vector", lambda e, src_b=src_b: e.tensor_tensor(
                        out=t["ep1"][:], in0=src_b, in1=tabs[:, 0:TG], op=ALU.mult), None)
                    ctx.op("vector", lambda e, src_a=src_a: e.tensor_tensor(
                        out=t["xout"][0][:], in0=src_a, in1=tabs[:, TG:2 * TG], op=ALU.mult), None)
                    ctx.op("vector", lambda e, src_a=src_a: e.tensor_tensor(
                        out=src_a, in0=qtmp[0][:], in1=qtmp[1][:], op=ALU.add), None)
                    rv = ctx.op("vector", lambda e, src_b=src_b: e.tensor_tensor(
                        out=src_b, in0=t["ep1"][:], in1=t["xout"][0][:], op=ALU.add), V)
                if qk == 0:
                    ctx.wait("gpsimd", V, rv)
                    for c in range(DC):
                        ctx.dma("gpsimd", d["qs"].ap()[c * 128:(c + 1) * 128,
                                                       tgl * TG:(tgl + 1) * TG],
                                qtile[:, c * TG:(c + 1) * TG], "QS")
                    ctx.wait("scalar", "QS", ctx.cnt["QS"])
        v_ev = [0, 0]
        for fg in range(2):
            for c in range(DC):
                ctx.wait("sync", V, ctx.cnt[V] if (fg == 0 and c == 0) else cast_done)
                wv = ctx.dma("sync", wsl[0][:, 0:512],
                             d["wqkv"].ap()[c * 128:(c + 1) * 128,
                                            2048 + fg * 512: 2048 + (fg + 1) * 512], "WL")
                ctx.wait("vector", "WL", wv)
                ctx.wait("vector", MM, ctx.cnt[MM])
                cast_done = ctx.op("vector", lambda e, c=c: e.tensor_copy(
                    out=wv0[:, c * 512:(c + 1) * 512], in_=wsl[0][:, 0:512]), V)
            ctx.wait("tensor", V, cast_done)
            for tc in range(16):
                tgl = tc // 4
                toff = (tc % 4) * 128
                pm = t["ps"][4 + tc % 2]
                ctx.wait("tensor", V, v_ev[tc % 2])
                for k in range(DC):
                    ctx.op("tensor", lambda e, k=k, tgl=tgl, toff=toff, pm=pm, hq=hq: e.matmul(
                        out=pm[:], lhsT=hq[:, k * S + tgl * TG + toff:
                                           k * S + tgl * TG + toff + 128],
                        rhs=wv0[:, k * 512:(k + 1) * 512],
                        start=(k == 0), stop=(k == DC - 1)), MM if k == DC - 1 else None)
                mmv = ctx.cnt[MM]
                ctx.wait("vector", MM, mmv)
                v_ev[tc % 2] = ctx.op("vector", lambda e, tc=tc, fg=fg, pm=pm: e.tensor_tensor(
                    out=vt[:, tc * 1024 + fg * 512: tc * 1024 + (fg + 1) * 512],
                    in0=pm[:], in1=bvb[:, fg * 512:(fg + 1) * 512], op=ALU.add), V)
        v_done = max(v_ev)
        es_hq.close()
        es_at = ExitStack()
        Pt = ctx.sbuf(f"p2_P{s_}", [128, S], BF16, es_at)
        PT_sb = ctx.sbuf(f"p2_PT{s_}", [128, 16 * TG], BF16, es_at)
        osl = ctx.sbuf(f"p2_o{s_}", [128, DC * TG], BF16, es_at)
        ctx.wait("sync", "QS", ctx.cnt["QS"])
        pt_cp = 0
        pv_mm = 0
        for gl in range(GPS):
            ctx.wait("sync", V, ctx.cnt[V])
            qv = 0
            for c in range(DC):
                qv = ctx.dma("sync", qtile[:, c * TG:(c + 1) * TG],
                             d["qs"].ap()[c * 128:(c + 1) * 128, gl * TG:(gl + 1) * TG], "QL")
            ctx.wait("tensor", "QL", qv)
            ctx.wait("tensor", V, v_done)
            exp_sv = 0
            for qt in range(4):
                ctx.wait("tensor", SC, exp_sv)
                for kg in range(4):
                    for c in range(DC):
                        ctx.op("tensor", lambda e, c=c, kg=kg, qt=qt: e.matmul(
                            out=t["ps"][kg][:, 0:TG],
                            lhsT=qtile[:, c * TG + qt * 128: c * TG + qt * 128 + 128],
                            rhs=kf[:, c * S + kg * TG: c * S + (kg + 1) * TG],
                            start=(c == 0), stop=(c == DC - 1)),
                            MM if (c == DC - 1 and kg == 3) else None)
                mmv = ctx.cnt[MM]
                ctx.wait("vector", MM, mmv)
                for kg in range(4):
                    ctx.op("vector", lambda e, kg=kg: e.tensor_reduce(
                        out=scol[:, kg:kg + 1], in_=t["ps"][kg][:, 0:TG], axis=AX,
                        op=ALU.max), None)
                ctx.drain()
                ctx.op("vector", lambda e: e.tensor_tensor(
                    out=scol[:, 4:5], in0=scol[:, 0:1], in1=scol[:, 1:2], op=ALU.max), None)
                ctx.op("vector", lambda e: e.tensor_tensor(
                    out=scol[:, 5:6], in0=scol[:, 2:3], in1=scol[:, 3:4], op=ALU.max), None)
                ctx.drain()
                ctx.op("vector", lambda e: e.tensor_tensor(
                    out=scol[:, 4:5], in0=scol[:, 4:5], in1=scol[:, 5:6], op=ALU.max), None)
                ctx.drain()
                vb = ctx.op("vector", lambda e: e.tensor_scalar(
                    out=scol[:, 6:7], in0=scol[:, 4:5], scalar1=-LN3 / 32.0, scalar2=None,
                    op0=ALU.mult), V)
                ctx.drain()
                ctx.wait("scalar", V, vb)
                for kg in range(4):
                    ctx.op("scalar", lambda e, kg=kg, Pt=Pt: e.activation(
                        out=Pt[:, kg * TG:(kg + 1) * TG], in_=t["ps"][kg][:, 0:TG],
                        func=AF.Exp, scale=LN3 / 32.0, bias=scol[:, 6:7],
                        accum_out=scol[:, kg:kg + 1]), SC if kg == 3 else None)
                exp_sv = ctx.cnt[SC]
                ctx.wait("vector", SC, exp_sv)
                ctx.op("vector", lambda e: e.tensor_tensor(
                    out=scol[:, 4:5], in0=scol[:, 0:1], in1=scol[:, 1:2], op=ALU.add), None)
                ctx.op("vector", lambda e: e.tensor_tensor(
                    out=scol[:, 5:6], in0=scol[:, 2:3], in1=scol[:, 3:4], op=ALU.add), None)
                ctx.drain()
                ctx.op("vector", lambda e: e.tensor_tensor(
                    out=scol[:, 4:5], in0=scol[:, 4:5], in1=scol[:, 5:6], op=ALU.add), None)
                ctx.drain()
                ctx.op("vector", lambda e: e.reciprocal(out=scol[:, 7:8], in_=scol[:, 4:5]), None)
                ctx.drain()
                vn = ctx.op("vector", lambda e, Pt=Pt: e.tensor_scalar(
                    out=Pt[:], in0=Pt[:], scalar1=scol[:, 7:8], scalar2=None,
                    op0=ALU.mult), V)
                ctx.wait("tensor", V, vn)
                ctx.wait("tensor", V, pt_cp)
                for kc in range(16):
                    bb = kc // 8
                    ctx.op("tensor", lambda e, kc=kc, bb=bb, Pt=Pt: e.transpose(
                        out=t["psb"][bb][:, (kc % 8) * 128:(kc % 8 + 1) * 128],
                        in_=Pt[:, kc * 128:(kc + 1) * 128], identity=t["idb"][:]),
                        MM if kc == 15 else None)
                mtv = ctx.cnt[MM]
                ctx.wait("vector", MM, mtv)
                ctx.wait("vector", MM, pv_mm)
                for bb in range(2):
                    pt_cp = ctx.op("vector", lambda e, bb=bb, qt=qt, PT_sb=PT_sb: e.tensor_copy(
                        out=PT_sb[:].rearrange("p (kc q) -> p kc q", kc=16)[
                            :, bb * 8:(bb + 1) * 8, qt * 128:(qt + 1) * 128],
                        in_=t["psb"][bb][:, 0:1024].rearrange(
                            "p (kc q) -> p kc q", kc=8)), V)
            o_sc = [0, 0]
            for dc in range(DC):
                pm = t["ps"][4 + dc % 2]
                ctx.wait("tensor", V, pt_cp)
                ctx.wait("tensor", SC, o_sc[dc % 2])
                for kc in range(16):
                    ctx.op("tensor", lambda e, kc=kc, dc=dc, pm=pm, PT_sb=PT_sb: e.matmul(
                        out=pm[:], lhsT=vt[:, kc * 1024 + dc * 128: kc * 1024 + (dc + 1) * 128],
                        rhs=PT_sb[:, kc * TG:(kc + 1) * TG],
                        start=(kc == 0), stop=(kc == 15)), MM if kc == 15 else None)
                pv_mm = ctx.cnt[MM]
                ctx.wait("scalar", MM, pv_mm)
                o_sc[dc % 2] = ctx.op("scalar", lambda e, dc=dc, pm=pm, osl=osl: e.activation(
                    out=osl[:, dc * TG:(dc + 1) * TG], in_=pm[:], func=AF.Copy), SC)
            pj_ev = [0, 0]
            ctx.wait("tensor", SC, max(o_sc))
            for fc in range(DC):
                pm = t["ps"][fc % 2]
                ctx.wait("tensor", V, pj_ev[fc % 2])
                for dc in range(DC):
                    ctx.op("tensor", lambda e, dc=dc, fc=fc, pm=pm, osl=osl: e.matmul(
                        out=pm[:], lhsT=wprojb[:, dc * 1024 + fc * 128:
                                               dc * 1024 + (fc + 1) * 128],
                        rhs=osl[:, dc * TG:(dc + 1) * TG],
                        start=(dc == 0), stop=(dc == DC - 1)), MM if dc == DC - 1 else None)
                pjm = ctx.cnt[MM]
                if fc == 0:
                    ctx.wait("sync", "S3", ctx.cnt["S3"])
                    ctx.wait("sync", V, ctx.cnt[V])
                    s3v = 0
                    s3v = ctx.dma("sync", t["rowf"][:], d["stats3"].ap()[0:1, gl * TG:(gl + 1) * TG], "S3L")
                    s3v = ctx.dma("sync", t["rowf3"][:], d["stats3"].ap()[1:2, gl * TG:(gl + 1) * TG], "S3L")
                    ctx.wait("vector", "S3L", s3v)
                    ctx.wait("vector", SC, max(o_sc))
                    ctx.op("vector", lambda e: e.tensor_copy(
                        out=t["hrow"][:, 0:TG], in_=t["rowf"][:]), None)
                    ctx.op("vector", lambda e: e.tensor_tensor(
                        out=t["hrow"][:, TG:2 * TG], in0=t["rowf"][:],
                        in1=t["hrow"][:, 0:TG], op=ALU.subtract), None)
                    ctx.op("vector", lambda e: e.tensor_copy(
                        out=t["hrow"][:, 2 * TG:3 * TG], in_=t["rowf3"][:]), None)
                    v1 = ctx.op("vector", lambda e: e.tensor_tensor(
                        out=t["hrow"][:, 3 * TG:4 * TG], in0=t["rowf3"][:],
                        in1=t["hrow"][:, 2 * TG:3 * TG], op=ALU.subtract), V)
                    ctx.wait("tensor", V, v1)
                    ctx.op("tensor", lambda e: e.matmul(
                        out=t["ps"][2][:], lhsT=t["ones"][:1, :], rhs=t["hrow"][:, 0:TG],
                        start=True, stop=True), None)
                    v3 = ctx.op("tensor", lambda e: e.matmul(
                        out=t["ps"][3][:], lhsT=t["ones"][:1, :], rhs=t["hrow"][:, TG:2 * TG],
                        start=True, stop=True), MM)
                    ctx.wait("scalar", MM, v3)
                    s1 = ctx.op("scalar", lambda e: e.activation(
                        out=t["mbf"][:], in_=t["ps"][2][:], func=AF.Copy), SC)
                    ctx.wait("vector", SC, s1)
                    vmb = ctx.op("vector", lambda e: e.tensor_tensor(
                        out=t["mbf"][:], in0=t["mbf"][:], in1=t["ps"][3][:], op=ALU.add), V)
                    ctx.wait("tensor", V, vmb)
                    ctx.op("tensor", lambda e: e.matmul(
                        out=t["ps"][2][:], lhsT=t["ones"][:1, :], rhs=t["hrow"][:, 2 * TG:3 * TG],
                        start=True, stop=True), None)
                    v3 = ctx.op("tensor", lambda e: e.matmul(
                        out=t["ps"][3][:], lhsT=t["ones"][:1, :], rhs=t["hrow"][:, 3 * TG:4 * TG],
                        start=True, stop=True), MM)
                    ctx.wait("scalar", MM, v3)
                    s1 = ctx.op("scalar", lambda e: e.activation(
                        out=t["rbf"][:], in_=t["ps"][2][:], func=AF.Copy), SC)
                    ctx.wait("vector", SC, s1)
                    vrb = ctx.op("vector", lambda e: e.tensor_tensor(
                        out=t["rbf"][:], in0=t["rbf"][:], in1=t["ps"][3][:], op=ALU.add), V)
                    ctx.wait("sync", V, vrb)
                    lv = 0
                    for c in range(DC):
                        lv = ctx.dma("sync", t["stage"][:, c * TG:(c + 1) * TG],
                                     d["x2T"].ap()[c * 128:(c + 1) * 128,
                                                   (s_ * GPS + gl) * TG:(s_ * GPS + gl + 1) * TG], LD)
                    ctx.wait("vector", LD, lv)
                ctx.wait("vector", MM, pjm)
                if jj >= 2:
                    ctx.wait("vector", "X3S", 16 * (jj - 1))
                ctx.op("vector", lambda e, fc=fc: e.tensor_tensor(
                    out=t["ep1"][:], in0=t["stage"][:, fc * TG:(fc + 1) * TG],
                    in1=t["mbf"][:], op=ALU.subtract), None)
                ctx.op("vector", lambda e: e.tensor_tensor(
                    out=t["ep1"][:], in0=t["ep1"][:], in1=t["rbf"][:], op=ALU.mult), None)
                ctx.op("vector", lambda e, fc=fc, pm=pm: e.scalar_tensor_tensor(
                    out=t["ep1"][:], in0=pm[:], scalar=bproj[:, fc:fc + 1],
                    in1=t["ep1"][:], op0=ALU.add, op1=ALU.add), None)
                vv = ctx.op("vector", lambda e, fc=fc, jj=jj: e.tensor_tensor(
                    out=t["xout"][jj % 2][:], in0=t["ep1"][:],
                    in1=t["stage"][:, fc * TG:(fc + 1) * TG], op=ALU.add), V)
                pj_ev[fc % 2] = vv
                ctx.wait("gpsimd", V, vv)
                ctx.dma("gpsimd", d["x3T"].ap()[fc * 128:(fc + 1) * 128,
                                                (s_ * GPS + gl) * TG:(s_ * GPS + gl + 1) * TG],
                        t["xout"][jj % 2][:], "X3S")
                jj += 1
            last_v = ctx.cnt[V]
        ctx.drain_dma(["X3S", LD, "WL", "QS", "QL", "TB", "S3", "S3L"])
        ctx.barrier()
        es_at.close()
    es.close()


def declare_p2(nc, d, x2_is_input):
    for name, shape, dt in [
            ("wqkv", [D, 3072], F32), ("wproj", [D, D], F32),
            ("bqkv24", [128, 24], F32), ("bvrow", [1, 1024], F32),
            ("bproj8", [128, 8], F32), ("g8", [128, 8], F32),
            ("ropeC", [512, S], BF16), ("ropeSp", [512, S], BF16),
            ("ropeSn", [512, S], BF16)]:
        d[name] = nc.dram_tensor(name, shape, dt, kind="ExternalInput")
    if x2_is_input:
        d["x2T"] = nc.dram_tensor("x2T", [D, T], F32, kind="ExternalInput")
        for name, shape, dt in [
                ("ones_bf", [128, 128], BF16), ("ident_f", [128, 128], F32),
                ("ident_b", [128, 128], BF16)]:
            d[name] = nc.dram_tensor(name, shape, dt, kind="ExternalInput")
    d["qs"] = nc.dram_tensor("qs", [D, S], BF16)
    d["stats3"] = nc.dram_tensor("stats3", [2, S], F32)
    d["x3T"] = nc.dram_tensor("x3T", [D, T], F32, kind="ExternalOutput")


def build_p2():
    nc = bass.Bass("TRN2", target_bir_lowering=False)
    d = {}
    declare_p2(nc, d, x2_is_input=True)
    es = ExitStack()
    ctx = Ctx(nc, es)
    with es:
        t = alloc_common(ctx)
        emit_phase2(ctx, t, d, load_consts=True)
        ctx.build_block()
    return nc



def emit_phase3(ctx, t, d, load_consts=False, upto="G"):
    t = dict(t)
    V, SC, MM, LD, ST = "VE", "SC", "MM", "LD", "ST"
    for nm in (ST, "WL", LD, "H3S", "H3L", "X4S"):
        ctx.sem(nm)
    es = ExitStack()
    t["lnf"] = ctx.sbuf("p3_lnf", [128, DC * TG], F32, es)
    xq4 = ctx.sbuf("p3_xq4", [128, DC * T], BF16, es)
    h3sl = [ctx.sbuf(f"p3_h3{i}", [128, T], BF16, es) for i in range(2)]
    wsl = [ctx.sbuf(f"p3_wsl{i}", [128, 2 * 1024], I32, es) for i in range(2)]
    wqg = [ctx.sbuf(f"p3_wqg{i}", [128, DC * 128], BF16, es) for i in range(2)]
    wqu = [ctx.sbuf(f"p3_wqu{i}", [128, DC * 128], BF16, es) for i in range(2)]
    silu_t = [ctx.sbuf(f"p3_si{i}", [128, TG], F32, es) for i in range(2)]
    if load_consts:
        ctx.dma("sync", t["ones"][:], d["ones_bf"].ap(), LD)
        ctx.dma("sync", t["idf"][:], d["ident_f"].ap(), LD)
        ctx.dma("sync", t["idb"][:], d["ident_b"].ap(), LD)
    last_v = 0
    for g in range(NG):
        lv = load_stage(ctx, t, d["x3T"].ap(), g, guard_v=last_v)
        ctx.wait("vector", LD, lv)
        last_v = emit_ln_group(ctx, t, g, want_lnbf=True, want_absmax=False,
                               lnbf_dst=xq4, lnbf_g=g)
    if upto == "LN":
        for c in range(DC):
            vv = ctx.op("vector", lambda e, c=c: e.tensor_copy(out=t["xout"][0][:], in_=xq4[:, c * T: c * T + TG]), "VE")
            ctx.wait("gpsimd", "VE", vv)
            ctx.dma("gpsimd", d["dbg3"].ap()[c * 128:(c + 1) * 128, 0:TG], t["xout"][0][:], "X4S")
            ctx.wait("vector", "X4S", ctx.cnt["X4S"])
        for c in range(DC):
            vv = ctx.op("vector", lambda e, c=c: e.tensor_copy(out=t["xout"][0][:], in_=xq4[:, c * T + 3 * TG: c * T + 4 * TG]), "VE")
            ctx.wait("gpsimd", "VE", vv)
            ctx.dma("gpsimd", d["dbg3"].ap()[c * 128:(c + 1) * 128, TG:2 * TG], t["xout"][0][:], "X4S")
            ctx.wait("vector", "X4S", ctx.cnt["X4S"])
        ctx.drain_dma(["X4S", "ST", LD, "WL"])
        ctx.barrier()
        es.close()
        return
    h3_store_v = [0, 0]
    h2v = [0, 0]
    cast_v = [0, 0]
    mm_done = [0, 0]
    def p3_load(i):
        ib = i % 2
        ctx.wait("sync", V, cast_v[ib])
        for c in range(DC):
            ctx.dma("sync", wsl[ib][:, c * 128:(c + 1) * 128],
                    d["w1gT"].ap()[c * 128:(c + 1) * 128, i * 128:(i + 1) * 128], "WL")
        for c in range(DC):
            ctx.dma("sync", wsl[ib][:, 1024 + c * 128: 1024 + (c + 1) * 128],
                    d["w1uT"].ap()[c * 128:(c + 1) * 128, i * 128:(i + 1) * 128], "WL")
        return ctx.cnt["WL"]
    def p3_cast(i, lv):
        ib = i % 2
        ctx.wait("vector", "WL", lv)
        ctx.wait("vector", MM, mm_done[ib])
        ctx.op("vector", lambda e, ib=ib: e.tensor_copy(
            out=wqg[ib][:], in_=wsl[ib][:, 0:1024]), None)
        cast_v[ib] = ctx.op("vector", lambda e, ib=ib: e.tensor_copy(
            out=wqu[ib][:], in_=wsl[ib][:, 1024:2048]), V)
    lv0 = p3_load(0)
    p3_cast(0, lv0)
    for i in range(IC1):
        ib = i % 2
        if i + 1 < IC1:
            lvn = p3_load(i + 1)
            p3_cast(i + 1, lvn)
        ctx.wait("vector", "H3S", h3_store_v[i % 2])
        ctx.wait("tensor", V, cast_v[ib])
        for tg in range(NG):
            pA = t["ps"][4] if tg % 2 == 0 else t["ps"][2]
            pB = t["ps"][5] if tg % 2 == 0 else t["ps"][3]
            if tg >= 2:
                ctx.wait("tensor", V, h2v[tg % 2])
            for k in range(DC):
                ctx.op("tensor", lambda e, k=k, tg=tg, pA=pA, ib=ib: e.matmul(
                    out=pA[:], lhsT=wqg[ib][:, k * 128:(k + 1) * 128],
                    rhs=xq4[:, k * T + tg * TG: k * T + (tg + 1) * TG],
                    start=(k == 0), stop=(k == DC - 1)), None)
            for k in range(DC):
                ctx.op("tensor", lambda e, k=k, tg=tg, pB=pB, ib=ib: e.matmul(
                    out=pB[:], lhsT=wqu[ib][:, k * 128:(k + 1) * 128],
                    rhs=xq4[:, k * T + tg * TG: k * T + (tg + 1) * TG],
                    start=(k == 0), stop=(k == DC - 1)), MM if k == DC - 1 else None)
            mmv = ctx.cnt[MM]
            ctx.wait("scalar", MM, mmv)
            sv = ctx.op("scalar", lambda e, tg=tg, pA=pA: e.activation(
                out=silu_t[tg % 2][:], in_=pA[:], func=AF.Silu,
                scale=1.0 / QMAX), SC)
            ctx.wait("vector", SC, sv)
            h2v[tg % 2] = ctx.op("vector", lambda e, tg=tg, i=i, pB=pB: e.tensor_tensor(
                out=h3sl[i % 2][:, tg * TG:(tg + 1) * TG], in0=pB[:],
                in1=silu_t[tg % 2][:], op=ALU.mult), V)
        ctx.wait("gpsimd", V, h2v[(NG - 1) % 2])
        h3_store_v[i % 2] = ctx.dma(
            "gpsimd", d["h3s"].ap()[i * 128:(i + 1) * 128, :], h3sl[i % 2][:], "H3S")
    if upto.startswith("E"):
        ioff = {"E": 0, "E2": 8, "E3": 16}[upto]
        ctx.wait("gpsimd", "H3S", ctx.cnt["H3S"])
        for i in range(DC):
            ctx.dma("gpsimd", d["dbg3"].ap()[i * 128:(i + 1) * 128, :],
                    d["h3s"].ap()[(ioff + i) * 128:(ioff + i + 1) * 128, :], "X4S")
        ctx.drain_dma(["X4S", "ST", LD, "WL", "H3S"])
        ctx.barrier()
        es.close()
        return
    es.close()

    es2 = ExitStack()
    wdqT = ctx.sbuf("p3_wdqT", [128, IC1 * 1024], BF16, es2)
    h3q = [ctx.sbuf(f"p3_h3q{i}", [128, IC1 * TG], BF16, es2) for i in range(2)]
    wTsl = [ctx.sbuf(f"p3_wTsl{i}", [128, 1024], I32, es2) for i in range(2)]
    ctx.wait("sync", "H3S", ctx.cnt["H3S"])
    prev_v = ctx.cnt[V]
    cast3 = [prev_v, prev_v]
    wlv3 = [0, 0]
    def p3d_load(i):
        ib = i % 2
        ctx.wait("sync", V, cast3[ib])
        wlv3[ib] = ctx.dma("sync", wTsl[ib][:], d["w1dT"].ap()[i * 128:(i + 1) * 128, :], "WL")
    p3d_load(0)
    for i in range(IC1):
        ib = i % 2
        if i + 1 < IC1:
            p3d_load(i + 1)
        ctx.wait("vector", "WL", wlv3[ib])
        prev_v = ctx.op("vector", lambda e, i=i, ib=ib: e.tensor_copy(
            out=wdqT[:, i * 1024:(i + 1) * 1024], in_=wTsl[ib][:]), V)
        cast3[ib] = prev_v
    ctx.wait("sync", "H3S", ctx.cnt["H3S"])
    dmm = 0
    dmm_b = [0, 0]
    h3lv = [0, 0]
    last_vv = prev_v
    jj = 0
    def p3h_load(tg):
        tb = tg % 2
        ctx.wait("sync", MM, dmm_b[tb])
        for i in range(IC1):
            h3lv[tb] = ctx.dma("sync", h3q[tb][:, i * TG:(i + 1) * TG],
                               d["h3s"].ap()[i * 128:(i + 1) * 128, tg * TG:(tg + 1) * TG], "H3L")
    p3h_load(0)
    for tg in range(NG):
        tb = tg % 2
        if tg + 1 < NG:
            p3h_load(tg + 1)
        if tg == 0:
            ctx.wait("tensor", V, prev_v)
        ctx.wait("tensor", "H3L", h3lv[tb])
        for hh in range(2):
            if not (tg == 0 and hh == 0):
                ctx.wait("tensor", V, last_vv)
            for i in range(IC1):
                for h4 in range(4):
                    h = hh * 4 + h4
                    ctx.op("tensor", lambda e, i=i, h=h, h4=h4, tb=tb: e.matmul(
                        out=t["ps"][h4][:], lhsT=wdqT[:, i * 1024 + h * 128:
                                                      i * 1024 + (h + 1) * 128],
                        rhs=h3q[tb][:, i * TG:(i + 1) * TG],
                        start=(i == 0), stop=(i == IC1 - 1)),
                        MM if (i == IC1 - 1 and h4 == 3) else None)
            dmm = ctx.cnt[MM]
            dmm_b[tb] = dmm
            ctx.wait("sync", V, last_vv)
            lv = 0
            for h4 in range(4):
                h = hh * 4 + h4
                lv = ctx.dma("sync", t["stage"][:, h * TG:(h + 1) * TG],
                             d["x3T"].ap()[h * 128:(h + 1) * 128, tg * TG:(tg + 1) * TG], LD)
            ctx.wait("vector", LD, lv)
            ctx.wait("vector", MM, dmm)
            for h4 in range(4):
                h = hh * 4 + h4
                if jj >= 2:
                    ctx.wait("vector", "X4S", 16 * (jj - 1))
                if upto == "G2":
                    vv = ctx.op("vector", lambda e, h4=h4, jj=jj: e.tensor_scalar(
                        out=t["xout"][jj % 2][:], in0=t["ps"][h4][:],
                        scalar1=1.0 / (QMAX * QMAX), scalar2=None, op0=ALU.mult), V)
                else:
                    vv = ctx.op("vector", lambda e, h=h, h4=h4, jj=jj: e.scalar_tensor_tensor(
                        out=t["xout"][jj % 2][:], in0=t["ps"][h4][:],
                        scalar=1.0 / (QMAX * QMAX), in1=t["stage"][:, h * TG:(h + 1) * TG],
                        op0=ALU.mult, op1=ALU.add), V)
                last_vv = vv
                ctx.wait("gpsimd", V, vv)
                ctx.dma("gpsimd", d["x4T"].ap()[h * 128:(h + 1) * 128, tg * TG:(tg + 1) * TG],
                        t["xout"][jj % 2][:], "X4S")
                jj += 1
    ctx.drain_dma(["X4S", LD, "WL", "H3L", "H3S"])
    ctx.barrier()
    es2.close()


def declare_p3(nc, d, x3_is_input):
    for name, shape, dt in [
            ("w1gT", [D, 3072], I32), ("w1uT", [D, 3072], I32),
            ("w1dT", [3072, D], I32)]:
        d[name] = nc.dram_tensor(name, shape, dt, kind="ExternalInput")
    if x3_is_input:
        d["x3T"] = nc.dram_tensor("x3T", [D, T], F32, kind="ExternalInput")
        for name, shape, dt in [
                ("ones_bf", [128, 128], BF16), ("ident_f", [128, 128], F32),
                ("ident_b", [128, 128], BF16)]:
            d[name] = nc.dram_tensor(name, shape, dt, kind="ExternalInput")
    d["h3s"] = nc.dram_tensor("h3s", [3072, T], BF16)
    d["x4T"] = nc.dram_tensor("x4T", [D, T], F32, kind="ExternalOutput")
    d["dbg3"] = nc.dram_tensor("dbg3", [D, T], F32, kind="ExternalOutput")


def build_p3(upto="G"):
    nc = bass.Bass("TRN2", target_bir_lowering=False)
    d = {}
    declare_p3(nc, d, x3_is_input=True)
    es = ExitStack()
    ctx = Ctx(nc, es)
    with es:
        t = alloc_common(ctx)
        emit_phase3(ctx, t, d, load_consts=True, upto=upto)
        ctx.build_block()
    return nc



def host_prep(inputs):
    c = {}
    c["WdT"] = np.ascontiguousarray(inputs["L0_attn_W_delta"].T)
    c["bdelta8"] = np.ascontiguousarray(inputs["L0_attn_b_delta"].reshape(8, 128).T)
    c["WbT128"] = np.ascontiguousarray(
        inputs["L0_attn_W_B"].T.reshape(8, 128, 16).transpose(1, 0, 2).reshape(128, 128))
    c["bB"] = np.ascontiguousarray(inputs["L0_attn_b_B"].reshape(16, 1))
    c["Alog"] = np.ascontiguousarray(inputs["L0_attn_A_log"].reshape(16, 1))
    c["Dcol8"] = np.ascontiguousarray(inputs["L0_attn_D_param"].reshape(8, 128).T)
    c["WoT"] = np.ascontiguousarray(inputs["L0_attn_W_out"].T)
    c["bout8"] = np.ascontiguousarray(inputs["L0_attn_b_out"].reshape(8, 128).T)
    c["ones_bf"] = np.ones((128, 128), dtype=BF)
    c["ident_f"] = np.eye(128, dtype=np.float32)
    c["ident_b"] = np.eye(128, dtype=BF)
    c["clip2"] = np.concatenate([inputs["L0_ffn_clip_x"],
                                 inputs["L0_ffn_clip_down"]]).reshape(1, 2).astype(np.float32)
    c["wg_nat"] = np.ascontiguousarray(inputs["L0_ffn_w_gate"])
    c["wu_nat"] = np.ascontiguousarray(inputs["L0_ffn_w_up"])
    c["wgT"] = np.ascontiguousarray(inputs["L0_ffn_w_gate"].T)
    c["wuT"] = np.ascontiguousarray(inputs["L0_ffn_w_up"].T)
    c["wd_nat"] = np.ascontiguousarray(inputs["L0_ffn_w_down"])
    c["wdT"] = np.ascontiguousarray(inputs["L0_ffn_w_down"].T)
    c["wqkv"] = np.ascontiguousarray(inputs["L1_attn_w_qkv"])
    c["wproj"] = np.ascontiguousarray(inputs["L1_attn_w_proj"])
    c["bqkv24"] = np.ascontiguousarray(inputs["L1_attn_b_qkv"].reshape(24, 128).T)
    c["bvrow"] = np.ascontiguousarray(inputs["L1_attn_b_qkv"][2048:].reshape(1, 1024))
    c["bproj8"] = np.ascontiguousarray(inputs["L1_attn_b_proj"].reshape(8, 128).T)
    c["g8"] = np.ascontiguousarray(inputs["L1_attn_g_norm"].reshape(8, 128).T)
    half = D // 2
    invf = 1.0 / 10000.0 ** (np.arange(half) / half)
    ang = np.arange(S)[:, None] * invf[None, :]
    cosr = np.round(np.cos(ang)).astype(BF)
    sinr = np.round(np.sin(ang)).astype(BF)
    c["ropeC"] = np.ascontiguousarray(cosr.T)
    c["ropeSp"] = np.ascontiguousarray(sinr.T)
    c["ropeSn"] = np.ascontiguousarray((-sinr).T)
    c["w1gT"] = np.ascontiguousarray(inputs["L1_ffn_w_gate"].T)
    c["w1uT"] = np.ascontiguousarray(inputs["L1_ffn_w_up"].T)
    c["w1dT"] = np.ascontiguousarray(inputs["L1_ffn_w_down"].T)
    return c


_P0_KEYS = ["ones_bf", "ident_f", "ident_b", "x0T", "WdT", "bdelta8", "WbT128",
            "bB", "Alog", "Dcol8", "WoT", "bout8"]
_P1_KEYS = ["ones_bf", "ident_f", "ident_b", "clip2", "wg_nat", "wu_nat",
            "wgT", "wuT", "wd_nat", "wdT"]
_P3_KEYS = ["ones_bf", "ident_f", "ident_b", "w1gT", "w1uT", "w1dT"]
_P2_KEYS = ["ones_bf", "ident_f", "ident_b", "wqkv", "wproj", "bqkv24", "bvrow",
            "bproj8", "g8", "ropeC", "ropeSp", "ropeSn"]


def declare_p0(nc, d):
    for name, shape, dt in [
            ("ones_bf", [128, 128], BF16), ("ident_f", [128, 128], F32),
            ("ident_b", [128, 128], BF16), ("x0T", [D, T], F32),
            ("WdT", [D, D], F32), ("bdelta8", [128, 8], F32),
            ("WbT128", [128, 128], F32), ("bB", [16, 1], F32),
            ("Alog", [16, 1], F32), ("Dcol8", [128, 8], F32),
            ("WoT", [16, D], F32), ("bout8", [128, 8], F32)]:
        d[name] = nc.dram_tensor(name, shape, dt, kind="ExternalInput")


def declare_p1(nc, d, x1_is_input):
    for name, shape, dt in [
            ("clip2", [1, 2], F32), ("wg_nat", [2816, D], F32),
            ("wu_nat", [2816, D], F32), ("wgT", [D, 2816], F32),
            ("wuT", [D, 2816], F32), ("wd_nat", [D, 2816], F32),
            ("wdT", [2816, D], F32)]:
        d[name] = nc.dram_tensor(name, shape, dt, kind="ExternalInput")
    if x1_is_input:
        d["x1T"] = nc.dram_tensor("x1T", [D, T], F32, kind="ExternalInput")
        for name, shape, dt in [
                ("ones_bf", [128, 128], BF16), ("ident_f", [128, 128], F32),
                ("ident_b", [128, 128], BF16)]:
            d[name] = nc.dram_tensor(name, shape, dt, kind="ExternalInput")
    d["lnfs"] = nc.dram_tensor("lnfs", [D, T], F32)
    d["h2s"] = nc.dram_tensor("h2s", [2816, T], F32)
    d["cci"] = nc.dram_tensor("cci", [1, 1], F32)
    d["cco"] = nc.dram_tensor("cco", [1, 1], F32)
    d["x2T"] = nc.dram_tensor("x2T", [D, T], F32, kind="ExternalOutput")
    d["dbg"] = nc.dram_tensor("dbg", [D, T], F32, kind="ExternalOutput")


def build_p0():
    nc = bass.Bass("TRN2", target_bir_lowering=False)
    d = {}
    declare_p0(nc, d)
    d["x1T"] = nc.dram_tensor("x1T", [D, T], F32, kind="ExternalOutput")
    es = ExitStack()
    ctx = Ctx(nc, es)
    with es:
        t = alloc_common(ctx)
        emit_phase0(ctx, t, d)
        ctx.build_block()
    return nc


def build_p1(upto="G"):
    nc = bass.Bass("TRN2", target_bir_lowering=False)
    d = {}
    declare_p1(nc, d, x1_is_input=True)
    es = ExitStack()
    ctx = Ctx(nc, es)
    with es:
        t = alloc_common(ctx)
        emit_phase1(ctx, t, d, load_consts=True, upto=upto)
        ctx.build_block()
    return nc


def kernel_unfused(**inputs):
    x = np.asarray(inputs["L0_attn_x"])
    com = host_prep(inputs)
    xouts = {}
    nc0 = build_p0()
    maps = []
    for core in range(NCORE):
        m = {k: com[k] for k in _P0_KEYS if k != "x0T"}
        xc = x[core * SPC:(core + 1) * SPC].reshape(T, D)
        m["x0T"] = np.ascontiguousarray(xc.T)
        maps.append(m)
    r0 = run_bass_kernel_spmd(nc0, maps, core_ids=list(range(NCORE)))
    nc1 = build_p1()
    maps = []
    for core in range(NCORE):
        m = {k: com[k] for k in _P1_KEYS}
        m["x1T"] = np.asarray(r0.results[core]["x1T"])
        maps.append(m)
    r1 = run_bass_kernel_spmd(nc1, maps, core_ids=list(range(NCORE)))
    nc2 = build_p2()
    maps = []
    for core in range(NCORE):
        m = {k: com[k] for k in _P2_KEYS}
        m["x2T"] = np.asarray(r1.results[core]["x2T"])
        maps.append(m)
    r2 = run_bass_kernel_spmd(nc2, maps, core_ids=list(range(NCORE)))
    nc3 = build_p3()
    maps = []
    for core in range(NCORE):
        m = {k: com[k] for k in _P3_KEYS}
        m["x3T"] = np.asarray(r2.results[core]["x3T"])
        maps.append(m)
    r3 = run_bass_kernel_spmd(nc3, maps, core_ids=list(range(NCORE)))
    out = np.empty((B, S, D), dtype=np.float32)
    for core in range(NCORE):
        out[core * SPC:(core + 1) * SPC] = (
            np.asarray(r3.results[core]["x4T"]).T.reshape(SPC, S, D))
    return out


def build_fused(dbg=False, nphases=4, p1upto="G"):
    nc = bass.Bass("TRN2", target_bir_lowering=False)
    d = {}
    declare_p0(nc, d)
    declare_p1(nc, d, x1_is_input=False)
    declare_p2(nc, d, x2_is_input=False)
    declare_p3(nc, d, x3_is_input=False)
    kind = "ExternalOutput" if dbg else "Internal"
    d["x1T"] = nc.dram_tensor("x1T", [D, T], F32, kind=kind)
    d["x2T"] = nc.dram_tensor("x2T_i", [D, T], F32, kind=kind)
    d["x3T"] = nc.dram_tensor("x3T_i", [D, T], F32, kind=kind)
    es = ExitStack()
    ctx = Ctx(nc, es)
    with es:
        t = alloc_common(ctx)
        emit_phase0(ctx, t, d)
        if nphases == -1:
            ctx.barrier()
        if nphases == -2:
            for nm in ("H2S", "CC", "CD", "H2LA", "H2LB", "X2S"):
                ctx.sem(nm)
        if nphases == -3:
            cols = ctx.sbuf("p1_cols", [128, 8], F32)
            es3 = ExitStack()
            a1 = ctx.sbuf("p1_lnf", [128, DC * TG], F32, es3)
            a2 = ctx.sbuf("p1_xint", [128, DC * T], BF16, es3)
            a3 = [ctx.sbuf(f"p1_h2{i}", [128, T], F32, es3) for i in range(2)]
            ctx.barrier()
            es3.close()
        if nphases == -4:
            ctx.drain_dma(["ST", "LD", "WL"])
            ctx.barrier()
        if nphases == -5:
            cols = ctx.sbuf("p1_cols", [128, 8], F32)
            es3 = ExitStack()
            names = [("p1_lnf", [128, DC * TG], F32), ("p1_xint", [128, DC * T], BF16),
                     ("p1_h20", [128, T], F32), ("p1_h21", [128, T], F32),
                     ("p1_wnat", [128, 2 * D], F32), ("p1_wTg", [128, DC * 128], F32),
                     ("p1_wTu", [128, DC * 128], F32), ("p1_wqg", [128, DC * 128], BF16),
                     ("p1_wqu", [128, DC * 128], BF16), ("p1_itmp", [128, 1024], I32),
                     ("p1_rig", [128, 128], F32), ("p1_riu", [128, 128], F32),
                     ("p1_rc2", [128, 2], BF16), ("p1_rr2", [1, 256], BF16),
                     ("p1_am2", [128, 1], F32), ("p1_cx", [1, 2], F32),
                     ("p1_si0", [128, TG], F32), ("p1_si1", [128, TG], F32)]
            for nm_, sh_, dt_ in names:
                ctx.sbuf(nm_, sh_, dt_, es3)
            ctx.barrier()
            es3.close()
        if nphases > 1:
            emit_phase1(ctx, t, d, upto=p1upto)
        if nphases > 2:
            emit_phase2(ctx, t, d)
        if nphases > 3:
            emit_phase3(ctx, t, d)
        ctx.build_block()
    return nc


_ALL_KEYS = sorted(set(_P0_KEYS + _P1_KEYS + _P2_KEYS + _P3_KEYS))


def kernel(**inputs):
    x = np.asarray(inputs["L0_attn_x"])
    com = host_prep(inputs)
    nc = build_fused()
    maps = []
    for core in range(NCORE):
        m = {k: com[k] for k in _ALL_KEYS if k != "x0T"}
        xc = x[core * SPC:(core + 1) * SPC].reshape(T, D)
        m["x0T"] = np.ascontiguousarray(xc.T)
        maps.append(m)
    r = run_bass_kernel_spmd(nc, maps, core_ids=list(range(NCORE)))
    out = np.empty((B, S, D), dtype=np.float32)
    for core in range(NCORE):
        out[core * SPC:(core + 1) * SPC] = (
            np.asarray(r.results[core]["x4T"]).T.reshape(SPC, S, D))
    return out


# version 09 (final)
# speedup vs baseline: 1.0341x; 1.0068x over previous
import numpy as np
import ml_dtypes
from contextlib import ExitStack
import concourse.bass as bass
import concourse.mybir as mybir
from concourse.bass_utils import run_bass_kernel_spmd

F32 = mybir.dt.float32
BF16 = mybir.dt.bfloat16
I32 = mybir.dt.int32
AF = mybir.ActivationFunctionType
ALU = mybir.AluOpType
AX = mybir.AxisListType.X
BF = ml_dtypes.bfloat16

B, S, D = 16, 2048, 1024
NCORE = 8
SPC = B // NCORE
T = SPC * S
TG = 512
NG = T // TG
GPS = S // TG
DC = D // 128
IC0 = 2816 // 128
IC1 = 3072 // 128
MAGIC = float(3 * 2.0 ** 22)
LN3 = 1.0986122886681098
QMAX = 127.0

ENGINES = ("sync", "scalar", "vector", "tensor", "gpsimd")


class Ctx:
    def __init__(self, nc, es):
        self.nc = nc
        self.es = es
        self.emit = {e: [] for e in ENGINES}
        self._sems = {}
        self.cnt = {}
        self.waits = []
        self.nbarrier = 0

    def sem(self, name):
        if name not in self._sems:
            self._sems[name] = self.es.enter_context(self.nc.semaphore(name))
            self.cnt[name] = 0
        return self._sems[name]

    def bump(self, name, n):
        self.sem(name)
        self.cnt[name] += n
        return self.cnt[name]

    def sbuf(self, name, shape, dt, scope=None):
        return (scope or self.es).enter_context(
            self.nc.sbuf_tensor(name, list(shape), dt))

    def psum(self, name, shape, dt, scope=None):
        return (scope or self.es).enter_context(
            self.nc.psum_tensor(name, list(shape), dt))

    def dma(self, q, out, in_, sem="LD"):
        s = self.sem(sem)
        v = self.bump(sem, 16)
        self.emit[q].append(
            lambda e, out=out, in_=in_, s=s: e.dma_start(out=out, in_=in_).then_inc(s, 16))
        return v

    def wait(self, eng, sem, val):
        if val <= 0:
            return
        s = self.sem(sem)
        self.waits.append((sem, val))
        self.emit[eng].append(lambda e, s=s, val=val: e.wait_ge(s, val))

    def op(self, eng, fn, sem=None, n=1):
        if sem is not None:
            s = self.sem(sem)
            v = self.bump(sem, n)
            self.emit[eng].append(lambda e, fn=fn, s=s, n=n: fn(e).then_inc(s, n))
            return v
        self.emit[eng].append(fn)
        return None

    def drain(self, eng="vector"):
        self.emit[eng].append(lambda e: e.drain())

    def barrier(self):
        self.nbarrier += 1
        s = self.sem("PH")
        tgt = len(ENGINES) * self.nbarrier
        self.cnt["PH"] = tgt
        for eng in ENGINES:
            self.emit[eng].append(lambda e, s=s: e.sem_inc(s, 1))
            self.emit[eng].append(lambda e, s=s, tgt=tgt: e.wait_ge(s, tgt))

    def drain_dma(self, sems):
        for sname in sems:
            if sname in self.cnt:
                self.wait("sync", sname, self.cnt[sname])

    def build_block(self):
        for sem_name, val in self.waits:
            assert val <= self.cnt[sem_name], (sem_name, val, self.cnt[sem_name])
        block = self.es.enter_context(self.nc.Block())
        for eng in ENGINES:
            if self.emit[eng]:
                def runner(e, eng=eng):
                    for f in self.emit[eng]:
                        f(e)
                getattr(block, eng)(runner)


def alloc_common(ctx):
    t = {}
    t["stage"] = ctx.sbuf("t_stage", [128, DC * TG], F32)
    t["xb"] = ctx.sbuf("t_xb", [128, DC * TG], BF16)
    t["sq"] = ctx.sbuf("t_sq", [128, DC * TG], BF16)
    t["rowf"] = ctx.sbuf("t_rowf", [1, TG], F32)
    t["rowf2"] = ctx.sbuf("t_rowf2", [1, TG], F32)
    t["rowf3"] = ctx.sbuf("t_rowf3", [1, TG], F32)
    t["hrow"] = ctx.sbuf("t_hrow", [1, 4 * TG], BF16)
    t["mbf"] = ctx.sbuf("t_mbf", [128, TG], F32)
    t["rbf"] = ctx.sbuf("t_rbf", [128, TG], F32)
    t["amax"] = ctx.sbuf("t_amax", [128, 1], F32)
    t["ones"] = ctx.sbuf("t_ones", [128, 128], BF16)
    t["idf"] = ctx.sbuf("t_idf", [128, 128], F32)
    t["idb"] = ctx.sbuf("t_idb", [128, 128], BF16)
    t["scl"] = ctx.sbuf("t_scl", [1, 32], F32)
    t["sclb"] = ctx.sbuf("t_sclb", [1, 32], BF16)
    t["col1"] = ctx.sbuf("t_col1", [128, 1], F32)
    t["col2"] = ctx.sbuf("t_col2", [128, 1], F32)
    t["xout"] = [ctx.sbuf(f"t_xo{i}", [128, TG], F32) for i in range(2)]
    t["ep1"] = ctx.sbuf("t_ep1", [128, TG], F32)
    t["ps"] = [ctx.psum(f"t_ps{i}", [128, TG], F32) for i in range(6)]
    t["psb"] = [ctx.psum(f"t_psb{i}", [128, 2 * TG], BF16) for i in range(2)]
    return t


def emit_ln_group(ctx, t, g, want_lnbf=True, want_absmax=False, eps=1e-5, lnbf_dst=None, lnbf_g=0, want_apply=True):
    V, SC, MM = "VE", "SC", "MM"
    ps_s0, ps_s1, ps_b0, ps_b1 = t["ps"][0], t["ps"][1], t["ps"][2], t["ps"][3]
    rowf, rowf2, rowf3, hrow = t["rowf"], t["rowf2"], t["rowf3"], t["hrow"]
    for c in range(DC):
        ctx.op("vector", lambda e, c=c: e.tensor_copy(
            out=t["xb"][:, c * TG:(c + 1) * TG], in_=t["stage"][:, c * TG:(c + 1) * TG]), None)
        ctx.op("vector", lambda e, c=c: e.tensor_tensor(
            out=t["sq"][:, c * TG:(c + 1) * TG], in0=t["xb"][:, c * TG:(c + 1) * TG],
            in1=t["xb"][:, c * TG:(c + 1) * TG], op=ALU.mult),
            V if c == DC - 1 else None)
    v_cast = ctx.cnt[V]
    ctx.wait("tensor", V, v_cast)
    for c in range(DC):
        ctx.op("tensor", lambda e, c=c: e.matmul(
            out=ps_s0[:1, :], lhsT=t["ones"][:, :1], rhs=t["xb"][:, c * TG:(c + 1) * TG],
            start=(c == 0), stop=(c == DC - 1)), None)
    for c in range(DC):
        ctx.op("tensor", lambda e, c=c: e.matmul(
            out=ps_s1[:1, :], lhsT=t["ones"][:, :1], rhs=t["sq"][:, c * TG:(c + 1) * TG],
            start=(c == 0), stop=(c == DC - 1)), MM if c == DC - 1 else None)
    v_stat = ctx.cnt[MM]
    ctx.wait("vector", MM, v_stat)
    ctx.op("vector", lambda e: e.tensor_scalar(
        out=rowf[:], in0=ps_s0[:1, :], scalar1=1.0 / D, scalar2=None, op0=ALU.mult), None)
    ctx.op("vector", lambda e: e.tensor_tensor(
        out=rowf2[:], in0=rowf[:], in1=rowf[:], op=ALU.mult), None)
    ctx.op("vector", lambda e: e.scalar_tensor_tensor(
        out=rowf3[:], in0=ps_s1[:1, :], scalar=1.0 / D, in1=rowf2[:],
        op0=ALU.mult, op1=ALU.subtract), None)
    v_var = ctx.op("vector", lambda e, eps=eps: e.tensor_scalar(
        out=rowf3[:], in0=rowf3[:], scalar1=eps, scalar2=None, op0=ALU.add), V)
    ctx.wait("scalar", V, v_var)
    s_sq = ctx.op("scalar", lambda e: e.activation(
        out=rowf2[:], in_=rowf3[:], func=AF.Sqrt), SC)
    ctx.wait("vector", SC, s_sq)
    ctx.op("vector", lambda e: e.reciprocal(out=rowf3[:], in_=rowf2[:]), None)
    ctx.op("vector", lambda e: e.tensor_copy(out=hrow[:, 0:TG], in_=rowf[:]), None)
    ctx.op("vector", lambda e: e.tensor_tensor(
        out=hrow[:, TG:2 * TG], in0=rowf[:], in1=hrow[:, 0:TG], op=ALU.subtract), None)
    ctx.op("vector", lambda e: e.tensor_copy(out=hrow[:, 2 * TG:3 * TG], in_=rowf3[:]), None)
    v_r = ctx.op("vector", lambda e: e.tensor_tensor(
        out=hrow[:, 3 * TG:4 * TG], in0=rowf3[:], in1=hrow[:, 2 * TG:3 * TG],
        op=ALU.subtract), V)
    ctx.wait("tensor", V, v_r)
    ctx.op("tensor", lambda e: e.matmul(
        out=ps_b0[:], lhsT=t["ones"][:1, :], rhs=hrow[:, 0:TG], start=True, stop=True), None)
    v_m = ctx.op("tensor", lambda e: e.matmul(
        out=ps_b1[:], lhsT=t["ones"][:1, :], rhs=hrow[:, TG:2 * TG], start=True, stop=True), MM)
    ctx.wait("scalar", MM, v_m)
    s_m = ctx.op("scalar", lambda e: e.activation(
        out=t["mbf"][:], in_=ps_b0[:], func=AF.Copy), SC)
    ctx.wait("vector", SC, s_m)
    v_mb = ctx.op("vector", lambda e: e.tensor_tensor(
        out=t["mbf"][:], in0=t["mbf"][:], in1=ps_b1[:], op=ALU.add), V)
    ctx.wait("tensor", V, v_mb)
    ctx.op("tensor", lambda e: e.matmul(
        out=ps_b0[:], lhsT=t["ones"][:1, :], rhs=hrow[:, 2 * TG:3 * TG],
        start=True, stop=True), None)
    v_rr = ctx.op("tensor", lambda e: e.matmul(
        out=ps_b1[:], lhsT=t["ones"][:1, :], rhs=hrow[:, 3 * TG:4 * TG],
        start=True, stop=True), MM)
    ctx.wait("scalar", MM, v_rr)
    s_r = ctx.op("scalar", lambda e: e.activation(
        out=t["rbf"][:], in_=ps_b0[:], func=AF.Copy), SC)
    ctx.wait("vector", SC, s_r)
    ctx.op("vector", lambda e: e.tensor_tensor(
        out=t["rbf"][:], in0=t["rbf"][:], in1=ps_b1[:], op=ALU.add), None)
    if not want_apply:
        return ctx.op("vector", lambda e: e.tensor_copy(
            out=t["rowf"][:, :1], in_=t["rowf"][:, :1]), "VE")
    for c in range(DC):
        ctx.op("vector", lambda e, c=c: e.tensor_tensor(
            out=t["ep1"][:], in0=t["stage"][:, c * TG:(c + 1) * TG],
            in1=t["mbf"][:], op=ALU.subtract), None)
        ctx.op("vector", lambda e, c=c: e.tensor_tensor(
            out=t["lnf"][:, c * TG:(c + 1) * TG], in0=t["ep1"][:], in1=t["rbf"][:],
            op=ALU.mult), None)
        if want_lnbf:
            dst = (t["lnbf"][:, c * TG:(c + 1) * TG] if lnbf_dst is None else
                   lnbf_dst[:, c * T + lnbf_g * TG: c * T + (lnbf_g + 1) * TG])
            ctx.op("vector", lambda e, c=c, dst=dst: e.tensor_copy(
                out=dst, in_=t["lnf"][:, c * TG:(c + 1) * TG]), None)
        if want_absmax:
            ctx.op("vector", lambda e, c=c: e.tensor_reduce(
                out=t["col1"][:], in_=t["lnf"][:, c * TG:(c + 1) * TG], axis=AX,
                op=ALU.max, apply_absolute_value=True), None)
            ctx.drain()
            if g == 0 and c == 0:
                ctx.op("vector", lambda e: e.tensor_copy(
                    out=t["amax"][:], in_=t["col1"][:]), None)
            else:
                ctx.op("vector", lambda e: e.tensor_tensor(
                    out=t["amax"][:], in0=t["amax"][:], in1=t["col1"][:],
                    op=ALU.max), None)
            ctx.drain()
    return ctx.op("vector", lambda e: e.tensor_copy(
        out=t["rowf"][:, :1], in_=t["rowf"][:, :1]), V)


def load_stage(ctx, t, src_ap, g, guard_v=0):
    ctx.wait("sync", "VE", guard_v)
    v = 0
    for c in range(DC):
        v = ctx.dma("sync", t["stage"][:, c * TG:(c + 1) * TG],
                    src_ap[c * 128:(c + 1) * 128, g * TG:(g + 1) * TG], "LD")
    return v


def bcast_scalar_col(ctx, t, src11, dst_col):
    V, SC, MM = "VE", "SC", "MM"
    sb = t["sclb"]
    ctx.op("vector", lambda e: e.tensor_copy(out=sb[:, 0:1], in_=src11), None)
    ctx.drain()
    v1 = ctx.op("vector", lambda e: e.tensor_tensor(
        out=sb[:, 1:2], in0=src11, in1=sb[:, 0:1], op=ALU.subtract), V)
    ctx.drain()
    ctx.wait("tensor", V, v1)
    ctx.op("tensor", lambda e: e.matmul(
        out=t["ps"][2][:, 0:1], lhsT=t["ones"][:1, :], rhs=sb[:, 0:1],
        start=True, stop=True), None)
    v2 = ctx.op("tensor", lambda e: e.matmul(
        out=t["ps"][3][:, 0:1], lhsT=t["ones"][:1, :], rhs=sb[:, 1:2],
        start=True, stop=True), MM)
    ctx.wait("scalar", MM, v2)
    s1 = ctx.op("scalar", lambda e: e.activation(
        out=dst_col, in_=t["ps"][2][:, 0:1], func=AF.Copy), SC)
    ctx.wait("vector", SC, s1)
    r = ctx.op("vector", lambda e: e.tensor_tensor(
        out=dst_col, in0=dst_col, in1=t["ps"][3][:, 0:1], op=ALU.add), V)
    ctx.drain()
    return r



def emit_phase0(ctx, t, d):
    t = dict(t)
    V, SC, MM, LD, ST = "VE", "SC", "MM", "LD", "ST"
    es = ExitStack()
    t["lnf"] = ctx.sbuf("p0_lnf", [128, DC * TG], F32, es)
    t["lnbf"] = ctx.sbuf("p0_lnbf", [128, DC * TG], BF16, es)
    wd_bf = ctx.sbuf("p0_wd", [128, DC * D], BF16, es)
    wb_bf = ctx.sbuf("p0_wb", [128, 128], BF16, es)
    wo_bf = ctx.sbuf("p0_wo", [16, D], BF16, es)
    wstage = ctx.sbuf("p0_wst", [128, D], F32, es)
    Atmp = ctx.sbuf("p0_At", [16, 1], F32, es)
    bdelta = ctx.sbuf("p0_bd", [128, 8], F32, es)
    bB = ctx.sbuf("p0_bB", [16, 1], F32, es)
    Acol = ctx.sbuf("p0_Ac", [16, 1], F32, es)
    Dcol = ctx.sbuf("p0_Dc", [128, 8], F32, es)
    bout = ctx.sbuf("p0_bo", [128, 8], F32, es)
    sg8 = ctx.sbuf("p0_sg8", [128, 8 * TG], BF16, es)
    delta_bf = ctx.sbuf("p0_deltab", [1, TG], BF16, es)
    mh_bf = ctx.sbuf("p0_mh", [1, TG], BF16, es)
    dm_bf = ctx.sbuf("p0_dm", [1, TG], BF16, es)
    Bm = ctx.sbuf("p0_Bm", [16, TG], BF16, es)
    ab_t = ctx.sbuf("p0_ab", [16, TG], BF16, es)
    bu_t = ctx.sbuf("p0_bu", [16, TG], BF16, es)
    st_bf = ctx.sbuf("p0_st", [16, TG], BF16, es)
    carry = ctx.sbuf("p0_carry", [16, 1], BF16, es)
    ctx.dma("sync", t["ones"][:], d["ones_bf"].ap(), LD)
    ctx.dma("sync", bdelta[:], d["bdelta8"].ap(), LD)
    ctx.dma("sync", bB[:], d["bB"].ap(), LD)
    ctx.dma("sync", Acol[:], d["Alog"].ap(), LD)
    ctx.dma("sync", Dcol[:], d["Dcol8"].ap(), LD)
    ctx.dma("sync", bout[:], d["bout8"].ap(), LD)
    ctx.dma("sync", wstage[:, :128], d["WbT128"].ap(), LD)
    ctx.dma("sync", t["idf"][:], d["ident_f"].ap(), LD)
    ctx.dma("sync", t["idb"][:], d["ident_b"].ap(), LD)
    v = ctx.dma("sync", t["lnf"][:16, :D], d["WoT"].ap(), LD)
    ctx.wait("scalar", LD, v)
    ctx.op("scalar", lambda e: e.activation(out=Atmp[:], in_=Acol[:], func=AF.Exp), None)
    ctx.drain("scalar")
    ctx.op("scalar", lambda e: e.activation(
        out=Acol[:], in_=Atmp[:], func=AF.Copy, scale=-1.0), SC)
    ctx.drain("scalar")
    ctx.wait("vector", LD, v)
    ctx.op("vector", lambda e: e.tensor_copy(out=wb_bf[:], in_=wstage[:, :128]), None)
    ctx.op("vector", lambda e: e.tensor_copy(out=wo_bf[:], in_=t["lnf"][:16, :D]), V)
    for c in range(DC):
        ctx.wait("sync", V, ctx.cnt[V])
        v = ctx.dma("sync", wstage[:, :D], d["WdT"].ap()[c * 128:(c + 1) * 128, :], "WL")
        ctx.wait("vector", "WL", v)
        ctx.op("vector", lambda e, c=c: e.tensor_copy(
            out=wd_bf[:, c * D:(c + 1) * D], in_=wstage[:, :D]), V)

    last_ep_v = ctx.cnt[V]
    ctx.sem(ST)
    st_base = ctx.cnt[ST]
    for g in range(NG):
        lv = load_stage(ctx, t, d["x0T"].ap(), g, guard_v=last_ep_v)
        ctx.wait("vector", LD, lv)
        v_ln = emit_ln_group(ctx, t, g, want_lnbf=True, want_absmax=False)
        ctx.wait("tensor", V, v_ln)
        sv_prev = [0, 0]
        for ec in range(8):
            pm = t["ps"][4 + ec % 2]
            if ec >= 2:
                ctx.wait("tensor", SC, sv_prev[ec % 2])
            for k in range(DC):
                ctx.op("tensor", lambda e, ec=ec, k=k, pm=pm: e.matmul(
                    out=pm[:], lhsT=wd_bf[:, k * D + ec * 128: k * D + (ec + 1) * 128],
                    rhs=t["lnbf"][:, k * TG:(k + 1) * TG],
                    start=(k == 0), stop=(k == DC - 1)), MM if k == DC - 1 else None)
            mmv = ctx.cnt[MM]
            ctx.wait("scalar", MM, mmv)
            sv_prev[ec % 2] = ctx.op("scalar", lambda e, ec=ec, pm=pm: e.activation(
                out=sg8[:, ec * TG:(ec + 1) * TG], in_=pm[:], func=AF.Sigmoid,
                bias=bdelta[:, ec:ec + 1]), SC)
        ctx.wait("tensor", SC, sv_prev[1])
        for ec in range(8):
            ctx.op("tensor", lambda e, ec=ec: e.matmul(
                out=t["ps"][0][:1, :], lhsT=t["ones"][:, :1],
                rhs=sg8[:, ec * TG:(ec + 1) * TG],
                start=(ec == 0), stop=(ec == 7)), MM if ec == 7 else None)
        v_dsum = ctx.cnt[MM]
        ctx.wait("vector", MM, v_dsum)
        v_d = ctx.op("vector", lambda e: e.tensor_scalar(
            out=delta_bf[:], in0=t["ps"][0][:1, :], scalar1=0.099 / D, scalar2=0.001,
            op0=ALU.mult, op1=ALU.add), V)
        ctx.wait("tensor", V, v_d)
        for k in range(DC):
            ctx.op("tensor", lambda e, k=k: e.matmul(
                out=t["ps"][1][:16, :], lhsT=wb_bf[:, k * 16:(k + 1) * 16],
                rhs=t["lnbf"][:, k * TG:(k + 1) * TG],
                start=(k == 0), stop=(k == DC - 1)), None)
        for k in range(DC):
            ctx.op("tensor", lambda e, k=k: e.matmul(
                out=t["ps"][0][:1, :], lhsT=t["ones"][:, :1],
                rhs=t["lnbf"][:, k * TG:(k + 1) * TG],
                start=(k == 0), stop=(k == DC - 1)), MM if k == DC - 1 else None)
        v_bmmh = ctx.cnt[MM]
        ctx.wait("scalar", MM, v_bmmh)
        s_bm = ctx.op("scalar", lambda e: e.activation(
            out=Bm[:], in_=t["ps"][1][:16, :], func=AF.Identity, bias=bB[:]), SC)
        ctx.wait("vector", MM, v_bmmh)
        ctx.op("vector", lambda e: e.tensor_scalar(
            out=mh_bf[:], in0=t["ps"][0][:1, :], scalar1=1.0 / D, scalar2=None,
            op0=ALU.mult), None)
        v_dm = ctx.op("vector", lambda e: e.tensor_tensor(
            out=dm_bf[:], in0=delta_bf[:], in1=mh_bf[:], op=ALU.mult), V)
        ctx.wait("tensor", V, v_dm)
        ctx.op("tensor", lambda e: e.matmul(
            out=t["ps"][2][:16, :], lhsT=t["ones"][:1, :16], rhs=delta_bf[:],
            start=True, stop=True), None)
        v_bc2 = ctx.op("tensor", lambda e: e.matmul(
            out=t["ps"][3][:16, :], lhsT=t["ones"][:1, :16], rhs=dm_bf[:],
            start=True, stop=True), MM)
        ctx.wait("scalar", MM, v_bc2)
        s_ab = ctx.op("scalar", lambda e: e.activation(
            out=ab_t[:], in_=t["ps"][2][:16, :], func=AF.Exp, scale=Acol[:]), SC)
        ctx.wait("vector", SC, s_ab)
        ctx.op("vector", lambda e: e.tensor_tensor(
            out=bu_t[:], in0=t["ps"][3][:16, :], in1=Bm[:], op=ALU.mult), None)
        if g % GPS == 0:
            ctx.op("vector", lambda e: e.tensor_tensor_scan(
                out=st_bf[:], data0=ab_t[:], data1=bu_t[:], initial=0.0,
                op0=ALU.mult, op1=ALU.add), None)
        else:
            ctx.op("vector", lambda e: e.tensor_tensor_scan(
                out=st_bf[:], data0=ab_t[:], data1=bu_t[:], initial=carry[:],
                op0=ALU.mult, op1=ALU.add), None)
        ctx.drain()
        v_scan = ctx.op("vector", lambda e: e.tensor_copy(
            out=carry[:], in_=st_bf[:, TG - 1:TG]), V)
        ctx.drain()
        ctx.wait("tensor", V, v_scan)
        prev_vv = None
        for c in range(DC):
            if prev_vv is not None:
                ctx.wait("tensor", V, prev_vv)
            mmv = ctx.op("tensor", lambda e, c=c: e.matmul(
                out=t["ps"][4][:], lhsT=wo_bf[:, c * 128:(c + 1) * 128],
                rhs=st_bf[:], start=True, stop=True), MM)
            j = g * DC + c
            if j >= 2:
                ctx.wait("vector", ST, st_base + 16 * (j - 1))
            ctx.wait("vector", MM, mmv)
            ctx.op("vector", lambda e, c=c: e.scalar_tensor_tensor(
                out=t["ep1"][:], in0=t["lnf"][:, c * TG:(c + 1) * TG],
                scalar=Dcol[:, c:c + 1], in1=t["stage"][:, c * TG:(c + 1) * TG],
                op0=ALU.mult, op1=ALU.add), None)
            vv = ctx.op("vector", lambda e, c=c, j=j: e.scalar_tensor_tensor(
                out=t["xout"][j % 2][:], in0=t["ps"][4][:], scalar=bout[:, c:c + 1],
                in1=t["ep1"][:], op0=ALU.add, op1=ALU.add), V)
            prev_vv = vv
            ctx.wait("gpsimd", V, vv)
            ctx.dma("gpsimd", d["x1T"].ap()[c * 128:(c + 1) * 128,
                                            g * TG:(g + 1) * TG],
                    t["xout"][j % 2][:], ST)
        last_ep_v = prev_vv
    ctx.drain_dma([ST, LD, "WL"])
    ctx.barrier()
    es.close()



def emit_phase1(ctx, t, d, load_consts=False, upto="G"):
    t = dict(t)
    V, SC, MM, LD, ST = "VE", "SC", "MM", "LD", "ST"
    cols = ctx.sbuf("p1_cols", [128, 8], F32)
    es = ExitStack()
    t["lnf"] = ctx.sbuf("p1_lnf", [128, DC * TG], F32, es)
    xint = ctx.sbuf("p1_xint", [128, DC * T], BF16, es)
    h2sl = [ctx.sbuf(f"p1_h2{i}", [128, T // 2], F32, es) for i in range(2)]
    wnat = [ctx.sbuf(f"p1_wnat{i}", [128, 2 * D], F32, es) for i in range(2)]
    wTg = [ctx.sbuf(f"p1_wTg{i}", [128, DC * 128], F32, es) for i in range(2)]
    wTu = [ctx.sbuf(f"p1_wTu{i}", [128, DC * 128], F32, es) for i in range(2)]
    wqg = [ctx.sbuf(f"p1_wqg{i}", [128, DC * 128], BF16, es) for i in range(2)]
    wqu = [ctx.sbuf(f"p1_wqu{i}", [128, DC * 128], BF16, es) for i in range(2)]
    itmp = ctx.sbuf("p1_itmp", [128, 1024], I32, es)
    rinvg = ctx.sbuf("p1_rig", [128, 128], F32, es)
    rinvu = ctx.sbuf("p1_riu", [128, 128], F32, es)
    rcol2 = ctx.sbuf("p1_rc2", [128, 2], BF16, es)
    rrow2 = ctx.sbuf("p1_rr2", [1, 256], BF16, es)
    am2 = ctx.sbuf("p1_am2", [128, 1], F32, es)
    clipx = ctx.sbuf("p1_cx", [1, 2], F32, es)
    silu_t = [ctx.sbuf(f"p1_si{i}", [128, TG], F32, es) for i in range(2)]
    if load_consts:
        ctx.dma("sync", t["ones"][:], d["ones_bf"].ap(), LD)
        ctx.dma("sync", t["idf"][:], d["ident_f"].ap(), LD)
        ctx.dma("sync", t["idb"][:], d["ident_b"].ap(), LD)
    for nm in (ST, "H2S", "CC", "CD", "H2LA", "H2LB", "X2S", "WL", LD):
        ctx.sem(nm)
    if upto == "NOP":
        ctx.drain_dma([ST, LD, "WL"])
        ctx.barrier()
        es.close()
        return
    if upto == "LOAD1":
        lv = load_stage(ctx, t, d["x1T"].ap(), 0, guard_v=0)
        ctx.wait("vector", LD, lv)
        vv = ctx.op("vector", lambda e: e.tensor_copy(out=t["xb"][:, 0:TG], in_=t["stage"][:, 0:TG]), "VE")
        ctx.wait("gpsimd", "VE", vv)
        ctx.dma("gpsimd", d["x2T"].ap()[0:128, 0:TG], t["xb"][:, 0:TG], "X2S")
        ctx.drain_dma([ST, LD, "WL", "X2S"])
        ctx.barrier()
        es.close()
        return
    last_v = 0
    for g in range(NG):
        lv = load_stage(ctx, t, d["x1T"].ap(), g, guard_v=last_v)
        ctx.wait("vector", LD, lv)
        ctx.wait("vector", ST, ctx.cnt[ST])
        last_v = emit_ln_group(ctx, t, g, want_lnbf=False, want_absmax=True)
        ctx.wait("gpsimd", V, last_v)
        for c in range(DC):
            ctx.dma("gpsimd", d["lnfs"].ap()[c * 128:(c + 1) * 128, g * TG:(g + 1) * TG],
                    t["lnf"][:, c * TG:(c + 1) * TG], ST)
    if upto == "A":
        vv = ctx.op("vector", lambda e: e.tensor_copy(out=t["xout"][0][:], in_=t["stage"][:, 0:TG]), "VE")
        ctx.wait("gpsimd", "VE", vv)
        ctx.dma("gpsimd", d["x2T"].ap()[0:128, 0:TG], t["xout"][0][:], "X2S")
        ctx.drain_dma(["X2S", "ST", "LD", "WL", "H2S", "CC"])
        ctx.barrier()
        es.close()
        return

    ctx.dma("sync", clipx[:, 0:2], d["clip2"].ap(), LD)
    lv_c = ctx.cnt[LD]
    ctx.wait("tensor", V, last_v)
    mt = ctx.op("tensor", lambda e: e.transpose(
        out=t["ps"][4][:1, :128], in_=t["amax"][:], identity=t["idf"][:]), MM)
    ctx.wait("vector", MM, mt)
    vv = ctx.op("vector", lambda e: e.tensor_reduce(
        out=t["scl"][:, 0:1], in_=t["ps"][4][:1, :128], axis=AX, op=ALU.max), V)
    ctx.wait("gpsimd", V, vv)
    ctx.dma("gpsimd", d["cci"].ap(), t["scl"][:1, 0:1], "CC")
    ctx.wait("gpsimd", "CC", ctx.cnt["CC"])
    cv = ctx.op("gpsimd", lambda e: e.collective_compute(
        "AllReduce", ALU.max, replica_groups=[list(range(NCORE))],
        ins=[d["cci"].ap()], outs=[d["cco"].ap()]), "CD")
    ctx.wait("gpsimd", "CD", cv)
    ctx.dma("gpsimd", t["scl"][:1, 2:3], d["cco"].ap(), "CC")
    ctx.wait("scalar", "CC", ctx.cnt["CC"])
    ctx.wait("scalar", LD, lv_c)
    s_sig = ctx.op("scalar", lambda e: e.activation(
        out=t["scl"][:, 4:5], in_=clipx[:, 0:1], func=AF.Sigmoid), SC)
    ctx.wait("vector", SC, s_sig)
    ctx.wait("vector", "CC", ctx.cnt["CC"])
    ctx.op("vector", lambda e: e.tensor_tensor(
        out=t["scl"][:, 5:6], in0=t["scl"][:, 2:3], in1=t["scl"][:, 4:5], op=ALU.mult), None)
    ctx.drain()
    ctx.op("vector", lambda e: e.reciprocal(out=t["scl"][:, 6:7], in_=t["scl"][:, 5:6]), None)
    ctx.drain()
    ctx.op("vector", lambda e: e.tensor_scalar(
        out=t["scl"][:, 7:8], in0=t["scl"][:, 6:7], scalar1=QMAX, scalar2=None,
        op0=ALU.mult), None)
    ctx.drain()
    ctx.op("vector", lambda e: e.tensor_scalar(
        out=t["scl"][:, 8:9], in0=t["scl"][:, 5:6], scalar1=1.0 / QMAX, scalar2=None,
        op0=ALU.mult), V)
    ctx.drain()
    bcast_scalar_col(ctx, t, t["scl"][:1, 7:8], cols[:, 4:5])
    v_c = bcast_scalar_col(ctx, t, t["scl"][:1, 8:9], cols[:, 5:6])
    if upto == "Cdump":
        ctx.op("vector", lambda e: e.tensor_copy(out=t["xout"][0][:1, 0:32], in_=t["scl"][:]), None)
        ctx.op("vector", lambda e: e.tensor_copy(out=t["xout"][0][:, 32:40], in_=cols[:]), None)
        vv = ctx.op("vector", lambda e: e.tensor_copy(out=t["xout"][0][:, 40:41], in_=t["amax"][:]), "VE")
        ctx.wait("gpsimd", "VE", vv)
        ctx.dma("gpsimd", d["dbg"].ap()[0:128, 0:TG], t["xout"][0][:], "X2S")
        ctx.drain_dma(["X2S", "ST", "LD", "WL", "H2S", "CC"])
        ctx.barrier()
        es.close()
        return
    if upto == "C":
        vv = ctx.op("vector", lambda e: e.tensor_copy(out=t["xout"][0][:], in_=t["stage"][:, 0:TG]), "VE")
        ctx.wait("gpsimd", "VE", vv)
        ctx.dma("gpsimd", d["x2T"].ap()[0:128, 0:TG], t["xout"][0][:], "X2S")
        ctx.drain_dma(["X2S", "ST", "LD", "WL", "H2S", "CC"])
        ctx.barrier()
        es.close()
        return

    ctx.wait("sync", ST, ctx.cnt[ST])
    vq_last = ctx.cnt[V]
    for g in range(NG):
        ctx.wait("sync", V, vq_last)
        lv = 0
        for c in range(DC):
            lv = ctx.dma("sync", t["lnf"][:, c * TG:(c + 1) * TG],
                         d["lnfs"].ap()[c * 128:(c + 1) * 128, g * TG:(g + 1) * TG], LD)
        ctx.wait("vector", LD, lv)
        for c in range(DC):
            ctx.op("vector", lambda e, c=c: e.tensor_scalar(
                out=t["lnf"][:, c * TG:(c + 1) * TG], in0=t["lnf"][:, c * TG:(c + 1) * TG],
                scalar1=cols[:, 4:5], scalar2=MAGIC, op0=ALU.mult, op1=ALU.add), None)
            ctx.op("vector", lambda e, c=c: e.tensor_scalar(
                out=t["lnf"][:, c * TG:(c + 1) * TG], in0=t["lnf"][:, c * TG:(c + 1) * TG],
                scalar1=MAGIC, scalar2=-128.0, op0=ALU.subtract, op1=ALU.max), None)
            vq_last = ctx.op("vector", lambda e, c=c, g=g: e.tensor_scalar(
                out=xint[:, c * T + g * TG: c * T + (g + 1) * TG],
                in0=t["lnf"][:, c * TG:(c + 1) * TG],
                scalar1=127.0, scalar2=None, op0=ALU.min), V)

    if upto == "Ddump":
        for c in range(DC):
            vv = ctx.op("vector", lambda e, c=c: e.tensor_copy(out=t["xout"][0][:], in_=xint[:, c * T: c * T + TG]), "VE")
            ctx.wait("gpsimd", "VE", vv)
            ctx.dma("gpsimd", d["dbg"].ap()[c * 128:(c + 1) * 128, 0:TG], t["xout"][0][:], "X2S")
            ctx.wait("vector", "X2S", ctx.cnt["X2S"])
        ctx.drain_dma(["X2S", "ST", "LD", "WL", "H2S", "CC"])
        ctx.barrier()
        es.close()
        return
    if upto == "D":
        vv = ctx.op("vector", lambda e: e.tensor_copy(out=t["xout"][0][:], in_=t["stage"][:, 0:TG]), "VE")
        ctx.wait("gpsimd", "VE", vv)
        ctx.dma("gpsimd", d["x2T"].ap()[0:128, 0:TG], t["xout"][0][:], "X2S")
        ctx.drain_dma(["X2S", "ST", "LD", "WL", "H2S", "CC"])
        ctx.barrier()
        es.close()
        return
    h2_store_v = [0, 0]
    quant_v = [0, 0]
    mm_done = [0, 0]
    wl_v = [0, 0]
    h2v = [0, 0]
    def p1_load(i):
        ib = i % 2
        ctx.wait("sync", V, quant_v[ib])
        ctx.dma("sync", wnat[ib][:, 0:D], d["wg_nat"].ap()[i * 128:(i + 1) * 128, :], "WL")
        ctx.dma("sync", wnat[ib][:, D:2 * D], d["wu_nat"].ap()[i * 128:(i + 1) * 128, :], "WL")
        for c in range(DC):
            ctx.dma("sync", wTg[ib][:, c * 128:(c + 1) * 128],
                    d["wgT"].ap()[c * 128:(c + 1) * 128, i * 128:(i + 1) * 128], "WL")
        for c in range(DC):
            ctx.dma("sync", wTu[ib][:, c * 128:(c + 1) * 128],
                    d["wuT"].ap()[c * 128:(c + 1) * 128, i * 128:(i + 1) * 128], "WL")
        wl_v[ib] = ctx.cnt["WL"]
    p1_load(0)
    for i in range(IC0):
        ib = i % 2
        if i + 1 < IC0:
            p1_load(i + 1)
        lv = wl_v[ib]
        ctx.wait("vector", "WL", lv)
        ctx.wait("vector", MM, mm_done[ib])
        for w, natsl, wT, wq, rinvb, smax_i, esc_i in (
                (0, wnat[ib][:, 0:D], wTg[ib], wqg[ib], rinvg, 0, 2 if ib == 0 else 1),
                (1, wnat[ib][:, D:2 * D], wTu[ib], wqu[ib], rinvu, 3, 6 if ib == 0 else 7)):
            ctx.op("vector", lambda e, natsl=natsl, smax_i=smax_i: e.tensor_reduce(
                out=cols[:, smax_i:smax_i + 1], in_=natsl, axis=AX, op=ALU.max,
                apply_absolute_value=True), None)
            ctx.drain()
            ctx.op("vector", lambda e, smax_i=smax_i: e.reciprocal(
                out=t["col1"][:], in_=cols[:, smax_i:smax_i + 1]), None)
            ctx.drain()
            ctx.op("vector", lambda e: e.tensor_scalar(
                out=t["col2"][:], in0=t["col1"][:], scalar1=QMAX, scalar2=None,
                op0=ALU.mult), None)
            ctx.drain()
            ctx.op("vector", lambda e: e.tensor_copy(out=rcol2[:, 0:1], in_=t["col2"][:]), None)
            ctx.drain()
            v1 = ctx.op("vector", lambda e: e.tensor_tensor(
                out=rcol2[:, 1:2], in0=t["col2"][:], in1=rcol2[:, 0:1],
                op=ALU.subtract), V)
            ctx.drain()
            ctx.wait("tensor", V, v1)
            ctx.op("tensor", lambda e: e.transpose(
                out=t["psb"][0][:1, 0:128], in_=rcol2[:, 0:1], identity=t["idb"][:]), None)
            mt = ctx.op("tensor", lambda e: e.transpose(
                out=t["psb"][0][:1, 128:256], in_=rcol2[:, 1:2], identity=t["idb"][:]), MM)
            ctx.wait("vector", MM, mt)
            v2 = ctx.op("vector", lambda e: e.tensor_copy(
                out=rrow2[:], in_=t["psb"][0][:1, 0:256]), V)
            ctx.wait("tensor", V, v2)
            ctx.op("tensor", lambda e: e.matmul(
                out=t["ps"][2][:, :128], lhsT=t["ones"][:1, :], rhs=rrow2[:, 0:128],
                start=True, stop=True), None)
            v3 = ctx.op("tensor", lambda e: e.matmul(
                out=t["ps"][3][:, :128], lhsT=t["ones"][:1, :], rhs=rrow2[:, 128:256],
                start=True, stop=True), MM)
            ctx.wait("scalar", MM, v3)
            s1 = ctx.op("scalar", lambda e, rinvb=rinvb: e.activation(
                out=rinvb[:], in_=t["ps"][2][:, :128], func=AF.Copy), SC)
            ctx.wait("vector", SC, s1)
            ctx.op("vector", lambda e, rinvb=rinvb: e.tensor_tensor(
                out=rinvb[:], in0=rinvb[:], in1=t["ps"][3][:, :128], op=ALU.add), None)
            for c in range(DC):
                ctx.op("vector", lambda e, c=c, wT=wT, rinvb=rinvb: e.tensor_tensor(
                    out=itmp[:, c * 128:(c + 1) * 128], in0=wT[:, c * 128:(c + 1) * 128],
                    in1=rinvb[:], op=ALU.mult), None)
            for c in range(DC):
                ctx.op("vector", lambda e, c=c, wq=wq: e.tensor_copy(
                    out=wq[:, c * 128:(c + 1) * 128], in_=itmp[:, c * 128:(c + 1) * 128]),
                    None)
            ctx.op("vector", lambda e, smax_i=smax_i: e.tensor_tensor(
                out=t["col1"][:], in0=cols[:, smax_i:smax_i + 1], in1=cols[:, 5:6],
                op=ALU.mult), None)
            ctx.drain()
            quant_v[ib] = ctx.op("vector", lambda e, esc_i=esc_i: e.tensor_scalar(
                out=cols[:, esc_i:esc_i + 1], in0=t["col1"][:], scalar1=1.0 / QMAX,
                scalar2=None, op0=ALU.mult), V)
            ctx.drain()
        ctx.wait("tensor", V, quant_v[ib])
        if i == 0:
            ctx.wait("tensor", V, vq_last)
        for tg in range(NG):
            pA = t["ps"][4] if tg % 2 == 0 else t["ps"][2]
            pB = t["ps"][5] if tg % 2 == 0 else t["ps"][3]
            if tg >= 2:
                ctx.wait("tensor", V, h2v[tg % 2])
            for k in range(DC):
                ctx.op("tensor", lambda e, k=k, tg=tg, pA=pA, ib=ib: e.matmul(
                    out=pA[:], lhsT=wqg[ib][:, k * 128:(k + 1) * 128],
                    rhs=xint[:, k * T + tg * TG: k * T + (tg + 1) * TG],
                    start=(k == 0), stop=(k == DC - 1)), None)
            for k in range(DC):
                ctx.op("tensor", lambda e, k=k, tg=tg, pB=pB, ib=ib: e.matmul(
                    out=pB[:], lhsT=wqu[ib][:, k * 128:(k + 1) * 128],
                    rhs=xint[:, k * T + tg * TG: k * T + (tg + 1) * TG],
                    start=(k == 0), stop=(k == DC - 1)), MM if k == DC - 1 else None)
            mmv = ctx.cnt[MM]
            ctx.wait("scalar", MM, mmv)
            sv = ctx.op("scalar", lambda e, tg=tg, pA=pA, gi=(2 if ib == 0 else 1): e.activation(
                out=silu_t[tg % 2][:], in_=pA[:], func=AF.Silu,
                scale=cols[:, gi:gi + 1]), SC)
            ctx.wait("vector", SC, sv)
            half = tg // 4
            if tg % 4 == 0:
                ctx.wait("vector", "H2S", h2_store_v[half])
            h2v[tg % 2] = ctx.op("vector", lambda e, tg=tg, pB=pB, half=half, ui=(6 if ib == 0 else 7): e.scalar_tensor_tensor(
                out=h2sl[half][:, (tg % 4) * TG:(tg % 4 + 1) * TG], in0=pB[:],
                scalar=cols[:, ui:ui + 1], in1=silu_t[tg % 2][:],
                op0=ALU.mult, op1=ALU.mult), V)
            if tg % 4 == 3:
                ctx.op("vector", lambda e, half=half: e.tensor_reduce(
                    out=t["col1"][:], in_=h2sl[half][:], axis=AX, op=ALU.max,
                    apply_absolute_value=True), None)
                ctx.drain()
                if i == 0 and half == 0:
                    hv = ctx.op("vector", lambda e: e.tensor_copy(out=am2[:], in_=t["col1"][:]), V)
                else:
                    hv = ctx.op("vector", lambda e: e.tensor_tensor(
                        out=am2[:], in0=am2[:], in1=t["col1"][:], op=ALU.max), V)
                ctx.drain()
                ctx.wait("gpsimd", V, hv)
                h2_store_v[half] = ctx.dma(
                    "gpsimd", d["h2s"].ap()[i * 128:(i + 1) * 128,
                                            half * (T // 2):(half + 1) * (T // 2)],
                    h2sl[half][:], "H2S")
        mm_done[ib] = ctx.cnt[MM]
    if upto == "E":
        vv = ctx.op("vector", lambda e: e.tensor_copy(out=t["xout"][0][:], in_=t["stage"][:, 0:TG]), "VE")
        ctx.wait("gpsimd", "VE", vv)
        ctx.dma("gpsimd", d["x2T"].ap()[0:128, 0:TG], t["xout"][0][:], "X2S")
        ctx.drain_dma(["X2S", "ST", "LD", "WL", "H2S", "CC"])
        ctx.barrier()
        es.close()
        return

    if upto == "Edump":
        ctx.wait("gpsimd", "H2S", ctx.cnt["H2S"])
        for i in range(DC):
            ctx.dma("gpsimd", d["dbg"].ap()[i * 128:(i + 1) * 128, :],
                    d["h2s"].ap()[i * 128:(i + 1) * 128, :], "X2S")
        ctx.drain_dma(["X2S", "ST", "LD", "WL", "H2S", "CC"])
        ctx.barrier()
        es.close()
        return
    ctx.wait("tensor", V, ctx.cnt[V])
    mt = ctx.op("tensor", lambda e: e.transpose(
        out=t["ps"][4][:1, :128], in_=am2[:], identity=t["idf"][:]), MM)
    ctx.wait("vector", MM, mt)
    vv = ctx.op("vector", lambda e: e.tensor_reduce(
        out=t["scl"][:, 10:11], in_=t["ps"][4][:1, :128], axis=AX, op=ALU.max), V)
    ctx.wait("gpsimd", V, vv)
    ctx.dma("gpsimd", d["cci"].ap(), t["scl"][:1, 10:11], "CC")
    ctx.wait("gpsimd", "CC", ctx.cnt["CC"])
    cv = ctx.op("gpsimd", lambda e: e.collective_compute(
        "AllReduce", ALU.max, replica_groups=[list(range(NCORE))],
        ins=[d["cci"].ap()], outs=[d["cco"].ap()]), "CD")
    ctx.wait("gpsimd", "CD", cv)
    ctx.dma("gpsimd", t["scl"][:1, 12:13], d["cco"].ap(), "CC")
    ctx.wait("scalar", "CC", ctx.cnt["CC"])
    s_sig = ctx.op("scalar", lambda e: e.activation(
        out=t["scl"][:, 14:15], in_=clipx[:, 1:2], func=AF.Sigmoid), SC)
    ctx.wait("vector", SC, s_sig)
    ctx.wait("vector", "CC", ctx.cnt["CC"])
    ctx.op("vector", lambda e: e.tensor_tensor(
        out=t["scl"][:, 15:16], in0=t["scl"][:, 12:13], in1=t["scl"][:, 14:15], op=ALU.mult), None)
    ctx.drain()
    ctx.op("vector", lambda e: e.reciprocal(out=t["scl"][:, 16:17], in_=t["scl"][:, 15:16]), None)
    ctx.drain()
    ctx.op("vector", lambda e: e.tensor_scalar(
        out=t["scl"][:, 17:18], in0=t["scl"][:, 16:17], scalar1=QMAX, scalar2=None,
        op0=ALU.mult), None)
    ctx.drain()
    ctx.op("vector", lambda e: e.tensor_scalar(
        out=t["scl"][:, 18:19], in0=t["scl"][:, 15:16], scalar1=1.0 / QMAX, scalar2=None,
        op0=ALU.mult), V)
    ctx.drain()
    bcast_scalar_col(ctx, t, t["scl"][:1, 17:18], cols[:, 4:5])
    v_sx2 = bcast_scalar_col(ctx, t, t["scl"][:1, 18:19], cols[:, 5:6])
    if upto == "F":
        vv = ctx.op("vector", lambda e: e.tensor_copy(out=t["xout"][0][:], in_=t["stage"][:, 0:TG]), "VE")
        ctx.wait("gpsimd", "VE", vv)
        ctx.dma("gpsimd", d["x2T"].ap()[0:128, 0:TG], t["xout"][0][:], "X2S")
        ctx.drain_dma(["X2S", "ST", "LD", "WL", "H2S", "CC"])
        ctx.barrier()
        es.close()
        return

    es.close()

    es2 = ExitStack()
    wdqT = ctx.sbuf("p1_wdqT", [128, IC0 * 1024], BF16, es2)
    h2ld = [ctx.sbuf(f"p1_h2ld{i}", [128, TG], F32, es2) for i in range(2)]
    h2q = ctx.sbuf("p1_h2q", [128, IC0 * TG], BF16, es2)
    wTsl = [ctx.sbuf(f"p1_wTsl{i}", [128, 1024], F32, es2) for i in range(2)]
    itmp2 = ctx.sbuf("p1_itmp2", [128, 1024], I32, es2)
    rinvd = ctx.sbuf("p1_rinvd", [128, 1024], F32, es2)
    rrowd = ctx.sbuf("p1_rrowd", [1, 2048], BF16, es2)
    rcol2b = ctx.sbuf("p1_rc2b", [128, 2], BF16, es2)
    dcols = ctx.sbuf("p1_dcols", [128, 16], F32, es2)
    wdnat = ctx.sbuf("p1_wdnat", [128, 2816], F32, es2)
    if upto == "F1":
        vv = ctx.op("vector", lambda e: e.tensor_copy(out=t["xout"][0][:], in_=t["stage"][:, 0:TG]), "VE")
        ctx.wait("gpsimd", "VE", vv)
        ctx.dma("gpsimd", d["x2T"].ap()[0:128, 0:TG], t["xout"][0][:], "X2S")
        ctx.drain_dma(["X2S", "ST", "LD", "WL", "H2S", "CC"])
        ctx.barrier()
        es2.close()
        return

    prev_v = v_sx2
    for h in range(DC):
        ctx.wait("sync", V, prev_v)
        lv = ctx.dma("sync", wdnat[:], d["wd_nat"].ap()[h * 128:(h + 1) * 128, :], "WL")
        ctx.wait("vector", "WL", lv)
        ctx.op("vector", lambda e, h=h: e.tensor_reduce(
            out=dcols[:, h:h + 1], in_=wdnat[:], axis=AX, op=ALU.max,
            apply_absolute_value=True), None)
        ctx.drain()
        ctx.op("vector", lambda e, h=h: e.reciprocal(out=t["col1"][:], in_=dcols[:, h:h + 1]), None)
        ctx.drain()
        ctx.op("vector", lambda e: e.tensor_scalar(
            out=t["col2"][:], in0=t["col1"][:], scalar1=QMAX, scalar2=None, op0=ALU.mult), None)
        ctx.drain()
        ctx.op("vector", lambda e: e.tensor_copy(out=rcol2b[:, 0:1], in_=t["col2"][:]), None)
        ctx.drain()
        v1 = ctx.op("vector", lambda e: e.tensor_tensor(
            out=rcol2b[:, 1:2], in0=t["col2"][:], in1=rcol2b[:, 0:1], op=ALU.subtract), V)
        ctx.drain()
        ctx.wait("tensor", V, v1)
        ctx.op("tensor", lambda e: e.transpose(
            out=t["psb"][0][:1, 0:128], in_=rcol2b[:, 0:1], identity=t["idb"][:]), None)
        mt = ctx.op("tensor", lambda e: e.transpose(
            out=t["psb"][0][:1, 128:256], in_=rcol2b[:, 1:2], identity=t["idb"][:]), MM)
        ctx.wait("vector", MM, mt)
        ctx.op("vector", lambda e, h=h: e.tensor_copy(
            out=rrowd[:, h * 128:(h + 1) * 128], in_=t["psb"][0][:1, 0:128]), None)
        ctx.op("vector", lambda e, h=h: e.tensor_copy(
            out=rrowd[:, 1024 + h * 128: 1024 + (h + 1) * 128],
            in_=t["psb"][0][:1, 128:256]), None)
        ctx.op("vector", lambda e, h=h: e.tensor_tensor(
            out=t["col1"][:], in0=dcols[:, h:h + 1], in1=cols[:, 5:6], op=ALU.mult), None)
        ctx.drain()
        prev_v = ctx.op("vector", lambda e, h=h: e.tensor_scalar(
            out=dcols[:, 8 + h:9 + h], in0=t["col1"][:], scalar1=1.0 / QMAX,
            scalar2=None, op0=ALU.mult), V)
        ctx.drain()
    if upto == "F1b":
        vv = ctx.op("vector", lambda e: e.tensor_copy(out=t["xout"][0][:], in_=t["stage"][:, 0:TG]), "VE")
        ctx.wait("gpsimd", "VE", vv)
        ctx.dma("gpsimd", d["x2T"].ap()[0:128, 0:TG], t["xout"][0][:], "X2S")
        ctx.drain_dma(["X2S", "ST", "LD", "WL", "H2S", "CC"])
        ctx.barrier()
        es2.close()
        return

    ctx.wait("tensor", V, prev_v)
    if upto in ("F2h", "F2i", "F2j", "F2m", "F2n", "F2o"):
        rhs_src = t["hrow"][:, 0:128] if upto == "F2h" else (
            rrowd[:, 0:128] if upto == "F2i" else rrowd[:, 0:512])
        outp = t["ps"][2][:, :128] if upto in ("F2h", "F2i") else t["ps"][2][:]
        ctx.op("tensor", lambda e: e.matmul(
            out=outp, lhsT=t["ones"][:1, :], rhs=rhs_src,
            start=True, stop=True), "MM")
        if upto in ("F2m", "F2n", "F2o"):
            rhs2 = rrowd[:, 0:512] if upto == "F2n" else rrowd[:, 1024:1536]
            ctx.op("tensor", lambda e, rhs2=rhs2: e.matmul(
                out=t["ps"][3][:], lhsT=t["ones"][:1, :], rhs=rhs2,
                start=True, stop=True), "MM")
        mmv = ctx.cnt["MM"]
        ctx.wait("vector", "MM", mmv)
        dsto = rinvd[:, 0:512] if upto == "F2o" else t["rbf"][:, :128]
        srco = t["ps"][2][:] if upto == "F2o" else t["ps"][2][:, :128]
        ctx.op("vector", lambda e, dsto=dsto, srco=srco: e.tensor_copy(out=dsto, in_=srco), "VE")
    if upto == "F2pad":
        for _ in range(8):
            ctx.op("vector", lambda e: e.tensor_copy(out=t["col2"][:], in_=t["col1"][:]), None)
    for half in range(2):
        if upto in ("F2a", "F2pad", "F2h", "F2i", "F2j", "F2m", "F2n", "F2o"):
            continue
        ctx.op("tensor", lambda e, half=half: e.matmul(
            out=t["ps"][2][:], lhsT=t["ones"][:1, :], rhs=rrowd[:, half * TG:(half + 1) * TG],
            start=True, stop=True), None)
        v3 = ctx.op("tensor", lambda e, half=half: e.matmul(
            out=t["ps"][3][:], lhsT=t["ones"][:1, :],
            rhs=rrowd[:, 1024 + half * TG: 1024 + (half + 1) * TG],
            start=True, stop=True), MM)
        if upto == "F2b":
            continue
        ctx.wait("scalar", MM, v3)
        if upto == "F2e":
            continue
        if upto == "F2f":
            s1 = ctx.op("vector", lambda e, half=half: e.tensor_copy(
                out=rinvd[:, half * TG:(half + 1) * TG], in_=t["ps"][2][:]), "VE")
            continue
        if upto == "F2d":
            s1 = ctx.op("scalar", lambda e: e.activation(
                out=t["rbf"][:], in_=t["ps"][2][:], func=AF.Copy), SC)
            continue
        s1 = ctx.op("scalar", lambda e, half=half: e.activation(
            out=rinvd[:, half * TG:(half + 1) * TG], in_=t["ps"][2][:], func=AF.Copy), SC)
        if upto == "F2c":
            continue
        ctx.wait("vector", SC, s1)
        v_add = ctx.op("vector", lambda e, half=half: e.tensor_tensor(
            out=rinvd[:, half * TG:(half + 1) * TG], in0=rinvd[:, half * TG:(half + 1) * TG],
            in1=t["ps"][3][:], op=ALU.add), V)
        ctx.wait("tensor", V, v_add)
    if upto.startswith("F2"):
        vv = ctx.op("vector", lambda e: e.tensor_copy(out=t["xout"][0][:], in_=t["stage"][:, 0:TG]), "VE")
        ctx.wait("gpsimd", "VE", vv)
        ctx.dma("gpsimd", d["x2T"].ap()[0:128, 0:TG], t["xout"][0][:], "X2S")
        ctx.drain_dma(["X2S", "ST", "LD", "WL", "H2S", "CC"])
        ctx.barrier()
        es2.close()
        return

    prev_v = ctx.op("vector", lambda e: e.tensor_copy(out=t["col1"][:], in_=t["col1"][:]), V)
    cast_done = [prev_v, prev_v]
    wlv = [0, 0]
    def p1d_load(i):
        ib = i % 2
        ctx.wait("sync", V, cast_done[ib])
        wlv[ib] = ctx.dma("sync", wTsl[ib][:], d["wdT"].ap()[i * 128:(i + 1) * 128, :], "WL")
    p1d_load(0)
    for i in range(IC0):
        ib = i % 2
        if i + 1 < IC0:
            p1d_load(i + 1)
        ctx.wait("vector", "WL", wlv[ib])
        ctx.op("vector", lambda e, ib=ib: e.tensor_tensor(
            out=itmp2[:], in0=wTsl[ib][:], in1=rinvd[:], op=ALU.mult), None)
        prev_v = ctx.op("vector", lambda e, i=i: e.tensor_copy(
            out=wdqT[:, i * 1024:(i + 1) * 1024], in_=itmp2[:]), V)
        cast_done[ib] = prev_v
    if upto == "F3":
        vv = ctx.op("vector", lambda e: e.tensor_copy(out=t["xout"][0][:], in_=t["stage"][:, 0:TG]), "VE")
        ctx.wait("gpsimd", "VE", vv)
        ctx.dma("gpsimd", d["x2T"].ap()[0:128, 0:TG], t["xout"][0][:], "X2S")
        ctx.drain_dma(["X2S", "ST", "LD", "WL", "H2S", "CC"])
        ctx.barrier()
        es2.close()
        return

    ctx.wait("sync", "H2S", ctx.cnt["H2S"])
    v_q = prev_v
    dmm = 0
    last_vv = prev_v
    jj = 0
    hq_read_v = [0, 0]
    for tg in range(NG):
        if tg > 0:
            ctx.wait("vector", MM, dmm)
        for i in range(IC0):
            hsem = "H2LA" if i % 2 == 0 else "H2LB"
            ctx.wait("sync", V, hq_read_v[i % 2])
            lv = ctx.dma("sync", h2ld[i % 2][:],
                         d["h2s"].ap()[i * 128:(i + 1) * 128, tg * TG:(tg + 1) * TG], hsem)
            ctx.wait("vector", hsem, lv)
            ctx.op("vector", lambda e, i=i: e.tensor_scalar(
                out=h2ld[i % 2][:], in0=h2ld[i % 2][:],
                scalar1=cols[:, 4:5], scalar2=MAGIC, op0=ALU.mult, op1=ALU.add), None)
            ctx.op("vector", lambda e, i=i: e.tensor_scalar(
                out=h2ld[i % 2][:], in0=h2ld[i % 2][:],
                scalar1=MAGIC, scalar2=-128.0, op0=ALU.subtract, op1=ALU.max), None)
            hq_read_v[i % 2] = ctx.op("vector", lambda e, i=i: e.tensor_scalar(
                out=h2q[:, i * TG:(i + 1) * TG], in0=h2ld[i % 2][:],
                scalar1=127.0, scalar2=None, op0=ALU.min), V)
        v_q = hq_read_v[(IC0 - 1) % 2]
        ctx.wait("tensor", V, v_q)
        for hh in range(2):
            if hh == 1:
                ctx.wait("tensor", V, last_vv)
            for i in range(IC0):
                for h4 in range(4):
                    h = hh * 4 + h4
                    ctx.op("tensor", lambda e, i=i, h=h, h4=h4: e.matmul(
                        out=t["ps"][h4][:], lhsT=wdqT[:, i * 1024 + h * 128:
                                                      i * 1024 + (h + 1) * 128],
                        rhs=h2q[:, i * TG:(i + 1) * TG],
                        start=(i == 0), stop=(i == IC0 - 1)),
                        MM if (i == IC0 - 1 and h4 == 3) else None)
            dmm = ctx.cnt[MM]
            ctx.wait("sync", V, last_vv)
            lv = 0
            for h4 in range(4):
                h = hh * 4 + h4
                lv = ctx.dma("sync", t["stage"][:, h * TG:(h + 1) * TG],
                             d["x1T"].ap()[h * 128:(h + 1) * 128, tg * TG:(tg + 1) * TG], LD)
            ctx.wait("vector", LD, lv)
            ctx.wait("vector", MM, dmm)
            for h4 in range(4):
                h = hh * 4 + h4
                if jj >= 2:
                    ctx.wait("vector", "X2S", 16 * (jj - 1))
                vv = ctx.op("vector", lambda e, h=h, h4=h4, jj=jj: e.scalar_tensor_tensor(
                    out=t["xout"][jj % 2][:], in0=t["ps"][h4][:],
                    scalar=dcols[:, 8 + h:9 + h], in1=t["stage"][:, h * TG:(h + 1) * TG],
                    op0=ALU.mult, op1=ALU.add), V)
                last_vv = vv
                ctx.wait("gpsimd", V, vv)
                ctx.dma("gpsimd", d["x2T"].ap()[h * 128:(h + 1) * 128, tg * TG:(tg + 1) * TG],
                        t["xout"][jj % 2][:], "X2S")
                jj += 1
        if upto == "G1":
            ctx.drain_dma(["X2S", "ST", "LD", "WL", "H2S", "CC", "H2LA", "H2LB"])
            ctx.barrier()
            es2.close()
            return

    ctx.drain_dma(["X2S", LD, "WL", "H2LA", "H2LB", ST, "H2S", "CC"])
    ctx.barrier()
    es2.close()



def emit_phase2(ctx, t, d, load_consts=False):
    t = dict(t)
    V, SC, MM, LD, ST = "VE", "SC", "MM", "LD", "ST"
    for nm in ("WL", LD, "QS", "QL", "X3S", "TB", "S3", "S3L"):
        ctx.sem(nm)
    es = ExitStack()
    wprojb = ctx.sbuf("p2_wp", [128, DC * 1024], BF16, es)
    wsl = [ctx.sbuf(f"p2_wsl{i}", [128, 1024], F32, es) for i in range(2)]
    wqb = [ctx.sbuf(f"p2_wqb{i}", [128, 1024], BF16, es) for i in range(2)]
    wv0 = ctx.sbuf("p2_wv0", [128, DC * 512], BF16, es)
    bqkv = ctx.sbuf("p2_bqkv", [128, 24], F32, es)
    bvb = ctx.sbuf("p2_bvb", [128, 1024], F32, es)
    bproj = ctx.sbuf("p2_bproj", [128, 8], F32, es)
    g8 = ctx.sbuf("p2_g8", [128, 8], F32, es)
    arb = ctx.sbuf("p2_arb", [128, TG], F32, es)
    kf = ctx.sbuf("p2_kf", [128, DC * S], BF16, es)
    vt = ctx.sbuf("p2_vt", [128, 16 * 1024], BF16, es)
    qtile = ctx.sbuf("p2_qt", [128, DC * TG], BF16, es)
    qtmp = [ctx.sbuf(f"p2_qtmp{i}", [128, TG], BF16, es) for i in range(2)]
    tabs = ctx.sbuf("p2_tabs", [128, 3 * TG], BF16, es)
    scol = ctx.sbuf("p2_scol", [128, 8], F32, es)


    if load_consts:
        ctx.dma("sync", t["ones"][:], d["ones_bf"].ap(), LD)
        ctx.dma("sync", t["idf"][:], d["ident_f"].ap(), LD)
        ctx.dma("sync", t["idb"][:], d["ident_b"].ap(), LD)
    ctx.dma("sync", bqkv[:], d["bqkv24"].ap(), LD)
    ctx.dma("sync", t["stage"][:1, 0:1024], d["bvrow"].ap(), LD)
    ctx.dma("sync", bproj[:], d["bproj8"].ap(), LD)
    lv = ctx.dma("sync", g8[:], d["g8"].ap(), LD)
    prev_v = 0
    for c in range(DC):
        ctx.wait("sync", V, prev_v)
        wv = ctx.dma("sync", wsl[0][:], d["wproj"].ap()[c * 128:(c + 1) * 128, :], "WL")
        ctx.wait("vector", "WL", wv)
        prev_v = ctx.op("vector", lambda e, c=c: e.tensor_copy(
            out=wprojb[:, c * 1024:(c + 1) * 1024], in_=wsl[0][:]), V)
    ctx.wait("vector", LD, lv)
    ctx.op("vector", lambda e: e.tensor_copy(out=t["hrow"][:, 0:1024], in_=t["stage"][:1, 0:1024]), None)
    v1 = ctx.op("vector", lambda e: e.tensor_tensor(
        out=t["hrow"][:, 1024:2048], in0=t["stage"][:1, 0:1024], in1=t["hrow"][:, 0:1024],
        op=ALU.subtract), V)
    ctx.wait("tensor", V, v1)
    for half in range(2):
        ctx.op("tensor", lambda e, half=half: e.matmul(
            out=t["ps"][2][:], lhsT=t["ones"][:1, :], rhs=t["hrow"][:, half * TG:(half + 1) * TG],
            start=True, stop=True), None)
        v3 = ctx.op("tensor", lambda e, half=half: e.matmul(
            out=t["ps"][3][:], lhsT=t["ones"][:1, :],
            rhs=t["hrow"][:, 1024 + half * TG: 1024 + (half + 1) * TG],
            start=True, stop=True), MM)
        ctx.wait("scalar", MM, v3)
        s1 = ctx.op("scalar", lambda e, half=half: e.activation(
            out=bvb[:, half * TG:(half + 1) * TG], in_=t["ps"][2][:], func=AF.Copy), SC)
        ctx.wait("vector", SC, s1)
        va = ctx.op("vector", lambda e, half=half: e.tensor_tensor(
            out=bvb[:, half * TG:(half + 1) * TG], in0=bvb[:, half * TG:(half + 1) * TG],
            in1=t["ps"][3][:], op=ALU.add), V)
        ctx.wait("tensor", V, va)

    last_v = ctx.cnt[V]
    jj = 0
    for s_ in range(SPC):
        es_hq = ExitStack()
        hq = ctx.sbuf(f"p2_hq{s_}", [128, DC * S], BF16, es_hq)
        for gl in range(GPS):
            g = s_ * GPS + gl
            lv = load_stage(ctx, t, d["x2T"].ap(), g, guard_v=last_v)
            ctx.wait("vector", LD, lv)
            emit_ln_group(ctx, t, g, want_lnbf=False, want_absmax=False, want_apply=False)
            ctx.op("vector", lambda e: e.tensor_tensor(
                out=t["rowf2"][:], in0=t["rowf3"][:], in1=t["rowf3"][:], op=ALU.mult), None)
            vv = ctx.op("vector", lambda e: e.tensor_scalar(
                out=t["rowf2"][:], in0=t["rowf2"][:], scalar1=-1e-5, scalar2=1.0 + 1e-6,
                op0=ALU.mult, op1=ALU.add), V)
            ctx.wait("scalar", V, vv)
            sv = ctx.op("scalar", lambda e: e.activation(
                out=t["rowf2"][:], in_=t["rowf2"][:], func=AF.Sqrt), SC)
            ctx.wait("vector", SC, sv)
            ctx.op("vector", lambda e: e.reciprocal(out=t["rowf2"][:], in_=t["rowf2"][:]), None)
            ctx.op("vector", lambda e: e.tensor_tensor(
                out=t["rowf2"][:], in0=t["rowf2"][:], in1=t["rowf3"][:], op=ALU.mult), None)
            vs3 = ctx.op("vector", lambda e: e.tensor_copy(
                out=t["rowf2"][:] if False else t["rowf"][:, :1], in_=t["rowf"][:, :1]), V)
            ctx.wait("gpsimd", V, vs3)
            ctx.dma("gpsimd", d["stats3"].ap()[0:1, gl * TG:(gl + 1) * TG], t["rowf"][:], "S3")
            ctx.dma("gpsimd", d["stats3"].ap()[1:2, gl * TG:(gl + 1) * TG], t["rowf3"][:], "S3")
            ctx.op("vector", lambda e: e.tensor_copy(out=t["hrow"][:, 0:TG], in_=t["rowf2"][:]), None)
            v1 = ctx.op("vector", lambda e: e.tensor_tensor(
                out=t["hrow"][:, TG:2 * TG], in0=t["rowf2"][:], in1=t["hrow"][:, 0:TG],
                op=ALU.subtract), V)
            ctx.wait("tensor", V, v1)
            ctx.op("tensor", lambda e: e.matmul(
                out=t["ps"][2][:], lhsT=t["ones"][:1, :], rhs=t["hrow"][:, 0:TG],
                start=True, stop=True), None)
            v3 = ctx.op("tensor", lambda e: e.matmul(
                out=t["ps"][3][:], lhsT=t["ones"][:1, :], rhs=t["hrow"][:, TG:2 * TG],
                start=True, stop=True), MM)
            ctx.wait("scalar", MM, v3)
            s1 = ctx.op("scalar", lambda e: e.activation(
                out=arb[:], in_=t["ps"][2][:], func=AF.Copy), SC)
            ctx.wait("vector", SC, s1)
            va = ctx.op("vector", lambda e: e.tensor_tensor(
                out=arb[:], in0=arb[:], in1=t["ps"][3][:], op=ALU.add), V)
            ctx.wait("tensor", V, va)
            for c in range(DC):
                ctx.op("vector", lambda e, c=c: e.tensor_tensor(
                    out=t["ep1"][:], in0=t["stage"][:, c * TG:(c + 1) * TG],
                    in1=t["mbf"][:], op=ALU.subtract), None)
                vv = ctx.op("vector", lambda e, c=c, gl=gl, hq=hq: e.scalar_tensor_tensor(
                    out=hq[:, c * S + gl * TG: c * S + (gl + 1) * TG], in0=arb[:],
                    scalar=g8[:, c:c + 1], in1=t["ep1"][:], op0=ALU.mult, op1=ALU.mult),
                    V if c == DC - 1 else None)
            last_v = ctx.cnt[V]
        for qk in range(2):
            cast_v = [0, 0]
            mm_last = [0, 0]
            sc_v = [0, 0]
            for tgl in range(GPS):
                for fc in range(DC):
                    b = fc % 2
                    ctx.wait("sync", V, cast_v[b])
                    wv = 0
                    for c in range(DC):
                        wv = ctx.dma("sync", wsl[b][:, c * 128:(c + 1) * 128],
                                     d["wqkv"].ap()[c * 128:(c + 1) * 128,
                                                    qk * 1024 + fc * 128:
                                                    qk * 1024 + (fc + 1) * 128], "WL")
                    ctx.wait("vector", "WL", wv)
                    ctx.wait("vector", MM, mm_last[b])
                    cast_v[b] = ctx.op("vector", lambda e, b=b: e.tensor_copy(
                        out=wqb[b][:], in_=wsl[b][:]), V)
                    ctx.wait("tensor", V, cast_v[b])
                    ctx.wait("tensor", SC, sc_v[b])
                    pm = t["ps"][4 + b]
                    for k in range(DC):
                        ctx.op("tensor", lambda e, k=k, tgl=tgl, pm=pm, b=b, hq=hq: e.matmul(
                            out=pm[:], lhsT=wqb[b][:, k * 128:(k + 1) * 128],
                            rhs=hq[:, k * S + tgl * TG: k * S + (tgl + 1) * TG],
                            start=(k == 0), stop=(k == DC - 1)), MM if k == DC - 1 else None)
                    mm_last[b] = ctx.cnt[MM]
                    ctx.wait("scalar", MM, mm_last[b])
                    dst = (qtile[:, fc * TG:(fc + 1) * TG] if qk == 0 else
                           kf[:, fc * S + tgl * TG: fc * S + (tgl + 1) * TG])
                    sc_v[b] = ctx.op("scalar", lambda e, pm=pm, dst=dst, qk=qk, fc=fc:
                        e.activation(out=dst, in_=pm[:], func=AF.Identity,
                                     bias=bqkv[:, qk * 8 + fc: qk * 8 + fc + 1]), SC)
                ctx.wait("vector", SC, max(sc_v))
                for hc in range(4):
                    ctx.wait("gpsimd", V, ctx.cnt[V])
                    tv = 0
                    for ti, nm in enumerate(("ropeC", "ropeSp", "ropeSn")):
                        tv = ctx.dma("gpsimd", tabs[:, ti * TG:(ti + 1) * TG],
                                     d[nm].ap()[hc * 128:(hc + 1) * 128,
                                                tgl * TG:(tgl + 1) * TG], "TB")
                    ctx.wait("vector", "TB", tv)
                    ca, cb = hc, hc + 4
                    if qk == 0:
                        src_a = qtile[:, ca * TG:(ca + 1) * TG]
                        src_b = qtile[:, cb * TG:(cb + 1) * TG]
                    else:
                        src_a = kf[:, ca * S + tgl * TG: ca * S + (tgl + 1) * TG]
                        src_b = kf[:, cb * S + tgl * TG: cb * S + (tgl + 1) * TG]
                    ctx.op("vector", lambda e, src_a=src_a: e.tensor_tensor(
                        out=qtmp[0][:], in0=src_a, in1=tabs[:, 0:TG], op=ALU.mult), None)
                    ctx.op("vector", lambda e, src_b=src_b: e.tensor_tensor(
                        out=qtmp[1][:], in0=src_b, in1=tabs[:, 2 * TG:3 * TG], op=ALU.mult), None)
                    ctx.op("vector", lambda e, src_b=src_b: e.tensor_tensor(
                        out=t["ep1"][:], in0=src_b, in1=tabs[:, 0:TG], op=ALU.mult), None)
                    ctx.op("vector", lambda e, src_a=src_a: e.tensor_tensor(
                        out=t["xout"][0][:], in0=src_a, in1=tabs[:, TG:2 * TG], op=ALU.mult), None)
                    ctx.op("vector", lambda e, src_a=src_a: e.tensor_tensor(
                        out=src_a, in0=qtmp[0][:], in1=qtmp[1][:], op=ALU.add), None)
                    rv = ctx.op("vector", lambda e, src_b=src_b: e.tensor_tensor(
                        out=src_b, in0=t["ep1"][:], in1=t["xout"][0][:], op=ALU.add), V)
                if qk == 0:
                    ctx.wait("gpsimd", V, rv)
                    for c in range(DC):
                        ctx.dma("gpsimd", d["qs"].ap()[c * 128:(c + 1) * 128,
                                                       tgl * TG:(tgl + 1) * TG],
                                qtile[:, c * TG:(c + 1) * TG], "QS")
                    ctx.wait("scalar", "QS", ctx.cnt["QS"])
        v_ev = [0, 0]
        for fg in range(2):
            for c in range(DC):
                ctx.wait("sync", V, ctx.cnt[V] if (fg == 0 and c == 0) else cast_done)
                wv = ctx.dma("sync", wsl[0][:, 0:512],
                             d["wqkv"].ap()[c * 128:(c + 1) * 128,
                                            2048 + fg * 512: 2048 + (fg + 1) * 512], "WL")
                ctx.wait("vector", "WL", wv)
                ctx.wait("vector", MM, ctx.cnt[MM])
                cast_done = ctx.op("vector", lambda e, c=c: e.tensor_copy(
                    out=wv0[:, c * 512:(c + 1) * 512], in_=wsl[0][:, 0:512]), V)
            ctx.wait("tensor", V, cast_done)
            for tc in range(16):
                tgl = tc // 4
                toff = (tc % 4) * 128
                pm = t["ps"][4 + tc % 2]
                ctx.wait("tensor", V, v_ev[tc % 2])
                for k in range(DC):
                    ctx.op("tensor", lambda e, k=k, tgl=tgl, toff=toff, pm=pm, hq=hq: e.matmul(
                        out=pm[:], lhsT=hq[:, k * S + tgl * TG + toff:
                                           k * S + tgl * TG + toff + 128],
                        rhs=wv0[:, k * 512:(k + 1) * 512],
                        start=(k == 0), stop=(k == DC - 1)), MM if k == DC - 1 else None)
                mmv = ctx.cnt[MM]
                ctx.wait("vector", MM, mmv)
                v_ev[tc % 2] = ctx.op("vector", lambda e, tc=tc, fg=fg, pm=pm: e.tensor_tensor(
                    out=vt[:, tc * 1024 + fg * 512: tc * 1024 + (fg + 1) * 512],
                    in0=pm[:], in1=bvb[:, fg * 512:(fg + 1) * 512], op=ALU.add), V)
        v_done = max(v_ev)
        es_hq.close()
        es_at = ExitStack()
        Pt = ctx.sbuf(f"p2_P{s_}", [128, S], BF16, es_at)
        PT_sb = ctx.sbuf(f"p2_PT{s_}", [128, 16 * TG], BF16, es_at)
        osl = ctx.sbuf(f"p2_o{s_}", [128, DC * TG], BF16, es_at)
        ctx.wait("sync", "QS", ctx.cnt["QS"])
        pt_cp = 0
        pv_mm = 0
        for gl in range(GPS):
            ctx.wait("sync", V, ctx.cnt[V])
            qv = 0
            for c in range(DC):
                qv = ctx.dma("sync", qtile[:, c * TG:(c + 1) * TG],
                             d["qs"].ap()[c * 128:(c + 1) * 128, gl * TG:(gl + 1) * TG], "QL")
            ctx.wait("tensor", "QL", qv)
            ctx.wait("tensor", V, v_done)
            exp_sv = 0
            for qt in range(4):
                ctx.wait("tensor", SC, exp_sv)
                for kg in range(4):
                    for c in range(DC):
                        ctx.op("tensor", lambda e, c=c, kg=kg, qt=qt: e.matmul(
                            out=t["ps"][kg][:, 0:TG],
                            lhsT=qtile[:, c * TG + qt * 128: c * TG + qt * 128 + 128],
                            rhs=kf[:, c * S + kg * TG: c * S + (kg + 1) * TG],
                            start=(c == 0), stop=(c == DC - 1)),
                            MM if (c == DC - 1 and kg == 3) else None)
                mmv = ctx.cnt[MM]
                ctx.wait("vector", MM, mmv)
                for kg in range(4):
                    ctx.op("vector", lambda e, kg=kg: e.tensor_reduce(
                        out=scol[:, kg:kg + 1], in_=t["ps"][kg][:, 0:TG], axis=AX,
                        op=ALU.max), None)
                ctx.drain()
                ctx.op("vector", lambda e: e.tensor_tensor(
                    out=scol[:, 4:5], in0=scol[:, 0:1], in1=scol[:, 1:2], op=ALU.max), None)
                ctx.op("vector", lambda e: e.tensor_tensor(
                    out=scol[:, 5:6], in0=scol[:, 2:3], in1=scol[:, 3:4], op=ALU.max), None)
                ctx.drain()
                ctx.op("vector", lambda e: e.tensor_tensor(
                    out=scol[:, 4:5], in0=scol[:, 4:5], in1=scol[:, 5:6], op=ALU.max), None)
                ctx.drain()
                vb = ctx.op("vector", lambda e: e.tensor_scalar(
                    out=scol[:, 6:7], in0=scol[:, 4:5], scalar1=-LN3 / 32.0, scalar2=None,
                    op0=ALU.mult), V)
                ctx.drain()
                ctx.wait("scalar", V, vb)
                for kg in range(4):
                    ctx.op("scalar", lambda e, kg=kg, Pt=Pt: e.activation(
                        out=Pt[:, kg * TG:(kg + 1) * TG], in_=t["ps"][kg][:, 0:TG],
                        func=AF.Exp, scale=LN3 / 32.0, bias=scol[:, 6:7],
                        accum_out=scol[:, kg:kg + 1]), SC if kg == 3 else None)
                exp_sv = ctx.cnt[SC]
                ctx.wait("vector", SC, exp_sv)
                ctx.op("vector", lambda e: e.tensor_tensor(
                    out=scol[:, 4:5], in0=scol[:, 0:1], in1=scol[:, 1:2], op=ALU.add), None)
                ctx.op("vector", lambda e: e.tensor_tensor(
                    out=scol[:, 5:6], in0=scol[:, 2:3], in1=scol[:, 3:4], op=ALU.add), None)
                ctx.drain()
                ctx.op("vector", lambda e: e.tensor_tensor(
                    out=scol[:, 4:5], in0=scol[:, 4:5], in1=scol[:, 5:6], op=ALU.add), None)
                ctx.drain()
                ctx.op("vector", lambda e: e.reciprocal(out=scol[:, 7:8], in_=scol[:, 4:5]), None)
                ctx.drain()
                vn = ctx.op("vector", lambda e, Pt=Pt: e.tensor_scalar(
                    out=Pt[:], in0=Pt[:], scalar1=scol[:, 7:8], scalar2=None,
                    op0=ALU.mult), V)
                ctx.wait("tensor", V, vn)
                ctx.wait("tensor", V, pt_cp)
                for kc in range(16):
                    bb = kc // 8
                    ctx.op("tensor", lambda e, kc=kc, bb=bb, Pt=Pt: e.transpose(
                        out=t["psb"][bb][:, (kc % 8) * 128:(kc % 8 + 1) * 128],
                        in_=Pt[:, kc * 128:(kc + 1) * 128], identity=t["idb"][:]),
                        MM if kc == 15 else None)
                mtv = ctx.cnt[MM]
                ctx.wait("vector", MM, mtv)
                ctx.wait("vector", MM, pv_mm)
                for bb in range(2):
                    pt_cp = ctx.op("vector", lambda e, bb=bb, qt=qt, PT_sb=PT_sb: e.tensor_copy(
                        out=PT_sb[:].rearrange("p (kc q) -> p kc q", kc=16)[
                            :, bb * 8:(bb + 1) * 8, qt * 128:(qt + 1) * 128],
                        in_=t["psb"][bb][:, 0:1024].rearrange(
                            "p (kc q) -> p kc q", kc=8)), V)
            o_sc = [0, 0]
            for dc in range(DC):
                pm = t["ps"][4 + dc % 2]
                ctx.wait("tensor", V, pt_cp)
                ctx.wait("tensor", SC, o_sc[dc % 2])
                for kc in range(16):
                    ctx.op("tensor", lambda e, kc=kc, dc=dc, pm=pm, PT_sb=PT_sb: e.matmul(
                        out=pm[:], lhsT=vt[:, kc * 1024 + dc * 128: kc * 1024 + (dc + 1) * 128],
                        rhs=PT_sb[:, kc * TG:(kc + 1) * TG],
                        start=(kc == 0), stop=(kc == 15)), MM if kc == 15 else None)
                pv_mm = ctx.cnt[MM]
                ctx.wait("scalar", MM, pv_mm)
                o_sc[dc % 2] = ctx.op("scalar", lambda e, dc=dc, pm=pm, osl=osl: e.activation(
                    out=osl[:, dc * TG:(dc + 1) * TG], in_=pm[:], func=AF.Copy), SC)
            pj_ev = [0, 0]
            ctx.wait("tensor", SC, max(o_sc))
            for fc in range(DC):
                pm = t["ps"][fc % 2]
                ctx.wait("tensor", V, pj_ev[fc % 2])
                for dc in range(DC):
                    ctx.op("tensor", lambda e, dc=dc, fc=fc, pm=pm, osl=osl: e.matmul(
                        out=pm[:], lhsT=wprojb[:, dc * 1024 + fc * 128:
                                               dc * 1024 + (fc + 1) * 128],
                        rhs=osl[:, dc * TG:(dc + 1) * TG],
                        start=(dc == 0), stop=(dc == DC - 1)), MM if dc == DC - 1 else None)
                pjm = ctx.cnt[MM]
                if fc == 0:
                    ctx.wait("sync", "S3", ctx.cnt["S3"])
                    ctx.wait("sync", V, ctx.cnt[V])
                    s3v = 0
                    s3v = ctx.dma("sync", t["rowf"][:], d["stats3"].ap()[0:1, gl * TG:(gl + 1) * TG], "S3L")
                    s3v = ctx.dma("sync", t["rowf3"][:], d["stats3"].ap()[1:2, gl * TG:(gl + 1) * TG], "S3L")
                    ctx.wait("vector", "S3L", s3v)
                    ctx.wait("vector", SC, max(o_sc))
                    ctx.op("vector", lambda e: e.tensor_copy(
                        out=t["hrow"][:, 0:TG], in_=t["rowf"][:]), None)
                    ctx.op("vector", lambda e: e.tensor_tensor(
                        out=t["hrow"][:, TG:2 * TG], in0=t["rowf"][:],
                        in1=t["hrow"][:, 0:TG], op=ALU.subtract), None)
                    ctx.op("vector", lambda e: e.tensor_copy(
                        out=t["hrow"][:, 2 * TG:3 * TG], in_=t["rowf3"][:]), None)
                    v1 = ctx.op("vector", lambda e: e.tensor_tensor(
                        out=t["hrow"][:, 3 * TG:4 * TG], in0=t["rowf3"][:],
                        in1=t["hrow"][:, 2 * TG:3 * TG], op=ALU.subtract), V)
                    ctx.wait("tensor", V, v1)
                    ctx.op("tensor", lambda e: e.matmul(
                        out=t["ps"][2][:], lhsT=t["ones"][:1, :], rhs=t["hrow"][:, 0:TG],
                        start=True, stop=True), None)
                    v3 = ctx.op("tensor", lambda e: e.matmul(
                        out=t["ps"][3][:], lhsT=t["ones"][:1, :], rhs=t["hrow"][:, TG:2 * TG],
                        start=True, stop=True), MM)
                    ctx.wait("scalar", MM, v3)
                    s1 = ctx.op("scalar", lambda e: e.activation(
                        out=t["mbf"][:], in_=t["ps"][2][:], func=AF.Copy), SC)
                    ctx.wait("vector", SC, s1)
                    vmb = ctx.op("vector", lambda e: e.tensor_tensor(
                        out=t["mbf"][:], in0=t["mbf"][:], in1=t["ps"][3][:], op=ALU.add), V)
                    ctx.wait("tensor", V, vmb)
                    ctx.op("tensor", lambda e: e.matmul(
                        out=t["ps"][2][:], lhsT=t["ones"][:1, :], rhs=t["hrow"][:, 2 * TG:3 * TG],
                        start=True, stop=True), None)
                    v3 = ctx.op("tensor", lambda e: e.matmul(
                        out=t["ps"][3][:], lhsT=t["ones"][:1, :], rhs=t["hrow"][:, 3 * TG:4 * TG],
                        start=True, stop=True), MM)
                    ctx.wait("scalar", MM, v3)
                    s1 = ctx.op("scalar", lambda e: e.activation(
                        out=t["rbf"][:], in_=t["ps"][2][:], func=AF.Copy), SC)
                    ctx.wait("vector", SC, s1)
                    vrb = ctx.op("vector", lambda e: e.tensor_tensor(
                        out=t["rbf"][:], in0=t["rbf"][:], in1=t["ps"][3][:], op=ALU.add), V)
                    ctx.wait("sync", V, vrb)
                    lv = 0
                    for c in range(DC):
                        lv = ctx.dma("sync", t["stage"][:, c * TG:(c + 1) * TG],
                                     d["x2T"].ap()[c * 128:(c + 1) * 128,
                                                   (s_ * GPS + gl) * TG:(s_ * GPS + gl + 1) * TG], LD)
                    ctx.wait("vector", LD, lv)
                ctx.wait("vector", MM, pjm)
                if jj >= 2:
                    ctx.wait("vector", "X3S", 16 * (jj - 1))
                ctx.op("vector", lambda e, fc=fc: e.tensor_tensor(
                    out=t["ep1"][:], in0=t["stage"][:, fc * TG:(fc + 1) * TG],
                    in1=t["mbf"][:], op=ALU.subtract), None)
                ctx.op("vector", lambda e: e.tensor_tensor(
                    out=t["ep1"][:], in0=t["ep1"][:], in1=t["rbf"][:], op=ALU.mult), None)
                ctx.op("vector", lambda e, fc=fc, pm=pm: e.scalar_tensor_tensor(
                    out=t["ep1"][:], in0=pm[:], scalar=bproj[:, fc:fc + 1],
                    in1=t["ep1"][:], op0=ALU.add, op1=ALU.add), None)
                vv = ctx.op("vector", lambda e, fc=fc, jj=jj: e.tensor_tensor(
                    out=t["xout"][jj % 2][:], in0=t["ep1"][:],
                    in1=t["stage"][:, fc * TG:(fc + 1) * TG], op=ALU.add), V)
                pj_ev[fc % 2] = vv
                ctx.wait("gpsimd", V, vv)
                ctx.dma("gpsimd", d["x3T"].ap()[fc * 128:(fc + 1) * 128,
                                                (s_ * GPS + gl) * TG:(s_ * GPS + gl + 1) * TG],
                        t["xout"][jj % 2][:], "X3S")
                jj += 1
            last_v = ctx.cnt[V]
        ctx.drain_dma(["X3S", LD, "WL", "QS", "QL", "TB", "S3", "S3L"])
        ctx.barrier()
        es_at.close()
    es.close()


def declare_p2(nc, d, x2_is_input):
    for name, shape, dt in [
            ("wqkv", [D, 3072], F32), ("wproj", [D, D], F32),
            ("bqkv24", [128, 24], F32), ("bvrow", [1, 1024], F32),
            ("bproj8", [128, 8], F32), ("g8", [128, 8], F32),
            ("ropeC", [512, S], BF16), ("ropeSp", [512, S], BF16),
            ("ropeSn", [512, S], BF16)]:
        d[name] = nc.dram_tensor(name, shape, dt, kind="ExternalInput")
    if x2_is_input:
        d["x2T"] = nc.dram_tensor("x2T", [D, T], F32, kind="ExternalInput")
        for name, shape, dt in [
                ("ones_bf", [128, 128], BF16), ("ident_f", [128, 128], F32),
                ("ident_b", [128, 128], BF16)]:
            d[name] = nc.dram_tensor(name, shape, dt, kind="ExternalInput")
    d["qs"] = nc.dram_tensor("qs", [D, S], BF16)
    d["stats3"] = nc.dram_tensor("stats3", [2, S], F32)
    d["x3T"] = nc.dram_tensor("x3T", [D, T], F32, kind="ExternalOutput")


def build_p2():
    nc = bass.Bass("TRN2", target_bir_lowering=False)
    d = {}
    declare_p2(nc, d, x2_is_input=True)
    es = ExitStack()
    ctx = Ctx(nc, es)
    with es:
        t = alloc_common(ctx)
        emit_phase2(ctx, t, d, load_consts=True)
        ctx.build_block()
    return nc



def emit_phase3(ctx, t, d, load_consts=False, upto="G"):
    t = dict(t)
    V, SC, MM, LD, ST = "VE", "SC", "MM", "LD", "ST"
    for nm in (ST, "WL", LD, "H3S", "H3L", "X4S"):
        ctx.sem(nm)
    es = ExitStack()
    t["lnf"] = ctx.sbuf("p3_lnf", [128, DC * TG], F32, es)
    xq4 = ctx.sbuf("p3_xq4", [128, DC * T], BF16, es)
    h3sl = [ctx.sbuf(f"p3_h3{i}", [128, T], BF16, es) for i in range(2)]
    wsl = [ctx.sbuf(f"p3_wsl{i}", [128, 2 * 1024], I32, es) for i in range(2)]
    wqg = [ctx.sbuf(f"p3_wqg{i}", [128, DC * 128], BF16, es) for i in range(2)]
    wqu = [ctx.sbuf(f"p3_wqu{i}", [128, DC * 128], BF16, es) for i in range(2)]
    silu_t = [ctx.sbuf(f"p3_si{i}", [128, TG], F32, es) for i in range(2)]
    if load_consts:
        ctx.dma("sync", t["ones"][:], d["ones_bf"].ap(), LD)
        ctx.dma("sync", t["idf"][:], d["ident_f"].ap(), LD)
        ctx.dma("sync", t["idb"][:], d["ident_b"].ap(), LD)
    last_v = 0
    for g in range(NG):
        lv = load_stage(ctx, t, d["x3T"].ap(), g, guard_v=last_v)
        ctx.wait("vector", LD, lv)
        last_v = emit_ln_group(ctx, t, g, want_lnbf=True, want_absmax=False,
                               lnbf_dst=xq4, lnbf_g=g)
    if upto == "LN":
        for c in range(DC):
            vv = ctx.op("vector", lambda e, c=c: e.tensor_copy(out=t["xout"][0][:], in_=xq4[:, c * T: c * T + TG]), "VE")
            ctx.wait("gpsimd", "VE", vv)
            ctx.dma("gpsimd", d["dbg3"].ap()[c * 128:(c + 1) * 128, 0:TG], t["xout"][0][:], "X4S")
            ctx.wait("vector", "X4S", ctx.cnt["X4S"])
        for c in range(DC):
            vv = ctx.op("vector", lambda e, c=c: e.tensor_copy(out=t["xout"][0][:], in_=xq4[:, c * T + 3 * TG: c * T + 4 * TG]), "VE")
            ctx.wait("gpsimd", "VE", vv)
            ctx.dma("gpsimd", d["dbg3"].ap()[c * 128:(c + 1) * 128, TG:2 * TG], t["xout"][0][:], "X4S")
            ctx.wait("vector", "X4S", ctx.cnt["X4S"])
        ctx.drain_dma(["X4S", "ST", LD, "WL"])
        ctx.barrier()
        es.close()
        return
    h3_store_v = [0, 0]
    h2v = [0, 0]
    cast_v = [0, 0]
    mm_done = [0, 0]
    def p3_load(i):
        ib = i % 2
        ctx.wait("sync", V, cast_v[ib])
        for c in range(DC):
            ctx.dma("sync", wsl[ib][:, c * 128:(c + 1) * 128],
                    d["w1gT"].ap()[c * 128:(c + 1) * 128, i * 128:(i + 1) * 128], "WL")
        for c in range(DC):
            ctx.dma("sync", wsl[ib][:, 1024 + c * 128: 1024 + (c + 1) * 128],
                    d["w1uT"].ap()[c * 128:(c + 1) * 128, i * 128:(i + 1) * 128], "WL")
        return ctx.cnt["WL"]
    def p3_cast(i, lv):
        ib = i % 2
        ctx.wait("vector", "WL", lv)
        ctx.wait("vector", MM, mm_done[ib])
        ctx.op("vector", lambda e, ib=ib: e.tensor_copy(
            out=wqg[ib][:], in_=wsl[ib][:, 0:1024]), None)
        cast_v[ib] = ctx.op("vector", lambda e, ib=ib: e.tensor_copy(
            out=wqu[ib][:], in_=wsl[ib][:, 1024:2048]), V)
    lv0 = p3_load(0)
    p3_cast(0, lv0)
    for i in range(IC1):
        ib = i % 2
        if i + 1 < IC1:
            lvn = p3_load(i + 1)
            p3_cast(i + 1, lvn)
        ctx.wait("vector", "H3S", h3_store_v[i % 2])
        ctx.wait("tensor", V, cast_v[ib])
        for tg in range(NG):
            pA = t["ps"][4] if tg % 2 == 0 else t["ps"][2]
            pB = t["ps"][5] if tg % 2 == 0 else t["ps"][3]
            if tg >= 2:
                ctx.wait("tensor", V, h2v[tg % 2])
            for k in range(DC):
                ctx.op("tensor", lambda e, k=k, tg=tg, pA=pA, ib=ib: e.matmul(
                    out=pA[:], lhsT=wqg[ib][:, k * 128:(k + 1) * 128],
                    rhs=xq4[:, k * T + tg * TG: k * T + (tg + 1) * TG],
                    start=(k == 0), stop=(k == DC - 1)), None)
            for k in range(DC):
                ctx.op("tensor", lambda e, k=k, tg=tg, pB=pB, ib=ib: e.matmul(
                    out=pB[:], lhsT=wqu[ib][:, k * 128:(k + 1) * 128],
                    rhs=xq4[:, k * T + tg * TG: k * T + (tg + 1) * TG],
                    start=(k == 0), stop=(k == DC - 1)), MM if k == DC - 1 else None)
            mmv = ctx.cnt[MM]
            ctx.wait("scalar", MM, mmv)
            sv = ctx.op("scalar", lambda e, tg=tg, pA=pA: e.activation(
                out=silu_t[tg % 2][:], in_=pA[:], func=AF.Silu,
                scale=1.0 / QMAX), SC)
            ctx.wait("vector", SC, sv)
            h2v[tg % 2] = ctx.op("vector", lambda e, tg=tg, i=i, pB=pB: e.tensor_tensor(
                out=h3sl[i % 2][:, tg * TG:(tg + 1) * TG], in0=pB[:],
                in1=silu_t[tg % 2][:], op=ALU.mult), V)
        ctx.wait("gpsimd", V, h2v[(NG - 1) % 2])
        h3_store_v[i % 2] = ctx.dma(
            "gpsimd", d["h3s"].ap()[i * 128:(i + 1) * 128, :], h3sl[i % 2][:], "H3S")
    if upto.startswith("E"):
        ioff = {"E": 0, "E2": 8, "E3": 16}[upto]
        ctx.wait("gpsimd", "H3S", ctx.cnt["H3S"])
        for i in range(DC):
            ctx.dma("gpsimd", d["dbg3"].ap()[i * 128:(i + 1) * 128, :],
                    d["h3s"].ap()[(ioff + i) * 128:(ioff + i + 1) * 128, :], "X4S")
        ctx.drain_dma(["X4S", "ST", LD, "WL", "H3S"])
        ctx.barrier()
        es.close()
        return
    es.close()

    es2 = ExitStack()
    wdqT = ctx.sbuf("p3_wdqT", [128, IC1 * 1024], BF16, es2)
    h3q = [ctx.sbuf(f"p3_h3q{i}", [128, IC1 * TG], BF16, es2) for i in range(2)]
    wTsl = [ctx.sbuf(f"p3_wTsl{i}", [128, 1024], I32, es2) for i in range(2)]
    ctx.wait("sync", "H3S", ctx.cnt["H3S"])
    prev_v = ctx.cnt[V]
    cast3 = [prev_v, prev_v]
    wlv3 = [0, 0]
    def p3d_load(i):
        ib = i % 2
        ctx.wait("sync", V, cast3[ib])
        wlv3[ib] = ctx.dma("sync", wTsl[ib][:], d["w1dT"].ap()[i * 128:(i + 1) * 128, :], "WL")
    p3d_load(0)
    for i in range(IC1):
        ib = i % 2
        if i + 1 < IC1:
            p3d_load(i + 1)
        ctx.wait("vector", "WL", wlv3[ib])
        prev_v = ctx.op("vector", lambda e, i=i, ib=ib: e.tensor_copy(
            out=wdqT[:, i * 1024:(i + 1) * 1024], in_=wTsl[ib][:]), V)
        cast3[ib] = prev_v
    ctx.wait("sync", "H3S", ctx.cnt["H3S"])
    dmm = 0
    dmm_b = [0, 0]
    h3lv = [0, 0]
    last_vv = prev_v
    jj = 0
    def p3h_load(tg):
        tb = tg % 2
        ctx.wait("sync", MM, dmm_b[tb])
        for i in range(IC1):
            h3lv[tb] = ctx.dma("sync", h3q[tb][:, i * TG:(i + 1) * TG],
                               d["h3s"].ap()[i * 128:(i + 1) * 128, tg * TG:(tg + 1) * TG], "H3L")
    p3h_load(0)
    for tg in range(NG):
        tb = tg % 2
        if tg + 1 < NG:
            p3h_load(tg + 1)
        if tg == 0:
            ctx.wait("tensor", V, prev_v)
        ctx.wait("tensor", "H3L", h3lv[tb])
        for hh in range(2):
            if not (tg == 0 and hh == 0):
                ctx.wait("tensor", V, last_vv)
            for i in range(IC1):
                for h4 in range(4):
                    h = hh * 4 + h4
                    ctx.op("tensor", lambda e, i=i, h=h, h4=h4, tb=tb: e.matmul(
                        out=t["ps"][h4][:], lhsT=wdqT[:, i * 1024 + h * 128:
                                                      i * 1024 + (h + 1) * 128],
                        rhs=h3q[tb][:, i * TG:(i + 1) * TG],
                        start=(i == 0), stop=(i == IC1 - 1)),
                        MM if (i == IC1 - 1 and h4 == 3) else None)
            dmm = ctx.cnt[MM]
            dmm_b[tb] = dmm
            ctx.wait("sync", V, last_vv)
            lv = 0
            for h4 in range(4):
                h = hh * 4 + h4
                lv = ctx.dma("sync", t["stage"][:, h * TG:(h + 1) * TG],
                             d["x3T"].ap()[h * 128:(h + 1) * 128, tg * TG:(tg + 1) * TG], LD)
            ctx.wait("vector", LD, lv)
            ctx.wait("vector", MM, dmm)
            for h4 in range(4):
                h = hh * 4 + h4
                if jj >= 2:
                    ctx.wait("vector", "X4S", 16 * (jj - 1))
                if upto == "G2":
                    vv = ctx.op("vector", lambda e, h4=h4, jj=jj: e.tensor_scalar(
                        out=t["xout"][jj % 2][:], in0=t["ps"][h4][:],
                        scalar1=1.0 / (QMAX * QMAX), scalar2=None, op0=ALU.mult), V)
                else:
                    vv = ctx.op("vector", lambda e, h=h, h4=h4, jj=jj: e.scalar_tensor_tensor(
                        out=t["xout"][jj % 2][:], in0=t["ps"][h4][:],
                        scalar=1.0 / (QMAX * QMAX), in1=t["stage"][:, h * TG:(h + 1) * TG],
                        op0=ALU.mult, op1=ALU.add), V)
                last_vv = vv
                ctx.wait("gpsimd", V, vv)
                ctx.dma("gpsimd", d["x4T"].ap()[h * 128:(h + 1) * 128, tg * TG:(tg + 1) * TG],
                        t["xout"][jj % 2][:], "X4S")
                jj += 1
    ctx.drain_dma(["X4S", LD, "WL", "H3L", "H3S"])
    ctx.barrier()
    es2.close()


def declare_p3(nc, d, x3_is_input):
    for name, shape, dt in [
            ("w1gT", [D, 3072], I32), ("w1uT", [D, 3072], I32),
            ("w1dT", [3072, D], I32)]:
        d[name] = nc.dram_tensor(name, shape, dt, kind="ExternalInput")
    if x3_is_input:
        d["x3T"] = nc.dram_tensor("x3T", [D, T], F32, kind="ExternalInput")
        for name, shape, dt in [
                ("ones_bf", [128, 128], BF16), ("ident_f", [128, 128], F32),
                ("ident_b", [128, 128], BF16)]:
            d[name] = nc.dram_tensor(name, shape, dt, kind="ExternalInput")
    d["h3s"] = nc.dram_tensor("h3s", [3072, T], BF16)
    d["x4T"] = nc.dram_tensor("x4T", [D, T], F32, kind="ExternalOutput")
    d["dbg3"] = nc.dram_tensor("dbg3", [D, T], F32, kind="ExternalOutput")


def build_p3(upto="G"):
    nc = bass.Bass("TRN2", target_bir_lowering=False)
    d = {}
    declare_p3(nc, d, x3_is_input=True)
    es = ExitStack()
    ctx = Ctx(nc, es)
    with es:
        t = alloc_common(ctx)
        emit_phase3(ctx, t, d, load_consts=True, upto=upto)
        ctx.build_block()
    return nc



def host_prep(inputs):
    c = {}
    c["WdT"] = np.ascontiguousarray(inputs["L0_attn_W_delta"].T)
    c["bdelta8"] = np.ascontiguousarray(inputs["L0_attn_b_delta"].reshape(8, 128).T)
    c["WbT128"] = np.ascontiguousarray(
        inputs["L0_attn_W_B"].T.reshape(8, 128, 16).transpose(1, 0, 2).reshape(128, 128))
    c["bB"] = np.ascontiguousarray(inputs["L0_attn_b_B"].reshape(16, 1))
    c["Alog"] = np.ascontiguousarray(inputs["L0_attn_A_log"].reshape(16, 1))
    c["Dcol8"] = np.ascontiguousarray(inputs["L0_attn_D_param"].reshape(8, 128).T)
    c["WoT"] = np.ascontiguousarray(inputs["L0_attn_W_out"].T)
    c["bout8"] = np.ascontiguousarray(inputs["L0_attn_b_out"].reshape(8, 128).T)
    c["ones_bf"] = np.ones((128, 128), dtype=BF)
    c["ident_f"] = np.eye(128, dtype=np.float32)
    c["ident_b"] = np.eye(128, dtype=BF)
    c["clip2"] = np.concatenate([inputs["L0_ffn_clip_x"],
                                 inputs["L0_ffn_clip_down"]]).reshape(1, 2).astype(np.float32)
    c["wg_nat"] = np.ascontiguousarray(inputs["L0_ffn_w_gate"])
    c["wu_nat"] = np.ascontiguousarray(inputs["L0_ffn_w_up"])
    c["wgT"] = np.ascontiguousarray(inputs["L0_ffn_w_gate"].T)
    c["wuT"] = np.ascontiguousarray(inputs["L0_ffn_w_up"].T)
    c["wd_nat"] = np.ascontiguousarray(inputs["L0_ffn_w_down"])
    c["wdT"] = np.ascontiguousarray(inputs["L0_ffn_w_down"].T)
    c["wqkv"] = np.ascontiguousarray(inputs["L1_attn_w_qkv"])
    c["wproj"] = np.ascontiguousarray(inputs["L1_attn_w_proj"])
    c["bqkv24"] = np.ascontiguousarray(inputs["L1_attn_b_qkv"].reshape(24, 128).T)
    c["bvrow"] = np.ascontiguousarray(inputs["L1_attn_b_qkv"][2048:].reshape(1, 1024))
    c["bproj8"] = np.ascontiguousarray(inputs["L1_attn_b_proj"].reshape(8, 128).T)
    c["g8"] = np.ascontiguousarray(inputs["L1_attn_g_norm"].reshape(8, 128).T)
    half = D // 2
    invf = 1.0 / 10000.0 ** (np.arange(half) / half)
    ang = np.arange(S)[:, None] * invf[None, :]
    cosr = np.round(np.cos(ang)).astype(BF)
    sinr = np.round(np.sin(ang)).astype(BF)
    c["ropeC"] = np.ascontiguousarray(cosr.T)
    c["ropeSp"] = np.ascontiguousarray(sinr.T)
    c["ropeSn"] = np.ascontiguousarray((-sinr).T)
    c["w1gT"] = np.ascontiguousarray(inputs["L1_ffn_w_gate"].T)
    c["w1uT"] = np.ascontiguousarray(inputs["L1_ffn_w_up"].T)
    c["w1dT"] = np.ascontiguousarray(inputs["L1_ffn_w_down"].T)
    return c


_P0_KEYS = ["ones_bf", "ident_f", "ident_b", "x0T", "WdT", "bdelta8", "WbT128",
            "bB", "Alog", "Dcol8", "WoT", "bout8"]
_P1_KEYS = ["ones_bf", "ident_f", "ident_b", "clip2", "wg_nat", "wu_nat",
            "wgT", "wuT", "wd_nat", "wdT"]
_P3_KEYS = ["ones_bf", "ident_f", "ident_b", "w1gT", "w1uT", "w1dT"]
_P2_KEYS = ["ones_bf", "ident_f", "ident_b", "wqkv", "wproj", "bqkv24", "bvrow",
            "bproj8", "g8", "ropeC", "ropeSp", "ropeSn"]


def declare_p0(nc, d):
    for name, shape, dt in [
            ("ones_bf", [128, 128], BF16), ("ident_f", [128, 128], F32),
            ("ident_b", [128, 128], BF16), ("x0T", [D, T], F32),
            ("WdT", [D, D], F32), ("bdelta8", [128, 8], F32),
            ("WbT128", [128, 128], F32), ("bB", [16, 1], F32),
            ("Alog", [16, 1], F32), ("Dcol8", [128, 8], F32),
            ("WoT", [16, D], F32), ("bout8", [128, 8], F32)]:
        d[name] = nc.dram_tensor(name, shape, dt, kind="ExternalInput")


def declare_p1(nc, d, x1_is_input):
    for name, shape, dt in [
            ("clip2", [1, 2], F32), ("wg_nat", [2816, D], F32),
            ("wu_nat", [2816, D], F32), ("wgT", [D, 2816], F32),
            ("wuT", [D, 2816], F32), ("wd_nat", [D, 2816], F32),
            ("wdT", [2816, D], F32)]:
        d[name] = nc.dram_tensor(name, shape, dt, kind="ExternalInput")
    if x1_is_input:
        d["x1T"] = nc.dram_tensor("x1T", [D, T], F32, kind="ExternalInput")
        for name, shape, dt in [
                ("ones_bf", [128, 128], BF16), ("ident_f", [128, 128], F32),
                ("ident_b", [128, 128], BF16)]:
            d[name] = nc.dram_tensor(name, shape, dt, kind="ExternalInput")
    d["lnfs"] = nc.dram_tensor("lnfs", [D, T], F32)
    d["h2s"] = nc.dram_tensor("h2s", [2816, T], F32)
    d["cci"] = nc.dram_tensor("cci", [1, 1], F32)
    d["cco"] = nc.dram_tensor("cco", [1, 1], F32)
    d["x2T"] = nc.dram_tensor("x2T", [D, T], F32, kind="ExternalOutput")
    d["dbg"] = nc.dram_tensor("dbg", [D, T], F32, kind="ExternalOutput")


def build_p0():
    nc = bass.Bass("TRN2", target_bir_lowering=False)
    d = {}
    declare_p0(nc, d)
    d["x1T"] = nc.dram_tensor("x1T", [D, T], F32, kind="ExternalOutput")
    es = ExitStack()
    ctx = Ctx(nc, es)
    with es:
        t = alloc_common(ctx)
        emit_phase0(ctx, t, d)
        ctx.build_block()
    return nc


def build_p1(upto="G"):
    nc = bass.Bass("TRN2", target_bir_lowering=False)
    d = {}
    declare_p1(nc, d, x1_is_input=True)
    es = ExitStack()
    ctx = Ctx(nc, es)
    with es:
        t = alloc_common(ctx)
        emit_phase1(ctx, t, d, load_consts=True, upto=upto)
        ctx.build_block()
    return nc


def kernel_unfused(**inputs):
    x = np.asarray(inputs["L0_attn_x"])
    com = host_prep(inputs)
    xouts = {}
    nc0 = build_p0()
    maps = []
    for core in range(NCORE):
        m = {k: com[k] for k in _P0_KEYS if k != "x0T"}
        xc = x[core * SPC:(core + 1) * SPC].reshape(T, D)
        m["x0T"] = np.ascontiguousarray(xc.T)
        maps.append(m)
    r0 = run_bass_kernel_spmd(nc0, maps, core_ids=list(range(NCORE)))
    nc1 = build_p1()
    maps = []
    for core in range(NCORE):
        m = {k: com[k] for k in _P1_KEYS}
        m["x1T"] = np.asarray(r0.results[core]["x1T"])
        maps.append(m)
    r1 = run_bass_kernel_spmd(nc1, maps, core_ids=list(range(NCORE)))
    nc2 = build_p2()
    maps = []
    for core in range(NCORE):
        m = {k: com[k] for k in _P2_KEYS}
        m["x2T"] = np.asarray(r1.results[core]["x2T"])
        maps.append(m)
    r2 = run_bass_kernel_spmd(nc2, maps, core_ids=list(range(NCORE)))
    nc3 = build_p3()
    maps = []
    for core in range(NCORE):
        m = {k: com[k] for k in _P3_KEYS}
        m["x3T"] = np.asarray(r2.results[core]["x3T"])
        maps.append(m)
    r3 = run_bass_kernel_spmd(nc3, maps, core_ids=list(range(NCORE)))
    out = np.empty((B, S, D), dtype=np.float32)
    for core in range(NCORE):
        out[core * SPC:(core + 1) * SPC] = (
            np.asarray(r3.results[core]["x4T"]).T.reshape(SPC, S, D))
    return out


def build_fused(dbg=False, nphases=4, p1upto="G"):
    nc = bass.Bass("TRN2", target_bir_lowering=False)
    d = {}
    declare_p0(nc, d)
    declare_p1(nc, d, x1_is_input=False)
    declare_p2(nc, d, x2_is_input=False)
    declare_p3(nc, d, x3_is_input=False)
    kind = "ExternalOutput" if dbg else "Internal"
    d["x1T"] = nc.dram_tensor("x1T", [D, T], F32, kind=kind)
    d["x2T"] = nc.dram_tensor("x2T_i", [D, T], F32, kind=kind)
    d["x3T"] = nc.dram_tensor("x3T_i", [D, T], F32, kind=kind)
    es = ExitStack()
    ctx = Ctx(nc, es)
    with es:
        t = alloc_common(ctx)
        emit_phase0(ctx, t, d)
        if nphases == -1:
            ctx.barrier()
        if nphases == -2:
            for nm in ("H2S", "CC", "CD", "H2LA", "H2LB", "X2S"):
                ctx.sem(nm)
        if nphases == -3:
            cols = ctx.sbuf("p1_cols", [128, 8], F32)
            es3 = ExitStack()
            a1 = ctx.sbuf("p1_lnf", [128, DC * TG], F32, es3)
            a2 = ctx.sbuf("p1_xint", [128, DC * T], BF16, es3)
            a3 = [ctx.sbuf(f"p1_h2{i}", [128, T], F32, es3) for i in range(2)]
            ctx.barrier()
            es3.close()
        if nphases == -4:
            ctx.drain_dma(["ST", "LD", "WL"])
            ctx.barrier()
        if nphases == -5:
            cols = ctx.sbuf("p1_cols", [128, 8], F32)
            es3 = ExitStack()
            names = [("p1_lnf", [128, DC * TG], F32), ("p1_xint", [128, DC * T], BF16),
                     ("p1_h20", [128, T], F32), ("p1_h21", [128, T], F32),
                     ("p1_wnat", [128, 2 * D], F32), ("p1_wTg", [128, DC * 128], F32),
                     ("p1_wTu", [128, DC * 128], F32), ("p1_wqg", [128, DC * 128], BF16),
                     ("p1_wqu", [128, DC * 128], BF16), ("p1_itmp", [128, 1024], I32),
                     ("p1_rig", [128, 128], F32), ("p1_riu", [128, 128], F32),
                     ("p1_rc2", [128, 2], BF16), ("p1_rr2", [1, 256], BF16),
                     ("p1_am2", [128, 1], F32), ("p1_cx", [1, 2], F32),
                     ("p1_si0", [128, TG], F32), ("p1_si1", [128, TG], F32)]
            for nm_, sh_, dt_ in names:
                ctx.sbuf(nm_, sh_, dt_, es3)
            ctx.barrier()
            es3.close()
        if nphases > 1:
            emit_phase1(ctx, t, d, upto=p1upto)
        if nphases > 2:
            emit_phase2(ctx, t, d)
        if nphases > 3:
            emit_phase3(ctx, t, d)
        ctx.build_block()
    return nc


_ALL_KEYS = sorted(set(_P0_KEYS + _P1_KEYS + _P2_KEYS + _P3_KEYS))


def kernel(**inputs):
    x = np.asarray(inputs["L0_attn_x"])
    com = host_prep(inputs)
    nc = build_fused()
    maps = []
    for core in range(NCORE):
        m = {k: com[k] for k in _ALL_KEYS if k != "x0T"}
        xc = x[core * SPC:(core + 1) * SPC].reshape(T, D)
        m["x0T"] = np.ascontiguousarray(xc.T)
        maps.append(m)
    r = run_bass_kernel_spmd(nc, maps, core_ids=list(range(NCORE)))
    out = np.empty((B, S, D), dtype=np.float32)
    for core in range(NCORE):
        out[core * SPC:(core + 1) * SPC] = (
            np.asarray(r.results[core]["x4T"]).T.reshape(SPC, S, D))
    return out
